# version 15
# baseline (speedup 1.0000x reference)
import numpy as np
import concourse.bass as bass
import concourse.mybir as mybir
from concourse.alu_op_type import AluOpType as OP
from concourse.bass_utils import run_bass_kernel_spmd
from contextlib import ExitStack

F32 = mybir.dt.float32
F16 = mybir.dt.float16
ACT = mybir.ActivationFunctionType
AX = mybir.AxisListType

NCORES = 8
D = 1024
HC = 8
DFF = 3072
OCF = 24
D2 = 2048
KC2 = 16
NH = 8
MAGIC = 12582912.0
P23 = 8388608.0
QA = 32767.0
QW = 7.0
LN_EPS = 1e-5
ATTN_EPS = 1e-6

SC_RY, SC_SY = 0, 1
SC_RU, SC_SU = 2, 3
SC_RG, SC_SG, SC_BG = 4, 5, 6
SC_R2, SC_S2 = 7, 8
SC_RO, SC_SO = 9, 10
SC_HSO = 11


class MK:
    def __init__(self, T, L):
        self.T = T
        self.L = L
        self.ntt = T // 512
        self.nch = T // 128
        self.TE = min(T, 2048)
        self.cnt = {}
        self.sems = {}
        self.gatev = {}
        self.nc = bass.Bass("TRN2", target_bir_lowering=False, debug=False)

    def mksem(self, name):
        self.sems[name] = self.es.enter_context(self.nc.semaphore(name))
        self.cnt[name] = 0

    def inc(self, name, instr, n=1):
        instr.then_inc(self.sems[name], n)
        self.cnt[name] += n
        return self.cnt[name]

    def w(self, eng, name, v=None):
        eng.wait_ge(self.sems[name], self.cnt[name] if v is None else v)

    def snap(self):
        self.gatev = {e: self.cnt["ms" + e] for e in ("v", "s", "t", "g", "y")}

    def mkring(self, name, R):
        for r in range(R):
            self.mksem(f"{name}{r}")
        return name

    def rsem(self, name, i, R):
        return self.sems[f"{name}{i % R}"]

    def rissue(self, name, i, R, instr, n=16):
        nm = f"{name}{i % R}"
        instr.then_inc(self.sems[nm], n)
        self.cnt[nm] += n

    def rsnap(self, name, R):
        return [self.cnt[f"{name}{r}"] for r in range(R)]

    def rwait(self, eng, name, i, R, snap, per, occ=None):
        o = (i // R if occ is None else occ)
        eng.wait_ge(self.sems[f"{name}{i % R}"], snap[i % R] + per * (o + 1))

    def rtotal(self, eng, name, R):
        for r in range(R):
            nm = f"{name}{r}"
            eng.wait_ge(self.sems[nm], self.cnt[nm])

    def entry(self, eng, me):
        for e in ("v", "s", "t", "g", "y"):
            if e != me and self.gatev.get(e, 0) > 0:
                eng.wait_ge(self.sems["ms" + e], self.gatev[e])

    def f32(self, buf, lo16, n32):
        return buf[:, lo16:lo16 + 2 * n32].bitcast(F32)

    def pdh(self, i, b):
        return self.pd[i][:, b * 512:(b + 1) * 512].bitcast(F16)

    def pbh(self, i):
        return self.pb[i][:, :].bitcast(F16)

    def build(self):
        nc = self.nc
        T = self.T
        es = ExitStack()
        self.es = es
        with es:
            E = es.enter_context
            gx = lambda n, s, dt=F32, k="ExternalInput": nc.dram_tensor(n, s, dt, kind=k).ap()
            gi = lambda n, s, dt=F32: nc.dram_tensor(n, s, dt).ap()
            self.dx0 = gx("x0", [T, D])
            self.dwq = gx("wq", [D, D])
            self.dwm = gx("wm", [D, D])
            self.dw1 = gx("w1", [D, D2])
            self.dw2 = gx("w2", [D2, D])
            self.dg1 = gx("g1", [1, D])
            self.db1 = gx("b1", [1, D])
            self.dg2 = gx("g2", [1, D])
            self.db2 = gx("b2", [1, D])
            self.dident = gx("ident", [128, 128])
            self.dout = gx("out", [T, D], F32, "ExternalOutput")
            self.dxx1 = gi("xx1", [T, D])
            self.dxx2 = gi("xx2", [T, D])
            self.dy2sp = gi("y2sp", [T, D], F16)
            self.dya1 = gi("ya1", [T, D], F16)
            self.dupsp = gi("upsp", [DFF, T], F16)
            self.dgatesp = gi("gatesp", [DFF, T], F16)
            self.dg2sp = gi("g2sp", [DFF, T], F16)
            self.dosp = gi("osp", [DFF, T], F16)
            self.doqsp = gi("oqsp", [DFF, T], F16)
            self.du1sp = gi("u1sp", [D2, T], F16)
            self.dwq16 = gi("wq16", [D, D], F16)
            self.dwm16 = gi("wm16", [D, D], F16)
            self.dw116 = gi("w116", [D, D2], F16)
            self.dw216 = gi("w216", [D2, D], F16)
            self.dwsh = gx("wsh", [3 * 128, DFF])
            self.dwall_sh = gi("wall_sh", [3, OCF, 128, 128], F16)
            self.dwall = gi("wall", [OCF, OCF, 128, 128], F16)
            self.ddcol = gi("dcol", [128, 2])
            self.dbm_in = gi("bm_in", [1, 2])
            self.dbm_out = gi("bm_out", [1, 2])
            self.dbscal = gi("bscal", [1, 16])
            self.dkv_in = gi("kv_in", [128, NH * 128 + NH])
            self.dkv_out = gi("kv_out", [128, NH * 128 + NH])
            self.dzl = gi("dzl", [1, 2 * NH * 512], F16)
            self.did16 = gi("id16d", [128, 128], F16)

            ACOLS = max(HC * T, OCF * D, 31744)
            self.ACOLS = ACOLS
            self.A16 = E(nc.sbuf_tensor("A16", [128, ACOLS], F16))
            self.B16 = E(nc.sbuf_tensor("B16", [128, ACOLS], F16))
            self.C16 = E(nc.sbuf_tensor("C16", [128, 18432], F16))
            self.WAR = E(nc.sbuf_tensor("WAR", [128, 16384], F16))
            self.SC = E(nc.sbuf_tensor("SCt", [128, 16], F32))
            self.MAGCOL = E(nc.sbuf_tensor("MAGCOL", [128, 1], F32))
            self.MAGNEG = E(nc.sbuf_tensor("MAGNEG", [128, 1], F32))
            self.EPSCOL = E(nc.sbuf_tensor("EPSCOL", [128, 1], F32))
            self.srow = E(nc.sbuf_tensor("srow", [2, 128], F32))
            self.gmaxr = E(nc.sbuf_tensor("gmaxr", [2, 16], F32))
            self.dervr = E(nc.sbuf_tensor("dervr", [1, 16], F32))
            self.id16 = E(nc.sbuf_tensor("id16", [128, 128], F16))
            self.ones16 = E(nc.sbuf_tensor("ones16", [128, 1], F16))
            self.onesr16 = E(nc.sbuf_tensor("onesr16", [1, 128], F16))
            self.g1r = E(nc.sbuf_tensor("g1r", [128, D], F16))
            self.g2r = E(nc.sbuf_tensor("g2r", [128, D], F16))
            self.b1r = E(nc.sbuf_tensor("b1r", [128, D], F16))
            self.b2r = E(nc.sbuf_tensor("b2r", [128, D], F16))
            self.NLN = 4
            mkln = lambda p: {k: E(nc.sbuf_tensor(p + k, [128, self.NLN], F32))
                              for k in ["sum", "sq", "mu", "mu2", "vpe",
                                        "std", "rstd"]}
            self.LNa = mkln("lna")
            self.LNb = mkln("lnb")
            self.rmaxA = E(nc.sbuf_tensor("rmaxA", [128, 1], F32))
            self.rmaxB = E(nc.sbuf_tensor("rmaxB", [128, 1], F32))
            self.tmax = E(nc.sbuf_tensor("tmax", [128, 8], F32))

            self.pd = [E(nc.psum_tensor(f"pd{i}", [128, 1024], F32))
                       for i in range(3)]
            self.pb = [E(nc.psum_tensor(f"pb{i}", [128, 512], F32))
                       for i in range(2)]

            for s in ["dL", "dSP", "dW", "dG", "dE", "dX", "dO", "dY", "dZ",
                      "dI",
                      "msv", "mss", "mst", "msg", "msy",
                      "bar", "cc", "kvb", "lnv", "lns", "wtg",
                      "fua", "fub", "fuc", "fud", "fue", "fuf",
                      "pha", "phb", "phc", "phd", "phe", "phf", "phg"]:
                self.mksem(s)
            for name, R in [("p0l", 2), ("p0s", 2), ("yl", 4), ("ugw", 4),
                            ("ugs", 2), ("e1i", 3), ("e1o", 2), ("e2i", 3),
                            ("e2o", 2), ("dqr", 6), ("dqo", 2), ("dxl", 4),
                            ("dxs", 2), ("dys", 2), ("ucc", 2),
                            ("ylg", 2)]:
                self.mkring(name, R)

            self.blk = E(nc.Block())
            self.emit_all()
        return nc

    def ln_v_pre(self, v, k, LNS, xin):
        v.tensor_reduce(out=LNS["sum"][:, k:k + 1], in_=xin, axis=AX.X,
                        op=OP.add)
        v.drain()
        v.tensor_scalar(out=LNS["mu"][:, k:k + 1], in0=LNS["sum"][:, k:k + 1],
                        scalar1=1.0 / D, scalar2=None, op0=OP.mult)
        return self.inc("lnv", v.drain())

    def ln_s(self, s, k, LNS, xin, sqscr, lnv_val):
        s.wait_ge(self.sems["lnv"], lnv_val)
        s.activation(out=sqscr, in_=xin, func=ACT.Square,
                     accum_out=LNS["sq"][:, k:k + 1])
        s.drain()
        s.activation(out=LNS["mu2"][:, k:k + 1], in_=LNS["sum"][:, k:k + 1],
                     func=ACT.Square, scale=1.0 / D)
        s.drain()
        s.activation(out=LNS["vpe"][:, k:k + 1], in_=LNS["mu2"][:, k:k + 1],
                     func=ACT.Identity, scale=-1.0, bias=self.EPSCOL[:, 0:1])
        s.drain()
        s.activation(out=LNS["vpe"][:, k:k + 1], in_=LNS["sq"][:, k:k + 1],
                     func=ACT.Identity, scale=1.0 / D,
                     bias=LNS["vpe"][:, k:k + 1])
        s.drain()
        s.activation(out=LNS["std"][:, k:k + 1], in_=LNS["vpe"][:, k:k + 1],
                     func=ACT.Sqrt)
        return self.inc("lns", s.drain())

    def ln_v_apply(self, v, k, LNS, xin, out_ap, lns_val):
        v.wait_ge(self.sems["lns"], lns_val)
        v.reciprocal(out=LNS["rstd"][:, k:k + 1], in_=LNS["std"][:, k:k + 1])
        v.drain()
        return v.tensor_scalar(out=out_ap, in0=xin,
                               scalar1=LNS["mu"][:, k:k + 1],
                               scalar2=LNS["rstd"][:, k:k + 1],
                               op0=OP.subtract, op1=OP.mult)

    def absfold(self, v, tile, slot, rmax):
        ts = self.tmax[:, slot % 8:slot % 8 + 1]
        v.tensor_reduce(out=ts, in_=tile, axis=AX.X, op=OP.max,
                        apply_absolute_value=True)
        v.drain()
        v.tensor_tensor(out=rmax[:], in0=rmax[:], in1=ts, op=OP.max)
        v.drain()

    def emit_all(self):
        self.p0()
        self.phase_y(self.dx0, split_loads=False)
        self.barrier([(self.rmaxA, [(SC_RY, "rinv"), (SC_SY, "s")])])
        self.phase_qy()
        self.phase_ug_gate()
        self.barrier([(self.rmaxB, [(SC_RG, "rinv"), (SC_SG, "s"),
                                    (SC_BG, "bg")])])
        self.phase_ug_up_e1()
        self.barrier([(self.rmaxB, [(SC_RU, "rinv"), (SC_SU, "s")]),
                      (self.rmaxA, [(SC_R2, "rinv"), (SC_S2, "s")])])
        self.phase_e2()
        self.barrier([(self.rmaxA, [(SC_RO, "rinv"), (SC_SO, "s"),
                                    (SC_HSO, "hs")])])
        self.phase_d(self.dx0, self.dxx1, fuse_ln=True)
        self.phase_q()
        self.phase_kvx()
        self.phase_m1()
        self.phase_m2a()
        self.phase_m2b()
        self.phase_m2c()
        self.phase_y(self.dxx2)
        self.barrier([(self.rmaxA, [(SC_RY, "rinv"), (SC_SY, "s")])])
        self.phase_qy()
        self.phase_ug_gate()
        self.barrier([(self.rmaxB, [(SC_RG, "rinv"), (SC_SG, "s"),
                                    (SC_BG, "bg")])])
        self.phase_ug_up_e1()
        self.barrier([(self.rmaxB, [(SC_RU, "rinv"), (SC_SU, "s")]),
                      (self.rmaxA, [(SC_R2, "rinv"), (SC_S2, "s")])])
        self.phase_e2()
        self.barrier([(self.rmaxA, [(SC_RO, "rinv"), (SC_SO, "s"),
                                    (SC_HSO, "hs")])])
        self.phase_d(self.dxx2, self.dout, fuse_ln=False)
        self.finale()

    def p0(self):
        mk = self
        nun = 3

        def fg(g):
            g.dma_start(out=mk.did16, in_=mk.dident).then_inc(mk.sems["dE"], 16)
            mk.cnt["dE"] += 16
            for (srcw, dst) in [(mk.dwq, mk.dwq16), (mk.dwm, mk.dwm16),
                                (mk.dw1, mk.dw116), (mk.dw2, mk.dw216)]:
                g.dma_start(out=dst, in_=srcw).then_inc(mk.sems["dL"], 16)
                mk.cnt["dL"] += 16
            for (srcw, dst) in [(mk.dg1, mk.g1r), (mk.db1, mk.b1r),
                                (mk.dg2, mk.g2r), (mk.db2, mk.b2r)]:
                g.dma_start(out=dst[:], in_=srcw.partition_broadcast(128))\
                    .then_inc(mk.sems["dL"], 16)
                mk.cnt["dL"] += 16
            mk.inc("msg", g.nop())
            mk.w(g, "dE")
            g.dma_start(out=mk.id16[:], in_=mk.did16).then_inc(mk.sems["dI"], 16)
            mk.cnt["dI"] += 16
            g.wait_ge(mk.sems["phe"], 1)
            ic = mk.inc("cc", g.collective_compute(
                "AllGather", OP.bypass,
                replica_groups=[list(range(NCORES))],
                ins=[mk.dwall_sh.opt()], outs=[mk.dwall.opt()]))
            g.wait_ge(mk.sems["cc"], ic)
            mk.inc("wtg", g.drain())

        def fsync(sync):
            for i in range(nun):
                if i >= 2:
                    sync.wait_ge(mk.sems["pha"], i - 1)
                dst = mk.f32(mk.C16, (i % 2) * (2 * DFF), DFF)
                mk.rissue("p0l", i, 2, sync.dma_start(
                    out=dst, in_=mk.dwsh[i * 128:(i + 1) * 128, :]))
            for i in range(nun):
                sync.wait_ge(mk.sems["phb"], 6 * (i + 1))
                stg = mk.A16[:, (i % 2) * 3072:(i % 2) * 3072 + 3072]
                mk.rissue("p0s", i, 2, sync.dma_start(
                    out=mk.dwall_sh[i].rearrange("b p o -> p b o"),
                    in_=stg.rearrange("p (b o) -> p b o", o=128)))
            mk.rtotal(sync, "p0s", 2)
            mk.rtotal(sync, "p0l", 2)
            ins = sync.nop()
            mk.inc("phe", ins)
            mk.inc("msy", sync.nop())

        def fv(v):
            v.memset(mk.MAGCOL[:], MAGIC)
            v.memset(mk.MAGNEG[:], -MAGIC)
            v.memset(mk.EPSCOL[:], LN_EPS)
            v.memset(mk.ones16[:], 1.0)
            v.memset(mk.onesr16[:], 1.0)
            v.memset(mk.dervr[:], 1.0)
            v.memset(mk.gmaxr[:], 1.0)
            v.drain()
            p0lsnap = [0, 0]
            for i in range(nun):
                nb = DFF // 32
                mk.rwait(v, "p0l", i, 2, p0lsnap, 16)
                if i >= 2:
                    v.wait_ge(mk.sems["phd"], 6 * (i - 1))
                wch = mk.f32(mk.C16, (i % 2) * (2 * DFF), DFF)
                bm = mk.f32(mk.C16, 12288, 96)
                v.tensor_reduce(out=bm,
                                in_=wch.rearrange("p (a b) -> p a b", b=32),
                                axis=AX.X, op=OP.max, apply_absolute_value=True)
                v.drain()
                sb = mk.f32(mk.C16, 12480, 96)
                v.tensor_scalar(out=sb, in0=bm, scalar1=1.0 / QW,
                                scalar2=1e-12, op0=OP.mult, op1=OP.max)
                v.drain()
                rs = mk.f32(mk.C16, 12672, 96)
                v.reciprocal(out=rs, in_=sb)
                v.drain()
                t1 = mk.f32(mk.WAR, 0, DFF)
                v.tensor_tensor(out=t1.rearrange("p (a b) -> p a b", b=32),
                                in0=wch.rearrange("p (a b) -> p a b", b=32),
                                in1=rs.unsqueeze(2).broadcast_to([128, 96, 32]),
                                op=OP.mult)
                v.drain()
                v.tensor_scalar(out=t1, in0=t1, scalar1=MAGIC, scalar2=MAGIC,
                                op0=OP.add, op1=OP.subtract)
                v.drain()
                wq16 = mk.WAR[:, 8192 + (i % 2) * DFF:
                              8192 + (i % 2) * DFF + DFF]
                v.tensor_tensor(
                    out=wq16.rearrange("p (a b) -> p a b", b=32),
                    in0=t1.rearrange("p (a b) -> p a b", b=32),
                    in1=sb.unsqueeze(2).broadcast_to([128, 96, 32]),
                    op=OP.mult)
                mk.inc("pha", v.drain())
            mk.inc("msv", v.drain())

        def ft(t):
            t.wait_ge(mk.sems["dI"], 16)
            for i in range(nun):
                t.wait_ge(mk.sems["pha"], i + 1)
                srcw = mk.WAR[:, 8192 + (i % 2) * DFF:
                              8192 + (i % 2) * DFF + DFF]
                for gr in range(6):
                    gg = i * 6 + gr
                    if gg >= 2:
                        t.wait_ge(mk.sems["phb"], gg - 1)
                    last = None
                    for r in range(4):
                        rr = gr * 4 + r
                        last = t.transpose(
                            mk.pbh(gg % 2)[:, r * 128:(r + 1) * 128],
                            srcw[:, rr * 128:(rr + 1) * 128], mk.id16[:])
                    mk.inc("phd", last)
            mk.inc("mst", t.drain())

        def fs(s):
            for i in range(nun):
                for gr in range(6):
                    gg = i * 6 + gr
                    s.wait_ge(mk.sems["phd"], gg + 1)
                    if i >= 2 and gr == 0:
                        slot = i % 2
                        prev = 16 * sum(1 for k in range(i - 1)
                                        if k % 2 == slot)
                        s.wait_ge(mk.sems[f"p0s{slot}"], prev)
                    stg = mk.A16[:, (i % 2) * 3072 + gr * 512:
                                 (i % 2) * 3072 + gr * 512 + 512]
                    s.copy(out=stg, in_=mk.pbh(gg % 2)[:, 0:512])
                    mk.inc("phb", s.drain())
            mk.inc("mss", s.drain())

        self.blk.gpsimd(fg)
        self.blk.sync(fsync)
        self.blk.vector(fv)
        self.blk.tensor(ft)
        self.blk.scalar(fs)
        self.dg_p0 = self.cnt["dL"]
        self.snap()

    def barrier(self, quants):
        mk = self
        nq = len(quants)
        phb0 = self.cnt["phc"]

        def fg(g):
            mk.entry(g, "g")
            for qi, (rm, _) in enumerate(quants):
                g.dma_start(out=mk.ddcol[:, qi:qi + 1], in_=rm[:],
                            allow_slow_non_contiguous=True)\
                    .then_inc(mk.sems["dG"], 16)
                mk.cnt["dG"] += 16
            mk.w(g, "dG")
            g.dma_start(out=mk.srow[0:nq, :],
                        in_=mk.ddcol[:, 0:nq].rearrange("p q -> q p"),
                        allow_slow_non_contiguous=True)\
                .then_inc(mk.sems["dG"], 16)
            mk.cnt["dG"] += 16
            mk.w(g, "dG")
            mk.inc("phe", g.drain())
            g.wait_ge(mk.sems["phc"], phb0 + 1)
            g.dma_start(out=mk.dbm_in[0:1, 0:nq],
                        in_=mk.gmaxr[0:nq, 0:1]).then_inc(mk.sems["dG"], 16)
            mk.cnt["dG"] += 16
            mk.w(g, "dG")
            ic = mk.inc("cc", g.collective_compute(
                "AllReduce", OP.max,
                replica_groups=[list(range(NCORES))],
                ins=[mk.dbm_in[0:1, 0:nq].opt()],
                outs=[mk.dbm_out[0:1, 0:nq].opt()]))
            g.wait_ge(mk.sems["cc"], ic)
            g.dma_start(out=mk.gmaxr[0:1, 8:8 + nq],
                        in_=mk.dbm_out[0:1, 0:nq]).then_inc(mk.sems["dG"], 16)
            mk.cnt["dG"] += 16
            mk.w(g, "dG")
            mk.inc("phe", g.drain())
            g.wait_ge(mk.sems["phc"], phb0 + 2)
            g.dma_start(out=mk.dbscal[0:1, :], in_=mk.dervr[0:1, :])\
                .then_inc(mk.sems["dG"], 16)
            mk.cnt["dG"] += 16
            mk.w(g, "dG")
            g.dma_start(out=mk.SC[:, :],
                        in_=mk.dbscal.partition_broadcast(128))\
                .then_inc(mk.sems["dG"], 16)
            mk.cnt["dG"] += 16
            mk.w(g, "dG")
            mk.inc("bar", g.drain())
            mk.inc("msg", g.drain())

        phe0 = self.cnt["phe"]

        def fv(v):
            mk.entry(v, "v")
            v.wait_ge(mk.sems["phe"], phe0 + 1)
            v.tensor_reduce(out=mk.gmaxr[0:nq, 0:1], in_=mk.srow[0:nq, :],
                            axis=AX.X, op=OP.max)
            v.drain()
            mk.inc("phc", v.drain())
            v.wait_ge(mk.sems["phe"], phe0 + 2)
            for qi, (_, outs) in enumerate(quants):
                mcol = mk.gmaxr[0:1, 8 + qi:8 + qi + 1]
                scol = mk.dervr[0:1, 14:15]
                v.tensor_scalar(out=scol, in0=mcol, scalar1=1.0 / QA,
                                scalar2=1e-12, op0=OP.mult, op1=OP.max)
                v.drain()
                for colx, kind in outs:
                    if kind == "s":
                        v.tensor_scalar(out=mk.dervr[0:1, colx:colx + 1],
                                        in0=scol, scalar1=1.0, scalar2=None,
                                        op0=OP.mult)
                    elif kind == "rinv":
                        v.reciprocal(out=mk.dervr[0:1, colx:colx + 1],
                                     in_=scol)
                    elif kind == "bg":
                        v.tensor_scalar(out=mk.dervr[0:1, colx:colx + 1],
                                        in0=scol, scalar1=-MAGIC,
                                        scalar2=None, op0=OP.mult)
                    elif kind == "hs":
                        v.tensor_scalar(out=mk.dervr[0:1, colx:colx + 1],
                                        in0=scol, scalar1=0.5,
                                        scalar2=None, op0=OP.mult)
                    v.drain()
            mk.inc("phc", v.drain())
            mk.inc("msv", v.drain())

        self.blk.gpsimd(fg)
        self.blk.vector(fv)
        self.snap()

    def phase_y(self, src_dram, split_loads=True):
        mk = self
        nch = self.nch
        T = self.T
        ylsnap = self.rsnap("yl", 4)
        ylgsnap = self.rsnap("ylg", 2)
        lnv0 = self.cnt["lnv"]
        lns0 = self.cnt["lns"]
        pha0 = self.cnt["pha"]
        phb0 = self.cnt["phb"]
        phc0 = self.cnt["phc"]

        step = 2 if split_loads else 1

        def fload(sync):
            mk.entry(sync, "y")
            for ch in range(0, nch, step):
                if ch >= 4:
                    sync.wait_ge(mk.sems["pha"], pha0 + ch - 3)
                    sync.wait_ge(mk.sems["lns"], lns0 + ch - 3)
                xa = mk.f32(mk.B16, (ch % 4) * 2048, 1024)
                mk.rissue("yl", ch if not split_loads else ch // 2,
                          4 if not split_loads else 2, sync.dma_start(
                              out=xa,
                              in_=src_dram[ch * 128:(ch + 1) * 128, :]))
            mk.inc("msy", sync.nop())

        def fgl(g):
            mk.entry(g, "g")
            for ch in range(1, nch, 2):
                if ch >= 4:
                    g.wait_ge(mk.sems["pha"], pha0 + ch - 3)
                    g.wait_ge(mk.sems["lns"], lns0 + ch - 3)
                xa = mk.f32(mk.B16, (ch % 4) * 2048, 1024)
                mk.rissue("ylg", ch // 2, 2, g.dma_start(
                    out=xa, in_=src_dram[ch * 128:(ch + 1) * 128, :]))
            mk.inc("msg", g.drain())

        def ld_rwait(eng, ch, snp, snpg):
            if not split_loads:
                mk.rwait(eng, "yl", ch, 4, snp, 16)
            elif ch % 2 == 0:
                mk.rwait(eng, "yl", ch // 2, 2, snp, 16)
            else:
                mk.rwait(eng, "ylg", ch // 2, 2, snpg, 16)

        def fv(v):
            mk.entry(v, "v")
            v.memset(mk.rmaxA[:], 0.0)
            v.drain()
            ld_rwait(v, 0, ylsnap, ylgsnap)
            mk.ln_v_pre(v, 0, mk.LNa, mk.f32(mk.B16, 0, 1024))
            for ch in range(nch):
                if ch + 1 < nch:
                    ld_rwait(v, ch + 1, ylsnap, ylgsnap)
                    xan = mk.f32(mk.B16, ((ch + 1) % 4) * 2048, 1024)
                    mk.ln_v_pre(v, (ch + 1) % mk.NLN, mk.LNa, xan)
                xa = mk.f32(mk.B16, (ch % 4) * 2048, 1024)
                k = ch % mk.NLN
                y16 = mk.B16[:, 8192 + (ch % 2) * 1024:
                             8192 + (ch % 2) * 1024 + 1024]
                if ch >= 2:
                    v.wait_ge(mk.sems["phb"], phb0 + ch - 1)
                mk.ln_v_apply(v, k, mk.LNa, xa, y16, lns0 + ch + 1)
                mk.inc("pha", v.drain())
                mk.absfold(v, y16, ch, mk.rmaxA)
            mk.inc("msv", v.drain())

        def fs(s):
            mk.entry(s, "s")
            for ch in range(nch):
                ld_rwait(s, ch, ylsnap, ylgsnap)
                xa = mk.f32(mk.B16, (ch % 4) * 2048, 1024)
                sqscr = mk.f32(mk.B16, 10240 + (ch % 2) * 2048, 1024)
                mk.ln_s(s, ch % mk.NLN, mk.LNa, xa, sqscr, lnv0 + ch + 1)
                if ch >= 2:
                    self._y_evac(s, ch - 2, phb0)
            for ch in (nch - 2, nch - 1):
                self._y_evac(s, ch, phb0)
            mk.inc("mss", s.drain())

        def ft(t):
            mk.entry(t, "t")
            for ch in range(nch):
                t.wait_ge(mk.sems["pha"], pha0 + ch + 1)
                if ch >= 2:
                    t.wait_ge(mk.sems["phc"], phc0 + ch - 1)
                y16 = mk.B16[:, 8192 + (ch % 2) * 1024:
                             8192 + (ch % 2) * 1024 + 1024]
                last = None
                for hc in range(HC):
                    last = t.transpose(
                        mk.pdh(ch % 2, 0)[:, hc * 128:(hc + 1) * 128],
                        y16[:, hc * 128:(hc + 1) * 128], mk.id16[:])
                mk.inc("phb", last)
            mk.inc("mst", t.drain())

        self.blk.sync(fload)
        if split_loads:
            self.blk.gpsimd(fgl)
        self.blk.vector(fv)
        self.blk.tensor(ft)
        self.blk.scalar(fs)
        self.snap()

    def _y_evac(self, s, ch, phb0):
        mk = self
        s.wait_ge(mk.sems["phb"], phb0 + ch + 1)
        dst = mk.A16[:, 0:HC * mk.T].rearrange(
            "p (h t) -> p h t", h=HC)[:, :, ch * 128:(ch + 1) * 128]
        s.copy(out=dst, in_=mk.pdh(ch % 2, 0).rearrange(
            "p (h t) -> p h t", h=HC))
        mk.inc("phc", s.drain())

    def phase_qy(self):
        mk = self
        nsl = (HC * self.T) // 2048
        phb0 = self.cnt["phb"]
        bar0 = self.cnt["bar"]

        def fs(s):
            mk.entry(s, "s")
            s.wait_ge(mk.sems["bar"], bar0)
            for j in range(nsl):
                if j >= 2:
                    s.wait_ge(mk.sems["phb"], phb0 + j - 1)
                ap = mk.f32(mk.C16, (j % 2) * 4096, 2048)
                s.activation(out=ap, in_=mk.A16[:, j * 2048:(j + 1) * 2048],
                             func=ACT.Identity, scale=mk.SC[:, SC_RY:SC_RY + 1],
                             bias=mk.MAGCOL[:, 0:1])
                mk.inc("pha", s.drain())
            mk.inc("mss", s.drain())

        pha0 = self.cnt["pha"]

        def fv(v):
            mk.entry(v, "v")
            for j in range(nsl):
                v.wait_ge(mk.sems["pha"], pha0 + j + 1)
                ap = mk.f32(mk.C16, (j % 2) * 4096, 2048)
                v.tensor_scalar(out=mk.B16[:, j * 2048:(j + 1) * 2048],
                                in0=ap, scalar1=MAGIC,
                                scalar2=mk.SC[:, SC_SY:SC_SY + 1],
                                op0=OP.subtract, op1=OP.mult)
                mk.inc("phb", v.drain())
            mk.inc("msv", v.drain())

        self.blk.scalar(fs)
        self.blk.vector(fv)
        self.snap()

    def phase_ug_gate(self):
        mk = self
        T, ntt = self.T, self.ntt
        nhalf = 2 if ntt >= 2 else 1
        tph = ntt // nhalf
        stw = tph * 512
        self._ug_nhalf, self._ug_tph, self._ug_stw = nhalf, tph, stw
        groups = [(oc, hf) for oc in range(OCF) for hf in range(nhalf)]
        ng = len(groups)
        banksets = [
            [self.pd[0][:, 0:512], self.pd[0][:, 512:1024],
             self.pd[1][:, 0:512], self.pd[1][:, 512:1024]][:tph],
            [self.pd[2][:, 0:512], self.pd[2][:, 512:1024],
             self.pb[0][:, :], self.pb[1][:, :]][:tph],
        ]
        self._ug_banksets = banksets
        ugwsnap = self.rsnap("ugw", 4)
        ugssnap = self.rsnap("ugs", 2)
        pha0 = self.cnt["pha"]
        phb0 = self.cnt["phb"]
        phc0 = self.cnt["phc"]
        bar0 = self.cnt["bar"]

        def fw(sync):
            mk.entry(sync, "y")
            sync.wait_ge(mk.sems["wtg"], 1)

            def loadslice(oc):
                if oc >= 2:
                    sync.wait_ge(mk.sems["pha"],
                                 pha0 + (oc - 2 + 1) * nhalf)
                dst = mk.A16[:, (oc % 2) * 1024:(oc % 2) * 1024 + 1024]
                unit = HC + oc // 3
                sub = oc % 3
                mk.rissue("ugw", oc % 2, 4, sync.dma_start(
                    out=dst.rearrange("p (a o) -> p a o", a=HC),
                    in_=mk.dwall[unit, sub * 8:(sub + 1) * 8]
                    .rearrange("a p o -> p a o")))

            def spill(gi):
                oc, hf = groups[gi]
                sync.wait_ge(mk.sems["phb"], phb0 + gi + 1)
                stg = mk.C16[:, 12288 + (gi % 2) * stw:
                             12288 + (gi % 2) * stw + stw]
                mk.rissue("ugs", gi, 2, sync.dma_start(
                    out=mk.dgatesp[oc * 128:(oc + 1) * 128,
                                   hf * stw:(hf + 1) * stw], in_=stg))

            for oc in range(2):
                loadslice(oc)
            for gi in range(ng):
                oc = gi // nhalf
                if gi % nhalf == 0 and oc + 2 < OCF:
                    loadslice(oc + 2)
                if gi >= 1:
                    spill(gi - 1)
            spill(ng - 1)
            mk.inc("msy", sync.nop())

        def ft(t):
            mk.entry(t, "t")
            t.wait_ge(mk.sems["bar"], bar0)
            for gi, (oc, hf) in enumerate(groups):
                mk.rwait(t, "ugw", oc % 2, 4, ugwsnap, 16, occ=oc // 2)
                if gi >= 2:
                    t.wait_ge(mk.sems["phb"], phb0 + gi - 1)
                wb = mk.A16[:, (oc % 2) * 1024:(oc % 2) * 1024 + 1024]
                bs = banksets[gi % 2]
                last = None
                for hc in range(HC):
                    for j in range(tph):
                        tt = hf * tph + j
                        last = t.matmul(
                            bs[j],
                            lhsT=wb[:, hc * 128:(hc + 1) * 128],
                            rhs=mk.B16[:, hc * T + tt * 512:
                                       hc * T + tt * 512 + 512],
                            start=(hc == 0), stop=(hc == HC - 1))
                mk.inc("pha", last)
            mk.inc("mst", t.drain())

        def fs(s):
            mk.entry(s, "s")
            for gi in range(ng):
                s.wait_ge(mk.sems["pha"], pha0 + gi + 1)
                if gi >= 2:
                    mk.rwait(s, "ugs", gi, 2, ugssnap, 16, occ=gi // 2 - 1)
                    s.wait_ge(mk.sems["phc"], phc0 + gi - 1)
                stg = mk.C16[:, 12288 + (gi % 2) * stw:
                             12288 + (gi % 2) * stw + stw]
                bs = banksets[gi % 2]
                last = None
                for j in range(tph):
                    last = s.copy(out=stg[:, j * 512:(j + 1) * 512], in_=bs[j])
                mk.inc("phb", last)
            mk.inc("mss", s.drain())

        def fv(v):
            mk.entry(v, "v")
            v.memset(mk.rmaxB[:], 0.0)
            v.drain()
            for gi in range(ng):
                v.wait_ge(mk.sems["phb"], phb0 + gi + 1)
                stg = mk.C16[:, 12288 + (gi % 2) * stw:
                             12288 + (gi % 2) * stw + stw]
                ts = mk.tmax[:, gi % 8:gi % 8 + 1]
                v.tensor_reduce(out=ts, in_=stg, axis=AX.X, op=OP.max,
                                apply_absolute_value=True)
                v.drain()
                v.tensor_tensor(out=mk.rmaxB[:], in0=mk.rmaxB[:], in1=ts,
                                op=OP.max)
                mk.inc("phc", v.drain())
            mk.inc("msv", v.drain())

        self.blk.sync(fw)
        self.blk.tensor(ft)
        self.blk.scalar(fs)
        self.blk.vector(fv)
        self._ug_gate_phb_end = self.cnt["phb"]
        self.snap()

    def phase_ug_up_e1(self):
        mk = self
        T, ntt, TE = self.T, self.ntt, self.TE
        nhalf, tph, stw = self._ug_nhalf, self._ug_tph, self._ug_stw
        groups = [(oc, hf) for oc in range(OCF) for hf in range(nhalf)]
        ng = len(groups)
        nsl = (DFF // 128) * (T // TE)
        assert nsl == ng, (nsl, ng)
        npc = T // TE
        banksets = self._ug_banksets
        gphb_end = self._ug_gate_phb_end
        ugwsnap = self.rsnap("ugw", 4)
        ugssnap = self.rsnap("ugs", 2)
        gatesnap = [ugssnap[0] - 16 * (ng // 2), ugssnap[1] - 16 * (ng // 2)]
        e1isnap = self.rsnap("e1i", 3)
        e1osnap = self.rsnap("e1o", 2)
        fua0 = self.cnt["fua"]
        fub0 = self.cnt["fub"]
        fuc0 = self.cnt["fuc"]
        fud0 = self.cnt["fud"]
        fue0 = self.cnt["fue"]
        fuf0 = self.cnt["fuf"]
        bar0 = self.cnt["bar"]
        EB = 4096

        def pos(j):
            pc, jj = divmod(j, npc)
            return pc * 128, jj * TE

        def fw(sync):
            mk.entry(sync, "y")

            def loadslice(oc):
                if oc >= 2:
                    sync.wait_ge(mk.sems["fua"],
                                 fua0 + (oc - 2 + 1) * nhalf)
                dst = mk.A16[:, 2048 + (oc % 2) * 1024:
                             2048 + (oc % 2) * 1024 + 1024]
                mk.rissue("ugw", 2 + oc % 2, 4, sync.dma_start(
                    out=dst.rearrange("p (a o) -> p a o", a=HC),
                    in_=mk.dwall[oc // 3, (oc % 3) * 8:(oc % 3) * 8 + 8]
                    .rearrange("a p o -> p a o")))

            def upspill(gi):
                oc, hf = groups[gi]
                sync.wait_ge(mk.sems["fub"], fub0 + gi + 1)
                stg = mk.C16[:, 12288 + (gi % 2) * stw:
                             12288 + (gi % 2) * stw + stw]
                mk.rissue("ugs", gi, 2, sync.dma_start(
                    out=mk.dupsp[oc * 128:(oc + 1) * 128,
                                 hf * stw:(hf + 1) * stw], in_=stg))

            def e1load(j):
                if j >= 3:
                    sync.wait_ge(mk.sems["fud"], fud0 + j - 2)
                r0, c0 = pos(j)
                mk.rwait(sync, "ugs", j, 2, gatesnap, 16, occ=j // 2)
                gin = mk.C16[:, (j % 3) * TE:(j % 3) * TE + TE]
                mk.rissue("e1i", j, 3, sync.dma_start(
                    out=gin, in_=mk.dgatesp[r0:r0 + 128, c0:c0 + TE]))

            def e1spill(j):
                sync.wait_ge(mk.sems["fue"], fue0 + j + 1)
                r0, c0 = pos(j)
                gout = mk.C16[:, 3 * TE + (j % 2) * TE:
                              3 * TE + (j % 2) * TE + TE]
                mk.rissue("e1o", j, 2, sync.dma_start(
                    out=mk.dg2sp[r0:r0 + 128, c0:c0 + TE], in_=gout))

            for oc in range(2):
                loadslice(oc)
            e1load(0)
            e1load(1)
            for gi in range(ng):
                oc = gi // nhalf
                if gi % nhalf == 0 and oc + 2 < OCF:
                    loadslice(oc + 2)
                if gi + 2 < nsl:
                    e1load(gi + 2)
                upspill(gi)
                if gi >= 1:
                    e1spill(gi - 1)
            e1spill(nsl - 1)
            mk.rtotal(sync, "ugs", 2)
            mk.rtotal(sync, "e1o", 2)
            mk.inc("msy", sync.nop())

        def ft(t):
            mk.entry(t, "t")
            for gi, (oc, hf) in enumerate(groups):
                mk.rwait(t, "ugw", 2 + oc % 2, 4, ugwsnap, 16, occ=oc // 2)
                if gi < 2:
                    t.wait_ge(mk.sems["phb"], gphb_end)
                else:
                    t.wait_ge(mk.sems["fub"], fub0 + gi - 1)
                wb = mk.A16[:, 2048 + (oc % 2) * 1024:
                            2048 + (oc % 2) * 1024 + 1024]
                bs = banksets[gi % 2]
                last = None
                for hc in range(HC):
                    for j in range(tph):
                        tt = hf * tph + j
                        last = t.matmul(
                            bs[j],
                            lhsT=wb[:, hc * 128:(hc + 1) * 128],
                            rhs=mk.B16[:, hc * T + tt * 512:
                                       hc * T + tt * 512 + 512],
                            start=(hc == 0), stop=(hc == HC - 1))
                mk.inc("fua", last)
            mk.inc("mst", t.drain())

        def fs(s):
            mk.entry(s, "s")
            s.wait_ge(mk.sems["bar"], bar0)
            for j in range(ng):
                mk.rwait(s, "e1i", j, 3, e1isnap, 16)
                if j >= 2:
                    s.wait_ge(mk.sems["fuf"], fuf0 + 3 * (j - 2) + 1)
                gin = mk.C16[:, (j % 3) * TE:(j % 3) * TE + TE]
                ap = mk.f32(mk.A16, EB + (j % 2) * 2 * TE, TE)
                s.activation(out=ap, in_=gin, func=ACT.Identity,
                             scale=mk.SC[:, SC_RG:SC_RG + 1],
                             bias=mk.MAGCOL[:, 0:1])
                mk.inc("fud", s.drain())
                sg = mk.A16[:, EB + 4 * TE + (j % 2) * TE:
                            EB + 4 * TE + (j % 2) * TE + TE]
                if j >= 2:
                    s.wait_ge(mk.sems["fuf"], fuf0 + 3 * (j - 2) + 2)
                s.activation(out=sg, in_=ap, func=ACT.Sigmoid,
                             scale=mk.SC[:, SC_SG:SC_SG + 1],
                             bias=mk.SC[:, SC_BG:SC_BG + 1])
                mk.inc("fuc", s.drain())
                s.wait_ge(mk.sems["fua"], fua0 + j + 1)
                if j >= 2:
                    mk.rwait(s, "ugs", j, 2, ugssnap, 16, occ=j // 2 - 1)
                    s.wait_ge(mk.sems["fue"], fue0 + j - 1)
                else:
                    s.wait_ge(mk.sems[f"ugs{j % 2}"], ugssnap[j % 2])
                stg = mk.C16[:, 12288 + (j % 2) * stw:
                             12288 + (j % 2) * stw + stw]
                bs = banksets[j % 2]
                last = None
                for jj in range(tph):
                    last = s.copy(out=stg[:, jj * 512:(jj + 1) * 512],
                                  in_=bs[jj])
                mk.inc("fub", last)
            mk.inc("mss", s.drain())

        def fv(v):
            mk.entry(v, "v")
            for k in range(8):
                v.memset(mk.WAR[:, k * 2048:(k + 1) * 2048],
                         -60000.0 if k < 2 or 4 <= k < 6 else 60000.0)
            v.drain()
            for j in range(ng):
                p = j % 2
                qg = mk.A16[:, EB + 12 * TE + (j % 2) * TE:
                            EB + 12 * TE + (j % 2) * TE + TE]
                v.wait_ge(mk.sems["fuf"], fuf0 + 3 * (j + 1))
                sigq = mk.A16[:, EB + 10 * TE + (j % 2) * TE:
                              EB + 10 * TE + (j % 2) * TE + TE]
                gout = mk.C16[:, 3 * TE + (j % 2) * TE:
                              3 * TE + (j % 2) * TE + TE]
                if j >= 2:
                    mk.rwait(v, "e1o", j, 2, e1osnap, 16, occ=j // 2 - 1)
                v.tensor_tensor(out=gout, in0=qg, in1=sigq, op=OP.mult)
                v.drain()
                v.wait_ge(mk.sems["fub"], fub0 + j + 1)
                stg = mk.C16[:, 12288 + (j % 2) * stw:
                             12288 + (j % 2) * stw + stw]
                gmx = mk.WAR[:, (4 + p) * 2048:(4 + p) * 2048 + TE]
                gmn = mk.WAR[:, (6 + p) * 2048:(6 + p) * 2048 + TE]
                umx = mk.WAR[:, p * 2048:p * 2048 + stw]
                umn = mk.WAR[:, (2 + p) * 2048:(2 + p) * 2048 + stw]
                v.tensor_tensor(out=gmx, in0=gmx, in1=gout, op=OP.max)
                v.tensor_tensor(out=gmn, in0=gmn, in1=gout, op=OP.min)
                v.tensor_tensor(out=umx, in0=umx, in1=stg, op=OP.max)
                v.tensor_tensor(out=umn, in0=umn, in1=stg, op=OP.min)
                mk.inc("fue", v.drain())
            for k in range(4):
                v.tensor_reduce(out=mk.tmax[:, 2 * k:2 * k + 1],
                                in_=mk.WAR[:, 2 * k * 2048:
                                           2 * k * 2048 + 2048],
                                axis=AX.X,
                                op=OP.max if k in (0, 2) else OP.min,
                                negate=(k in (1, 3)))
                v.tensor_reduce(out=mk.tmax[:, 2 * k + 1:2 * k + 2],
                                in_=mk.WAR[:, (2 * k + 1) * 2048:
                                           (2 * k + 1) * 2048 + 2048],
                                axis=AX.X,
                                op=OP.max if k in (0, 2) else OP.min,
                                negate=(k in (1, 3)))
            v.drain()
            v.tensor_tensor(out=mk.tmax[:, 0:1], in0=mk.tmax[:, 0:1],
                            in1=mk.tmax[:, 1:2], op=OP.max)
            v.tensor_tensor(out=mk.tmax[:, 2:3], in0=mk.tmax[:, 2:3],
                            in1=mk.tmax[:, 3:4], op=OP.max)
            v.tensor_tensor(out=mk.tmax[:, 4:5], in0=mk.tmax[:, 4:5],
                            in1=mk.tmax[:, 5:6], op=OP.max)
            v.tensor_tensor(out=mk.tmax[:, 6:7], in0=mk.tmax[:, 6:7],
                            in1=mk.tmax[:, 7:8], op=OP.max)
            v.drain()
            v.tensor_tensor(out=mk.rmaxB[:], in0=mk.tmax[:, 0:1],
                            in1=mk.tmax[:, 2:3], op=OP.max)
            v.tensor_tensor(out=mk.rmaxA[:], in0=mk.tmax[:, 4:5],
                            in1=mk.tmax[:, 6:7], op=OP.max)
            mk.inc("msv", v.drain())

        def fg(g):
            mk.entry(g, "g")
            for j in range(ng):
                g.wait_ge(mk.sems["fud"], fud0 + j + 1)
                if j >= 2:
                    g.wait_ge(mk.sems["fue"], fue0 + j - 1)
                ap = mk.f32(mk.A16, EB + (j % 2) * 2 * TE, TE)
                qg = mk.A16[:, EB + 12 * TE + (j % 2) * TE:
                            EB + 12 * TE + (j % 2) * TE + TE]
                g.tensor_scalar(out=qg, in0=ap, scalar1=-MAGIC,
                                scalar2=mk.SC[:, SC_SG:SC_SG + 1],
                                op0=OP.add, op1=OP.mult)
                mk.inc("fuf", g.drain())
                g.wait_ge(mk.sems["fuc"], fuc0 + j + 1)
                sg = mk.A16[:, EB + 4 * TE + (j % 2) * TE:
                            EB + 4 * TE + (j % 2) * TE + TE]
                tt_ = mk.f32(mk.A16, EB + 6 * TE + (j % 2) * 2 * TE, TE)
                g.tensor_scalar(out=tt_, in0=sg, scalar1=65536.0, scalar2=P23,
                                op0=OP.mult, op1=OP.add)
                mk.inc("fuf", g.drain())
                sigq = mk.A16[:, EB + 10 * TE + (j % 2) * TE:
                              EB + 10 * TE + (j % 2) * TE + TE]
                g.tensor_scalar(out=sigq, in0=tt_, scalar1=1.0 / 65536.0,
                                scalar2=-128.0, op0=OP.mult, op1=OP.add)
                mk.inc("fuf", g.drain())
            mk.inc("msg", g.drain())

        self.blk.sync(fw)
        self.blk.tensor(ft)
        self.blk.scalar(fs)
        self.blk.vector(fv)
        self.blk.gpsimd(fg)
        self.snap()

    def phase_e2(self):
        mk = self
        T, TE = self.T, self.TE
        nsl = (DFF // 128) * (T // TE)
        npc = T // TE
        e2isnap = self.rsnap("e2i", 3)
        e2osnap = self.rsnap("e2o", 2)
        uccsnap = self.rsnap("ucc", 2)
        pha0 = self.cnt["pha"]
        phb0 = self.cnt["phb"]
        bar0 = self.cnt["bar"]

        def pos(j):
            pc, jj = divmod(j, npc)
            return pc * 128, jj * TE

        def fld(sync):
            mk.entry(sync, "y")
            for j in range(nsl):
                if j >= 3:
                    sync.wait_ge(mk.sems["pha"], pha0 + 2 * (j - 2))
                r0, c0 = pos(j)
                g2in = mk.C16[:, (j % 3) * TE:(j % 3) * TE + TE]
                mk.rissue("e2i", j, 3, sync.dma_start(
                    out=g2in, in_=mk.dg2sp[r0:r0 + 128, c0:c0 + TE]))
            mk.inc("msy", sync.nop())

        def fs(s):
            mk.entry(s, "s")
            s.wait_ge(mk.sems["bar"], bar0)
            for j in range(2):
                r0, c0 = pos(j)
                uin = mk.C16[:, 3 * TE + (j % 3) * TE:
                             3 * TE + (j % 3) * TE + TE]
                mk.rissue("ucc", j, 2, s.dma_start(
                    out=uin, in_=mk.dupsp[r0:r0 + 128, c0:c0 + TE]))
            for j in range(nsl):
                mk.rwait(s, "e2i", j, 3, e2isnap, 16)
                mk.rwait(s, "ucc", j, 2, uccsnap, 16)
                if j >= 2:
                    s.wait_ge(mk.sems["phb"], phb0 + j - 1)
                g2in = mk.C16[:, (j % 3) * TE:(j % 3) * TE + TE]
                uin = mk.C16[:, 3 * TE + (j % 3) * TE:
                             3 * TE + (j % 3) * TE + TE]
                g2p = mk.f32(mk.A16, (j % 2) * 2 * TE, TE)
                s.activation(out=g2p, in_=g2in, func=ACT.Identity,
                             scale=mk.SC[:, SC_R2:SC_R2 + 1],
                             bias=mk.MAGCOL[:, 0:1])
                up = mk.f32(mk.A16, 4 * TE + (j % 2) * 2 * TE, TE)
                s.activation(out=up, in_=uin, func=ACT.Identity,
                             scale=mk.SC[:, SC_RU:SC_RU + 1],
                             bias=mk.MAGCOL[:, 0:1])
                mk.inc("pha", s.drain(), 2)
                if j + 2 < nsl:
                    jn = j + 2
                    r0, c0 = pos(jn)
                    uinn = mk.C16[:, 3 * TE + (jn % 3) * TE:
                                  3 * TE + (jn % 3) * TE + TE]
                    mk.rissue("ucc", jn, 2, s.dma_start(
                        out=uinn, in_=mk.dupsp[r0:r0 + 128, c0:c0 + TE]))
            mk.inc("mss", s.drain())

        phg0 = self.cnt["phg"]

        def spill2(g, j):
            g.wait_ge(mk.sems["phb"], phb0 + j + 1)
            r0, c0 = pos(j)
            oout = mk.B16[:, 4 * TE + (j % 2) * TE:4 * TE + (j % 2) * TE + TE]
            mk.rissue("e2o", j, 2, g.dma_start(
                out=mk.dosp[r0:r0 + 128, c0:c0 + TE], in_=oout))

        def fg(g):
            mk.entry(g, "g")
            for j in range(nsl):
                g.wait_ge(mk.sems["pha"], pha0 + 2 * (j + 1))
                if j >= 2:
                    g.wait_ge(mk.sems["phb"], phb0 + j - 1)
                up = mk.f32(mk.A16, 4 * TE + (j % 2) * 2 * TE, TE)
                uq = mk.B16[:, 2 * TE + (j % 2) * TE:2 * TE + (j % 2) * TE + TE]
                g.tensor_scalar(out=uq, in0=up, scalar1=-MAGIC,
                                scalar2=mk.SC[:, SC_SU:SC_SU + 1],
                                op0=OP.add, op1=OP.mult)
                mk.inc("phg", g.drain())
                if j >= 1:
                    spill2(g, j - 1)
            spill2(g, nsl - 1)
            mk.rtotal(g, "e2o", 2)
            mk.inc("msg", g.drain())

        def fv(v):
            mk.entry(v, "v")
            v.memset(mk.WAR[:, 8192:8192 + 2048], -60000.0)
            v.memset(mk.WAR[:, 10240:10240 + 2048], 60000.0)
            v.drain()
            for j in range(nsl):
                v.wait_ge(mk.sems["pha"], pha0 + 2 * (j + 1))
                g2p = mk.f32(mk.A16, (j % 2) * 2 * TE, TE)
                qg2 = mk.B16[:, (j % 2) * TE:(j % 2) * TE + TE]
                v.tensor_scalar(out=qg2, in0=g2p, scalar1=MAGIC,
                                scalar2=mk.SC[:, SC_S2:SC_S2 + 1],
                                op0=OP.subtract, op1=OP.mult)
                v.drain()
                v.wait_ge(mk.sems["phg"], phg0 + j + 1)
                uq = mk.B16[:, 2 * TE + (j % 2) * TE:2 * TE + (j % 2) * TE + TE]
                oout = mk.B16[:, 4 * TE + (j % 2) * TE:
                              4 * TE + (j % 2) * TE + TE]
                if j >= 2:
                    mk.rwait(v, "e2o", j, 2, e2osnap, 16, occ=j // 2 - 1)
                v.tensor_tensor(out=oout, in0=qg2, in1=uq, op=OP.mult)
                v.drain()
                v.tensor_tensor(out=mk.WAR[:, 8192:8192 + TE],
                                in0=mk.WAR[:, 8192:8192 + TE], in1=oout,
                                op=OP.max)
                v.tensor_tensor(out=mk.WAR[:, 10240:10240 + TE],
                                in0=mk.WAR[:, 10240:10240 + TE], in1=oout,
                                op=OP.min)
                mk.inc("phb", v.drain())
            v.tensor_reduce(out=mk.tmax[:, 0:1], in_=mk.WAR[:, 8192:8192 + 2048],
                            axis=AX.X, op=OP.max)
            v.tensor_reduce(out=mk.tmax[:, 1:2],
                            in_=mk.WAR[:, 10240:10240 + 2048],
                            axis=AX.X, op=OP.min, negate=True)
            v.drain()
            v.tensor_tensor(out=mk.rmaxA[:], in0=mk.tmax[:, 0:1],
                            in1=mk.tmax[:, 1:2], op=OP.max)
            mk.inc("msv", v.drain())

        self.blk.sync(fld)
        self.blk.scalar(fs)
        self.blk.gpsimd(fg)
        self.blk.vector(fv)
        self.snap()

    def phase_d(self, xin_dram, xout_dram, fuse_ln):
        mk = self
        T, nch = self.T, self.nch
        npair = nch // 2
        NW = T // 512
        NSLAB = NW * OCF
        QRING = 28
        ORING = 6
        bar0 = self.cnt["bar"]
        dw0 = self.cnt["dW"]
        dqrsnap = self.rsnap("dqr", ORING)
        dxlsnap = self.rsnap("dxl", 4)
        dxssnap = self.rsnap("dxs", 2)
        dyssnap = self.rsnap("dys", 2)
        e2o_fin = [self.cnt["e2o0"], self.cnt["e2o1"]]
        pha0 = self.cnt["pha"]
        phb0 = self.cnt["phb"]
        phg0 = self.cnt["phg"]
        phc0 = self.cnt["phc"]
        phd0 = self.cnt["phd"]
        phe0 = self.cnt["phe"]
        phf0 = self.cnt["phf"]
        lnv0 = self.cnt["lnv"]
        lns0 = self.cnt["lns"]
        gv = dict(self.gatev)

        def slab_wait(eng, g):
            w, hc = divmod(g, OCF)
            if hc == OCF - 1:
                eng.wait_ge(mk.sems["phc"], phc0 + 2 * w + 2)
            else:
                eng.wait_ge(mk.sems["phg"], phg0 + w * (OCF - 1) + hc + 1)

        def chunk_psum(ch):
            k = ch // 2
            if k % 2 == 0:
                p = mk.pd[ch % 2]
                return ([p[:, 0:512], p[:, 512:1024]], p[:, :],
                        mk.pdh(ch % 2, 0))
            if ch % 2 == 0:
                p = mk.pd[2]
                return ([p[:, 0:512], p[:, 512:1024]], p[:, :], mk.pdh(2, 0))
            return ([mk.pb[0][:, :], mk.pb[1][:, :]], None, mk.pbh(0))

        def xa_ap(ch):
            return mk.f32(mk.C16, (ch % 4) * 2048, 1024)

        def xo_ap(ch):
            return mk.f32(mk.C16, 8192 + (ch % 2) * 2048, 1024)

        def y2_ap(ch):
            return mk.C16[:, 12288 + (ch % 2) * 1024:
                          12288 + (ch % 2) * 1024 + 1024]

        def qon_ap(g):
            return mk.WAR[:, 2048 + (g % QRING) * 512:
                          2048 + (g % QRING) * 512 + 512]

        def fsy(sync):
            sync.wait_ge(mk.sems["wtg"], 1)
            for ic in list(range(12, 24)) + list(range(12)):
                if ic == 0:
                    sync.wait_ge(mk.sems["msv"], gv["v"])
                    sync.wait_ge(mk.sems["msg"], gv["g"])
                    sync.wait_ge(mk.sems["mss"], gv["s"])
                    for r in range(2):
                        sync.wait_ge(mk.sems[f"e2o{r}"], e2o_fin[r])
                sync.dma_start(
                    out=mk.B16[:, ic * D:(ic + 1) * D].rearrange(
                        "p (u o) -> p u o", o=128),
                    in_=mk.dwall[2 * HC:3 * HC, ic].rearrange(
                        "u p o -> p u o"))                    .then_inc(mk.sems["dW"], 16)
                mk.cnt["dW"] += 16

            def xa_issue(ch):
                if not 2 <= ch < nch:
                    return
                if ch >= 4:
                    sync.wait_ge(mk.sems["phf"], phf0 + ch - 3)
                mk.rissue("dxl", ch, 4, sync.dma_start(
                    out=xa_ap(ch), in_=xin_dram[ch * 128:(ch + 1) * 128, :]))

            for ch in (0, 1):
                mk.rissue("dxl", ch, 4, sync.dma_start(
                    out=xa_ap(ch), in_=xin_dram[ch * 128:(ch + 1) * 128, :]))
            for g in range(NSLAB):
                w, hc = divmod(g, OCF)
                if g >= ORING:
                    sync.wait_ge(mk.sems["pha"], pha0 + g - (ORING - 1))
                osl = mk.B16[:, 28672 + (g % ORING) * 512:
                             28672 + (g % ORING) * 512 + 512]
                mk.rissue("dqr", g, ORING, sync.dma_start(
                    out=osl, in_=mk.dosp[hc * 128:(hc + 1) * 128,
                                         w * 512:(w + 1) * 512]))
                if hc == 0:
                    xa_issue(4 * w - 2)
                elif hc == 2:
                    xa_issue(4 * w - 1)
                elif hc == 16:
                    xa_issue(4 * w)
                elif hc == 18:
                    xa_issue(4 * w + 1)
            xa_issue(nch - 2)
            xa_issue(nch - 1)
            mk.rtotal(sync, "dqr", ORING)
            mk.rtotal(sync, "dxl", 4)
            mk.inc("msy", sync.nop())

        def fg(g):
            mk.entry(g, "g")
            g.wait_ge(mk.sems["bar"], bar0)
            for gi in range(NSLAB):
                mk.rwait(g, "dqr", gi, ORING, dqrsnap, 16)
                if gi >= 2:
                    g.wait_ge(mk.sems["phb"], phb0 + gi - 1)
                osl = mk.B16[:, 28672 + (gi % ORING) * 512:
                             28672 + (gi % ORING) * 512 + 512]
                op_ = mk.f32(mk.WAR, (gi % 2) * 1024, 512)
                g.tensor_scalar(out=op_, in0=osl,
                                scalar1=mk.SC[:, SC_RO:SC_RO + 1],
                                scalar2=MAGIC, op0=OP.mult, op1=OP.add)
                mk.inc("pha", g.drain())
            mk.inc("msg", g.drain())

        def do_lns(s, ch):
            mk.ln_s(s, ch % mk.NLN, mk.LNa, xo_ap(ch),
                    mk.f32(mk.B16, 24576 + (ch % 2) * 2048, 1024),
                    lnv0 + ch + 1)

        def st_issue(s, ch):
            s.wait_ge(mk.sems["phf"], phf0 + ch + 1)
            mk.rissue("dxs", ch, 2, s.dma_start(
                out=xout_dram[ch * 128:(ch + 1) * 128, :], in_=xo_ap(ch)))
            if fuse_ln:
                mk.rissue("dys", ch, 2, s.dma_start(
                    out=mk.dy2sp[ch * 128:(ch + 1) * 128, :], in_=y2_ap(ch)))

        def fs(s):
            mk.entry(s, "s")
            sched = {}

            def put(slot, kind, ch):
                if 0 <= ch < nch and (fuse_ln or kind == "st"):
                    sched.setdefault(slot, []).append((kind, ch))

            for w in range(1, NW):
                base = w * OCF
                put(base + 0, "st", 4 * w - 6)
                put(base + 2, "st", 4 * w - 5)
                put(base + 8, "ln", 4 * w - 4)
                put(base + 12, "ln", 4 * w - 3)
                put(base + 14, "st", 4 * w - 4)
                put(base + 16, "st", 4 * w - 3)
                put(base + 20, "ln", 4 * w - 2)
                put(base + 22, "ln", 4 * w - 1)
            for g in range(NSLAB):
                s.wait_ge(mk.sems["pha"], pha0 + g + 1)
                if g >= QRING:
                    slab_wait(s, g - QRING)
                op_ = mk.f32(mk.WAR, (g % 2) * 1024, 512)
                s.activation(out=qon_ap(g), in_=op_, func=ACT.Identity,
                             scale=1.0, bias=mk.MAGNEG[:, 0:1])
                mk.inc("phb", s.drain())
                for kind, ch in sched.get(g, []):
                    if kind == "ln":
                        do_lns(s, ch)
                    else:
                        st_issue(s, ch)
            for ch in range(nch - 4, nch):
                if ch - 2 >= 0:
                    st_issue(s, ch - 2)
                if fuse_ln:
                    do_lns(s, ch)
            st_issue(s, nch - 2)
            st_issue(s, nch - 1)
            mk.rtotal(s, "dxs", 2)
            if fuse_ln:
                mk.rtotal(s, "dys", 2)
            mk.inc("mss", s.drain())

        def emit_tr(t, ch):
            t.wait_ge(mk.sems["phf"], phf0 + ch + 1)
            tgt = chunk_psum(ch)[2]
            y2t = y2_ap(ch)
            last = None
            for hcx in range(HC):
                last = t.transpose(tgt[:, hcx * 128:(hcx + 1) * 128],
                                   y2t[:, hcx * 128:(hcx + 1) * 128],
                                   mk.id16[:])
            mk.inc("phd", last)

        def ft(t):
            mk.entry(t, "t")
            for k in range(npair):
                w = k // 2
                for hc in range(OCF):
                    g = w * OCF + hc
                    t.wait_ge(mk.sems["phb"], phb0 + g + 1)
                    if k == 0 and hc == 0:
                        t.wait_ge(mk.sems["dW"], dw0 + 16 * OCF)
                    elif hc == 0:
                        if k >= 2:
                            if fuse_ln:
                                t.wait_ge(mk.sems["phe"],
                                          phe0 + 2 * (k - 2) + 2)
                            else:
                                t.wait_ge(mk.sems["phf"],
                                          phf0 + 2 * (k - 2) + 2)
                    slab = qon_ap(g)
                    off = (k % 2) * 256
                    last = None
                    for ci in range(2):
                        halves = chunk_psum(2 * k + ci)[0]
                        lt = slab[:, off + ci * 128:off + ci * 128 + 128]
                        for half in range(2):
                            last = t.matmul(
                                halves[half], lhsT=lt,
                                rhs=mk.B16[:, hc * D + half * 512:
                                           hc * D + half * 512 + 512],
                                start=(hc == 0), stop=(hc == OCF - 1))
                    if hc == OCF - 1:
                        mk.inc("phc", last)
                    elif k % 2 == 1:
                        mk.inc("phg", last)
                    if fuse_ln and k >= 1:
                        if hc == 14:
                            emit_tr(t, 2 * (k - 1))
                        elif hc == 18:
                            emit_tr(t, 2 * (k - 1) + 1)
            if fuse_ln:
                emit_tr(t, nch - 2)
                emit_tr(t, nch - 1)
            mk.inc("mst", t.drain())

        def fm_copy(v, ch):
            v.wait_ge(mk.sems["phd"], phd0 + ch + 1)
            dst = mk.A16[:, 0:HC * T].rearrange(
                "p (h t) -> p h t", h=HC)[:, :, ch * 128:(ch + 1) * 128]
            v.tensor_copy(out=dst,
                          in_=chunk_psum(ch)[2].rearrange(
                              "p (h t) -> p h t", h=HC))
            mk.inc("phe", v.drain())

        def fvt(v):
            mk.entry(v, "v")
            for ch in range(nch):
                k = ch // 2
                v.wait_ge(mk.sems["phc"], phc0 + k + 1)
                mk.rwait(v, "dxl", ch, 4, dxlsnap, 16)
                halves, full, _ = chunk_psum(ch)
                xa = xa_ap(ch)
                xo = xo_ap(ch)
                if ch >= 2:
                    mk.rwait(v, "dxs", ch, 2, dxssnap, 16, occ=ch // 2 - 1)
                if full is not None:
                    v.scalar_tensor_tensor(
                        out=xo, in0=full, scalar=mk.SC[:, SC_HSO:SC_HSO + 1],
                        in1=xa, op0=OP.mult, op1=OP.add)
                else:
                    for j in range(2):
                        v.scalar_tensor_tensor(
                            out=xo[:, j * 512:(j + 1) * 512], in0=halves[j],
                            scalar=mk.SC[:, SC_HSO:SC_HSO + 1],
                            in1=xa[:, j * 512:(j + 1) * 512],
                            op0=OP.mult, op1=OP.add)
                v.drain()
                if fuse_ln:
                    mk.ln_v_pre(v, ch % mk.NLN, mk.LNa, xo)
                    if ch >= 1:
                        fm_copy(v, ch - 1)
                    y2t = y2_ap(ch)
                    if ch >= 2:
                        v.wait_ge(mk.sems["phd"], phd0 + ch - 1)
                        mk.rwait(v, "dys", ch, 2, dyssnap, 16,
                                 occ=ch // 2 - 1)
                    mk.ln_v_apply(v, ch % mk.NLN, mk.LNa, xo, y2t,
                                  lns0 + ch + 1)
                mk.inc("phf", v.drain())
            if fuse_ln:
                fm_copy(v, nch - 1)
            mk.inc("msv", v.drain())

        self.blk.sync(fsy)
        self.blk.gpsimd(fg)
        self.blk.scalar(fs)
        self.blk.tensor(ft)
        self.blk.vector(fvt)
        self.snap()

    def phase_q(self):
        mk = self
        T, nch = self.T, self.nch
        dw0 = self.cnt["dW"]
        pha0 = self.cnt["pha"]
        phb0 = self.cnt["phb"]
        phc0 = self.cnt["phc"]
        phd0 = self.cnt["phd"]
        phe0 = self.cnt["phe"]
        kvb0 = self.cnt["kvb"]

        def fwl(sync):
            mk.entry(sync, "y")
            sync.wait_ge(mk.sems["dL"], mk.dg_p0)
            sync.dma_start(
                out=mk.WAR[:, 0:8192].rearrange("p (a o) -> p a o", a=HC),
                in_=mk.dwq16.rearrange("(a p) o -> p a o", p=128))\
                .then_inc(mk.sems["dW"], 16)
            mk.cnt["dW"] += 16
            mk.inc("msy", sync.nop())

        def emit_q(t, tc):
            if tc >= 2:
                t.wait_ge(mk.sems["kvb"], kvb0 + tc - 1)
                t.wait_ge(mk.sems["phc"], phc0 + tc - 1)
            last = None
            for hc in range(HC):
                for half in range(2):
                    last = t.matmul(
                        mk.pd[tc % 2][:, half * 512:(half + 1) * 512],
                        lhsT=mk.A16[:, hc * T + tc * 128:
                                    hc * T + tc * 128 + 128],
                        rhs=mk.WAR[:, hc * 1024 + half * 512:
                                   hc * 1024 + half * 512 + 512],
                        start=(hc == 0), stop=(hc == HC - 1))
            mk.inc("pha", last)

        def emit_kvtr(t, tc):
            t.wait_ge(mk.sems["phb"], phb0 + tc + 1)
            ftm = mk.C16[:, 8192 + (tc % 3) * 1024:
                         8192 + (tc % 3) * 1024 + 1024]
            vt = mk.C16[:, 12288 + (tc % 3) * 1024:
                        12288 + (tc % 3) * 1024 + 1024]
            for h in range(NH):
                t.matmul(
                    mk.pb[h // 4][:, (h % 4) * 128:(h % 4) * 128 + 128],
                    lhsT=ftm[:, h * 128:(h + 1) * 128],
                    rhs=vt[:, h * 128:(h + 1) * 128],
                    start=(tc == 0 and h % 4 == 0),
                    stop=(tc == nch - 1 and h % 4 == 3))
                t.matmul(
                    mk.pd[2][:, h:h + 1],
                    lhsT=ftm[:, h * 128:(h + 1) * 128],
                    rhs=mk.ones16[:, 0:1],
                    start=(tc == 0 and h == 0),
                    stop=(tc == nch - 1 and h == NH - 1))
            for wave in range(2):
                w = 2 * tc + wave
                if w >= 1:
                    t.wait_ge(mk.sems["phe"], phe0 + w)
                last = None
                for r in range(4):
                    hh = wave * 4 + r
                    last = t.transpose(
                        mk.pdh(2, 1)[:, r * 128:(r + 1) * 128],
                        ftm[:, hh * 128:(hh + 1) * 128], mk.id16[:])
                mk.inc("phd", last)

        def ft(t):
            mk.entry(t, "t")
            t.wait_ge(mk.sems["dW"], dw0 + 16)
            for tc in range(nch):
                emit_q(t, tc)
                if tc >= 2:
                    emit_kvtr(t, tc - 2)
            emit_kvtr(t, nch - 2)
            emit_kvtr(t, nch - 1)
            mk.inc("mst", t.drain())

        def fv(v):
            mk.entry(v, "v")
            for tc in range(nch):
                v.wait_ge(mk.sems["pha"], pha0 + tc + 1)
                pq = mk.pd[tc % 2]
                ein = mk.f32(mk.C16, (tc % 2) * 2048, 1024)
                v.tensor_scalar(out=ein, in0=pq[:, :], scalar1=0.0,
                                scalar2=None, op0=OP.min)
                vt = mk.C16[:, 12288 + (tc % 3) * 1024:
                            12288 + (tc % 3) * 1024 + 1024]
                if tc >= 3:
                    v.wait_ge(mk.sems["phd"], phd0 + 2 * (tc - 2))
                v.tensor_copy(out=vt, in_=pq[:, :])
                mk.inc("kvb", v.drain())
                v.wait_ge(mk.sems["phc"], phc0 + tc + 1)
                e16 = mk.C16[:, 4096 + (tc % 2) * 1024:
                             4096 + (tc % 2) * 1024 + 1024]
                rl = mk.C16[:, 6144 + (tc % 2) * 1024:
                            6144 + (tc % 2) * 1024 + 1024]
                ftm = mk.C16[:, 8192 + (tc % 3) * 1024:
                             8192 + (tc % 3) * 1024 + 1024]
                v.tensor_tensor(out=ftm, in0=e16, in1=rl, op=OP.add)
                mk.inc("phb", v.drain())
            mk.inc("msv", v.drain())

        def emit_fevac(s, tc):
            for wave in range(2):
                s.wait_ge(mk.sems["phd"], phd0 + 2 * tc + wave + 1)
                dst = mk.B16[:, 0:HC * T].rearrange(
                    "p (h t) -> p h t", h=HC)[
                    :, wave * 4:(wave + 1) * 4, tc * 128:(tc + 1) * 128]
                s.copy(out=dst,
                       in_=mk.pdh(2, 1)[:, 0:512].rearrange(
                           "p (h t) -> p h t", h=4))
                mk.inc("phe", s.drain())

        def fs(s):
            mk.entry(s, "s")
            for tc in range(nch):
                s.wait_ge(mk.sems["kvb"], kvb0 + tc + 1)
                if tc >= 2:
                    s.wait_ge(mk.sems["phb"], phb0 + tc - 1)
                ein = mk.f32(mk.C16, (tc % 2) * 2048, 1024)
                e16 = mk.C16[:, 4096 + (tc % 2) * 1024:
                             4096 + (tc % 2) * 1024 + 1024]
                rl = mk.C16[:, 6144 + (tc % 2) * 1024:
                            6144 + (tc % 2) * 1024 + 1024]
                s.activation(out=rl, in_=mk.pd[tc % 2][:, :], func=ACT.Relu)
                s.drain()
                s.activation(out=e16, in_=ein, func=ACT.Exp)
                mk.inc("phc", s.drain())
                if tc >= 1:
                    emit_fevac(s, tc - 1)
            emit_fevac(s, nch - 1)
            mk.inc("mss", s.drain())

        self.blk.sync(fwl)
        self.blk.tensor(ft)
        self.blk.vector(fv)
        self.blk.scalar(fs)
        self.snap()

    def phase_kvx(self):
        mk = self
        phb0 = self.cnt["phb"]

        def fv(v):
            mk.entry(v, "v")
            kvp = mk.f32(mk.C16, 0, NH * 128)
            v.tensor_copy(out=kvp[:, 0:512], in_=mk.pb[0][:, :])
            v.tensor_copy(out=kvp[:, 512:1024], in_=mk.pb[1][:, :])
            ksp = mk.f32(mk.C16, 2 * NH * 128, NH)
            v.tensor_copy(out=ksp, in_=mk.pd[2][:, 0:NH])
            mk.inc("phb", v.drain())
            mk.inc("msv", v.drain())

        def fg(g):
            mk.entry(g, "g")
            g.wait_ge(mk.sems["phb"], phb0 + 1)
            kvp = mk.f32(mk.C16, 0, NH * 128 + NH)
            g.dma_start(out=mk.dkv_in, in_=kvp).then_inc(mk.sems["dG"], 16)
            mk.cnt["dG"] += 16
            mk.w(g, "dG")
            ic = mk.inc("cc", g.collective_compute(
                "AllReduce", OP.add,
                replica_groups=[[0, 1], [2, 3], [4, 5], [6, 7]],
                ins=[mk.dkv_in.opt()], outs=[mk.dkv_out.opt()]))
            g.wait_ge(mk.sems["cc"], ic)
            g.dma_start(out=mk.WAR[:, 0:NH * 128],
                        in_=mk.dkv_out[:, 0:NH * 128])\
                .then_inc(mk.sems["dG"], 16)
            mk.cnt["dG"] += 16
            g.dma_start(out=mk.f32(mk.WAR, 1024, NH),
                        in_=mk.dkv_out[:, NH * 128:NH * 128 + NH])\
                .then_inc(mk.sems["dG"], 16)
            mk.cnt["dG"] += 16
            mk.w(g, "dG")
            mk.inc("kvb", g.drain())
            mk.inc("msg", g.drain())

        kvb0 = self.cnt["kvb"]

        def fv2(v):
            v.wait_ge(mk.sems["kvb"], kvb0 + 1)
            ksbd = mk.WAR[:, 1040:1040 + NH * NH]
            v.memset(ksbd, 0.0)
            v.drain()
            ks = mk.f32(mk.WAR, 1024, NH)
            for h in range(NH):
                v.tensor_copy(out=ksbd[:, h * NH + h:h * NH + h + 1],
                              in_=ks[:, h:h + 1])
            mk.inc("phb", v.drain())
            mk.inc("msv", v.drain())

        self.blk.vector(fv)
        self.blk.gpsimd(fg)
        self.blk.vector(fv2)
        self.snap()

    def phase_m1(self):
        mk = self
        T, ntt = self.T, self.ntt
        pha0 = self.cnt["pha"]
        phb0 = self.cnt["phb"]
        uccsnap = self.rsnap("ucc", 2)
        p0lsnap = self.rsnap("p0l", 2)
        phc0 = self.cnt["phc"]
        phd0 = self.cnt["phd"]
        phe0 = self.cnt["phe"]
        phf0 = self.cnt["phf"]
        dz0 = self.cnt["dZ"]

        def ft(t):
            mk.entry(t, "t")
            for tt in range(ntt):
                if tt >= 1:
                    t.wait_ge(mk.sems["phb"], phb0 + tt)
                last = None
                for fc in range(HC):
                    last = t.matmul(
                        mk.pd[2][0:NH, 0:512],
                        lhsT=mk.WAR[:, 1040 + fc * NH:1040 + (fc + 1) * NH],
                        rhs=mk.B16[:, fc * T + tt * 512:fc * T + tt * 512 + 512],
                        start=(fc == 0), stop=(fc == HC - 1))
                mk.inc("pha", last)
                mk.rwait(t, "p0l", tt, 2, p0lsnap, 16)
                rb = 2560 if tt % 2 == 0 else 10240
                for h in range(NH):
                    hi = tt * NH + h
                    if hi >= 2:
                        t.wait_ge(mk.sems["phd"], phd0 + hi - 1)
                    last = t.matmul(
                        mk.pb[h % 2][:, :],
                        lhsT=mk.onesr16[0:1, :],
                        rhs=mk.C16[0:1, rb + h * 512:rb + h * 512 + 512],
                        start=True, stop=True)
                    mk.inc("phc", last)
                    t.wait_ge(mk.sems["phd"], phd0 + hi + 1)
                    if hi >= 4:
                        t.wait_ge(mk.sems["phf"], phf0 + hi - 3)
                    mmb = [mk.pd[0][:, 0:512], mk.pd[0][:, 512:1024],
                           mk.pd[1][:, 0:512], mk.pd[1][:, 512:1024]][hi % 4]
                    last = t.matmul(
                        mmb,
                        lhsT=mk.WAR[:, h * 128:(h + 1) * 128],
                        rhs=mk.C16[:, 6656 + (hi % 2) * 512:
                                   6656 + (hi % 2) * 512 + 512],
                        start=True, stop=True)
                    mk.inc("phe", last)
            mk.inc("mst", t.drain())

        def fv(v):
            mk.entry(v, "v")
            for tt in range(ntt):
                v.wait_ge(mk.sems["pha"], pha0 + tt + 1)
                if tt >= 1:
                    mk.rwait(v, "ucc", tt - 1, 2, uccsnap, 16)
                zf = mk.f32(mk.C16, 0, 512)[0:NH, :]
                v.tensor_scalar(out=zf, in0=mk.pd[2][0:NH, 0:512],
                                scalar1=ATTN_EPS, scalar2=None, op0=OP.add)
                v.drain()
                zf2 = mk.f32(mk.C16, 1024, 512)[0:NH, :]
                v.reciprocal(out=zf2, in_=zf)
                v.drain()
                v.tensor_scalar(out=mk.C16[0:NH, 2048:2560],
                                in0=zf2, scalar1=float(mk.L), scalar2=None,
                                op0=OP.mult)
                mk.inc("phb", v.drain())
                for h in range(NH):
                    hi = tt * NH + h
                    v.wait_ge(mk.sems["phc"], phc0 + hi + 1)
                    if hi >= 2:
                        v.wait_ge(mk.sems["phe"], phe0 + hi - 1)
                    fz = mk.C16[:, 6656 + (hi % 2) * 512:
                                6656 + (hi % 2) * 512 + 512]
                    v.tensor_tensor(
                        out=fz,
                        in0=mk.B16[:, h * T + tt * 512:h * T + tt * 512 + 512],
                        in1=mk.pb[h % 2][:, :], op=OP.mult)
                    mk.inc("phd", v.drain())
            mk.inc("msv", v.drain())

        def fz(sync):
            mk.entry(sync, "y")
            for tt in range(ntt):
                sync.wait_ge(mk.sems["phb"], phb0 + tt + 1)
                dsl = mk.dzl[0:1, (tt % 2) * NH * 512:
                             (tt % 2 + 1) * NH * 512]
                mk.rissue("ucc", tt, 2, sync.dma_start(
                    out=dsl.rearrange("o (h t) -> (o h) t", h=NH),
                    in_=mk.C16[0:NH, 2048:2560]))
                mk.rwait(sync, "ucc", tt, 2, uccsnap, 16)
                if tt >= 2:
                    sync.wait_ge(mk.sems["phc"], phc0 + NH * (tt - 1))
                rb = 2560 if tt % 2 == 0 else 10240
                mk.rissue("p0l", tt, 2, sync.dma_start(
                    out=mk.C16[0:1, rb:rb + NH * 512], in_=dsl))
            mk.inc("msy", sync.nop())

        def fs(s):
            mk.entry(s, "s")
            for tt in range(ntt):
                for h in range(NH):
                    hi = tt * NH + h
                    s.wait_ge(mk.sems["phe"], phe0 + hi + 1)
                    mmb = [mk.pd[0][:, 0:512], mk.pd[0][:, 512:1024],
                           mk.pd[1][:, 0:512], mk.pd[1][:, 512:1024]][hi % 4]
                    s.activation(out=mk.A16[:, h * T + tt * 512:
                                            h * T + tt * 512 + 512],
                                 in_=mmb, func=ACT.Identity,
                                 scale=1.0 / float(mk.L))
                    mk.inc("phf", s.drain())
            mk.inc("mss", s.drain())

        self.blk.tensor(ft)
        self.blk.vector(fv)
        self.blk.sync(fz)
        self.blk.scalar(fs)
        self.snap()

    def phase_m2a(self):
        mk = self
        T, ntt, nch = self.T, self.ntt, self.nch
        nhalf = 2 if ntt >= 2 else 1
        tph = ntt // nhalf
        dw0 = self.cnt["dW"]
        pha0 = self.cnt["pha"]
        phb0 = self.cnt["phb"]
        phc0 = self.cnt["phc"]
        phd0 = self.cnt["phd"]
        dyssnap = self.rsnap("dys", 2)
        lnv0 = self.cnt["lnv"]
        lns0 = self.cnt["lns"]
        groups = [(oc, hf) for oc in range(HC) for hf in range(nhalf)]
        banksets = [
            [self.pb[0][:, :], self.pb[1][:, :],
             self.pd[2][:, 0:512], self.pd[2][:, 512:1024]][:tph],
            [self.pd[0][:, 0:512], self.pd[0][:, 512:1024],
             self.pd[1][:, 0:512], self.pd[1][:, 512:1024]][:tph],
        ]

        def fwl(sync):
            mk.entry(sync, "y")
            sync.wait_ge(mk.sems["dL"], mk.dg_p0)
            sync.dma_start(
                out=mk.WAR[:, 0:8192].rearrange("p (a o) -> p a o", a=HC),
                in_=mk.dwm16.rearrange("(a p) o -> p a o", p=128))\
                .then_inc(mk.sems["dW"], 16)
            mk.cnt["dW"] += 16
            mk.inc("msy", sync.nop())

        def ft(t):
            mk.entry(t, "t")
            t.wait_ge(mk.sems["dW"], dw0 + 16)
            for gi, (oc, hf) in enumerate(groups):
                if gi >= 2:
                    t.wait_ge(mk.sems["phb"], phb0 + gi - 1)
                bs = banksets[gi % 2]
                last = None
                for vc in range(HC):
                    for j in range(tph):
                        tt = hf * tph + j
                        last = t.matmul(
                            bs[j],
                            lhsT=mk.WAR[:, vc * 1024 + oc * 128:
                                        vc * 1024 + oc * 128 + 128],
                            rhs=mk.A16[:, vc * T + tt * 512:
                                       vc * T + tt * 512 + 512],
                            start=(vc == 0), stop=(vc == HC - 1))
                mk.inc("pha", last)
            t.wait_ge(mk.sems["phb"], phb0 + len(groups))
            for ch in range(nch):
                if ch >= 2:
                    t.wait_ge(mk.sems["phd"], phd0 + ch - 1)
                last = None
                for oc in range(HC):
                    last = t.transpose(
                        mk.pdh(ch % 2, 0)[:, oc * 128:(oc + 1) * 128],
                        mk.B16[:, oc * T + ch * 128:oc * T + ch * 128 + 128],
                        mk.id16[:])
                mk.inc("phc", last)
            mk.inc("mst", t.drain())

        def fs(s):
            mk.entry(s, "s")
            for gi, (oc, hf) in enumerate(groups):
                s.wait_ge(mk.sems["pha"], pha0 + gi + 1)
                bs = banksets[gi % 2]
                last = None
                for j in range(tph):
                    tt = hf * tph + j
                    last = s.copy(out=mk.B16[:, oc * T + tt * 512:
                                             oc * T + tt * 512 + 512],
                                  in_=bs[j])
                mk.inc("phb", last)
            mk.inc("mss", s.drain())

        phg0 = self.cnt["phg"]

        def fvt(v):
            mk.entry(v, "v")
            for ch in range(nch):
                v.wait_ge(mk.sems["phc"], phc0 + ch + 1)
                ptm = mk.pdh(ch % 2, 0)
                k = ch % mk.NLN
                mk.ln_v_pre(v, k, mk.LNb, ptm)
                ya = mk.C16[:, (ch % 2) * 2048:(ch % 2) * 2048 + 1024]
                if ch >= 2:
                    v.wait_ge(mk.sems["phg"], phg0 + ch - 1)
                mk.ln_v_apply(v, k, mk.LNb, ptm, ya, lns0 + ch + 1)
                mk.inc("phd", v.drain())
            mk.inc("msv", v.drain())

        def fgt(g):
            mk.entry(g, "g")
            for ch in range(nch):
                g.wait_ge(mk.sems["phd"], phd0 + ch + 1)
                ya = mk.C16[:, (ch % 2) * 2048:(ch % 2) * 2048 + 1024]
                yb = mk.C16[:, (ch % 2) * 2048 + 1024:(ch % 2) * 2048 + 2048]
                if ch >= 2:
                    mk.rwait(g, "dys", ch, 2, dyssnap, 16, occ=ch // 2 - 1)
                g.tensor_tensor(out=yb, in0=ya, in1=mk.g1r[:], op=OP.mult)
                g.drain()
                g.tensor_tensor(out=yb, in0=yb, in1=mk.b1r[:], op=OP.add)
                mk.inc("phg", g.drain())
            mk.inc("msg", g.drain())

        def fst(s):
            for ch in range(nch):
                s.wait_ge(mk.sems["phc"], phc0 + ch + 1)
                ptm = mk.pdh(ch % 2, 0)
                sqscr = mk.f32(mk.C16, 4096 + (ch % 2) * 2048, 1024)
                mk.ln_s(s, ch % mk.NLN, mk.LNb, ptm, sqscr,
                        lnv0 + ch + 1)
            mk.inc("mss", s.drain())

        def fys(sync):
            for ch in range(nch):
                sync.wait_ge(mk.sems["phg"], phg0 + ch + 1)
                yb = mk.C16[:, (ch % 2) * 2048 + 1024:(ch % 2) * 2048 + 2048]
                mk.rissue("dys", ch, 2, sync.dma_start(
                    out=mk.dya1[ch * 128:(ch + 1) * 128, :], in_=yb))
            mk.rtotal(sync, "dys", 2)
            mk.inc("msy", sync.nop())

        self.blk.sync(fwl)
        self.blk.tensor(ft)
        self.blk.scalar(fs)
        self.blk.vector(fvt)
        self.blk.gpsimd(fgt)
        self.blk.scalar(fst)
        self.blk.sync(fys)
        self.snap()

    def phase_m2b(self):
        mk = self
        T, ntt = self.T, self.ntt
        nhalf = 2 if ntt >= 2 else 1
        tph = ntt // nhalf
        stw = tph * 512
        dw0 = self.cnt["dW"]
        pha0 = self.cnt["pha"]
        phb0 = self.cnt["phb"]
        ugssnap = self.rsnap("ugs", 2)
        groups = [(oc2, hf) for oc2 in range(KC2) for hf in range(nhalf)]
        banksets = [
            [self.pd[0][:, 0:512], self.pd[0][:, 512:1024],
             self.pd[1][:, 0:512], self.pd[1][:, 512:1024]][:tph],
            [self.pd[2][:, 0:512], self.pd[2][:, 512:1024],
             self.pb[0][:, :], self.pb[1][:, :]][:tph],
        ]

        def fwl(sync):
            mk.entry(sync, "y")
            sync.wait_ge(mk.sems["dL"], mk.dg_p0)
            sync.dma_start(
                out=mk.WAR[:, 0:16384].rearrange("p (a o) -> p a o", a=HC),
                in_=mk.dw116.rearrange("(a p) o -> p a o", p=128))\
                .then_inc(mk.sems["dW"], 16)
            mk.cnt["dW"] += 16
            mk.inc("msy", sync.nop())

        def ft(t):
            mk.entry(t, "t")
            t.wait_ge(mk.sems["dW"], dw0 + 16)
            for gi, (oc2, hf) in enumerate(groups):
                if gi >= 2:
                    t.wait_ge(mk.sems["phb"], phb0 + gi - 1)
                bs = banksets[gi % 2]
                last = None
                for vc in range(HC):
                    for j in range(tph):
                        tt = hf * tph + j
                        last = t.matmul(
                            bs[j],
                            lhsT=mk.WAR[:, vc * 2048 + oc2 * 128:
                                        vc * 2048 + oc2 * 128 + 128],
                            rhs=mk.B16[:, vc * T + tt * 512:
                                       vc * T + tt * 512 + 512],
                            start=(vc == 0), stop=(vc == HC - 1))
                mk.inc("pha", last)
            mk.inc("mst", t.drain())

        def fs(s):
            mk.entry(s, "s")
            for gi, (oc2, hf) in enumerate(groups):
                s.wait_ge(mk.sems["pha"], pha0 + gi + 1)
                if gi >= 2:
                    mk.rwait(s, "ugs", gi, 2, ugssnap, 16, occ=gi // 2 - 1)
                stg = mk.C16[:, (gi % 2) * stw:(gi % 2) * stw + stw]
                bs = banksets[gi % 2]
                last = None
                for j in range(tph):
                    last = s.activation(out=stg[:, j * 512:(j + 1) * 512],
                                        in_=bs[j], func=ACT.Relu)
                mk.inc("phb", last)
            mk.inc("mss", s.drain())

        def fsp(sync):
            for gi, (oc2, hf) in enumerate(groups):
                sync.wait_ge(mk.sems["phb"], phb0 + gi + 1)
                stg = mk.C16[:, (gi % 2) * stw:(gi % 2) * stw + stw]
                mk.rissue("ugs", gi, 2, sync.dma_start(
                    out=mk.du1sp[oc2 * 128:(oc2 + 1) * 128,
                                 hf * stw:hf * stw + stw], in_=stg))
            mk.rtotal(sync, "ugs", 2)
            mk.inc("msy", sync.nop())

        self.blk.sync(fwl)
        self.blk.tensor(ft)
        self.blk.scalar(fs)
        self.blk.sync(fsp)
        self.snap()

    def phase_m2c(self):
        mk = self
        T, ntt, nch = self.T, self.ntt, self.nch
        dw0 = self.cnt["dW"]
        dxlsnap = self.rsnap("dxl", 2)
        dxssnap = self.rsnap("dxs", 2)
        pha0 = self.cnt["pha"]
        phb0 = self.cnt["phb"]
        phc0 = self.cnt["phc"]
        phd0 = self.cnt["phd"]
        dx0_ = self.cnt["dX"]
        lnv0 = self.cnt["lnv"]
        lns0 = self.cnt["lns"]

        def fwl(sync):
            mk.entry(sync, "y")
            sync.wait_ge(mk.sems["dL"], mk.dg_p0)
            sync.dma_start(
                out=mk.WAR[:, 0:16384].rearrange("p (a o) -> p a o", a=KC2),
                in_=mk.dw216.rearrange("(a p) o -> p a o", p=128))\
                .then_inc(mk.sems["dW"], 16)
            mk.cnt["dW"] += 16
            for kc in range(KC2):
                buf = mk.A16 if kc < 8 else mk.B16
                sync.dma_start(out=buf[:, (kc % 8) * T:(kc % 8) * T + T],
                               in_=mk.du1sp[kc * 128:(kc + 1) * 128, :])\
                    .then_inc(mk.sems["dW"], 16)
                mk.cnt["dW"] += 16
            mk.inc("msy", sync.nop())

        def xstore(sync, ch, phe_base):
            sync.wait_ge(mk.sems["phe"], phe_base + ch + 1)
            xa = mk.f32(mk.C16, 8192 + (ch % 2) * 2048, 1024)
            mk.rissue("dxs", ch, 2, sync.dma_start(
                out=mk.dxx2[ch * 128:(ch + 1) * 128, :], in_=xa))

        phe_base = self.cnt["phe"]

        def fxl(sync):
            for ch in range(nch):
                if ch >= 2:
                    sync.wait_ge(mk.sems["phe"], phe_base + ch - 1)
                    mk.rwait(sync, "dxs", ch - 2, 2, dxssnap, 16)
                xa = mk.f32(mk.C16, 8192 + (ch % 2) * 2048, 1024)
                mk.rissue("dxl", ch, 2, sync.dma_start(
                    out=xa, in_=mk.dxx1[ch * 128:(ch + 1) * 128, :]))
                y2t = mk.C16[:, 12288 + (ch % 2) * 1024:
                             12288 + (ch % 2) * 1024 + 1024]
                mk.rissue("dxl", ch, 2, sync.dma_start(
                    out=y2t, in_=mk.dy2sp[ch * 128:(ch + 1) * 128, :]))
                ya1t = mk.C16[:, 14336 + (ch % 2) * 1024:
                              14336 + (ch % 2) * 1024 + 1024]
                mk.rissue("dxl", ch, 2, sync.dma_start(
                    out=ya1t, in_=mk.dya1[ch * 128:(ch + 1) * 128, :]))
                if ch >= 1:
                    xstore(sync, ch - 1, phe_base)
            xstore(sync, nch - 1, phe_base)
            mk.rtotal(sync, "dxs", 2)
            mk.inc("msy", sync.nop())

        banks4 = [self.pd[2][:, 0:512], self.pd[2][:, 512:1024],
                  self.pb[0][:, :], self.pb[1][:, :]]

        def ft(t):
            mk.entry(t, "t")
            t.wait_ge(mk.sems["dW"], dw0 + 16 * (KC2 + 1))
            gi = 0
            for tt in range(ntt):
                for oc in range(HC):
                    gi += 1
                    if gi > 4:
                        t.wait_ge(mk.sems["phb"], phb0 + gi - 4)
                    last = None
                    for kc in range(KC2):
                        buf = mk.A16 if kc < 8 else mk.B16
                        last = t.matmul(
                            banks4[gi % 4],
                            lhsT=mk.WAR[:, kc * 1024 + oc * 128:
                                        kc * 1024 + oc * 128 + 128],
                            rhs=buf[:, (kc % 8) * T + tt * 512:
                                    (kc % 8) * T + tt * 512 + 512],
                            start=(kc == 0), stop=(kc == KC2 - 1))
                    mk.inc("pha", last)
                for c2 in range(4):
                    ch = tt * 4 + c2
                    t.wait_ge(mk.sems["phb"], phb0 + (tt + 1) * HC)
                    if ch >= 4:
                        t.wait_ge(mk.sems["phe"], phe_base + ch - 3)
                    last = None
                    for oc in range(HC):
                        cc = (tt % 2) * 4096 + c2 * 1024 + oc * 128
                        last = t.transpose(
                            mk.pdh(ch % 2, (ch // 2) % 2)[:, oc * 128:
                                                          (oc + 1) * 128],
                            mk.C16[:, cc:cc + 128], mk.id16[:])
                    mk.inc("phc", last)
            mk.inc("mst", t.drain())

        def fs(s):
            mk.entry(s, "s")
            gi = 0
            for tt in range(ntt):
                for oc in range(HC):
                    gi += 1
                    s.wait_ge(mk.sems["pha"], pha0 + gi)
                    if tt >= 2 and oc == 0:
                        s.wait_ge(mk.sems["phc"], phc0 + (tt - 2 + 1) * 4)
                    base = (tt % 2) * 4096
                    stg = mk.C16[:, base:base + 4096].rearrange(
                        "p (c q) -> p c q", q=1024)[:, :, oc * 128:(oc + 1) * 128]
                    s.copy(out=stg,
                           in_=banks4[gi % 4].rearrange("p (c o) -> p c o",
                                                        o=128))
                    mk.inc("phb", s.drain())
                for c2 in range(4):
                    ch = tt * 4 + c2
                    s.wait_ge(mk.sems["phc"], phc0 + ch + 1)
                    ptm = mk.pdh(ch % 2, (ch // 2) % 2)
                    sqscr = mk.C16[:, (tt % 2) * 4096 + c2 * 1024:
                                   (tt % 2) * 4096 + c2 * 1024 + 1024]
                    mk.ln_s(s, ch % mk.NLN, mk.LNb, ptm, sqscr,
                            lnv0 + ch + 1)
            mk.inc("mss", s.drain())

        def fv(v):
            mk.entry(v, "v")
            for tt in range(ntt):
                for c2 in range(4):
                    ch = tt * 4 + c2
                    v.wait_ge(mk.sems["phc"], phc0 + ch + 1)
                    mk.rwait(v, "dxl", ch, 2, dxlsnap, 48)
                    ptm = mk.pdh(ch % 2, (ch // 2) % 2)
                    k = ch % mk.NLN
                    mk.ln_v_pre(v, k, mk.LNb, ptm)
                    ya2 = mk.C16[:, (tt % 2) * 4096 + c2 * 1024:
                                 (tt % 2) * 4096 + c2 * 1024 + 1024]
                    mk.ln_v_apply(v, k, mk.LNb, ptm, ya2,
                                  lns0 + ch + 1)
                    v.drain()
                    v.tensor_tensor(out=ya2, in0=ya2, in1=mk.g2r[:],
                                    op=OP.mult)
                    v.drain()
                    v.tensor_tensor(out=ya2, in0=ya2, in1=mk.b2r[:],
                                    op=OP.add)
                    v.drain()
                    xa = mk.f32(mk.C16, 8192 + (ch % 2) * 2048, 1024)
                    y2t = mk.C16[:, 12288 + (ch % 2) * 1024:
                                 12288 + (ch % 2) * 1024 + 1024]
                    ya1t = mk.C16[:, 14336 + (ch % 2) * 1024:
                                  14336 + (ch % 2) * 1024 + 1024]
                    if ch >= 2:
                        mk.rwait(v, "dxs", ch, 2, dxssnap, 16,
                                 occ=ch // 2 - 1)
                    v.tensor_tensor(out=xa, in0=xa, in1=y2t, op=OP.add)
                    v.drain()
                    v.tensor_tensor(out=xa, in0=xa, in1=ya1t, op=OP.add)
                    v.drain()
                    v.tensor_tensor(out=xa, in0=xa, in1=ya2, op=OP.add)
                    mk.inc("phe", v.drain())
            mk.inc("msv", v.drain())

        self.blk.sync(fwl)
        self.blk.sync(fxl)
        self.blk.tensor(ft)
        self.blk.scalar(fs)
        self.blk.vector(fv)
        self.snap()

    def finale(self):
        mk = self

        def f(sync):
            mk.entry(sync, "y")
            sync.nop()

        self.blk.sync(f)


def build_program(T, L):
    mk = MK(T, L)
    nc = mk.build()
    return nc


def make_wsh(wup, wgate, wdown, c):
    parts = []
    for j in range(3 * c, 3 * c + 3):
        if j < 8:
            u = np.concatenate([wup[(3 * j + s) * 128:(3 * j + s + 1) * 128, :]
                                for s in range(3)], axis=1)
        elif j < 16:
            jj = j - 8
            u = np.concatenate(
                [wgate[(3 * jj + s) * 128:(3 * jj + s + 1) * 128, :]
                 for s in range(3)], axis=1)
        else:
            u = wdown[(j - 16) * 128:(j - 16 + 1) * 128, :]
        parts.append(u)
    return np.ascontiguousarray(np.concatenate(parts, axis=0))


def kernel(**inputs):
    x = np.ascontiguousarray(inputs["L0_attn_x"])
    B, L, _ = x.shape
    T = L // 2
    nc = build_program(T, L)
    mybir.codegen_inst_isa_subclasses(nc)
    ident = np.eye(128, dtype=np.float32)
    wupf = np.ascontiguousarray(inputs["L0_ffn_w_up"], dtype=np.float32)
    wgatef = np.ascontiguousarray(inputs["L0_ffn_w_gate"], dtype=np.float32)
    wdownf = np.ascontiguousarray(inputs["L0_ffn_w_down"], dtype=np.float32)
    in_maps = []
    for c in range(NCORES):
        b, h = c // 2, c % 2
        in_maps.append({
            "wsh": make_wsh(wupf, wgatef, wdownf, c),
            "x0": np.ascontiguousarray(x[b, h * T:(h + 1) * T, :]),
            "wq": np.ascontiguousarray(inputs["L0_attn_Wq"], dtype=np.float32),
            "wm": np.ascontiguousarray(inputs["L0_attn_Wm"], dtype=np.float32),
            "w1": np.ascontiguousarray(inputs["L0_attn_W1"], dtype=np.float32),
            "w2": np.ascontiguousarray(inputs["L0_attn_W2"], dtype=np.float32),
            "g1": np.ascontiguousarray(inputs["L0_attn_g1"],
                                       dtype=np.float32).reshape(1, D),
            "b1": np.ascontiguousarray(inputs["L0_attn_b1"],
                                       dtype=np.float32).reshape(1, D),
            "g2": np.ascontiguousarray(inputs["L0_attn_g2"],
                                       dtype=np.float32).reshape(1, D),
            "b2": np.ascontiguousarray(inputs["L0_attn_b2"],
                                       dtype=np.float32).reshape(1, D),
            "ident": ident,
        })
    res = run_bass_kernel_spmd(nc, in_maps, core_ids=list(range(NCORES)))
    out = np.empty((B, L, D), dtype=np.float32)
    for c in range(NCORES):
        b, h = c // 2, c % 2
        out[b, h * T:(h + 1) * T, :] = np.asarray(res.results[c]["out"])
    return out


# version 16
# speedup vs baseline: 1.0050x; 1.0050x over previous
import numpy as np
import concourse.bass as bass
import concourse.mybir as mybir
from concourse.alu_op_type import AluOpType as OP
from concourse.bass_utils import run_bass_kernel_spmd
from contextlib import ExitStack

F32 = mybir.dt.float32
F16 = mybir.dt.float16
ACT = mybir.ActivationFunctionType
AX = mybir.AxisListType

NCORES = 8
D = 1024
HC = 8
DFF = 3072
OCF = 24
D2 = 2048
KC2 = 16
NH = 8
MAGIC = 12582912.0
P23 = 8388608.0
QA = 32767.0
QW = 7.0
LN_EPS = 1e-5
ATTN_EPS = 1e-6

SC_RY, SC_SY = 0, 1
SC_RU, SC_SU = 2, 3
SC_RG, SC_SG, SC_BG = 4, 5, 6
SC_R2, SC_S2 = 7, 8
SC_RO, SC_SO = 9, 10
SC_HSO = 11


class MK:
    def __init__(self, T, L):
        self.T = T
        self.L = L
        self.ntt = T // 512
        self.nch = T // 128
        self.TE = min(T, 2048)
        self.cnt = {}
        self.sems = {}
        self.gatev = {}
        self.nc = bass.Bass("TRN2", target_bir_lowering=False, debug=False)

    def mksem(self, name):
        self.sems[name] = self.es.enter_context(self.nc.semaphore(name))
        self.cnt[name] = 0

    def inc(self, name, instr, n=1):
        instr.then_inc(self.sems[name], n)
        self.cnt[name] += n
        return self.cnt[name]

    def w(self, eng, name, v=None):
        eng.wait_ge(self.sems[name], self.cnt[name] if v is None else v)

    def snap(self):
        self.gatev = {e: self.cnt["ms" + e] for e in ("v", "s", "t", "g", "y")}

    def mkring(self, name, R):
        for r in range(R):
            self.mksem(f"{name}{r}")
        return name

    def rsem(self, name, i, R):
        return self.sems[f"{name}{i % R}"]

    def rissue(self, name, i, R, instr, n=16):
        nm = f"{name}{i % R}"
        instr.then_inc(self.sems[nm], n)
        self.cnt[nm] += n

    def rsnap(self, name, R):
        return [self.cnt[f"{name}{r}"] for r in range(R)]

    def rwait(self, eng, name, i, R, snap, per, occ=None):
        o = (i // R if occ is None else occ)
        eng.wait_ge(self.sems[f"{name}{i % R}"], snap[i % R] + per * (o + 1))

    def rtotal(self, eng, name, R):
        for r in range(R):
            nm = f"{name}{r}"
            eng.wait_ge(self.sems[nm], self.cnt[nm])

    def entry(self, eng, me):
        for e in ("v", "s", "t", "g", "y"):
            if e != me and self.gatev.get(e, 0) > 0:
                eng.wait_ge(self.sems["ms" + e], self.gatev[e])

    def f32(self, buf, lo16, n32):
        return buf[:, lo16:lo16 + 2 * n32].bitcast(F32)

    def pdh(self, i, b):
        return self.pd[i][:, b * 512:(b + 1) * 512].bitcast(F16)

    def pbh(self, i):
        return self.pb[i][:, :].bitcast(F16)

    def build(self):
        nc = self.nc
        T = self.T
        es = ExitStack()
        self.es = es
        with es:
            E = es.enter_context
            gx = lambda n, s, dt=F32, k="ExternalInput": nc.dram_tensor(n, s, dt, kind=k).ap()
            gi = lambda n, s, dt=F32: nc.dram_tensor(n, s, dt).ap()
            self.dx0 = gx("x0", [T, D])
            self.dwq = gx("wq", [D, D])
            self.dwm = gx("wm", [D, D])
            self.dw1 = gx("w1", [D, D2])
            self.dw2 = gx("w2", [D2, D])
            self.dg1 = gx("g1", [1, D])
            self.db1 = gx("b1", [1, D])
            self.dg2 = gx("g2", [1, D])
            self.db2 = gx("b2", [1, D])
            self.dident = gx("ident", [128, 128])
            self.dout = gx("out", [T, D], F32, "ExternalOutput")
            self.dxx1 = gi("xx1", [T, D])
            self.dxx2 = gi("xx2", [T, D])
            self.dy2sp = gi("y2sp", [T, D], F16)
            self.dya1 = gi("ya1", [T, D], F16)
            self.dupsp = gi("upsp", [DFF, T], F16)
            self.dgatesp = gi("gatesp", [DFF, T], F16)
            self.dg2sp = gi("g2sp", [DFF, T], F16)
            self.dosp = gi("osp", [DFF, T], F16)
            self.doqsp = gi("oqsp", [DFF, T], F16)
            self.du1sp = gi("u1sp", [D2, T], F16)
            self.dwq16 = gi("wq16", [D, D], F16)
            self.dwm16 = gi("wm16", [D, D], F16)
            self.dw116 = gi("w116", [D, D2], F16)
            self.dw216 = gi("w216", [D2, D], F16)
            self.dwsh = gx("wsh", [3 * 128, DFF])
            self.dwall_sh = gi("wall_sh", [3, OCF, 128, 128], F16)
            self.dwall = gi("wall", [OCF, OCF, 128, 128], F16)
            self.ddcol = gi("dcol", [128, 2])
            self.dbm_in = gi("bm_in", [1, 2])
            self.dbm_out = gi("bm_out", [1, 2])
            self.dbscal = gi("bscal", [1, 16])
            self.dkv_in = gi("kv_in", [128, NH * 128 + NH])
            self.dkv_out = gi("kv_out", [128, NH * 128 + NH])
            self.dzl = gi("dzl", [1, 2 * NH * 512], F16)
            self.did16 = gi("id16d", [128, 128], F16)

            ACOLS = max(HC * T, OCF * D, 31744)
            self.ACOLS = ACOLS
            self.A16 = E(nc.sbuf_tensor("A16", [128, ACOLS], F16))
            self.B16 = E(nc.sbuf_tensor("B16", [128, ACOLS], F16))
            self.C16 = E(nc.sbuf_tensor("C16", [128, 18432], F16))
            self.WAR = E(nc.sbuf_tensor("WAR", [128, 16384], F16))
            self.SC = E(nc.sbuf_tensor("SCt", [128, 16], F32))
            self.MAGCOL = E(nc.sbuf_tensor("MAGCOL", [128, 1], F32))
            self.MAGNEG = E(nc.sbuf_tensor("MAGNEG", [128, 1], F32))
            self.EPSCOL = E(nc.sbuf_tensor("EPSCOL", [128, 1], F32))
            self.srow = E(nc.sbuf_tensor("srow", [2, 128], F32))
            self.gmaxr = E(nc.sbuf_tensor("gmaxr", [2, 16], F32))
            self.dervr = E(nc.sbuf_tensor("dervr", [1, 16], F32))
            self.id16 = E(nc.sbuf_tensor("id16", [128, 128], F16))
            self.ones16 = E(nc.sbuf_tensor("ones16", [128, 1], F16))
            self.onesr16 = E(nc.sbuf_tensor("onesr16", [1, 128], F16))
            self.g1r = E(nc.sbuf_tensor("g1r", [128, D], F16))
            self.g2r = E(nc.sbuf_tensor("g2r", [128, D], F16))
            self.b1r = E(nc.sbuf_tensor("b1r", [128, D], F16))
            self.b2r = E(nc.sbuf_tensor("b2r", [128, D], F16))
            self.NLN = 4
            mkln = lambda p: {k: E(nc.sbuf_tensor(p + k, [128, self.NLN], F32))
                              for k in ["sum", "sq", "mu", "mu2", "vpe",
                                        "std", "rstd"]}
            self.LNa = mkln("lna")
            self.LNb = mkln("lnb")
            self.rmaxA = E(nc.sbuf_tensor("rmaxA", [128, 1], F32))
            self.rmaxB = E(nc.sbuf_tensor("rmaxB", [128, 1], F32))
            self.tmax = E(nc.sbuf_tensor("tmax", [128, 8], F32))

            self.pd = [E(nc.psum_tensor(f"pd{i}", [128, 1024], F32))
                       for i in range(3)]
            self.pb = [E(nc.psum_tensor(f"pb{i}", [128, 512], F32))
                       for i in range(2)]

            for s in ["dL", "dSP", "dW", "dG", "dE", "dX", "dO", "dY", "dZ",
                      "dI",
                      "msv", "mss", "mst", "msg", "msy",
                      "bar", "cc", "kvb", "lnv", "lns", "wtg",
                      "fua", "fub", "fuc", "fud", "fue", "fuf",
                      "pha", "phb", "phc", "phd", "phe", "phf", "phg"]:
                self.mksem(s)
            for name, R in [("p0l", 2), ("p0s", 2), ("yl", 4), ("ugw", 4),
                            ("ugs", 2), ("e1i", 3), ("e1o", 2), ("e2i", 3),
                            ("e2o", 2), ("dqr", 6), ("dqo", 2), ("dxl", 4),
                            ("dxs", 2), ("dys", 2), ("ucc", 2),
                            ("ylg", 2)]:
                self.mkring(name, R)

            self.blk = E(nc.Block())
            self.emit_all()
        return nc

    def ln_v_pre(self, v, k, LNS, xin):
        v.tensor_reduce(out=LNS["sum"][:, k:k + 1], in_=xin, axis=AX.X,
                        op=OP.add)
        v.drain()
        v.tensor_scalar(out=LNS["mu"][:, k:k + 1], in0=LNS["sum"][:, k:k + 1],
                        scalar1=1.0 / D, scalar2=None, op0=OP.mult)
        return self.inc("lnv", v.drain())

    def ln_s(self, s, k, LNS, xin, sqscr, lnv_val):
        s.wait_ge(self.sems["lnv"], lnv_val)
        s.activation(out=sqscr, in_=xin, func=ACT.Square,
                     accum_out=LNS["sq"][:, k:k + 1])
        s.drain()
        s.activation(out=LNS["mu2"][:, k:k + 1], in_=LNS["sum"][:, k:k + 1],
                     func=ACT.Square, scale=1.0 / D)
        s.drain()
        s.activation(out=LNS["vpe"][:, k:k + 1], in_=LNS["mu2"][:, k:k + 1],
                     func=ACT.Identity, scale=-1.0, bias=self.EPSCOL[:, 0:1])
        s.drain()
        s.activation(out=LNS["vpe"][:, k:k + 1], in_=LNS["sq"][:, k:k + 1],
                     func=ACT.Identity, scale=1.0 / D,
                     bias=LNS["vpe"][:, k:k + 1])
        s.drain()
        s.activation(out=LNS["std"][:, k:k + 1], in_=LNS["vpe"][:, k:k + 1],
                     func=ACT.Sqrt)
        return self.inc("lns", s.drain())

    def ln_v_apply(self, v, k, LNS, xin, out_ap, lns_val):
        v.wait_ge(self.sems["lns"], lns_val)
        v.reciprocal(out=LNS["rstd"][:, k:k + 1], in_=LNS["std"][:, k:k + 1])
        v.drain()
        return v.tensor_scalar(out=out_ap, in0=xin,
                               scalar1=LNS["mu"][:, k:k + 1],
                               scalar2=LNS["rstd"][:, k:k + 1],
                               op0=OP.subtract, op1=OP.mult)

    def absfold(self, v, tile, slot, rmax):
        ts = self.tmax[:, slot % 8:slot % 8 + 1]
        v.tensor_reduce(out=ts, in_=tile, axis=AX.X, op=OP.max,
                        apply_absolute_value=True)
        v.drain()
        v.tensor_tensor(out=rmax[:], in0=rmax[:], in1=ts, op=OP.max)
        v.drain()

    def emit_all(self):
        self.p0()
        self.phase_y(self.dx0, split_loads=False)
        self.barrier([(self.rmaxA, [(SC_RY, "rinv"), (SC_SY, "s")])])
        self.phase_qy()
        self.phase_ug_gate()
        self.barrier([(self.rmaxB, [(SC_RG, "rinv"), (SC_SG, "s"),
                                    (SC_BG, "bg")])])
        self.phase_ug_up_e1()
        self.barrier([(self.rmaxB, [(SC_RU, "rinv"), (SC_SU, "s")]),
                      (self.rmaxA, [(SC_R2, "rinv"), (SC_S2, "s")])])
        self.phase_e2()
        self.barrier([(self.rmaxA, [(SC_RO, "rinv"), (SC_SO, "s"),
                                    (SC_HSO, "hs")])])
        self.phase_d(self.dx0, self.dxx1, fuse_ln=True)
        self.phase_q()
        self.phase_kvx()
        self.phase_m1()
        self.phase_m2a()
        self.phase_m2b()
        self.phase_m2c()
        self.phase_y(self.dxx2)
        self.barrier([(self.rmaxA, [(SC_RY, "rinv"), (SC_SY, "s")])])
        self.phase_qy()
        self.phase_ug_gate()
        self.barrier([(self.rmaxB, [(SC_RG, "rinv"), (SC_SG, "s"),
                                    (SC_BG, "bg")])])
        self.phase_ug_up_e1()
        self.barrier([(self.rmaxB, [(SC_RU, "rinv"), (SC_SU, "s")]),
                      (self.rmaxA, [(SC_R2, "rinv"), (SC_S2, "s")])])
        self.phase_e2()
        self.barrier([(self.rmaxA, [(SC_RO, "rinv"), (SC_SO, "s"),
                                    (SC_HSO, "hs")])])
        self.phase_d(self.dxx2, self.dout, fuse_ln=False)
        self.finale()

    def p0(self):
        mk = self
        nun = 3

        def fg(g):
            g.dma_start(out=mk.did16, in_=mk.dident).then_inc(mk.sems["dE"], 16)
            mk.cnt["dE"] += 16
            for (srcw, dst) in [(mk.dwq, mk.dwq16), (mk.dwm, mk.dwm16),
                                (mk.dw1, mk.dw116), (mk.dw2, mk.dw216)]:
                g.dma_start(out=dst, in_=srcw).then_inc(mk.sems["dL"], 16)
                mk.cnt["dL"] += 16
            for (srcw, dst) in [(mk.dg1, mk.g1r), (mk.db1, mk.b1r),
                                (mk.dg2, mk.g2r), (mk.db2, mk.b2r)]:
                g.dma_start(out=dst[:], in_=srcw.partition_broadcast(128))\
                    .then_inc(mk.sems["dL"], 16)
                mk.cnt["dL"] += 16
            mk.inc("msg", g.nop())
            mk.w(g, "dE")
            g.dma_start(out=mk.id16[:], in_=mk.did16).then_inc(mk.sems["dI"], 16)
            mk.cnt["dI"] += 16
            g.wait_ge(mk.sems["phe"], 1)
            ic = mk.inc("cc", g.collective_compute(
                "AllGather", OP.bypass,
                replica_groups=[list(range(NCORES))],
                ins=[mk.dwall_sh.opt()], outs=[mk.dwall.opt()]))
            g.wait_ge(mk.sems["cc"], ic)
            mk.inc("wtg", g.drain())

        def fsync(sync):
            for i in range(nun):
                if i >= 2:
                    sync.wait_ge(mk.sems["pha"], i - 1)
                dst = mk.f32(mk.C16, (i % 2) * (2 * DFF), DFF)
                mk.rissue("p0l", i, 2, sync.dma_start(
                    out=dst, in_=mk.dwsh[i * 128:(i + 1) * 128, :]))
            for i in range(nun):
                sync.wait_ge(mk.sems["phb"], 6 * (i + 1))
                stg = mk.A16[:, (i % 2) * 3072:(i % 2) * 3072 + 3072]
                mk.rissue("p0s", i, 2, sync.dma_start(
                    out=mk.dwall_sh[i].rearrange("b p o -> p b o"),
                    in_=stg.rearrange("p (b o) -> p b o", o=128)))
            mk.rtotal(sync, "p0s", 2)
            mk.rtotal(sync, "p0l", 2)
            ins = sync.nop()
            mk.inc("phe", ins)
            mk.inc("msy", sync.nop())

        def fv(v):
            v.memset(mk.MAGCOL[:], MAGIC)
            v.memset(mk.MAGNEG[:], -MAGIC)
            v.memset(mk.EPSCOL[:], LN_EPS)
            v.memset(mk.ones16[:], 1.0)
            v.memset(mk.onesr16[:], 1.0)
            v.memset(mk.dervr[:], 1.0)
            v.memset(mk.gmaxr[:], 1.0)
            v.drain()
            p0lsnap = [0, 0]
            for i in range(nun):
                nb = DFF // 32
                mk.rwait(v, "p0l", i, 2, p0lsnap, 16)
                if i >= 2:
                    v.wait_ge(mk.sems["phd"], 6 * (i - 1))
                wch = mk.f32(mk.C16, (i % 2) * (2 * DFF), DFF)
                bm = mk.f32(mk.C16, 12288, 96)
                v.tensor_reduce(out=bm,
                                in_=wch.rearrange("p (a b) -> p a b", b=32),
                                axis=AX.X, op=OP.max, apply_absolute_value=True)
                v.drain()
                sb = mk.f32(mk.C16, 12480, 96)
                v.tensor_scalar(out=sb, in0=bm, scalar1=1.0 / QW,
                                scalar2=1e-12, op0=OP.mult, op1=OP.max)
                v.drain()
                rs = mk.f32(mk.C16, 12672, 96)
                v.reciprocal(out=rs, in_=sb)
                v.drain()
                t1 = mk.f32(mk.WAR, 0, DFF)
                v.tensor_tensor(out=t1.rearrange("p (a b) -> p a b", b=32),
                                in0=wch.rearrange("p (a b) -> p a b", b=32),
                                in1=rs.unsqueeze(2).broadcast_to([128, 96, 32]),
                                op=OP.mult)
                v.drain()
                v.tensor_scalar(out=t1, in0=t1, scalar1=MAGIC, scalar2=MAGIC,
                                op0=OP.add, op1=OP.subtract)
                v.drain()
                wq16 = mk.WAR[:, 8192 + (i % 2) * DFF:
                              8192 + (i % 2) * DFF + DFF]
                v.tensor_tensor(
                    out=wq16.rearrange("p (a b) -> p a b", b=32),
                    in0=t1.rearrange("p (a b) -> p a b", b=32),
                    in1=sb.unsqueeze(2).broadcast_to([128, 96, 32]),
                    op=OP.mult)
                mk.inc("pha", v.drain())
            mk.inc("msv", v.drain())

        def ft(t):
            t.wait_ge(mk.sems["dI"], 16)
            for i in range(nun):
                t.wait_ge(mk.sems["pha"], i + 1)
                srcw = mk.WAR[:, 8192 + (i % 2) * DFF:
                              8192 + (i % 2) * DFF + DFF]
                for gr in range(6):
                    gg = i * 6 + gr
                    if gg >= 2:
                        t.wait_ge(mk.sems["phb"], gg - 1)
                    last = None
                    for r in range(4):
                        rr = gr * 4 + r
                        last = t.transpose(
                            mk.pbh(gg % 2)[:, r * 128:(r + 1) * 128],
                            srcw[:, rr * 128:(rr + 1) * 128], mk.id16[:])
                    mk.inc("phd", last)
            mk.inc("mst", t.drain())

        def fs(s):
            for i in range(nun):
                for gr in range(6):
                    gg = i * 6 + gr
                    s.wait_ge(mk.sems["phd"], gg + 1)
                    if i >= 2 and gr == 0:
                        slot = i % 2
                        prev = 16 * sum(1 for k in range(i - 1)
                                        if k % 2 == slot)
                        s.wait_ge(mk.sems[f"p0s{slot}"], prev)
                    stg = mk.A16[:, (i % 2) * 3072 + gr * 512:
                                 (i % 2) * 3072 + gr * 512 + 512]
                    s.copy(out=stg, in_=mk.pbh(gg % 2)[:, 0:512])
                    mk.inc("phb", s.drain())
            mk.inc("mss", s.drain())

        self.blk.gpsimd(fg)
        self.blk.sync(fsync)
        self.blk.vector(fv)
        self.blk.tensor(ft)
        self.blk.scalar(fs)
        self.dg_p0 = self.cnt["dL"]
        self.snap()

    def barrier(self, quants):
        mk = self
        nq = len(quants)
        phb0 = self.cnt["phc"]

        def fg(g):
            mk.entry(g, "g")
            for qi, (rm, _) in enumerate(quants):
                g.dma_start(out=mk.ddcol[:, qi:qi + 1], in_=rm[:],
                            allow_slow_non_contiguous=True)\
                    .then_inc(mk.sems["dG"], 16)
                mk.cnt["dG"] += 16
            mk.w(g, "dG")
            g.dma_start(out=mk.srow[0:nq, :],
                        in_=mk.ddcol[:, 0:nq].rearrange("p q -> q p"),
                        allow_slow_non_contiguous=True)\
                .then_inc(mk.sems["dG"], 16)
            mk.cnt["dG"] += 16
            mk.w(g, "dG")
            mk.inc("phe", g.drain())
            g.wait_ge(mk.sems["phc"], phb0 + 1)
            g.dma_start(out=mk.dbm_in[0:1, 0:nq],
                        in_=mk.gmaxr[0:nq, 0:1]).then_inc(mk.sems["dG"], 16)
            mk.cnt["dG"] += 16
            mk.w(g, "dG")
            ic = mk.inc("cc", g.collective_compute(
                "AllReduce", OP.max,
                replica_groups=[list(range(NCORES))],
                ins=[mk.dbm_in[0:1, 0:nq].opt()],
                outs=[mk.dbm_out[0:1, 0:nq].opt()]))
            g.wait_ge(mk.sems["cc"], ic)
            g.dma_start(out=mk.gmaxr[0:1, 8:8 + nq],
                        in_=mk.dbm_out[0:1, 0:nq]).then_inc(mk.sems["dG"], 16)
            mk.cnt["dG"] += 16
            mk.w(g, "dG")
            mk.inc("phe", g.drain())
            g.wait_ge(mk.sems["phc"], phb0 + 2)
            g.dma_start(out=mk.dbscal[0:1, :], in_=mk.dervr[0:1, :])\
                .then_inc(mk.sems["dG"], 16)
            mk.cnt["dG"] += 16
            mk.w(g, "dG")
            g.dma_start(out=mk.SC[:, :],
                        in_=mk.dbscal.partition_broadcast(128))\
                .then_inc(mk.sems["dG"], 16)
            mk.cnt["dG"] += 16
            mk.w(g, "dG")
            mk.inc("bar", g.drain())
            mk.inc("msg", g.drain())

        phe0 = self.cnt["phe"]

        def fv(v):
            mk.entry(v, "v")
            v.wait_ge(mk.sems["phe"], phe0 + 1)
            v.tensor_reduce(out=mk.gmaxr[0:nq, 0:1], in_=mk.srow[0:nq, :],
                            axis=AX.X, op=OP.max)
            v.drain()
            mk.inc("phc", v.drain())
            v.wait_ge(mk.sems["phe"], phe0 + 2)
            for qi, (_, outs) in enumerate(quants):
                mcol = mk.gmaxr[0:1, 8 + qi:8 + qi + 1]
                scol = mk.dervr[0:1, 14:15]
                v.tensor_scalar(out=scol, in0=mcol, scalar1=1.0 / QA,
                                scalar2=1e-12, op0=OP.mult, op1=OP.max)
                v.drain()
                for colx, kind in outs:
                    if kind == "s":
                        v.tensor_scalar(out=mk.dervr[0:1, colx:colx + 1],
                                        in0=scol, scalar1=1.0, scalar2=None,
                                        op0=OP.mult)
                    elif kind == "rinv":
                        v.reciprocal(out=mk.dervr[0:1, colx:colx + 1],
                                     in_=scol)
                    elif kind == "bg":
                        v.tensor_scalar(out=mk.dervr[0:1, colx:colx + 1],
                                        in0=scol, scalar1=-MAGIC,
                                        scalar2=None, op0=OP.mult)
                    elif kind == "hs":
                        v.tensor_scalar(out=mk.dervr[0:1, colx:colx + 1],
                                        in0=scol, scalar1=0.5,
                                        scalar2=None, op0=OP.mult)
                    v.drain()
            mk.inc("phc", v.drain())
            mk.inc("msv", v.drain())

        self.blk.gpsimd(fg)
        self.blk.vector(fv)
        self.snap()

    def phase_y(self, src_dram, split_loads=True):
        mk = self
        nch = self.nch
        T = self.T
        ylsnap = self.rsnap("yl", 4)
        ylgsnap = self.rsnap("ylg", 2)
        lnv0 = self.cnt["lnv"]
        lns0 = self.cnt["lns"]
        pha0 = self.cnt["pha"]
        phb0 = self.cnt["phb"]
        phc0 = self.cnt["phc"]

        step = 2 if split_loads else 1

        def fload(sync):
            mk.entry(sync, "y")
            for ch in range(0, nch, step):
                if ch >= 4:
                    sync.wait_ge(mk.sems["pha"], pha0 + ch - 3)
                    sync.wait_ge(mk.sems["lns"], lns0 + ch - 3)
                xa = mk.f32(mk.B16, (ch % 4) * 2048, 1024)
                mk.rissue("yl", ch if not split_loads else ch // 2,
                          4 if not split_loads else 2, sync.dma_start(
                              out=xa,
                              in_=src_dram[ch * 128:(ch + 1) * 128, :]))
            mk.inc("msy", sync.nop())

        def fgl(g):
            mk.entry(g, "g")
            for ch in range(1, nch, 2):
                if ch >= 4:
                    g.wait_ge(mk.sems["pha"], pha0 + ch - 3)
                    g.wait_ge(mk.sems["lns"], lns0 + ch - 3)
                xa = mk.f32(mk.B16, (ch % 4) * 2048, 1024)
                mk.rissue("ylg", ch // 2, 2, g.dma_start(
                    out=xa, in_=src_dram[ch * 128:(ch + 1) * 128, :]))
            mk.inc("msg", g.drain())

        def ld_rwait(eng, ch, snp, snpg):
            if not split_loads:
                mk.rwait(eng, "yl", ch, 4, snp, 16)
            elif ch % 2 == 0:
                mk.rwait(eng, "yl", ch // 2, 2, snp, 16)
            else:
                mk.rwait(eng, "ylg", ch // 2, 2, snpg, 16)

        def fv(v):
            mk.entry(v, "v")
            v.memset(mk.rmaxA[:], 0.0)
            v.drain()
            ld_rwait(v, 0, ylsnap, ylgsnap)
            mk.ln_v_pre(v, 0, mk.LNa, mk.f32(mk.B16, 0, 1024))
            for ch in range(nch):
                if ch + 1 < nch:
                    ld_rwait(v, ch + 1, ylsnap, ylgsnap)
                    xan = mk.f32(mk.B16, ((ch + 1) % 4) * 2048, 1024)
                    mk.ln_v_pre(v, (ch + 1) % mk.NLN, mk.LNa, xan)
                xa = mk.f32(mk.B16, (ch % 4) * 2048, 1024)
                k = ch % mk.NLN
                y16 = mk.B16[:, 8192 + (ch % 2) * 1024:
                             8192 + (ch % 2) * 1024 + 1024]
                if ch >= 2:
                    v.wait_ge(mk.sems["phb"], phb0 + ch - 1)
                mk.ln_v_apply(v, k, mk.LNa, xa, y16, lns0 + ch + 1)
                mk.inc("pha", v.drain())
                mk.absfold(v, y16, ch, mk.rmaxA)
            mk.inc("msv", v.drain())

        def fs(s):
            mk.entry(s, "s")
            for ch in range(nch):
                ld_rwait(s, ch, ylsnap, ylgsnap)
                xa = mk.f32(mk.B16, (ch % 4) * 2048, 1024)
                sqscr = mk.f32(mk.B16, 10240 + (ch % 2) * 2048, 1024)
                mk.ln_s(s, ch % mk.NLN, mk.LNa, xa, sqscr, lnv0 + ch + 1)
                if ch >= 2:
                    self._y_evac(s, ch - 2, phb0)
            for ch in (nch - 2, nch - 1):
                self._y_evac(s, ch, phb0)
            mk.inc("mss", s.drain())

        def ft(t):
            mk.entry(t, "t")
            for ch in range(nch):
                t.wait_ge(mk.sems["pha"], pha0 + ch + 1)
                if ch >= 2:
                    t.wait_ge(mk.sems["phc"], phc0 + ch - 1)
                y16 = mk.B16[:, 8192 + (ch % 2) * 1024:
                             8192 + (ch % 2) * 1024 + 1024]
                last = None
                for hc in range(HC):
                    last = t.transpose(
                        mk.pdh(ch % 2, 0)[:, hc * 128:(hc + 1) * 128],
                        y16[:, hc * 128:(hc + 1) * 128], mk.id16[:])
                mk.inc("phb", last)
            mk.inc("mst", t.drain())

        self.blk.sync(fload)
        if split_loads:
            self.blk.gpsimd(fgl)
        self.blk.vector(fv)
        self.blk.tensor(ft)
        self.blk.scalar(fs)
        self.snap()

    def _y_evac(self, s, ch, phb0):
        mk = self
        s.wait_ge(mk.sems["phb"], phb0 + ch + 1)
        dst = mk.A16[:, 0:HC * mk.T].rearrange(
            "p (h t) -> p h t", h=HC)[:, :, ch * 128:(ch + 1) * 128]
        s.copy(out=dst, in_=mk.pdh(ch % 2, 0).rearrange(
            "p (h t) -> p h t", h=HC))
        mk.inc("phc", s.drain())

    def phase_qy(self):
        mk = self
        nsl = (HC * self.T) // 2048
        phb0 = self.cnt["phb"]
        bar0 = self.cnt["bar"]

        def fs(s):
            mk.entry(s, "s")
            s.wait_ge(mk.sems["bar"], bar0)
            for j in range(nsl):
                if j >= 2:
                    s.wait_ge(mk.sems["phb"], phb0 + j - 1)
                ap = mk.f32(mk.C16, (j % 2) * 4096, 2048)
                s.activation(out=ap, in_=mk.A16[:, j * 2048:(j + 1) * 2048],
                             func=ACT.Identity, scale=mk.SC[:, SC_RY:SC_RY + 1],
                             bias=mk.MAGCOL[:, 0:1])
                mk.inc("pha", s.drain())
            mk.inc("mss", s.drain())

        pha0 = self.cnt["pha"]

        def fv(v):
            mk.entry(v, "v")
            for j in range(nsl):
                v.wait_ge(mk.sems["pha"], pha0 + j + 1)
                ap = mk.f32(mk.C16, (j % 2) * 4096, 2048)
                v.tensor_scalar(out=mk.B16[:, j * 2048:(j + 1) * 2048],
                                in0=ap, scalar1=MAGIC,
                                scalar2=mk.SC[:, SC_SY:SC_SY + 1],
                                op0=OP.subtract, op1=OP.mult)
                mk.inc("phb", v.drain())
            mk.inc("msv", v.drain())

        self.blk.scalar(fs)
        self.blk.vector(fv)
        self.snap()

    def phase_ug_gate(self):
        mk = self
        T, ntt = self.T, self.ntt
        nhalf = 2 if ntt >= 2 else 1
        tph = ntt // nhalf
        stw = tph * 512
        self._ug_nhalf, self._ug_tph, self._ug_stw = nhalf, tph, stw
        groups = [(oc, hf) for oc in range(OCF) for hf in range(nhalf)]
        ng = len(groups)
        banksets = [
            [self.pd[0][:, 0:512], self.pd[0][:, 512:1024],
             self.pd[1][:, 0:512], self.pd[1][:, 512:1024]][:tph],
            [self.pd[2][:, 0:512], self.pd[2][:, 512:1024],
             self.pb[0][:, :], self.pb[1][:, :]][:tph],
        ]
        self._ug_banksets = banksets
        ugwsnap = self.rsnap("ugw", 4)
        ugssnap = self.rsnap("ugs", 2)
        pha0 = self.cnt["pha"]
        phb0 = self.cnt["phb"]
        phc0 = self.cnt["phc"]
        bar0 = self.cnt["bar"]

        def fw(sync):
            mk.entry(sync, "y")
            sync.wait_ge(mk.sems["wtg"], 1)

            def loadslice(oc):
                if oc >= 2:
                    sync.wait_ge(mk.sems["pha"],
                                 pha0 + (oc - 2 + 1) * nhalf)
                dst = mk.A16[:, (oc % 2) * 1024:(oc % 2) * 1024 + 1024]
                unit = HC + oc // 3
                sub = oc % 3
                mk.rissue("ugw", oc % 2, 4, sync.dma_start(
                    out=dst.rearrange("p (a o) -> p a o", a=HC),
                    in_=mk.dwall[unit, sub * 8:(sub + 1) * 8]
                    .rearrange("a p o -> p a o")))

            def spill(gi):
                oc, hf = groups[gi]
                sync.wait_ge(mk.sems["phb"], phb0 + gi + 1)
                stg = mk.C16[:, 12288 + (gi % 2) * stw:
                             12288 + (gi % 2) * stw + stw]
                mk.rissue("ugs", gi, 2, sync.dma_start(
                    out=mk.dgatesp[oc * 128:(oc + 1) * 128,
                                   hf * stw:(hf + 1) * stw], in_=stg))

            for oc in range(2):
                loadslice(oc)
            for gi in range(ng):
                oc = gi // nhalf
                if gi % nhalf == 0 and oc + 2 < OCF:
                    loadslice(oc + 2)
                if gi >= 1:
                    spill(gi - 1)
            spill(ng - 1)
            mk.inc("msy", sync.nop())

        def ft(t):
            mk.entry(t, "t")
            t.wait_ge(mk.sems["bar"], bar0)
            for gi, (oc, hf) in enumerate(groups):
                mk.rwait(t, "ugw", oc % 2, 4, ugwsnap, 16, occ=oc // 2)
                if gi >= 2:
                    t.wait_ge(mk.sems["phb"], phb0 + gi - 1)
                wb = mk.A16[:, (oc % 2) * 1024:(oc % 2) * 1024 + 1024]
                bs = banksets[gi % 2]
                last = None
                for hc in range(HC):
                    for j in range(tph):
                        tt = hf * tph + j
                        last = t.matmul(
                            bs[j],
                            lhsT=wb[:, hc * 128:(hc + 1) * 128],
                            rhs=mk.B16[:, hc * T + tt * 512:
                                       hc * T + tt * 512 + 512],
                            start=(hc == 0), stop=(hc == HC - 1))
                mk.inc("pha", last)
            mk.inc("mst", t.drain())

        def fs(s):
            mk.entry(s, "s")
            for gi in range(ng):
                s.wait_ge(mk.sems["pha"], pha0 + gi + 1)
                if gi >= 2:
                    mk.rwait(s, "ugs", gi, 2, ugssnap, 16, occ=gi // 2 - 1)
                    s.wait_ge(mk.sems["phc"], phc0 + gi - 1)
                stg = mk.C16[:, 12288 + (gi % 2) * stw:
                             12288 + (gi % 2) * stw + stw]
                bs = banksets[gi % 2]
                last = None
                for j in range(tph):
                    last = s.copy(out=stg[:, j * 512:(j + 1) * 512], in_=bs[j])
                mk.inc("phb", last)
            mk.inc("mss", s.drain())

        def fv(v):
            mk.entry(v, "v")
            v.memset(mk.rmaxB[:], 0.0)
            v.drain()
            for gi in range(ng):
                v.wait_ge(mk.sems["phb"], phb0 + gi + 1)
                stg = mk.C16[:, 12288 + (gi % 2) * stw:
                             12288 + (gi % 2) * stw + stw]
                ts = mk.tmax[:, gi % 8:gi % 8 + 1]
                v.tensor_reduce(out=ts, in_=stg, axis=AX.X, op=OP.max,
                                apply_absolute_value=True)
                v.drain()
                v.tensor_tensor(out=mk.rmaxB[:], in0=mk.rmaxB[:], in1=ts,
                                op=OP.max)
                mk.inc("phc", v.drain())
            mk.inc("msv", v.drain())

        self.blk.sync(fw)
        self.blk.tensor(ft)
        self.blk.scalar(fs)
        self.blk.vector(fv)
        self._ug_gate_phb_end = self.cnt["phb"]
        self.snap()

    def phase_ug_up_e1(self):
        mk = self
        T, ntt, TE = self.T, self.ntt, self.TE
        nhalf, tph, stw = self._ug_nhalf, self._ug_tph, self._ug_stw
        groups = [(oc, hf) for oc in range(OCF) for hf in range(nhalf)]
        ng = len(groups)
        nsl = (DFF // 128) * (T // TE)
        assert nsl == ng, (nsl, ng)
        npc = T // TE
        banksets = self._ug_banksets
        gphb_end = self._ug_gate_phb_end
        ugwsnap = self.rsnap("ugw", 4)
        ugssnap = self.rsnap("ugs", 2)
        gatesnap = [ugssnap[0] - 16 * (ng // 2), ugssnap[1] - 16 * (ng // 2)]
        e1isnap = self.rsnap("e1i", 3)
        e1osnap = self.rsnap("e1o", 2)
        fua0 = self.cnt["fua"]
        fub0 = self.cnt["fub"]
        fuc0 = self.cnt["fuc"]
        fud0 = self.cnt["fud"]
        fue0 = self.cnt["fue"]
        fuf0 = self.cnt["fuf"]
        bar0 = self.cnt["bar"]
        EB = 4096

        def pos(j):
            pc, jj = divmod(j, npc)
            return pc * 128, jj * TE

        def fw(sync):
            mk.entry(sync, "y")

            def loadslice(oc):
                if oc >= 2:
                    sync.wait_ge(mk.sems["fua"],
                                 fua0 + (oc - 2 + 1) * nhalf)
                dst = mk.A16[:, 2048 + (oc % 2) * 1024:
                             2048 + (oc % 2) * 1024 + 1024]
                mk.rissue("ugw", 2 + oc % 2, 4, sync.dma_start(
                    out=dst.rearrange("p (a o) -> p a o", a=HC),
                    in_=mk.dwall[oc // 3, (oc % 3) * 8:(oc % 3) * 8 + 8]
                    .rearrange("a p o -> p a o")))

            def upspill(gi):
                oc, hf = groups[gi]
                sync.wait_ge(mk.sems["fub"], fub0 + gi + 1)
                stg = mk.C16[:, 12288 + (gi % 2) * stw:
                             12288 + (gi % 2) * stw + stw]
                mk.rissue("ugs", gi, 2, sync.dma_start(
                    out=mk.dupsp[oc * 128:(oc + 1) * 128,
                                 hf * stw:(hf + 1) * stw], in_=stg))

            def e1load(j):
                if j >= 3:
                    sync.wait_ge(mk.sems["fud"], fud0 + j - 2)
                r0, c0 = pos(j)
                mk.rwait(sync, "ugs", j, 2, gatesnap, 16, occ=j // 2)
                gin = mk.C16[:, (j % 3) * TE:(j % 3) * TE + TE]
                mk.rissue("e1i", j, 3, sync.dma_start(
                    out=gin, in_=mk.dgatesp[r0:r0 + 128, c0:c0 + TE]))

            def e1spill(j):
                sync.wait_ge(mk.sems["fue"], fue0 + j + 1)
                r0, c0 = pos(j)
                gout = mk.C16[:, 3 * TE + (j % 2) * TE:
                              3 * TE + (j % 2) * TE + TE]
                mk.rissue("e1o", j, 2, sync.dma_start(
                    out=mk.dg2sp[r0:r0 + 128, c0:c0 + TE], in_=gout))

            for oc in range(2):
                loadslice(oc)
            e1load(0)
            e1load(1)
            for gi in range(ng):
                oc = gi // nhalf
                if gi % nhalf == 0 and oc + 2 < OCF:
                    loadslice(oc + 2)
                if gi + 2 < nsl:
                    e1load(gi + 2)
                upspill(gi)
                if gi >= 1:
                    e1spill(gi - 1)
            e1spill(nsl - 1)
            mk.rtotal(sync, "ugs", 2)
            mk.rtotal(sync, "e1o", 2)
            mk.inc("msy", sync.nop())

        def ft(t):
            mk.entry(t, "t")
            for gi, (oc, hf) in enumerate(groups):
                mk.rwait(t, "ugw", 2 + oc % 2, 4, ugwsnap, 16, occ=oc // 2)
                if gi < 2:
                    t.wait_ge(mk.sems["phb"], gphb_end)
                else:
                    t.wait_ge(mk.sems["fub"], fub0 + gi - 1)
                wb = mk.A16[:, 2048 + (oc % 2) * 1024:
                            2048 + (oc % 2) * 1024 + 1024]
                bs = banksets[gi % 2]
                last = None
                for hc in range(HC):
                    for j in range(tph):
                        tt = hf * tph + j
                        last = t.matmul(
                            bs[j],
                            lhsT=wb[:, hc * 128:(hc + 1) * 128],
                            rhs=mk.B16[:, hc * T + tt * 512:
                                       hc * T + tt * 512 + 512],
                            start=(hc == 0), stop=(hc == HC - 1))
                mk.inc("fua", last)
            mk.inc("mst", t.drain())

        def fs(s):
            mk.entry(s, "s")
            s.wait_ge(mk.sems["bar"], bar0)
            for j in range(ng):
                mk.rwait(s, "e1i", j, 3, e1isnap, 16)
                if j >= 2:
                    s.wait_ge(mk.sems["fuf"], fuf0 + 3 * (j - 2) + 1)
                gin = mk.C16[:, (j % 3) * TE:(j % 3) * TE + TE]
                ap = mk.f32(mk.A16, EB + (j % 2) * 2 * TE, TE)
                s.activation(out=ap, in_=gin, func=ACT.Identity,
                             scale=mk.SC[:, SC_RG:SC_RG + 1],
                             bias=mk.MAGCOL[:, 0:1])
                mk.inc("fud", s.drain())
                sg = mk.A16[:, EB + 4 * TE + (j % 2) * TE:
                            EB + 4 * TE + (j % 2) * TE + TE]
                if j >= 2:
                    s.wait_ge(mk.sems["fuf"], fuf0 + 3 * (j - 2) + 2)
                s.activation(out=sg, in_=ap, func=ACT.Sigmoid,
                             scale=mk.SC[:, SC_SG:SC_SG + 1],
                             bias=mk.SC[:, SC_BG:SC_BG + 1])
                mk.inc("fuc", s.drain())
                s.wait_ge(mk.sems["fua"], fua0 + j + 1)
                if j >= 2:
                    mk.rwait(s, "ugs", j, 2, ugssnap, 16, occ=j // 2 - 1)
                    s.wait_ge(mk.sems["fue"], fue0 + j - 1)
                else:
                    s.wait_ge(mk.sems[f"ugs{j % 2}"], ugssnap[j % 2])
                stg = mk.C16[:, 12288 + (j % 2) * stw:
                             12288 + (j % 2) * stw + stw]
                bs = banksets[j % 2]
                last = None
                for jj in range(tph):
                    last = s.copy(out=stg[:, jj * 512:(jj + 1) * 512],
                                  in_=bs[jj])
                mk.inc("fub", last)
            mk.inc("mss", s.drain())

        def fv(v):
            mk.entry(v, "v")
            for k in range(8):
                v.memset(mk.WAR[:, k * 2048:(k + 1) * 2048],
                         -60000.0 if k < 2 or 4 <= k < 6 else 60000.0)
            v.drain()
            for j in range(ng):
                p = j % 2
                qg = mk.A16[:, EB + 12 * TE + (j % 2) * TE:
                            EB + 12 * TE + (j % 2) * TE + TE]
                v.wait_ge(mk.sems["fuf"], fuf0 + 3 * (j + 1))
                sigq = mk.A16[:, EB + 10 * TE + (j % 2) * TE:
                              EB + 10 * TE + (j % 2) * TE + TE]
                gout = mk.C16[:, 3 * TE + (j % 2) * TE:
                              3 * TE + (j % 2) * TE + TE]
                if j >= 2:
                    mk.rwait(v, "e1o", j, 2, e1osnap, 16, occ=j // 2 - 1)
                v.tensor_tensor(out=gout, in0=qg, in1=sigq, op=OP.mult)
                v.drain()
                v.wait_ge(mk.sems["fub"], fub0 + j + 1)
                stg = mk.C16[:, 12288 + (j % 2) * stw:
                             12288 + (j % 2) * stw + stw]
                gmx = mk.WAR[:, (4 + p) * 2048:(4 + p) * 2048 + TE]
                gmn = mk.WAR[:, (6 + p) * 2048:(6 + p) * 2048 + TE]
                umx = mk.WAR[:, p * 2048:p * 2048 + stw]
                umn = mk.WAR[:, (2 + p) * 2048:(2 + p) * 2048 + stw]
                v.tensor_tensor(out=gmx, in0=gmx, in1=gout, op=OP.max)
                v.tensor_tensor(out=gmn, in0=gmn, in1=gout, op=OP.min)
                v.tensor_tensor(out=umx, in0=umx, in1=stg, op=OP.max)
                v.tensor_tensor(out=umn, in0=umn, in1=stg, op=OP.min)
                mk.inc("fue", v.drain())
            for k in range(4):
                v.tensor_reduce(out=mk.tmax[:, 2 * k:2 * k + 1],
                                in_=mk.WAR[:, 2 * k * 2048:
                                           2 * k * 2048 + 2048],
                                axis=AX.X,
                                op=OP.max if k in (0, 2) else OP.min,
                                negate=(k in (1, 3)))
                v.tensor_reduce(out=mk.tmax[:, 2 * k + 1:2 * k + 2],
                                in_=mk.WAR[:, (2 * k + 1) * 2048:
                                           (2 * k + 1) * 2048 + 2048],
                                axis=AX.X,
                                op=OP.max if k in (0, 2) else OP.min,
                                negate=(k in (1, 3)))
            v.drain()
            v.tensor_tensor(out=mk.tmax[:, 0:1], in0=mk.tmax[:, 0:1],
                            in1=mk.tmax[:, 1:2], op=OP.max)
            v.tensor_tensor(out=mk.tmax[:, 2:3], in0=mk.tmax[:, 2:3],
                            in1=mk.tmax[:, 3:4], op=OP.max)
            v.tensor_tensor(out=mk.tmax[:, 4:5], in0=mk.tmax[:, 4:5],
                            in1=mk.tmax[:, 5:6], op=OP.max)
            v.tensor_tensor(out=mk.tmax[:, 6:7], in0=mk.tmax[:, 6:7],
                            in1=mk.tmax[:, 7:8], op=OP.max)
            v.drain()
            v.tensor_tensor(out=mk.rmaxB[:], in0=mk.tmax[:, 0:1],
                            in1=mk.tmax[:, 2:3], op=OP.max)
            v.tensor_tensor(out=mk.rmaxA[:], in0=mk.tmax[:, 4:5],
                            in1=mk.tmax[:, 6:7], op=OP.max)
            mk.inc("msv", v.drain())

        def fg(g):
            mk.entry(g, "g")
            for j in range(ng):
                g.wait_ge(mk.sems["fud"], fud0 + j + 1)
                if j >= 2:
                    g.wait_ge(mk.sems["fue"], fue0 + j - 1)
                ap = mk.f32(mk.A16, EB + (j % 2) * 2 * TE, TE)
                qg = mk.A16[:, EB + 12 * TE + (j % 2) * TE:
                            EB + 12 * TE + (j % 2) * TE + TE]
                g.tensor_scalar(out=qg, in0=ap, scalar1=-MAGIC,
                                scalar2=mk.SC[:, SC_SG:SC_SG + 1],
                                op0=OP.add, op1=OP.mult)
                mk.inc("fuf", g.drain())
                g.wait_ge(mk.sems["fuc"], fuc0 + j + 1)
                sg = mk.A16[:, EB + 4 * TE + (j % 2) * TE:
                            EB + 4 * TE + (j % 2) * TE + TE]
                tt_ = mk.f32(mk.A16, EB + 6 * TE + (j % 2) * 2 * TE, TE)
                g.tensor_scalar(out=tt_, in0=sg, scalar1=65536.0, scalar2=P23,
                                op0=OP.mult, op1=OP.add)
                mk.inc("fuf", g.drain())
                sigq = mk.A16[:, EB + 10 * TE + (j % 2) * TE:
                              EB + 10 * TE + (j % 2) * TE + TE]
                g.tensor_scalar(out=sigq, in0=tt_, scalar1=1.0 / 65536.0,
                                scalar2=-128.0, op0=OP.mult, op1=OP.add)
                mk.inc("fuf", g.drain())
            mk.inc("msg", g.drain())

        self.blk.sync(fw)
        self.blk.tensor(ft)
        self.blk.scalar(fs)
        self.blk.vector(fv)
        self.blk.gpsimd(fg)
        self.snap()

    def phase_e2(self):
        mk = self
        T, TE = self.T, self.TE
        nsl = (DFF // 128) * (T // TE)
        npc = T // TE
        e2isnap = self.rsnap("e2i", 3)
        e2osnap = self.rsnap("e2o", 2)
        uccsnap = self.rsnap("ucc", 2)
        pha0 = self.cnt["pha"]
        phb0 = self.cnt["phb"]
        bar0 = self.cnt["bar"]

        def pos(j):
            pc, jj = divmod(j, npc)
            return pc * 128, jj * TE

        def fld(sync):
            mk.entry(sync, "y")
            for j in range(nsl):
                if j >= 3:
                    sync.wait_ge(mk.sems["pha"], pha0 + 2 * (j - 2))
                r0, c0 = pos(j)
                g2in = mk.C16[:, (j % 3) * TE:(j % 3) * TE + TE]
                mk.rissue("e2i", j, 3, sync.dma_start(
                    out=g2in, in_=mk.dg2sp[r0:r0 + 128, c0:c0 + TE]))
            mk.inc("msy", sync.nop())

        def fs(s):
            mk.entry(s, "s")
            s.wait_ge(mk.sems["bar"], bar0)
            for j in range(2):
                r0, c0 = pos(j)
                uin = mk.C16[:, 3 * TE + (j % 3) * TE:
                             3 * TE + (j % 3) * TE + TE]
                mk.rissue("ucc", j, 2, s.dma_start(
                    out=uin, in_=mk.dupsp[r0:r0 + 128, c0:c0 + TE]))
            for j in range(nsl):
                mk.rwait(s, "e2i", j, 3, e2isnap, 16)
                mk.rwait(s, "ucc", j, 2, uccsnap, 16)
                if j >= 2:
                    s.wait_ge(mk.sems["phb"], phb0 + j - 1)
                g2in = mk.C16[:, (j % 3) * TE:(j % 3) * TE + TE]
                uin = mk.C16[:, 3 * TE + (j % 3) * TE:
                             3 * TE + (j % 3) * TE + TE]
                g2p = mk.f32(mk.A16, (j % 2) * 2 * TE, TE)
                s.activation(out=g2p, in_=g2in, func=ACT.Identity,
                             scale=mk.SC[:, SC_R2:SC_R2 + 1],
                             bias=mk.MAGCOL[:, 0:1])
                up = mk.f32(mk.A16, 4 * TE + (j % 2) * 2 * TE, TE)
                s.activation(out=up, in_=uin, func=ACT.Identity,
                             scale=mk.SC[:, SC_RU:SC_RU + 1],
                             bias=mk.MAGCOL[:, 0:1])
                mk.inc("pha", s.drain(), 2)
                if j + 2 < nsl:
                    jn = j + 2
                    r0, c0 = pos(jn)
                    uinn = mk.C16[:, 3 * TE + (jn % 3) * TE:
                                  3 * TE + (jn % 3) * TE + TE]
                    mk.rissue("ucc", jn, 2, s.dma_start(
                        out=uinn, in_=mk.dupsp[r0:r0 + 128, c0:c0 + TE]))
            mk.inc("mss", s.drain())

        phg0 = self.cnt["phg"]

        def spill2(g, j):
            g.wait_ge(mk.sems["phb"], phb0 + j + 1)
            r0, c0 = pos(j)
            oout = mk.B16[:, 4 * TE + (j % 2) * TE:4 * TE + (j % 2) * TE + TE]
            mk.rissue("e2o", j, 2, g.dma_start(
                out=mk.dosp[r0:r0 + 128, c0:c0 + TE], in_=oout))

        def fg(g):
            mk.entry(g, "g")
            for j in range(nsl):
                g.wait_ge(mk.sems["pha"], pha0 + 2 * (j + 1))
                if j >= 2:
                    g.wait_ge(mk.sems["phb"], phb0 + j - 1)
                up = mk.f32(mk.A16, 4 * TE + (j % 2) * 2 * TE, TE)
                uq = mk.B16[:, 2 * TE + (j % 2) * TE:2 * TE + (j % 2) * TE + TE]
                g.tensor_scalar(out=uq, in0=up, scalar1=-MAGIC,
                                scalar2=mk.SC[:, SC_SU:SC_SU + 1],
                                op0=OP.add, op1=OP.mult)
                mk.inc("phg", g.drain())
                if j >= 1:
                    spill2(g, j - 1)
            spill2(g, nsl - 1)
            mk.rtotal(g, "e2o", 2)
            mk.inc("msg", g.drain())

        def fv(v):
            mk.entry(v, "v")
            v.memset(mk.WAR[:, 8192:8192 + 2048], -60000.0)
            v.memset(mk.WAR[:, 10240:10240 + 2048], 60000.0)
            v.drain()
            for j in range(nsl):
                v.wait_ge(mk.sems["pha"], pha0 + 2 * (j + 1))
                g2p = mk.f32(mk.A16, (j % 2) * 2 * TE, TE)
                qg2 = mk.B16[:, (j % 2) * TE:(j % 2) * TE + TE]
                v.tensor_scalar(out=qg2, in0=g2p, scalar1=MAGIC,
                                scalar2=mk.SC[:, SC_S2:SC_S2 + 1],
                                op0=OP.subtract, op1=OP.mult)
                v.drain()
                v.wait_ge(mk.sems["phg"], phg0 + j + 1)
                uq = mk.B16[:, 2 * TE + (j % 2) * TE:2 * TE + (j % 2) * TE + TE]
                oout = mk.B16[:, 4 * TE + (j % 2) * TE:
                              4 * TE + (j % 2) * TE + TE]
                if j >= 2:
                    mk.rwait(v, "e2o", j, 2, e2osnap, 16, occ=j // 2 - 1)
                v.tensor_tensor(out=oout, in0=qg2, in1=uq, op=OP.mult)
                v.drain()
                v.tensor_tensor(out=mk.WAR[:, 8192:8192 + TE],
                                in0=mk.WAR[:, 8192:8192 + TE], in1=oout,
                                op=OP.max)
                v.tensor_tensor(out=mk.WAR[:, 10240:10240 + TE],
                                in0=mk.WAR[:, 10240:10240 + TE], in1=oout,
                                op=OP.min)
                mk.inc("phb", v.drain())
            v.tensor_reduce(out=mk.tmax[:, 0:1], in_=mk.WAR[:, 8192:8192 + 2048],
                            axis=AX.X, op=OP.max)
            v.tensor_reduce(out=mk.tmax[:, 1:2],
                            in_=mk.WAR[:, 10240:10240 + 2048],
                            axis=AX.X, op=OP.min, negate=True)
            v.drain()
            v.tensor_tensor(out=mk.rmaxA[:], in0=mk.tmax[:, 0:1],
                            in1=mk.tmax[:, 1:2], op=OP.max)
            mk.inc("msv", v.drain())

        self.blk.sync(fld)
        self.blk.scalar(fs)
        self.blk.gpsimd(fg)
        self.blk.vector(fv)
        self.snap()

    def phase_d(self, xin_dram, xout_dram, fuse_ln):
        mk = self
        T, nch = self.T, self.nch
        npair = nch // 2
        NW = T // 512
        NSLAB = NW * OCF
        QRING = 28
        ORING = 6
        bar0 = self.cnt["bar"]
        dw0 = self.cnt["dW"]
        dqrsnap = self.rsnap("dqr", ORING)
        dxlsnap = self.rsnap("dxl", 4)
        dxssnap = self.rsnap("dxs", 2)
        dyssnap = self.rsnap("dys", 2)
        e2o_fin = [self.cnt["e2o0"], self.cnt["e2o1"]]
        pha0 = self.cnt["pha"]
        phb0 = self.cnt["phb"]
        phg0 = self.cnt["phg"]
        phc0 = self.cnt["phc"]
        phd0 = self.cnt["phd"]
        phe0 = self.cnt["phe"]
        phf0 = self.cnt["phf"]
        lnv0 = self.cnt["lnv"]
        lns0 = self.cnt["lns"]
        gv = dict(self.gatev)

        def slab_wait(eng, g):
            w, hc = divmod(g, OCF)
            if hc == OCF - 1:
                eng.wait_ge(mk.sems["phc"], phc0 + 2 * w + 2)
            else:
                eng.wait_ge(mk.sems["phg"], phg0 + w * (OCF - 1) + hc + 1)

        def chunk_psum(ch):
            k = ch // 2
            if k % 2 == 0:
                p = mk.pd[ch % 2]
                return ([p[:, 0:512], p[:, 512:1024]], p[:, :],
                        mk.pdh(ch % 2, 0))
            if ch % 2 == 0:
                p = mk.pd[2]
                return ([p[:, 0:512], p[:, 512:1024]], p[:, :], mk.pdh(2, 0))
            return ([mk.pb[0][:, :], mk.pb[1][:, :]], None, mk.pbh(0))

        def xa_ap(ch):
            return mk.f32(mk.C16, (ch % 4) * 2048, 1024)

        def xo_ap(ch):
            return mk.f32(mk.C16, 8192 + (ch % 2) * 2048, 1024)

        def y2_ap(ch):
            return mk.C16[:, 12288 + (ch % 2) * 1024:
                          12288 + (ch % 2) * 1024 + 1024]

        def qon_ap(g):
            return mk.WAR[:, 2048 + (g % QRING) * 512:
                          2048 + (g % QRING) * 512 + 512]

        def oload(sync, g):
            w, hc = divmod(g, OCF)
            if g >= ORING:
                sync.wait_ge(mk.sems["pha"], pha0 + g - (ORING - 1))
            osl = mk.B16[:, 28672 + (g % ORING) * 512:
                         28672 + (g % ORING) * 512 + 512]
            mk.rissue("dqr", g, ORING, sync.dma_start(
                out=osl, in_=mk.dosp[hc * 128:(hc + 1) * 128,
                                     w * 512:(w + 1) * 512]))

        def fsy(sync):
            sync.wait_ge(mk.sems["wtg"], 1)
            for ic in range(12, 24):
                sync.dma_start(
                    out=mk.B16[:, ic * D:(ic + 1) * D].rearrange(
                        "p (u o) -> p u o", o=128),
                    in_=mk.dwall[2 * HC:3 * HC, ic].rearrange(
                        "u p o -> p u o")).then_inc(mk.sems["dW"], 16)
                mk.cnt["dW"] += 16
            for r in range(2):
                sync.wait_ge(mk.sems[f"e2o{r}"], e2o_fin[r])
            for g in range(ORING):
                oload(sync, g)
            sync.wait_ge(mk.sems["msv"], gv["v"])
            sync.wait_ge(mk.sems["msg"], gv["g"])
            sync.wait_ge(mk.sems["mss"], gv["s"])
            for ic in range(12):
                sync.dma_start(
                    out=mk.B16[:, ic * D:(ic + 1) * D].rearrange(
                        "p (u o) -> p u o", o=128),
                    in_=mk.dwall[2 * HC:3 * HC, ic].rearrange(
                        "u p o -> p u o")).then_inc(mk.sems["dW"], 16)
                mk.cnt["dW"] += 16

            def xa_issue(ch):
                if not 2 <= ch < nch:
                    return
                if ch >= 4:
                    sync.wait_ge(mk.sems["phf"], phf0 + ch - 3)
                mk.rissue("dxl", ch, 4, sync.dma_start(
                    out=xa_ap(ch), in_=xin_dram[ch * 128:(ch + 1) * 128, :]))

            for ch in (0, 1):
                mk.rissue("dxl", ch, 4, sync.dma_start(
                    out=xa_ap(ch), in_=xin_dram[ch * 128:(ch + 1) * 128, :]))
            for g in range(ORING, NSLAB):
                w, hc = divmod(g, OCF)
                oload(sync, g)
                if hc == 0:
                    xa_issue(4 * w - 2)
                elif hc == 2:
                    xa_issue(4 * w - 1)
                elif hc == 16:
                    xa_issue(4 * w)
                elif hc == 18:
                    xa_issue(4 * w + 1)
            xa_issue(nch - 2)
            xa_issue(nch - 1)
            mk.rtotal(sync, "dqr", ORING)
            mk.rtotal(sync, "dxl", 4)
            mk.inc("msy", sync.nop())

        def fg(g):
            mk.entry(g, "g")
            g.wait_ge(mk.sems["bar"], bar0)
            for gi in range(NSLAB):
                mk.rwait(g, "dqr", gi, ORING, dqrsnap, 16)
                if gi >= 2:
                    g.wait_ge(mk.sems["phb"], phb0 + gi - 1)
                osl = mk.B16[:, 28672 + (gi % ORING) * 512:
                             28672 + (gi % ORING) * 512 + 512]
                op_ = mk.f32(mk.WAR, (gi % 2) * 1024, 512)
                g.tensor_scalar(out=op_, in0=osl,
                                scalar1=mk.SC[:, SC_RO:SC_RO + 1],
                                scalar2=MAGIC, op0=OP.mult, op1=OP.add)
                mk.inc("pha", g.drain())
            mk.inc("msg", g.drain())

        def do_lns(s, ch):
            mk.ln_s(s, ch % mk.NLN, mk.LNa, xo_ap(ch),
                    mk.f32(mk.B16, 24576 + (ch % 2) * 2048, 1024),
                    lnv0 + ch + 1)

        def st_issue(s, ch):
            s.wait_ge(mk.sems["phf"], phf0 + ch + 1)
            mk.rissue("dxs", ch, 2, s.dma_start(
                out=xout_dram[ch * 128:(ch + 1) * 128, :], in_=xo_ap(ch)))
            if fuse_ln:
                mk.rissue("dys", ch, 2, s.dma_start(
                    out=mk.dy2sp[ch * 128:(ch + 1) * 128, :], in_=y2_ap(ch)))

        def fs(s):
            mk.entry(s, "s")
            sched = {}

            def put(slot, kind, ch):
                if 0 <= ch < nch and (fuse_ln or kind == "st"):
                    sched.setdefault(slot, []).append((kind, ch))

            for w in range(1, NW):
                base = w * OCF
                put(base + 0, "st", 4 * w - 6)
                put(base + 2, "st", 4 * w - 5)
                put(base + 8, "ln", 4 * w - 4)
                put(base + 12, "ln", 4 * w - 3)
                put(base + 14, "st", 4 * w - 4)
                put(base + 16, "st", 4 * w - 3)
                put(base + 20, "ln", 4 * w - 2)
                put(base + 22, "ln", 4 * w - 1)
            for g in range(NSLAB):
                s.wait_ge(mk.sems["pha"], pha0 + g + 1)
                if g >= QRING:
                    slab_wait(s, g - QRING)
                op_ = mk.f32(mk.WAR, (g % 2) * 1024, 512)
                s.activation(out=qon_ap(g), in_=op_, func=ACT.Identity,
                             scale=1.0, bias=mk.MAGNEG[:, 0:1])
                mk.inc("phb", s.drain())
                for kind, ch in sched.get(g, []):
                    if kind == "ln":
                        do_lns(s, ch)
                    else:
                        st_issue(s, ch)
            for ch in range(nch - 4, nch):
                if ch - 2 >= 0:
                    st_issue(s, ch - 2)
                if fuse_ln:
                    do_lns(s, ch)
            st_issue(s, nch - 2)
            st_issue(s, nch - 1)
            mk.rtotal(s, "dxs", 2)
            if fuse_ln:
                mk.rtotal(s, "dys", 2)
            mk.inc("mss", s.drain())

        def emit_tr(t, ch):
            t.wait_ge(mk.sems["phf"], phf0 + ch + 1)
            tgt = chunk_psum(ch)[2]
            y2t = y2_ap(ch)
            last = None
            for hcx in range(HC):
                last = t.transpose(tgt[:, hcx * 128:(hcx + 1) * 128],
                                   y2t[:, hcx * 128:(hcx + 1) * 128],
                                   mk.id16[:])
            mk.inc("phd", last)

        def ft(t):
            mk.entry(t, "t")
            for k in range(npair):
                w = k // 2
                for hc in range(OCF):
                    g = w * OCF + hc
                    t.wait_ge(mk.sems["phb"], phb0 + g + 1)
                    if k == 0 and hc == 0:
                        t.wait_ge(mk.sems["dW"], dw0 + 16 * OCF)
                    elif hc == 0:
                        if k >= 2:
                            if fuse_ln:
                                t.wait_ge(mk.sems["phe"],
                                          phe0 + 2 * (k - 2) + 2)
                            else:
                                t.wait_ge(mk.sems["phf"],
                                          phf0 + 2 * (k - 2) + 2)
                    slab = qon_ap(g)
                    off = (k % 2) * 256
                    last = None
                    for ci in range(2):
                        halves = chunk_psum(2 * k + ci)[0]
                        lt = slab[:, off + ci * 128:off + ci * 128 + 128]
                        for half in range(2):
                            last = t.matmul(
                                halves[half], lhsT=lt,
                                rhs=mk.B16[:, hc * D + half * 512:
                                           hc * D + half * 512 + 512],
                                start=(hc == 0), stop=(hc == OCF - 1))
                    if hc == OCF - 1:
                        mk.inc("phc", last)
                    elif k % 2 == 1:
                        mk.inc("phg", last)
                    if fuse_ln and k >= 1:
                        if hc == 14:
                            emit_tr(t, 2 * (k - 1))
                        elif hc == 18:
                            emit_tr(t, 2 * (k - 1) + 1)
            if fuse_ln:
                emit_tr(t, nch - 2)
                emit_tr(t, nch - 1)
            mk.inc("mst", t.drain())

        def fm_copy(v, ch):
            v.wait_ge(mk.sems["phd"], phd0 + ch + 1)
            dst = mk.A16[:, 0:HC * T].rearrange(
                "p (h t) -> p h t", h=HC)[:, :, ch * 128:(ch + 1) * 128]
            v.tensor_copy(out=dst,
                          in_=chunk_psum(ch)[2].rearrange(
                              "p (h t) -> p h t", h=HC))
            mk.inc("phe", v.drain())

        def fvt(v):
            mk.entry(v, "v")
            for ch in range(nch):
                k = ch // 2
                v.wait_ge(mk.sems["phc"], phc0 + k + 1)
                mk.rwait(v, "dxl", ch, 4, dxlsnap, 16)
                halves, full, _ = chunk_psum(ch)
                xa = xa_ap(ch)
                xo = xo_ap(ch)
                if ch >= 2:
                    mk.rwait(v, "dxs", ch, 2, dxssnap, 16, occ=ch // 2 - 1)
                if full is not None:
                    v.scalar_tensor_tensor(
                        out=xo, in0=full, scalar=mk.SC[:, SC_HSO:SC_HSO + 1],
                        in1=xa, op0=OP.mult, op1=OP.add)
                else:
                    for j in range(2):
                        v.scalar_tensor_tensor(
                            out=xo[:, j * 512:(j + 1) * 512], in0=halves[j],
                            scalar=mk.SC[:, SC_HSO:SC_HSO + 1],
                            in1=xa[:, j * 512:(j + 1) * 512],
                            op0=OP.mult, op1=OP.add)
                v.drain()
                if fuse_ln:
                    mk.ln_v_pre(v, ch % mk.NLN, mk.LNa, xo)
                    if ch >= 1:
                        fm_copy(v, ch - 1)
                    y2t = y2_ap(ch)
                    if ch >= 2:
                        v.wait_ge(mk.sems["phd"], phd0 + ch - 1)
                        mk.rwait(v, "dys", ch, 2, dyssnap, 16,
                                 occ=ch // 2 - 1)
                    mk.ln_v_apply(v, ch % mk.NLN, mk.LNa, xo, y2t,
                                  lns0 + ch + 1)
                mk.inc("phf", v.drain())
            if fuse_ln:
                fm_copy(v, nch - 1)
            mk.inc("msv", v.drain())

        self.blk.sync(fsy)
        self.blk.gpsimd(fg)
        self.blk.scalar(fs)
        self.blk.tensor(ft)
        self.blk.vector(fvt)
        self.snap()

    def phase_q(self):
        mk = self
        T, nch = self.T, self.nch
        dw0 = self.cnt["dW"]
        pha0 = self.cnt["pha"]
        phb0 = self.cnt["phb"]
        phc0 = self.cnt["phc"]
        phd0 = self.cnt["phd"]
        phe0 = self.cnt["phe"]
        kvb0 = self.cnt["kvb"]

        def fwl(sync):
            mk.entry(sync, "y")
            sync.wait_ge(mk.sems["dL"], mk.dg_p0)
            sync.dma_start(
                out=mk.WAR[:, 0:8192].rearrange("p (a o) -> p a o", a=HC),
                in_=mk.dwq16.rearrange("(a p) o -> p a o", p=128))\
                .then_inc(mk.sems["dW"], 16)
            mk.cnt["dW"] += 16
            mk.inc("msy", sync.nop())

        def emit_q(t, tc):
            if tc >= 2:
                t.wait_ge(mk.sems["kvb"], kvb0 + tc - 1)
                t.wait_ge(mk.sems["phc"], phc0 + tc - 1)
            last = None
            for hc in range(HC):
                for half in range(2):
                    last = t.matmul(
                        mk.pd[tc % 2][:, half * 512:(half + 1) * 512],
                        lhsT=mk.A16[:, hc * T + tc * 128:
                                    hc * T + tc * 128 + 128],
                        rhs=mk.WAR[:, hc * 1024 + half * 512:
                                   hc * 1024 + half * 512 + 512],
                        start=(hc == 0), stop=(hc == HC - 1))
            mk.inc("pha", last)

        def emit_kvtr(t, tc):
            t.wait_ge(mk.sems["phb"], phb0 + tc + 1)
            ftm = mk.C16[:, 8192 + (tc % 3) * 1024:
                         8192 + (tc % 3) * 1024 + 1024]
            vt = mk.C16[:, 12288 + (tc % 3) * 1024:
                        12288 + (tc % 3) * 1024 + 1024]
            for h in range(NH):
                t.matmul(
                    mk.pb[h // 4][:, (h % 4) * 128:(h % 4) * 128 + 128],
                    lhsT=ftm[:, h * 128:(h + 1) * 128],
                    rhs=vt[:, h * 128:(h + 1) * 128],
                    start=(tc == 0 and h % 4 == 0),
                    stop=(tc == nch - 1 and h % 4 == 3))
                t.matmul(
                    mk.pd[2][:, h:h + 1],
                    lhsT=ftm[:, h * 128:(h + 1) * 128],
                    rhs=mk.ones16[:, 0:1],
                    start=(tc == 0 and h == 0),
                    stop=(tc == nch - 1 and h == NH - 1))
            for wave in range(2):
                w = 2 * tc + wave
                if w >= 1:
                    t.wait_ge(mk.sems["phe"], phe0 + w)
                last = None
                for r in range(4):
                    hh = wave * 4 + r
                    last = t.transpose(
                        mk.pdh(2, 1)[:, r * 128:(r + 1) * 128],
                        ftm[:, hh * 128:(hh + 1) * 128], mk.id16[:])
                mk.inc("phd", last)

        def ft(t):
            mk.entry(t, "t")
            t.wait_ge(mk.sems["dW"], dw0 + 16)
            for tc in range(nch):
                emit_q(t, tc)
                if tc >= 2:
                    emit_kvtr(t, tc - 2)
            emit_kvtr(t, nch - 2)
            emit_kvtr(t, nch - 1)
            mk.inc("mst", t.drain())

        def fv(v):
            mk.entry(v, "v")
            for tc in range(nch):
                v.wait_ge(mk.sems["pha"], pha0 + tc + 1)
                pq = mk.pd[tc % 2]
                ein = mk.f32(mk.C16, (tc % 2) * 2048, 1024)
                v.tensor_scalar(out=ein, in0=pq[:, :], scalar1=0.0,
                                scalar2=None, op0=OP.min)
                vt = mk.C16[:, 12288 + (tc % 3) * 1024:
                            12288 + (tc % 3) * 1024 + 1024]
                if tc >= 3:
                    v.wait_ge(mk.sems["phd"], phd0 + 2 * (tc - 2))
                v.tensor_copy(out=vt, in_=pq[:, :])
                mk.inc("kvb", v.drain())
                v.wait_ge(mk.sems["phc"], phc0 + tc + 1)
                e16 = mk.C16[:, 4096 + (tc % 2) * 1024:
                             4096 + (tc % 2) * 1024 + 1024]
                rl = mk.C16[:, 6144 + (tc % 2) * 1024:
                            6144 + (tc % 2) * 1024 + 1024]
                ftm = mk.C16[:, 8192 + (tc % 3) * 1024:
                             8192 + (tc % 3) * 1024 + 1024]
                v.tensor_tensor(out=ftm, in0=e16, in1=rl, op=OP.add)
                mk.inc("phb", v.drain())
            mk.inc("msv", v.drain())

        def emit_fevac(s, tc):
            for wave in range(2):
                s.wait_ge(mk.sems["phd"], phd0 + 2 * tc + wave + 1)
                dst = mk.B16[:, 0:HC * T].rearrange(
                    "p (h t) -> p h t", h=HC)[
                    :, wave * 4:(wave + 1) * 4, tc * 128:(tc + 1) * 128]
                s.copy(out=dst,
                       in_=mk.pdh(2, 1)[:, 0:512].rearrange(
                           "p (h t) -> p h t", h=4))
                mk.inc("phe", s.drain())

        def fs(s):
            mk.entry(s, "s")
            for tc in range(nch):
                s.wait_ge(mk.sems["kvb"], kvb0 + tc + 1)
                if tc >= 2:
                    s.wait_ge(mk.sems["phb"], phb0 + tc - 1)
                ein = mk.f32(mk.C16, (tc % 2) * 2048, 1024)
                e16 = mk.C16[:, 4096 + (tc % 2) * 1024:
                             4096 + (tc % 2) * 1024 + 1024]
                rl = mk.C16[:, 6144 + (tc % 2) * 1024:
                            6144 + (tc % 2) * 1024 + 1024]
                s.activation(out=rl, in_=mk.pd[tc % 2][:, :], func=ACT.Relu)
                s.drain()
                s.activation(out=e16, in_=ein, func=ACT.Exp)
                mk.inc("phc", s.drain())
                if tc >= 1:
                    emit_fevac(s, tc - 1)
            emit_fevac(s, nch - 1)
            mk.inc("mss", s.drain())

        self.blk.sync(fwl)
        self.blk.tensor(ft)
        self.blk.vector(fv)
        self.blk.scalar(fs)
        self.snap()

    def phase_kvx(self):
        mk = self
        phb0 = self.cnt["phb"]

        def fv(v):
            mk.entry(v, "v")
            kvp = mk.f32(mk.C16, 0, NH * 128)
            v.tensor_copy(out=kvp[:, 0:512], in_=mk.pb[0][:, :])
            v.tensor_copy(out=kvp[:, 512:1024], in_=mk.pb[1][:, :])
            ksp = mk.f32(mk.C16, 2 * NH * 128, NH)
            v.tensor_copy(out=ksp, in_=mk.pd[2][:, 0:NH])
            mk.inc("phb", v.drain())
            mk.inc("msv", v.drain())

        def fg(g):
            mk.entry(g, "g")
            g.wait_ge(mk.sems["phb"], phb0 + 1)
            kvp = mk.f32(mk.C16, 0, NH * 128 + NH)
            g.dma_start(out=mk.dkv_in, in_=kvp).then_inc(mk.sems["dG"], 16)
            mk.cnt["dG"] += 16
            mk.w(g, "dG")
            ic = mk.inc("cc", g.collective_compute(
                "AllReduce", OP.add,
                replica_groups=[[0, 1], [2, 3], [4, 5], [6, 7]],
                ins=[mk.dkv_in.opt()], outs=[mk.dkv_out.opt()]))
            g.wait_ge(mk.sems["cc"], ic)
            g.dma_start(out=mk.WAR[:, 0:NH * 128],
                        in_=mk.dkv_out[:, 0:NH * 128])\
                .then_inc(mk.sems["dG"], 16)
            mk.cnt["dG"] += 16
            g.dma_start(out=mk.f32(mk.WAR, 1024, NH),
                        in_=mk.dkv_out[:, NH * 128:NH * 128 + NH])\
                .then_inc(mk.sems["dG"], 16)
            mk.cnt["dG"] += 16
            mk.w(g, "dG")
            mk.inc("kvb", g.drain())
            mk.inc("msg", g.drain())

        kvb0 = self.cnt["kvb"]

        def fv2(v):
            v.wait_ge(mk.sems["kvb"], kvb0 + 1)
            ksbd = mk.WAR[:, 1040:1040 + NH * NH]
            v.memset(ksbd, 0.0)
            v.drain()
            ks = mk.f32(mk.WAR, 1024, NH)
            for h in range(NH):
                v.tensor_copy(out=ksbd[:, h * NH + h:h * NH + h + 1],
                              in_=ks[:, h:h + 1])
            mk.inc("phb", v.drain())
            mk.inc("msv", v.drain())

        self.blk.vector(fv)
        self.blk.gpsimd(fg)
        self.blk.vector(fv2)
        self.snap()

    def phase_m1(self):
        mk = self
        T, ntt = self.T, self.ntt
        pha0 = self.cnt["pha"]
        phb0 = self.cnt["phb"]
        uccsnap = self.rsnap("ucc", 2)
        p0lsnap = self.rsnap("p0l", 2)
        phc0 = self.cnt["phc"]
        phd0 = self.cnt["phd"]
        phe0 = self.cnt["phe"]
        phf0 = self.cnt["phf"]
        dz0 = self.cnt["dZ"]

        def ft(t):
            mk.entry(t, "t")
            for tt in range(ntt):
                if tt >= 1:
                    t.wait_ge(mk.sems["phb"], phb0 + tt)
                last = None
                for fc in range(HC):
                    last = t.matmul(
                        mk.pd[2][0:NH, 0:512],
                        lhsT=mk.WAR[:, 1040 + fc * NH:1040 + (fc + 1) * NH],
                        rhs=mk.B16[:, fc * T + tt * 512:fc * T + tt * 512 + 512],
                        start=(fc == 0), stop=(fc == HC - 1))
                mk.inc("pha", last)
                mk.rwait(t, "p0l", tt, 2, p0lsnap, 16)
                rb = 2560 if tt % 2 == 0 else 10240
                for h in range(NH):
                    hi = tt * NH + h
                    if hi >= 2:
                        t.wait_ge(mk.sems["phd"], phd0 + hi - 1)
                    last = t.matmul(
                        mk.pb[h % 2][:, :],
                        lhsT=mk.onesr16[0:1, :],
                        rhs=mk.C16[0:1, rb + h * 512:rb + h * 512 + 512],
                        start=True, stop=True)
                    mk.inc("phc", last)
                    t.wait_ge(mk.sems["phd"], phd0 + hi + 1)
                    if hi >= 4:
                        t.wait_ge(mk.sems["phf"], phf0 + hi - 3)
                    mmb = [mk.pd[0][:, 0:512], mk.pd[0][:, 512:1024],
                           mk.pd[1][:, 0:512], mk.pd[1][:, 512:1024]][hi % 4]
                    last = t.matmul(
                        mmb,
                        lhsT=mk.WAR[:, h * 128:(h + 1) * 128],
                        rhs=mk.C16[:, 6656 + (hi % 2) * 512:
                                   6656 + (hi % 2) * 512 + 512],
                        start=True, stop=True)
                    mk.inc("phe", last)
            mk.inc("mst", t.drain())

        def fv(v):
            mk.entry(v, "v")
            for tt in range(ntt):
                v.wait_ge(mk.sems["pha"], pha0 + tt + 1)
                if tt >= 1:
                    mk.rwait(v, "ucc", tt - 1, 2, uccsnap, 16)
                zf = mk.f32(mk.C16, 0, 512)[0:NH, :]
                v.tensor_scalar(out=zf, in0=mk.pd[2][0:NH, 0:512],
                                scalar1=ATTN_EPS, scalar2=None, op0=OP.add)
                v.drain()
                zf2 = mk.f32(mk.C16, 1024, 512)[0:NH, :]
                v.reciprocal(out=zf2, in_=zf)
                v.drain()
                v.tensor_scalar(out=mk.C16[0:NH, 2048:2560],
                                in0=zf2, scalar1=float(mk.L), scalar2=None,
                                op0=OP.mult)
                mk.inc("phb", v.drain())
                for h in range(NH):
                    hi = tt * NH + h
                    v.wait_ge(mk.sems["phc"], phc0 + hi + 1)
                    if hi >= 2:
                        v.wait_ge(mk.sems["phe"], phe0 + hi - 1)
                    fz = mk.C16[:, 6656 + (hi % 2) * 512:
                                6656 + (hi % 2) * 512 + 512]
                    v.tensor_tensor(
                        out=fz,
                        in0=mk.B16[:, h * T + tt * 512:h * T + tt * 512 + 512],
                        in1=mk.pb[h % 2][:, :], op=OP.mult)
                    mk.inc("phd", v.drain())
            mk.inc("msv", v.drain())

        def fz(sync):
            mk.entry(sync, "y")
            for tt in range(ntt):
                sync.wait_ge(mk.sems["phb"], phb0 + tt + 1)
                dsl = mk.dzl[0:1, (tt % 2) * NH * 512:
                             (tt % 2 + 1) * NH * 512]
                mk.rissue("ucc", tt, 2, sync.dma_start(
                    out=dsl.rearrange("o (h t) -> (o h) t", h=NH),
                    in_=mk.C16[0:NH, 2048:2560]))
                mk.rwait(sync, "ucc", tt, 2, uccsnap, 16)
                if tt >= 2:
                    sync.wait_ge(mk.sems["phc"], phc0 + NH * (tt - 1))
                rb = 2560 if tt % 2 == 0 else 10240
                mk.rissue("p0l", tt, 2, sync.dma_start(
                    out=mk.C16[0:1, rb:rb + NH * 512], in_=dsl))
            mk.inc("msy", sync.nop())

        def fs(s):
            mk.entry(s, "s")
            for tt in range(ntt):
                for h in range(NH):
                    hi = tt * NH + h
                    s.wait_ge(mk.sems["phe"], phe0 + hi + 1)
                    mmb = [mk.pd[0][:, 0:512], mk.pd[0][:, 512:1024],
                           mk.pd[1][:, 0:512], mk.pd[1][:, 512:1024]][hi % 4]
                    s.activation(out=mk.A16[:, h * T + tt * 512:
                                            h * T + tt * 512 + 512],
                                 in_=mmb, func=ACT.Identity,
                                 scale=1.0 / float(mk.L))
                    mk.inc("phf", s.drain())
            mk.inc("mss", s.drain())

        self.blk.tensor(ft)
        self.blk.vector(fv)
        self.blk.sync(fz)
        self.blk.scalar(fs)
        self.snap()

    def phase_m2a(self):
        mk = self
        T, ntt, nch = self.T, self.ntt, self.nch
        nhalf = 2 if ntt >= 2 else 1
        tph = ntt // nhalf
        dw0 = self.cnt["dW"]
        pha0 = self.cnt["pha"]
        phb0 = self.cnt["phb"]
        phc0 = self.cnt["phc"]
        phd0 = self.cnt["phd"]
        dyssnap = self.rsnap("dys", 2)
        lnv0 = self.cnt["lnv"]
        lns0 = self.cnt["lns"]
        groups = [(oc, hf) for oc in range(HC) for hf in range(nhalf)]
        banksets = [
            [self.pb[0][:, :], self.pb[1][:, :],
             self.pd[2][:, 0:512], self.pd[2][:, 512:1024]][:tph],
            [self.pd[0][:, 0:512], self.pd[0][:, 512:1024],
             self.pd[1][:, 0:512], self.pd[1][:, 512:1024]][:tph],
        ]

        def fwl(sync):
            mk.entry(sync, "y")
            sync.wait_ge(mk.sems["dL"], mk.dg_p0)
            sync.dma_start(
                out=mk.WAR[:, 0:8192].rearrange("p (a o) -> p a o", a=HC),
                in_=mk.dwm16.rearrange("(a p) o -> p a o", p=128))\
                .then_inc(mk.sems["dW"], 16)
            mk.cnt["dW"] += 16
            mk.inc("msy", sync.nop())

        def ft(t):
            mk.entry(t, "t")
            t.wait_ge(mk.sems["dW"], dw0 + 16)
            for gi, (oc, hf) in enumerate(groups):
                if gi >= 2:
                    t.wait_ge(mk.sems["phb"], phb0 + gi - 1)
                bs = banksets[gi % 2]
                last = None
                for vc in range(HC):
                    for j in range(tph):
                        tt = hf * tph + j
                        last = t.matmul(
                            bs[j],
                            lhsT=mk.WAR[:, vc * 1024 + oc * 128:
                                        vc * 1024 + oc * 128 + 128],
                            rhs=mk.A16[:, vc * T + tt * 512:
                                       vc * T + tt * 512 + 512],
                            start=(vc == 0), stop=(vc == HC - 1))
                mk.inc("pha", last)
            t.wait_ge(mk.sems["phb"], phb0 + len(groups))
            for ch in range(nch):
                if ch >= 2:
                    t.wait_ge(mk.sems["phd"], phd0 + ch - 1)
                last = None
                for oc in range(HC):
                    last = t.transpose(
                        mk.pdh(ch % 2, 0)[:, oc * 128:(oc + 1) * 128],
                        mk.B16[:, oc * T + ch * 128:oc * T + ch * 128 + 128],
                        mk.id16[:])
                mk.inc("phc", last)
            mk.inc("mst", t.drain())

        def fs(s):
            mk.entry(s, "s")
            for gi, (oc, hf) in enumerate(groups):
                s.wait_ge(mk.sems["pha"], pha0 + gi + 1)
                bs = banksets[gi % 2]
                last = None
                for j in range(tph):
                    tt = hf * tph + j
                    last = s.copy(out=mk.B16[:, oc * T + tt * 512:
                                             oc * T + tt * 512 + 512],
                                  in_=bs[j])
                mk.inc("phb", last)
            mk.inc("mss", s.drain())

        phg0 = self.cnt["phg"]

        def fvt(v):
            mk.entry(v, "v")
            for ch in range(nch):
                v.wait_ge(mk.sems["phc"], phc0 + ch + 1)
                ptm = mk.pdh(ch % 2, 0)
                k = ch % mk.NLN
                mk.ln_v_pre(v, k, mk.LNb, ptm)
                ya = mk.C16[:, (ch % 2) * 2048:(ch % 2) * 2048 + 1024]
                if ch >= 2:
                    v.wait_ge(mk.sems["phg"], phg0 + ch - 1)
                mk.ln_v_apply(v, k, mk.LNb, ptm, ya, lns0 + ch + 1)
                mk.inc("phd", v.drain())
            mk.inc("msv", v.drain())

        def fgt(g):
            mk.entry(g, "g")
            for ch in range(nch):
                g.wait_ge(mk.sems["phd"], phd0 + ch + 1)
                ya = mk.C16[:, (ch % 2) * 2048:(ch % 2) * 2048 + 1024]
                yb = mk.C16[:, (ch % 2) * 2048 + 1024:(ch % 2) * 2048 + 2048]
                if ch >= 2:
                    mk.rwait(g, "dys", ch, 2, dyssnap, 16, occ=ch // 2 - 1)
                g.tensor_tensor(out=yb, in0=ya, in1=mk.g1r[:], op=OP.mult)
                g.drain()
                g.tensor_tensor(out=yb, in0=yb, in1=mk.b1r[:], op=OP.add)
                mk.inc("phg", g.drain())
            mk.inc("msg", g.drain())

        def fst(s):
            for ch in range(nch):
                s.wait_ge(mk.sems["phc"], phc0 + ch + 1)
                ptm = mk.pdh(ch % 2, 0)
                sqscr = mk.f32(mk.C16, 4096 + (ch % 2) * 2048, 1024)
                mk.ln_s(s, ch % mk.NLN, mk.LNb, ptm, sqscr,
                        lnv0 + ch + 1)
            mk.inc("mss", s.drain())

        def fys(sync):
            for ch in range(nch):
                sync.wait_ge(mk.sems["phg"], phg0 + ch + 1)
                yb = mk.C16[:, (ch % 2) * 2048 + 1024:(ch % 2) * 2048 + 2048]
                mk.rissue("dys", ch, 2, sync.dma_start(
                    out=mk.dya1[ch * 128:(ch + 1) * 128, :], in_=yb))
            mk.rtotal(sync, "dys", 2)
            mk.inc("msy", sync.nop())

        self.blk.sync(fwl)
        self.blk.tensor(ft)
        self.blk.scalar(fs)
        self.blk.vector(fvt)
        self.blk.gpsimd(fgt)
        self.blk.scalar(fst)
        self.blk.sync(fys)
        self.snap()

    def phase_m2b(self):
        mk = self
        T, ntt = self.T, self.ntt
        nhalf = 2 if ntt >= 2 else 1
        tph = ntt // nhalf
        stw = tph * 512
        dw0 = self.cnt["dW"]
        pha0 = self.cnt["pha"]
        phb0 = self.cnt["phb"]
        ugssnap = self.rsnap("ugs", 2)
        groups = [(oc2, hf) for oc2 in range(KC2) for hf in range(nhalf)]
        banksets = [
            [self.pd[0][:, 0:512], self.pd[0][:, 512:1024],
             self.pd[1][:, 0:512], self.pd[1][:, 512:1024]][:tph],
            [self.pd[2][:, 0:512], self.pd[2][:, 512:1024],
             self.pb[0][:, :], self.pb[1][:, :]][:tph],
        ]

        def fwl(sync):
            mk.entry(sync, "y")
            sync.wait_ge(mk.sems["dL"], mk.dg_p0)
            sync.dma_start(
                out=mk.WAR[:, 0:16384].rearrange("p (a o) -> p a o", a=HC),
                in_=mk.dw116.rearrange("(a p) o -> p a o", p=128))\
                .then_inc(mk.sems["dW"], 16)
            mk.cnt["dW"] += 16
            mk.inc("msy", sync.nop())

        def ft(t):
            mk.entry(t, "t")
            t.wait_ge(mk.sems["dW"], dw0 + 16)
            for gi, (oc2, hf) in enumerate(groups):
                if gi >= 2:
                    t.wait_ge(mk.sems["phb"], phb0 + gi - 1)
                bs = banksets[gi % 2]
                last = None
                for vc in range(HC):
                    for j in range(tph):
                        tt = hf * tph + j
                        last = t.matmul(
                            bs[j],
                            lhsT=mk.WAR[:, vc * 2048 + oc2 * 128:
                                        vc * 2048 + oc2 * 128 + 128],
                            rhs=mk.B16[:, vc * T + tt * 512:
                                       vc * T + tt * 512 + 512],
                            start=(vc == 0), stop=(vc == HC - 1))
                mk.inc("pha", last)
            mk.inc("mst", t.drain())

        def fs(s):
            mk.entry(s, "s")
            for gi, (oc2, hf) in enumerate(groups):
                s.wait_ge(mk.sems["pha"], pha0 + gi + 1)
                if gi >= 2:
                    mk.rwait(s, "ugs", gi, 2, ugssnap, 16, occ=gi // 2 - 1)
                stg = mk.C16[:, (gi % 2) * stw:(gi % 2) * stw + stw]
                bs = banksets[gi % 2]
                last = None
                for j in range(tph):
                    last = s.activation(out=stg[:, j * 512:(j + 1) * 512],
                                        in_=bs[j], func=ACT.Relu)
                mk.inc("phb", last)
            mk.inc("mss", s.drain())

        def fsp(sync):
            for gi, (oc2, hf) in enumerate(groups):
                sync.wait_ge(mk.sems["phb"], phb0 + gi + 1)
                stg = mk.C16[:, (gi % 2) * stw:(gi % 2) * stw + stw]
                mk.rissue("ugs", gi, 2, sync.dma_start(
                    out=mk.du1sp[oc2 * 128:(oc2 + 1) * 128,
                                 hf * stw:hf * stw + stw], in_=stg))
            mk.rtotal(sync, "ugs", 2)
            mk.inc("msy", sync.nop())

        self.blk.sync(fwl)
        self.blk.tensor(ft)
        self.blk.scalar(fs)
        self.blk.sync(fsp)
        self.snap()

    def phase_m2c(self):
        mk = self
        T, ntt, nch = self.T, self.ntt, self.nch
        dw0 = self.cnt["dW"]
        dxlsnap = self.rsnap("dxl", 2)
        dxssnap = self.rsnap("dxs", 2)
        pha0 = self.cnt["pha"]
        phb0 = self.cnt["phb"]
        phc0 = self.cnt["phc"]
        phd0 = self.cnt["phd"]
        dx0_ = self.cnt["dX"]
        lnv0 = self.cnt["lnv"]
        lns0 = self.cnt["lns"]

        def fwl(sync):
            mk.entry(sync, "y")
            sync.wait_ge(mk.sems["dL"], mk.dg_p0)
            sync.dma_start(
                out=mk.WAR[:, 0:16384].rearrange("p (a o) -> p a o", a=KC2),
                in_=mk.dw216.rearrange("(a p) o -> p a o", p=128))\
                .then_inc(mk.sems["dW"], 16)
            mk.cnt["dW"] += 16
            for kc in range(KC2):
                buf = mk.A16 if kc < 8 else mk.B16
                sync.dma_start(out=buf[:, (kc % 8) * T:(kc % 8) * T + T],
                               in_=mk.du1sp[kc * 128:(kc + 1) * 128, :])\
                    .then_inc(mk.sems["dW"], 16)
                mk.cnt["dW"] += 16
            mk.inc("msy", sync.nop())

        def xstore(sync, ch, phe_base):
            sync.wait_ge(mk.sems["phe"], phe_base + ch + 1)
            xa = mk.f32(mk.C16, 8192 + (ch % 2) * 2048, 1024)
            mk.rissue("dxs", ch, 2, sync.dma_start(
                out=mk.dxx2[ch * 128:(ch + 1) * 128, :], in_=xa))

        phe_base = self.cnt["phe"]

        def fxl(sync):
            for ch in range(nch):
                if ch >= 2:
                    sync.wait_ge(mk.sems["phe"], phe_base + ch - 1)
                    mk.rwait(sync, "dxs", ch - 2, 2, dxssnap, 16)
                xa = mk.f32(mk.C16, 8192 + (ch % 2) * 2048, 1024)
                mk.rissue("dxl", ch, 2, sync.dma_start(
                    out=xa, in_=mk.dxx1[ch * 128:(ch + 1) * 128, :]))
                y2t = mk.C16[:, 12288 + (ch % 2) * 1024:
                             12288 + (ch % 2) * 1024 + 1024]
                mk.rissue("dxl", ch, 2, sync.dma_start(
                    out=y2t, in_=mk.dy2sp[ch * 128:(ch + 1) * 128, :]))
                ya1t = mk.C16[:, 14336 + (ch % 2) * 1024:
                              14336 + (ch % 2) * 1024 + 1024]
                mk.rissue("dxl", ch, 2, sync.dma_start(
                    out=ya1t, in_=mk.dya1[ch * 128:(ch + 1) * 128, :]))
                if ch >= 1:
                    xstore(sync, ch - 1, phe_base)
            xstore(sync, nch - 1, phe_base)
            mk.rtotal(sync, "dxs", 2)
            mk.inc("msy", sync.nop())

        banks4 = [self.pd[2][:, 0:512], self.pd[2][:, 512:1024],
                  self.pb[0][:, :], self.pb[1][:, :]]

        def ft(t):
            mk.entry(t, "t")
            t.wait_ge(mk.sems["dW"], dw0 + 16 * (KC2 + 1))
            gi = 0
            for tt in range(ntt):
                for oc in range(HC):
                    gi += 1
                    if gi > 4:
                        t.wait_ge(mk.sems["phb"], phb0 + gi - 4)
                    last = None
                    for kc in range(KC2):
                        buf = mk.A16 if kc < 8 else mk.B16
                        last = t.matmul(
                            banks4[gi % 4],
                            lhsT=mk.WAR[:, kc * 1024 + oc * 128:
                                        kc * 1024 + oc * 128 + 128],
                            rhs=buf[:, (kc % 8) * T + tt * 512:
                                    (kc % 8) * T + tt * 512 + 512],
                            start=(kc == 0), stop=(kc == KC2 - 1))
                    mk.inc("pha", last)
                for c2 in range(4):
                    ch = tt * 4 + c2
                    t.wait_ge(mk.sems["phb"], phb0 + (tt + 1) * HC)
                    if ch >= 4:
                        t.wait_ge(mk.sems["phe"], phe_base + ch - 3)
                    last = None
                    for oc in range(HC):
                        cc = (tt % 2) * 4096 + c2 * 1024 + oc * 128
                        last = t.transpose(
                            mk.pdh(ch % 2, (ch // 2) % 2)[:, oc * 128:
                                                          (oc + 1) * 128],
                            mk.C16[:, cc:cc + 128], mk.id16[:])
                    mk.inc("phc", last)
            mk.inc("mst", t.drain())

        def fs(s):
            mk.entry(s, "s")
            gi = 0
            for tt in range(ntt):
                for oc in range(HC):
                    gi += 1
                    s.wait_ge(mk.sems["pha"], pha0 + gi)
                    if tt >= 2 and oc == 0:
                        s.wait_ge(mk.sems["phc"], phc0 + (tt - 2 + 1) * 4)
                    base = (tt % 2) * 4096
                    stg = mk.C16[:, base:base + 4096].rearrange(
                        "p (c q) -> p c q", q=1024)[:, :, oc * 128:(oc + 1) * 128]
                    s.copy(out=stg,
                           in_=banks4[gi % 4].rearrange("p (c o) -> p c o",
                                                        o=128))
                    mk.inc("phb", s.drain())
                for c2 in range(4):
                    ch = tt * 4 + c2
                    s.wait_ge(mk.sems["phc"], phc0 + ch + 1)
                    ptm = mk.pdh(ch % 2, (ch // 2) % 2)
                    sqscr = mk.C16[:, (tt % 2) * 4096 + c2 * 1024:
                                   (tt % 2) * 4096 + c2 * 1024 + 1024]
                    mk.ln_s(s, ch % mk.NLN, mk.LNb, ptm, sqscr,
                            lnv0 + ch + 1)
            mk.inc("mss", s.drain())

        def fv(v):
            mk.entry(v, "v")
            for tt in range(ntt):
                for c2 in range(4):
                    ch = tt * 4 + c2
                    v.wait_ge(mk.sems["phc"], phc0 + ch + 1)
                    mk.rwait(v, "dxl", ch, 2, dxlsnap, 48)
                    ptm = mk.pdh(ch % 2, (ch // 2) % 2)
                    k = ch % mk.NLN
                    mk.ln_v_pre(v, k, mk.LNb, ptm)
                    ya2 = mk.C16[:, (tt % 2) * 4096 + c2 * 1024:
                                 (tt % 2) * 4096 + c2 * 1024 + 1024]
                    mk.ln_v_apply(v, k, mk.LNb, ptm, ya2,
                                  lns0 + ch + 1)
                    v.drain()
                    v.tensor_tensor(out=ya2, in0=ya2, in1=mk.g2r[:],
                                    op=OP.mult)
                    v.drain()
                    v.tensor_tensor(out=ya2, in0=ya2, in1=mk.b2r[:],
                                    op=OP.add)
                    v.drain()
                    xa = mk.f32(mk.C16, 8192 + (ch % 2) * 2048, 1024)
                    y2t = mk.C16[:, 12288 + (ch % 2) * 1024:
                                 12288 + (ch % 2) * 1024 + 1024]
                    ya1t = mk.C16[:, 14336 + (ch % 2) * 1024:
                                  14336 + (ch % 2) * 1024 + 1024]
                    if ch >= 2:
                        mk.rwait(v, "dxs", ch, 2, dxssnap, 16,
                                 occ=ch // 2 - 1)
                    v.tensor_tensor(out=xa, in0=xa, in1=y2t, op=OP.add)
                    v.drain()
                    v.tensor_tensor(out=xa, in0=xa, in1=ya1t, op=OP.add)
                    v.drain()
                    v.tensor_tensor(out=xa, in0=xa, in1=ya2, op=OP.add)
                    mk.inc("phe", v.drain())
            mk.inc("msv", v.drain())

        self.blk.sync(fwl)
        self.blk.sync(fxl)
        self.blk.tensor(ft)
        self.blk.scalar(fs)
        self.blk.vector(fv)
        self.snap()

    def finale(self):
        mk = self

        def f(sync):
            mk.entry(sync, "y")
            sync.nop()

        self.blk.sync(f)


def build_program(T, L):
    mk = MK(T, L)
    nc = mk.build()
    return nc


def make_wsh(wup, wgate, wdown, c):
    parts = []
    for j in range(3 * c, 3 * c + 3):
        if j < 8:
            u = np.concatenate([wup[(3 * j + s) * 128:(3 * j + s + 1) * 128, :]
                                for s in range(3)], axis=1)
        elif j < 16:
            jj = j - 8
            u = np.concatenate(
                [wgate[(3 * jj + s) * 128:(3 * jj + s + 1) * 128, :]
                 for s in range(3)], axis=1)
        else:
            u = wdown[(j - 16) * 128:(j - 16 + 1) * 128, :]
        parts.append(u)
    return np.ascontiguousarray(np.concatenate(parts, axis=0))


def kernel(**inputs):
    x = np.ascontiguousarray(inputs["L0_attn_x"])
    B, L, _ = x.shape
    T = L // 2
    nc = build_program(T, L)
    mybir.codegen_inst_isa_subclasses(nc)
    ident = np.eye(128, dtype=np.float32)
    wupf = np.ascontiguousarray(inputs["L0_ffn_w_up"], dtype=np.float32)
    wgatef = np.ascontiguousarray(inputs["L0_ffn_w_gate"], dtype=np.float32)
    wdownf = np.ascontiguousarray(inputs["L0_ffn_w_down"], dtype=np.float32)
    in_maps = []
    for c in range(NCORES):
        b, h = c // 2, c % 2
        in_maps.append({
            "wsh": make_wsh(wupf, wgatef, wdownf, c),
            "x0": np.ascontiguousarray(x[b, h * T:(h + 1) * T, :]),
            "wq": np.ascontiguousarray(inputs["L0_attn_Wq"], dtype=np.float32),
            "wm": np.ascontiguousarray(inputs["L0_attn_Wm"], dtype=np.float32),
            "w1": np.ascontiguousarray(inputs["L0_attn_W1"], dtype=np.float32),
            "w2": np.ascontiguousarray(inputs["L0_attn_W2"], dtype=np.float32),
            "g1": np.ascontiguousarray(inputs["L0_attn_g1"],
                                       dtype=np.float32).reshape(1, D),
            "b1": np.ascontiguousarray(inputs["L0_attn_b1"],
                                       dtype=np.float32).reshape(1, D),
            "g2": np.ascontiguousarray(inputs["L0_attn_g2"],
                                       dtype=np.float32).reshape(1, D),
            "b2": np.ascontiguousarray(inputs["L0_attn_b2"],
                                       dtype=np.float32).reshape(1, D),
            "ident": ident,
        })
    res = run_bass_kernel_spmd(nc, in_maps, core_ids=list(range(NCORES)))
    out = np.empty((B, L, D), dtype=np.float32)
    for c in range(NCORES):
        b, h = c // 2, c % 2
        out[b, h * T:(h + 1) * T, :] = np.asarray(res.results[c]["out"])
    return out


# version 17
# speedup vs baseline: 1.0077x; 1.0027x over previous
import numpy as np
import concourse.bass as bass
import concourse.mybir as mybir
from concourse.alu_op_type import AluOpType as OP
from concourse.bass_utils import run_bass_kernel_spmd
from contextlib import ExitStack

F32 = mybir.dt.float32
F16 = mybir.dt.float16
ACT = mybir.ActivationFunctionType
AX = mybir.AxisListType

NCORES = 8
D = 1024
HC = 8
DFF = 3072
OCF = 24
D2 = 2048
KC2 = 16
NH = 8
MAGIC = 12582912.0
P23 = 8388608.0
QA = 32767.0
QW = 7.0
LN_EPS = 1e-5
ATTN_EPS = 1e-6

SC_RY, SC_SY = 0, 1
SC_RU, SC_SU = 2, 3
SC_RG, SC_SG, SC_BG = 4, 5, 6
SC_R2, SC_S2 = 7, 8
SC_RO, SC_SO = 9, 10
SC_HSO = 11


class MK:
    def __init__(self, T, L):
        self.T = T
        self.L = L
        self.ntt = T // 512
        self.nch = T // 128
        self.TE = min(T, 2048)
        self.cnt = {}
        self.sems = {}
        self.gatev = {}
        self.nc = bass.Bass("TRN2", target_bir_lowering=False, debug=False)

    def mksem(self, name):
        self.sems[name] = self.es.enter_context(self.nc.semaphore(name))
        self.cnt[name] = 0

    def inc(self, name, instr, n=1):
        instr.then_inc(self.sems[name], n)
        self.cnt[name] += n
        return self.cnt[name]

    def w(self, eng, name, v=None):
        eng.wait_ge(self.sems[name], self.cnt[name] if v is None else v)

    def snap(self):
        self.gatev = {e: self.cnt["ms" + e] for e in ("v", "s", "t", "g", "y")}

    def mkring(self, name, R):
        for r in range(R):
            self.mksem(f"{name}{r}")
        return name

    def rsem(self, name, i, R):
        return self.sems[f"{name}{i % R}"]

    def rissue(self, name, i, R, instr, n=16):
        nm = f"{name}{i % R}"
        instr.then_inc(self.sems[nm], n)
        self.cnt[nm] += n

    def rsnap(self, name, R):
        return [self.cnt[f"{name}{r}"] for r in range(R)]

    def rwait(self, eng, name, i, R, snap, per, occ=None):
        o = (i // R if occ is None else occ)
        eng.wait_ge(self.sems[f"{name}{i % R}"], snap[i % R] + per * (o + 1))

    def rtotal(self, eng, name, R):
        for r in range(R):
            nm = f"{name}{r}"
            eng.wait_ge(self.sems[nm], self.cnt[nm])

    def entry(self, eng, me):
        for e in ("v", "s", "t", "g", "y"):
            if e != me and self.gatev.get(e, 0) > 0:
                eng.wait_ge(self.sems["ms" + e], self.gatev[e])

    def f32(self, buf, lo16, n32):
        return buf[:, lo16:lo16 + 2 * n32].bitcast(F32)

    def pdh(self, i, b):
        return self.pd[i][:, b * 512:(b + 1) * 512].bitcast(F16)

    def pbh(self, i):
        return self.pb[i][:, :].bitcast(F16)

    def build(self):
        nc = self.nc
        T = self.T
        es = ExitStack()
        self.es = es
        with es:
            E = es.enter_context
            gx = lambda n, s, dt=F32, k="ExternalInput": nc.dram_tensor(n, s, dt, kind=k).ap()
            gi = lambda n, s, dt=F32: nc.dram_tensor(n, s, dt).ap()
            self.dx0 = gx("x0", [T, D])
            self.dwq = gx("wq", [D, D])
            self.dwm = gx("wm", [D, D])
            self.dw1 = gx("w1", [D, D2])
            self.dw2 = gx("w2", [D2, D])
            self.dg1 = gx("g1", [1, D])
            self.db1 = gx("b1", [1, D])
            self.dg2 = gx("g2", [1, D])
            self.db2 = gx("b2", [1, D])
            self.dident = gx("ident", [128, 128])
            self.dout = gx("out", [T, D], F32, "ExternalOutput")
            self.dxx1 = gi("xx1", [T, D])
            self.dxx2 = gi("xx2", [T, D])
            self.dy2sp = gi("y2sp", [T, D], F16)
            self.dya1 = gi("ya1", [T, D], F16)
            self.dupsp = gi("upsp", [DFF, T], F16)
            self.dgatesp = gi("gatesp", [DFF, T], F16)
            self.dg2sp = gi("g2sp", [DFF, T], F16)
            self.dosp = gi("osp", [DFF, T], F16)
            self.doqsp = gi("oqsp", [DFF, T], F16)
            self.du1sp = gi("u1sp", [D2, T], F16)
            self.dwq16 = gi("wq16", [D, D], F16)
            self.dwm16 = gi("wm16", [D, D], F16)
            self.dw116 = gi("w116", [D, D2], F16)
            self.dw216 = gi("w216", [D2, D], F16)
            self.dwsh = gx("wsh", [3 * 128, DFF])
            self.dwall_sh = gi("wall_sh", [3, OCF, 128, 128], F16)
            self.dwall = gi("wall", [OCF, OCF, 128, 128], F16)
            self.ddcol = gi("dcol", [128, 2])
            self.dbm_in = gi("bm_in", [1, 2])
            self.dbm_out = gi("bm_out", [1, 2])
            self.dbscal = gi("bscal", [1, 16])
            self.dkv_in = gi("kv_in", [128, NH * 128 + NH])
            self.dkv_out = gi("kv_out", [128, NH * 128 + NH])
            self.dzl = gi("dzl", [1, 2 * NH * 512], F16)
            self.did16 = gi("id16d", [128, 128], F16)

            ACOLS = max(HC * T, OCF * D, 31744)
            self.ACOLS = ACOLS
            self.A16 = E(nc.sbuf_tensor("A16", [128, ACOLS], F16))
            self.B16 = E(nc.sbuf_tensor("B16", [128, ACOLS], F16))
            self.C16 = E(nc.sbuf_tensor("C16", [128, 18432], F16))
            self.WAR = E(nc.sbuf_tensor("WAR", [128, 16384], F16))
            self.SC = E(nc.sbuf_tensor("SCt", [128, 16], F32))
            self.MAGCOL = E(nc.sbuf_tensor("MAGCOL", [128, 1], F32))
            self.MAGNEG = E(nc.sbuf_tensor("MAGNEG", [128, 1], F32))
            self.EPSCOL = E(nc.sbuf_tensor("EPSCOL", [128, 1], F32))
            self.srow = E(nc.sbuf_tensor("srow", [2, 128], F32))
            self.gmaxr = E(nc.sbuf_tensor("gmaxr", [2, 16], F32))
            self.dervr = E(nc.sbuf_tensor("dervr", [1, 16], F32))
            self.id16 = E(nc.sbuf_tensor("id16", [128, 128], F16))
            self.ones16 = E(nc.sbuf_tensor("ones16", [128, 1], F16))
            self.onesr16 = E(nc.sbuf_tensor("onesr16", [1, 128], F16))
            self.g1r = E(nc.sbuf_tensor("g1r", [128, D], F16))
            self.g2r = E(nc.sbuf_tensor("g2r", [128, D], F16))
            self.b1r = E(nc.sbuf_tensor("b1r", [128, D], F16))
            self.b2r = E(nc.sbuf_tensor("b2r", [128, D], F16))
            self.NLN = 4
            mkln = lambda p: {k: E(nc.sbuf_tensor(p + k, [128, self.NLN], F32))
                              for k in ["sum", "sq", "mu", "mu2", "vpe",
                                        "std", "rstd"]}
            self.LNa = mkln("lna")
            self.LNb = mkln("lnb")
            self.rmaxA = E(nc.sbuf_tensor("rmaxA", [128, 1], F32))
            self.rmaxB = E(nc.sbuf_tensor("rmaxB", [128, 1], F32))
            self.tmax = E(nc.sbuf_tensor("tmax", [128, 8], F32))

            self.pd = [E(nc.psum_tensor(f"pd{i}", [128, 1024], F32))
                       for i in range(3)]
            self.pb = [E(nc.psum_tensor(f"pb{i}", [128, 512], F32))
                       for i in range(2)]

            for s in ["dL", "dSP", "dW", "dG", "dE", "dX", "dO", "dY", "dZ",
                      "dI",
                      "msv", "mss", "mst", "msg", "msy",
                      "bar", "cc", "kvb", "lnv", "lns", "wtg",
                      "fua", "fub", "fuc", "fud", "fue", "fuf",
                      "pha", "phb", "phc", "phd", "phe", "phf", "phg"]:
                self.mksem(s)
            for name, R in [("p0l", 2), ("p0s", 2), ("yl", 4), ("ugw", 4),
                            ("ugs", 2), ("e1i", 3), ("e1o", 2), ("e2i", 3),
                            ("e2o", 2), ("dqr", 6), ("dqo", 2), ("dxl", 4),
                            ("dxs", 2), ("dys", 2), ("ucc", 2),
                            ("ylg", 2)]:
                self.mkring(name, R)

            self.blk = E(nc.Block())
            self.emit_all()
        return nc

    def ln_v_pre(self, v, k, LNS, xin):
        v.tensor_reduce(out=LNS["sum"][:, k:k + 1], in_=xin, axis=AX.X,
                        op=OP.add)
        v.drain()
        v.tensor_scalar(out=LNS["mu"][:, k:k + 1], in0=LNS["sum"][:, k:k + 1],
                        scalar1=1.0 / D, scalar2=None, op0=OP.mult)
        return self.inc("lnv", v.drain())

    def ln_s(self, s, k, LNS, xin, sqscr, lnv_val):
        s.wait_ge(self.sems["lnv"], lnv_val)
        s.activation(out=sqscr, in_=xin, func=ACT.Square,
                     accum_out=LNS["sq"][:, k:k + 1])
        s.drain()
        s.activation(out=LNS["mu2"][:, k:k + 1], in_=LNS["sum"][:, k:k + 1],
                     func=ACT.Square, scale=1.0 / D)
        s.drain()
        s.activation(out=LNS["vpe"][:, k:k + 1], in_=LNS["mu2"][:, k:k + 1],
                     func=ACT.Identity, scale=-1.0, bias=self.EPSCOL[:, 0:1])
        s.drain()
        s.activation(out=LNS["vpe"][:, k:k + 1], in_=LNS["sq"][:, k:k + 1],
                     func=ACT.Identity, scale=1.0 / D,
                     bias=LNS["vpe"][:, k:k + 1])
        s.drain()
        s.activation(out=LNS["std"][:, k:k + 1], in_=LNS["vpe"][:, k:k + 1],
                     func=ACT.Sqrt)
        return self.inc("lns", s.drain())

    def ln_v_apply(self, v, k, LNS, xin, out_ap, lns_val):
        v.wait_ge(self.sems["lns"], lns_val)
        v.reciprocal(out=LNS["rstd"][:, k:k + 1], in_=LNS["std"][:, k:k + 1])
        v.drain()
        return v.tensor_scalar(out=out_ap, in0=xin,
                               scalar1=LNS["mu"][:, k:k + 1],
                               scalar2=LNS["rstd"][:, k:k + 1],
                               op0=OP.subtract, op1=OP.mult)

    def absfold(self, v, tile, slot, rmax):
        ts = self.tmax[:, slot % 8:slot % 8 + 1]
        v.tensor_reduce(out=ts, in_=tile, axis=AX.X, op=OP.max,
                        apply_absolute_value=True)
        v.drain()
        v.tensor_tensor(out=rmax[:], in0=rmax[:], in1=ts, op=OP.max)
        v.drain()

    def emit_all(self):
        self.p0()
        self.phase_y(self.dx0, split_loads=False)
        self.barrier([(self.rmaxA, [(SC_RY, "rinv"), (SC_SY, "s")])])
        self.phase_qy()
        self.phase_ug_gate()
        self.barrier([(self.rmaxB, [(SC_RG, "rinv"), (SC_SG, "s"),
                                    (SC_BG, "bg")])])
        self.phase_ug_up_e1()
        self.barrier([(self.rmaxB, [(SC_RU, "rinv"), (SC_SU, "s")]),
                      (self.rmaxA, [(SC_R2, "rinv"), (SC_S2, "s")])])
        self.phase_e2()
        self.barrier([(self.rmaxA, [(SC_RO, "rinv"), (SC_SO, "s"),
                                    (SC_HSO, "hs")])])
        self.phase_d(self.dx0, self.dxx1, fuse_ln=True)
        self.phase_q()
        self.phase_kvx()
        self.phase_m1()
        self.phase_m2a()
        self.phase_m2b()
        self.phase_m2c()
        self.phase_y(self.dxx2)
        self.barrier([(self.rmaxA, [(SC_RY, "rinv"), (SC_SY, "s")])])
        self.phase_qy()
        self.phase_ug_gate()
        self.barrier([(self.rmaxB, [(SC_RG, "rinv"), (SC_SG, "s"),
                                    (SC_BG, "bg")])])
        self.phase_ug_up_e1()
        self.barrier([(self.rmaxB, [(SC_RU, "rinv"), (SC_SU, "s")]),
                      (self.rmaxA, [(SC_R2, "rinv"), (SC_S2, "s")])])
        self.phase_e2()
        self.barrier([(self.rmaxA, [(SC_RO, "rinv"), (SC_SO, "s"),
                                    (SC_HSO, "hs")])])
        self.phase_d(self.dxx2, self.dout, fuse_ln=False)
        self.finale()

    def p0(self):
        mk = self
        nun = 3

        def fg(g):
            g.dma_start(out=mk.did16, in_=mk.dident).then_inc(mk.sems["dE"], 16)
            mk.cnt["dE"] += 16
            for (srcw, dst) in [(mk.dwq, mk.dwq16), (mk.dwm, mk.dwm16),
                                (mk.dw1, mk.dw116), (mk.dw2, mk.dw216)]:
                g.dma_start(out=dst, in_=srcw).then_inc(mk.sems["dL"], 16)
                mk.cnt["dL"] += 16
            for (srcw, dst) in [(mk.dg1, mk.g1r), (mk.db1, mk.b1r),
                                (mk.dg2, mk.g2r), (mk.db2, mk.b2r)]:
                g.dma_start(out=dst[:], in_=srcw.partition_broadcast(128))\
                    .then_inc(mk.sems["dL"], 16)
                mk.cnt["dL"] += 16
            mk.inc("msg", g.nop())
            mk.w(g, "dE")
            g.dma_start(out=mk.id16[:], in_=mk.did16).then_inc(mk.sems["dI"], 16)
            mk.cnt["dI"] += 16
            g.wait_ge(mk.sems["phe"], 1)
            ic = mk.inc("cc", g.collective_compute(
                "AllGather", OP.bypass,
                replica_groups=[list(range(NCORES))],
                ins=[mk.dwall_sh.opt()], outs=[mk.dwall.opt()]))
            g.wait_ge(mk.sems["cc"], ic)
            mk.inc("wtg", g.drain())

        def fsync(sync):
            for i in range(nun):
                if i >= 2:
                    sync.wait_ge(mk.sems["pha"], i - 1)
                dst = mk.f32(mk.C16, (i % 2) * (2 * DFF), DFF)
                mk.rissue("p0l", i, 2, sync.dma_start(
                    out=dst, in_=mk.dwsh[i * 128:(i + 1) * 128, :]))
            for i in range(nun):
                sync.wait_ge(mk.sems["phb"], 6 * (i + 1))
                stg = mk.A16[:, (i % 2) * 3072:(i % 2) * 3072 + 3072]
                mk.rissue("p0s", i, 2, sync.dma_start(
                    out=mk.dwall_sh[i].rearrange("b p o -> p b o"),
                    in_=stg.rearrange("p (b o) -> p b o", o=128)))
            mk.rtotal(sync, "p0s", 2)
            mk.rtotal(sync, "p0l", 2)
            ins = sync.nop()
            mk.inc("phe", ins)
            mk.inc("msy", sync.nop())

        def fv(v):
            v.memset(mk.MAGCOL[:], MAGIC)
            v.memset(mk.MAGNEG[:], -MAGIC)
            v.memset(mk.EPSCOL[:], LN_EPS)
            v.memset(mk.ones16[:], 1.0)
            v.memset(mk.onesr16[:], 1.0)
            v.memset(mk.dervr[:], 1.0)
            v.memset(mk.gmaxr[:], 1.0)
            v.drain()
            p0lsnap = [0, 0]
            for i in range(nun):
                nb = DFF // 32
                mk.rwait(v, "p0l", i, 2, p0lsnap, 16)
                if i >= 2:
                    v.wait_ge(mk.sems["phd"], 6 * (i - 1))
                wch = mk.f32(mk.C16, (i % 2) * (2 * DFF), DFF)
                bm = mk.f32(mk.C16, 12288, 96)
                v.tensor_reduce(out=bm,
                                in_=wch.rearrange("p (a b) -> p a b", b=32),
                                axis=AX.X, op=OP.max, apply_absolute_value=True)
                v.drain()
                sb = mk.f32(mk.C16, 12480, 96)
                v.tensor_scalar(out=sb, in0=bm, scalar1=1.0 / QW,
                                scalar2=1e-12, op0=OP.mult, op1=OP.max)
                v.drain()
                rs = mk.f32(mk.C16, 12672, 96)
                v.reciprocal(out=rs, in_=sb)
                v.drain()
                t1 = mk.f32(mk.WAR, 0, DFF)
                v.tensor_tensor(out=t1.rearrange("p (a b) -> p a b", b=32),
                                in0=wch.rearrange("p (a b) -> p a b", b=32),
                                in1=rs.unsqueeze(2).broadcast_to([128, 96, 32]),
                                op=OP.mult)
                v.drain()
                v.tensor_scalar(out=t1, in0=t1, scalar1=MAGIC, scalar2=MAGIC,
                                op0=OP.add, op1=OP.subtract)
                v.drain()
                wq16 = mk.WAR[:, 8192 + (i % 2) * DFF:
                              8192 + (i % 2) * DFF + DFF]
                v.tensor_tensor(
                    out=wq16.rearrange("p (a b) -> p a b", b=32),
                    in0=t1.rearrange("p (a b) -> p a b", b=32),
                    in1=sb.unsqueeze(2).broadcast_to([128, 96, 32]),
                    op=OP.mult)
                mk.inc("pha", v.drain())
            mk.inc("msv", v.drain())

        def ft(t):
            t.wait_ge(mk.sems["dI"], 16)
            for i in range(nun):
                t.wait_ge(mk.sems["pha"], i + 1)
                srcw = mk.WAR[:, 8192 + (i % 2) * DFF:
                              8192 + (i % 2) * DFF + DFF]
                for gr in range(6):
                    gg = i * 6 + gr
                    if gg >= 2:
                        t.wait_ge(mk.sems["phb"], gg - 1)
                    last = None
                    for r in range(4):
                        rr = gr * 4 + r
                        last = t.transpose(
                            mk.pbh(gg % 2)[:, r * 128:(r + 1) * 128],
                            srcw[:, rr * 128:(rr + 1) * 128], mk.id16[:])
                    mk.inc("phd", last)
            mk.inc("mst", t.drain())

        def fs(s):
            for i in range(nun):
                for gr in range(6):
                    gg = i * 6 + gr
                    s.wait_ge(mk.sems["phd"], gg + 1)
                    if i >= 2 and gr == 0:
                        slot = i % 2
                        prev = 16 * sum(1 for k in range(i - 1)
                                        if k % 2 == slot)
                        s.wait_ge(mk.sems[f"p0s{slot}"], prev)
                    stg = mk.A16[:, (i % 2) * 3072 + gr * 512:
                                 (i % 2) * 3072 + gr * 512 + 512]
                    s.copy(out=stg, in_=mk.pbh(gg % 2)[:, 0:512])
                    mk.inc("phb", s.drain())
            mk.inc("mss", s.drain())

        self.blk.gpsimd(fg)
        self.blk.sync(fsync)
        self.blk.vector(fv)
        self.blk.tensor(ft)
        self.blk.scalar(fs)
        self.dg_p0 = self.cnt["dL"]
        self.snap()

    def barrier(self, quants):
        mk = self
        nq = len(quants)
        phb0 = self.cnt["phc"]

        def fg(g):
            mk.entry(g, "g")
            for qi, (rm, _) in enumerate(quants):
                g.dma_start(out=mk.ddcol[:, qi:qi + 1], in_=rm[:],
                            allow_slow_non_contiguous=True)\
                    .then_inc(mk.sems["dG"], 16)
                mk.cnt["dG"] += 16
            mk.w(g, "dG")
            g.dma_start(out=mk.srow[0:nq, :],
                        in_=mk.ddcol[:, 0:nq].rearrange("p q -> q p"),
                        allow_slow_non_contiguous=True)\
                .then_inc(mk.sems["dG"], 16)
            mk.cnt["dG"] += 16
            mk.w(g, "dG")
            mk.inc("phe", g.drain())
            g.wait_ge(mk.sems["phc"], phb0 + 1)
            g.dma_start(out=mk.dbm_in[0:1, 0:nq],
                        in_=mk.gmaxr[0:nq, 0:1]).then_inc(mk.sems["dG"], 16)
            mk.cnt["dG"] += 16
            mk.w(g, "dG")
            ic = mk.inc("cc", g.collective_compute(
                "AllReduce", OP.max,
                replica_groups=[list(range(NCORES))],
                ins=[mk.dbm_in[0:1, 0:nq].opt()],
                outs=[mk.dbm_out[0:1, 0:nq].opt()]))
            g.wait_ge(mk.sems["cc"], ic)
            g.dma_start(out=mk.gmaxr[0:1, 8:8 + nq],
                        in_=mk.dbm_out[0:1, 0:nq]).then_inc(mk.sems["dG"], 16)
            mk.cnt["dG"] += 16
            mk.w(g, "dG")
            mk.inc("phe", g.drain())
            g.wait_ge(mk.sems["phc"], phb0 + 2)
            g.dma_start(out=mk.dbscal[0:1, :], in_=mk.dervr[0:1, :])\
                .then_inc(mk.sems["dG"], 16)
            mk.cnt["dG"] += 16
            mk.w(g, "dG")
            g.dma_start(out=mk.SC[:, :],
                        in_=mk.dbscal.partition_broadcast(128))\
                .then_inc(mk.sems["dG"], 16)
            mk.cnt["dG"] += 16
            mk.w(g, "dG")
            mk.inc("bar", g.drain())
            mk.inc("msg", g.drain())

        phe0 = self.cnt["phe"]

        def fv(v):
            mk.entry(v, "v")
            v.wait_ge(mk.sems["phe"], phe0 + 1)
            v.tensor_reduce(out=mk.gmaxr[0:nq, 0:1], in_=mk.srow[0:nq, :],
                            axis=AX.X, op=OP.max)
            v.drain()
            mk.inc("phc", v.drain())
            v.wait_ge(mk.sems["phe"], phe0 + 2)
            for qi, (_, outs) in enumerate(quants):
                mcol = mk.gmaxr[0:1, 8 + qi:8 + qi + 1]
                scol = mk.dervr[0:1, 14:15]
                v.tensor_scalar(out=scol, in0=mcol, scalar1=1.0 / QA,
                                scalar2=1e-12, op0=OP.mult, op1=OP.max)
                v.drain()
                for colx, kind in outs:
                    if kind == "s":
                        v.tensor_scalar(out=mk.dervr[0:1, colx:colx + 1],
                                        in0=scol, scalar1=1.0, scalar2=None,
                                        op0=OP.mult)
                    elif kind == "rinv":
                        v.reciprocal(out=mk.dervr[0:1, colx:colx + 1],
                                     in_=scol)
                    elif kind == "bg":
                        v.tensor_scalar(out=mk.dervr[0:1, colx:colx + 1],
                                        in0=scol, scalar1=-MAGIC,
                                        scalar2=None, op0=OP.mult)
                    elif kind == "hs":
                        v.tensor_scalar(out=mk.dervr[0:1, colx:colx + 1],
                                        in0=scol, scalar1=0.5,
                                        scalar2=None, op0=OP.mult)
                    v.drain()
            mk.inc("phc", v.drain())
            mk.inc("msv", v.drain())

        self.blk.gpsimd(fg)
        self.blk.vector(fv)
        self.snap()

    def phase_y(self, src_dram, split_loads=True):
        mk = self
        nch = self.nch
        T = self.T
        ylsnap = self.rsnap("yl", 4)
        ylgsnap = self.rsnap("ylg", 2)
        lnv0 = self.cnt["lnv"]
        lns0 = self.cnt["lns"]
        pha0 = self.cnt["pha"]
        phb0 = self.cnt["phb"]
        phc0 = self.cnt["phc"]

        step = 2 if split_loads else 1

        def fload(sync):
            mk.entry(sync, "y")
            for ch in range(0, nch, step):
                if ch >= 4:
                    sync.wait_ge(mk.sems["pha"], pha0 + ch - 3)
                    sync.wait_ge(mk.sems["lns"], lns0 + ch - 3)
                xa = mk.f32(mk.B16, (ch % 4) * 2048, 1024)
                mk.rissue("yl", ch if not split_loads else ch // 2,
                          4 if not split_loads else 2, sync.dma_start(
                              out=xa,
                              in_=src_dram[ch * 128:(ch + 1) * 128, :]))
            mk.inc("msy", sync.nop())

        def fgl(g):
            mk.entry(g, "g")
            for ch in range(1, nch, 2):
                if ch >= 4:
                    g.wait_ge(mk.sems["pha"], pha0 + ch - 3)
                    g.wait_ge(mk.sems["lns"], lns0 + ch - 3)
                xa = mk.f32(mk.B16, (ch % 4) * 2048, 1024)
                mk.rissue("ylg", ch // 2, 2, g.dma_start(
                    out=xa, in_=src_dram[ch * 128:(ch + 1) * 128, :]))
            mk.inc("msg", g.drain())

        def ld_rwait(eng, ch, snp, snpg):
            if not split_loads:
                mk.rwait(eng, "yl", ch, 4, snp, 16)
            elif ch % 2 == 0:
                mk.rwait(eng, "yl", ch // 2, 2, snp, 16)
            else:
                mk.rwait(eng, "ylg", ch // 2, 2, snpg, 16)

        def fv(v):
            mk.entry(v, "v")
            v.memset(mk.rmaxA[:], 0.0)
            v.drain()
            ld_rwait(v, 0, ylsnap, ylgsnap)
            mk.ln_v_pre(v, 0, mk.LNa, mk.f32(mk.B16, 0, 1024))
            for ch in range(nch):
                if ch + 1 < nch:
                    ld_rwait(v, ch + 1, ylsnap, ylgsnap)
                    xan = mk.f32(mk.B16, ((ch + 1) % 4) * 2048, 1024)
                    mk.ln_v_pre(v, (ch + 1) % mk.NLN, mk.LNa, xan)
                xa = mk.f32(mk.B16, (ch % 4) * 2048, 1024)
                k = ch % mk.NLN
                y16 = mk.B16[:, 8192 + (ch % 2) * 1024:
                             8192 + (ch % 2) * 1024 + 1024]
                if ch >= 2:
                    v.wait_ge(mk.sems["phb"], phb0 + ch - 1)
                mk.ln_v_apply(v, k, mk.LNa, xa, y16, lns0 + ch + 1)
                mk.inc("pha", v.drain())
                mk.absfold(v, y16, ch, mk.rmaxA)
            mk.inc("msv", v.drain())

        def fs(s):
            mk.entry(s, "s")
            for ch in range(nch):
                ld_rwait(s, ch, ylsnap, ylgsnap)
                xa = mk.f32(mk.B16, (ch % 4) * 2048, 1024)
                sqscr = mk.f32(mk.B16, 10240 + (ch % 2) * 2048, 1024)
                mk.ln_s(s, ch % mk.NLN, mk.LNa, xa, sqscr, lnv0 + ch + 1)
                if ch >= 2:
                    self._y_evac(s, ch - 2, phb0)
            for ch in (nch - 2, nch - 1):
                self._y_evac(s, ch, phb0)
            mk.inc("mss", s.drain())

        def ft(t):
            mk.entry(t, "t")
            for ch in range(nch):
                t.wait_ge(mk.sems["pha"], pha0 + ch + 1)
                if ch >= 2:
                    t.wait_ge(mk.sems["phc"], phc0 + ch - 1)
                y16 = mk.B16[:, 8192 + (ch % 2) * 1024:
                             8192 + (ch % 2) * 1024 + 1024]
                last = None
                for hc in range(HC):
                    last = t.transpose(
                        mk.pdh(ch % 2, 0)[:, hc * 128:(hc + 1) * 128],
                        y16[:, hc * 128:(hc + 1) * 128], mk.id16[:])
                mk.inc("phb", last)
            mk.inc("mst", t.drain())

        self.blk.sync(fload)
        if split_loads:
            self.blk.gpsimd(fgl)
        self.blk.vector(fv)
        self.blk.tensor(ft)
        self.blk.scalar(fs)
        self.snap()

    def _y_evac(self, s, ch, phb0):
        mk = self
        s.wait_ge(mk.sems["phb"], phb0 + ch + 1)
        dst = mk.A16[:, 0:HC * mk.T].rearrange(
            "p (h t) -> p h t", h=HC)[:, :, ch * 128:(ch + 1) * 128]
        s.copy(out=dst, in_=mk.pdh(ch % 2, 0).rearrange(
            "p (h t) -> p h t", h=HC))
        mk.inc("phc", s.drain())

    def phase_qy(self):
        mk = self
        nsl = (HC * self.T) // 2048
        phb0 = self.cnt["phb"]
        bar0 = self.cnt["bar"]

        def fs(s):
            mk.entry(s, "s")
            s.wait_ge(mk.sems["bar"], bar0)
            for j in range(nsl):
                if j >= 2:
                    s.wait_ge(mk.sems["phb"], phb0 + j - 1)
                ap = mk.f32(mk.C16, (j % 2) * 4096, 2048)
                s.activation(out=ap, in_=mk.A16[:, j * 2048:(j + 1) * 2048],
                             func=ACT.Identity, scale=mk.SC[:, SC_RY:SC_RY + 1],
                             bias=mk.MAGCOL[:, 0:1])
                mk.inc("pha", s.drain())
            mk.inc("mss", s.drain())

        pha0 = self.cnt["pha"]

        def fv(v):
            mk.entry(v, "v")
            for j in range(nsl):
                v.wait_ge(mk.sems["pha"], pha0 + j + 1)
                ap = mk.f32(mk.C16, (j % 2) * 4096, 2048)
                v.tensor_scalar(out=mk.B16[:, j * 2048:(j + 1) * 2048],
                                in0=ap, scalar1=MAGIC,
                                scalar2=mk.SC[:, SC_SY:SC_SY + 1],
                                op0=OP.subtract, op1=OP.mult)
                mk.inc("phb", v.drain())
            mk.inc("msv", v.drain())

        self.blk.scalar(fs)
        self.blk.vector(fv)
        self.snap()

    def phase_ug_gate(self):
        mk = self
        T, ntt = self.T, self.ntt
        nhalf = 2 if ntt >= 2 else 1
        tph = ntt // nhalf
        stw = tph * 512
        self._ug_nhalf, self._ug_tph, self._ug_stw = nhalf, tph, stw
        groups = [(oc, hf) for oc in range(OCF) for hf in range(nhalf)]
        ng = len(groups)
        banksets = [
            [self.pd[0][:, 0:512], self.pd[0][:, 512:1024],
             self.pd[1][:, 0:512], self.pd[1][:, 512:1024]][:tph],
            [self.pd[2][:, 0:512], self.pd[2][:, 512:1024],
             self.pb[0][:, :], self.pb[1][:, :]][:tph],
        ]
        self._ug_banksets = banksets
        ugwsnap = self.rsnap("ugw", 4)
        ugssnap = self.rsnap("ugs", 2)
        pha0 = self.cnt["pha"]
        phb0 = self.cnt["phb"]
        phc0 = self.cnt["phc"]
        bar0 = self.cnt["bar"]

        def fw(sync):
            mk.entry(sync, "y")
            sync.wait_ge(mk.sems["wtg"], 1)

            def loadslice(oc):
                if oc >= 2:
                    sync.wait_ge(mk.sems["pha"],
                                 pha0 + (oc - 2 + 1) * nhalf)
                dst = mk.A16[:, (oc % 2) * 1024:(oc % 2) * 1024 + 1024]
                unit = HC + oc // 3
                sub = oc % 3
                mk.rissue("ugw", oc % 2, 4, sync.dma_start(
                    out=dst.rearrange("p (a o) -> p a o", a=HC),
                    in_=mk.dwall[unit, sub * 8:(sub + 1) * 8]
                    .rearrange("a p o -> p a o")))

            def spill(gi):
                oc, hf = groups[gi]
                sync.wait_ge(mk.sems["phb"], phb0 + gi + 1)
                stg = mk.C16[:, 12288 + (gi % 2) * stw:
                             12288 + (gi % 2) * stw + stw]
                mk.rissue("ugs", gi, 2, sync.dma_start(
                    out=mk.dgatesp[oc * 128:(oc + 1) * 128,
                                   hf * stw:(hf + 1) * stw], in_=stg))

            for oc in range(2):
                loadslice(oc)
            for gi in range(ng):
                oc = gi // nhalf
                if gi % nhalf == 0 and oc + 2 < OCF:
                    loadslice(oc + 2)
                if gi >= 1:
                    spill(gi - 1)
            spill(ng - 1)
            mk.inc("msy", sync.nop())

        def ft(t):
            mk.entry(t, "t")
            t.wait_ge(mk.sems["bar"], bar0)
            for gi, (oc, hf) in enumerate(groups):
                mk.rwait(t, "ugw", oc % 2, 4, ugwsnap, 16, occ=oc // 2)
                if gi >= 2:
                    t.wait_ge(mk.sems["phb"], phb0 + gi - 1)
                wb = mk.A16[:, (oc % 2) * 1024:(oc % 2) * 1024 + 1024]
                bs = banksets[gi % 2]
                last = None
                for hc in range(HC):
                    for j in range(tph):
                        tt = hf * tph + j
                        last = t.matmul(
                            bs[j],
                            lhsT=wb[:, hc * 128:(hc + 1) * 128],
                            rhs=mk.B16[:, hc * T + tt * 512:
                                       hc * T + tt * 512 + 512],
                            start=(hc == 0), stop=(hc == HC - 1))
                mk.inc("pha", last)
            mk.inc("mst", t.drain())

        def fs(s):
            mk.entry(s, "s")
            for gi in range(ng):
                s.wait_ge(mk.sems["pha"], pha0 + gi + 1)
                if gi >= 2:
                    mk.rwait(s, "ugs", gi, 2, ugssnap, 16, occ=gi // 2 - 1)
                    s.wait_ge(mk.sems["phc"], phc0 + gi - 1)
                stg = mk.C16[:, 12288 + (gi % 2) * stw:
                             12288 + (gi % 2) * stw + stw]
                bs = banksets[gi % 2]
                last = None
                for j in range(tph):
                    last = s.copy(out=stg[:, j * 512:(j + 1) * 512], in_=bs[j])
                mk.inc("phb", last)
            mk.inc("mss", s.drain())

        def fv(v):
            mk.entry(v, "v")
            v.memset(mk.rmaxB[:], 0.0)
            v.drain()
            for gi in range(ng):
                v.wait_ge(mk.sems["phb"], phb0 + gi + 1)
                stg = mk.C16[:, 12288 + (gi % 2) * stw:
                             12288 + (gi % 2) * stw + stw]
                ts = mk.tmax[:, gi % 8:gi % 8 + 1]
                v.tensor_reduce(out=ts, in_=stg, axis=AX.X, op=OP.max,
                                apply_absolute_value=True)
                v.drain()
                v.tensor_tensor(out=mk.rmaxB[:], in0=mk.rmaxB[:], in1=ts,
                                op=OP.max)
                mk.inc("phc", v.drain())
            mk.inc("msv", v.drain())

        self.blk.sync(fw)
        self.blk.tensor(ft)
        self.blk.scalar(fs)
        self.blk.vector(fv)
        self._ug_gate_phb_end = self.cnt["phb"]
        self.snap()

    def phase_ug_up_e1(self):
        mk = self
        T, ntt, TE = self.T, self.ntt, self.TE
        nhalf, tph, stw = self._ug_nhalf, self._ug_tph, self._ug_stw
        groups = [(oc, hf) for oc in range(OCF) for hf in range(nhalf)]
        ng = len(groups)
        nsl = (DFF // 128) * (T // TE)
        assert nsl == ng, (nsl, ng)
        npc = T // TE
        banksets = self._ug_banksets
        gphb_end = self._ug_gate_phb_end
        ugwsnap = self.rsnap("ugw", 4)
        ugssnap = self.rsnap("ugs", 2)
        gatesnap = [ugssnap[0] - 16 * (ng // 2), ugssnap[1] - 16 * (ng // 2)]
        e1isnap = self.rsnap("e1i", 3)
        e1osnap = self.rsnap("e1o", 2)
        fua0 = self.cnt["fua"]
        fub0 = self.cnt["fub"]
        fuc0 = self.cnt["fuc"]
        fud0 = self.cnt["fud"]
        fue0 = self.cnt["fue"]
        fuf0 = self.cnt["fuf"]
        bar0 = self.cnt["bar"]
        EB = 4096

        def pos(j):
            pc, jj = divmod(j, npc)
            return pc * 128, jj * TE

        def fw(sync):
            mk.entry(sync, "y")

            def loadslice(oc):
                if oc >= 2:
                    sync.wait_ge(mk.sems["fua"],
                                 fua0 + (oc - 2 + 1) * nhalf)
                dst = mk.A16[:, 2048 + (oc % 2) * 1024:
                             2048 + (oc % 2) * 1024 + 1024]
                mk.rissue("ugw", 2 + oc % 2, 4, sync.dma_start(
                    out=dst.rearrange("p (a o) -> p a o", a=HC),
                    in_=mk.dwall[oc // 3, (oc % 3) * 8:(oc % 3) * 8 + 8]
                    .rearrange("a p o -> p a o")))

            def upspill(gi):
                oc, hf = groups[gi]
                sync.wait_ge(mk.sems["fub"], fub0 + gi + 1)
                stg = mk.C16[:, 12288 + (gi % 2) * stw:
                             12288 + (gi % 2) * stw + stw]
                mk.rissue("ugs", gi, 2, sync.dma_start(
                    out=mk.dupsp[oc * 128:(oc + 1) * 128,
                                 hf * stw:(hf + 1) * stw], in_=stg))

            def e1load(j):
                if j >= 3:
                    sync.wait_ge(mk.sems["fud"], fud0 + j - 2)
                r0, c0 = pos(j)
                mk.rwait(sync, "ugs", j, 2, gatesnap, 16, occ=j // 2)
                gin = mk.C16[:, (j % 3) * TE:(j % 3) * TE + TE]
                mk.rissue("e1i", j, 3, sync.dma_start(
                    out=gin, in_=mk.dgatesp[r0:r0 + 128, c0:c0 + TE]))

            def e1spill(j):
                sync.wait_ge(mk.sems["fue"], fue0 + j + 1)
                r0, c0 = pos(j)
                gout = mk.C16[:, 3 * TE + (j % 2) * TE:
                              3 * TE + (j % 2) * TE + TE]
                mk.rissue("e1o", j, 2, sync.dma_start(
                    out=mk.dg2sp[r0:r0 + 128, c0:c0 + TE], in_=gout))

            for oc in range(2):
                loadslice(oc)
            e1load(0)
            e1load(1)
            for gi in range(ng):
                oc = gi // nhalf
                if gi % nhalf == 0 and oc + 2 < OCF:
                    loadslice(oc + 2)
                if gi + 2 < nsl:
                    e1load(gi + 2)
                upspill(gi)
                if gi >= 1:
                    e1spill(gi - 1)
            e1spill(nsl - 1)
            mk.rtotal(sync, "ugs", 2)
            mk.rtotal(sync, "e1o", 2)
            mk.inc("msy", sync.nop())

        def ft(t):
            mk.entry(t, "t")
            for gi, (oc, hf) in enumerate(groups):
                mk.rwait(t, "ugw", 2 + oc % 2, 4, ugwsnap, 16, occ=oc // 2)
                if gi < 2:
                    t.wait_ge(mk.sems["phb"], gphb_end)
                else:
                    t.wait_ge(mk.sems["fub"], fub0 + gi - 1)
                wb = mk.A16[:, 2048 + (oc % 2) * 1024:
                            2048 + (oc % 2) * 1024 + 1024]
                bs = banksets[gi % 2]
                last = None
                for hc in range(HC):
                    for j in range(tph):
                        tt = hf * tph + j
                        last = t.matmul(
                            bs[j],
                            lhsT=wb[:, hc * 128:(hc + 1) * 128],
                            rhs=mk.B16[:, hc * T + tt * 512:
                                       hc * T + tt * 512 + 512],
                            start=(hc == 0), stop=(hc == HC - 1))
                mk.inc("fua", last)
            mk.inc("mst", t.drain())

        def fs(s):
            mk.entry(s, "s")
            s.wait_ge(mk.sems["bar"], bar0)
            for j in range(ng):
                mk.rwait(s, "e1i", j, 3, e1isnap, 16)
                if j >= 2:
                    s.wait_ge(mk.sems["fuf"], fuf0 + 3 * (j - 2) + 1)
                gin = mk.C16[:, (j % 3) * TE:(j % 3) * TE + TE]
                ap = mk.f32(mk.A16, EB + (j % 2) * 2 * TE, TE)
                s.activation(out=ap, in_=gin, func=ACT.Identity,
                             scale=mk.SC[:, SC_RG:SC_RG + 1],
                             bias=mk.MAGCOL[:, 0:1])
                mk.inc("fud", s.drain())
                sg = mk.A16[:, EB + 4 * TE + (j % 2) * TE:
                            EB + 4 * TE + (j % 2) * TE + TE]
                if j >= 2:
                    s.wait_ge(mk.sems["fuf"], fuf0 + 3 * (j - 2) + 2)
                s.activation(out=sg, in_=ap, func=ACT.Sigmoid,
                             scale=mk.SC[:, SC_SG:SC_SG + 1],
                             bias=mk.SC[:, SC_BG:SC_BG + 1])
                mk.inc("fuc", s.drain())
                s.wait_ge(mk.sems["fua"], fua0 + j + 1)
                if j >= 2:
                    mk.rwait(s, "ugs", j, 2, ugssnap, 16, occ=j // 2 - 1)
                    s.wait_ge(mk.sems["fue"], fue0 + j - 1)
                else:
                    s.wait_ge(mk.sems[f"ugs{j % 2}"], ugssnap[j % 2])
                stg = mk.C16[:, 12288 + (j % 2) * stw:
                             12288 + (j % 2) * stw + stw]
                bs = banksets[j % 2]
                last = None
                for jj in range(tph):
                    last = s.copy(out=stg[:, jj * 512:(jj + 1) * 512],
                                  in_=bs[jj])
                mk.inc("fub", last)
            mk.inc("mss", s.drain())

        def fv(v):
            mk.entry(v, "v")
            for k in range(8):
                v.memset(mk.WAR[:, k * 2048:(k + 1) * 2048],
                         -60000.0 if k < 2 or 4 <= k < 6 else 60000.0)
            v.drain()
            for j in range(ng):
                p = j % 2
                qg = mk.A16[:, EB + 12 * TE + (j % 2) * TE:
                            EB + 12 * TE + (j % 2) * TE + TE]
                v.wait_ge(mk.sems["fuf"], fuf0 + 3 * (j + 1))
                sigq = mk.A16[:, EB + 10 * TE + (j % 2) * TE:
                              EB + 10 * TE + (j % 2) * TE + TE]
                gout = mk.C16[:, 3 * TE + (j % 2) * TE:
                              3 * TE + (j % 2) * TE + TE]
                if j >= 2:
                    mk.rwait(v, "e1o", j, 2, e1osnap, 16, occ=j // 2 - 1)
                v.tensor_tensor(out=gout, in0=qg, in1=sigq, op=OP.mult)
                v.drain()
                v.wait_ge(mk.sems["fub"], fub0 + j + 1)
                stg = mk.C16[:, 12288 + (j % 2) * stw:
                             12288 + (j % 2) * stw + stw]
                gmx = mk.WAR[:, (4 + p) * 2048:(4 + p) * 2048 + TE]
                gmn = mk.WAR[:, (6 + p) * 2048:(6 + p) * 2048 + TE]
                umx = mk.WAR[:, p * 2048:p * 2048 + stw]
                umn = mk.WAR[:, (2 + p) * 2048:(2 + p) * 2048 + stw]
                v.tensor_tensor(out=gmx, in0=gmx, in1=gout, op=OP.max)
                v.tensor_tensor(out=gmn, in0=gmn, in1=gout, op=OP.min)
                v.tensor_tensor(out=umx, in0=umx, in1=stg, op=OP.max)
                v.tensor_tensor(out=umn, in0=umn, in1=stg, op=OP.min)
                mk.inc("fue", v.drain())
            for k in range(4):
                v.tensor_reduce(out=mk.tmax[:, 2 * k:2 * k + 1],
                                in_=mk.WAR[:, 2 * k * 2048:
                                           2 * k * 2048 + 2048],
                                axis=AX.X,
                                op=OP.max if k in (0, 2) else OP.min,
                                negate=(k in (1, 3)))
                v.tensor_reduce(out=mk.tmax[:, 2 * k + 1:2 * k + 2],
                                in_=mk.WAR[:, (2 * k + 1) * 2048:
                                           (2 * k + 1) * 2048 + 2048],
                                axis=AX.X,
                                op=OP.max if k in (0, 2) else OP.min,
                                negate=(k in (1, 3)))
            v.drain()
            v.tensor_tensor(out=mk.tmax[:, 0:1], in0=mk.tmax[:, 0:1],
                            in1=mk.tmax[:, 1:2], op=OP.max)
            v.tensor_tensor(out=mk.tmax[:, 2:3], in0=mk.tmax[:, 2:3],
                            in1=mk.tmax[:, 3:4], op=OP.max)
            v.tensor_tensor(out=mk.tmax[:, 4:5], in0=mk.tmax[:, 4:5],
                            in1=mk.tmax[:, 5:6], op=OP.max)
            v.tensor_tensor(out=mk.tmax[:, 6:7], in0=mk.tmax[:, 6:7],
                            in1=mk.tmax[:, 7:8], op=OP.max)
            v.drain()
            v.tensor_tensor(out=mk.rmaxB[:], in0=mk.tmax[:, 0:1],
                            in1=mk.tmax[:, 2:3], op=OP.max)
            v.tensor_tensor(out=mk.rmaxA[:], in0=mk.tmax[:, 4:5],
                            in1=mk.tmax[:, 6:7], op=OP.max)
            mk.inc("msv", v.drain())

        def fg(g):
            mk.entry(g, "g")
            for j in range(ng):
                g.wait_ge(mk.sems["fud"], fud0 + j + 1)
                if j >= 2:
                    g.wait_ge(mk.sems["fue"], fue0 + j - 1)
                ap = mk.f32(mk.A16, EB + (j % 2) * 2 * TE, TE)
                qg = mk.A16[:, EB + 12 * TE + (j % 2) * TE:
                            EB + 12 * TE + (j % 2) * TE + TE]
                g.tensor_scalar(out=qg, in0=ap, scalar1=-MAGIC,
                                scalar2=mk.SC[:, SC_SG:SC_SG + 1],
                                op0=OP.add, op1=OP.mult)
                mk.inc("fuf", g.drain())
                g.wait_ge(mk.sems["fuc"], fuc0 + j + 1)
                sg = mk.A16[:, EB + 4 * TE + (j % 2) * TE:
                            EB + 4 * TE + (j % 2) * TE + TE]
                tt_ = mk.f32(mk.A16, EB + 6 * TE + (j % 2) * 2 * TE, TE)
                g.tensor_scalar(out=tt_, in0=sg, scalar1=65536.0, scalar2=P23,
                                op0=OP.mult, op1=OP.add)
                mk.inc("fuf", g.drain())
                sigq = mk.A16[:, EB + 10 * TE + (j % 2) * TE:
                              EB + 10 * TE + (j % 2) * TE + TE]
                g.tensor_scalar(out=sigq, in0=tt_, scalar1=1.0 / 65536.0,
                                scalar2=-128.0, op0=OP.mult, op1=OP.add)
                mk.inc("fuf", g.drain())
            mk.inc("msg", g.drain())

        self.blk.sync(fw)
        self.blk.tensor(ft)
        self.blk.scalar(fs)
        self.blk.vector(fv)
        self.blk.gpsimd(fg)
        self.snap()

    def phase_e2(self):
        mk = self
        T, TE = self.T, self.TE
        nsl = (DFF // 128) * (T // TE)
        npc = T // TE
        e2isnap = self.rsnap("e2i", 3)
        e2osnap = self.rsnap("e2o", 2)
        uccsnap = self.rsnap("ucc", 2)
        pha0 = self.cnt["pha"]
        phb0 = self.cnt["phb"]
        bar0 = self.cnt["bar"]

        def pos(j):
            pc, jj = divmod(j, npc)
            return pc * 128, jj * TE

        def fld(sync):
            mk.entry(sync, "y")
            for j in range(nsl):
                if j >= 3:
                    sync.wait_ge(mk.sems["pha"], pha0 + 2 * (j - 2))
                r0, c0 = pos(j)
                g2in = mk.C16[:, (j % 3) * TE:(j % 3) * TE + TE]
                mk.rissue("e2i", j, 3, sync.dma_start(
                    out=g2in, in_=mk.dg2sp[r0:r0 + 128, c0:c0 + TE]))
            mk.inc("msy", sync.nop())

        def fs(s):
            mk.entry(s, "s")
            s.wait_ge(mk.sems["bar"], bar0)
            for j in range(2):
                r0, c0 = pos(j)
                uin = mk.C16[:, 3 * TE + (j % 3) * TE:
                             3 * TE + (j % 3) * TE + TE]
                mk.rissue("ucc", j, 2, s.dma_start(
                    out=uin, in_=mk.dupsp[r0:r0 + 128, c0:c0 + TE]))
            for j in range(nsl):
                mk.rwait(s, "e2i", j, 3, e2isnap, 16)
                mk.rwait(s, "ucc", j, 2, uccsnap, 16)
                if j >= 2:
                    s.wait_ge(mk.sems["phb"], phb0 + j - 1)
                g2in = mk.C16[:, (j % 3) * TE:(j % 3) * TE + TE]
                uin = mk.C16[:, 3 * TE + (j % 3) * TE:
                             3 * TE + (j % 3) * TE + TE]
                g2p = mk.f32(mk.A16, (j % 2) * 2 * TE, TE)
                s.activation(out=g2p, in_=g2in, func=ACT.Identity,
                             scale=mk.SC[:, SC_R2:SC_R2 + 1],
                             bias=mk.MAGCOL[:, 0:1])
                up = mk.f32(mk.A16, 4 * TE + (j % 2) * 2 * TE, TE)
                s.activation(out=up, in_=uin, func=ACT.Identity,
                             scale=mk.SC[:, SC_RU:SC_RU + 1],
                             bias=mk.MAGCOL[:, 0:1])
                mk.inc("pha", s.drain(), 2)
                if j + 2 < nsl:
                    jn = j + 2
                    r0, c0 = pos(jn)
                    uinn = mk.C16[:, 3 * TE + (jn % 3) * TE:
                                  3 * TE + (jn % 3) * TE + TE]
                    mk.rissue("ucc", jn, 2, s.dma_start(
                        out=uinn, in_=mk.dupsp[r0:r0 + 128, c0:c0 + TE]))
            mk.inc("mss", s.drain())

        phg0 = self.cnt["phg"]

        def spill2(g, j):
            g.wait_ge(mk.sems["phb"], phb0 + j + 1)
            r0, c0 = pos(j)
            oout = mk.B16[:, 4 * TE + (j % 2) * TE:4 * TE + (j % 2) * TE + TE]
            mk.rissue("e2o", j, 2, g.dma_start(
                out=mk.dosp[r0:r0 + 128, c0:c0 + TE], in_=oout))

        def fg(g):
            mk.entry(g, "g")
            for j in range(nsl):
                g.wait_ge(mk.sems["pha"], pha0 + 2 * (j + 1))
                if j >= 2:
                    g.wait_ge(mk.sems["phb"], phb0 + j - 1)
                up = mk.f32(mk.A16, 4 * TE + (j % 2) * 2 * TE, TE)
                uq = mk.B16[:, 2 * TE + (j % 2) * TE:2 * TE + (j % 2) * TE + TE]
                g.tensor_scalar(out=uq, in0=up, scalar1=-MAGIC,
                                scalar2=mk.SC[:, SC_SU:SC_SU + 1],
                                op0=OP.add, op1=OP.mult)
                mk.inc("phg", g.drain())
                if j >= 1:
                    spill2(g, j - 1)
            spill2(g, nsl - 1)
            mk.rtotal(g, "e2o", 2)
            mk.inc("msg", g.drain())

        def fv(v):
            mk.entry(v, "v")
            v.memset(mk.WAR[:, 8192:8192 + 2048], -60000.0)
            v.memset(mk.WAR[:, 10240:10240 + 2048], 60000.0)
            v.drain()
            for j in range(nsl):
                v.wait_ge(mk.sems["pha"], pha0 + 2 * (j + 1))
                g2p = mk.f32(mk.A16, (j % 2) * 2 * TE, TE)
                qg2 = mk.B16[:, (j % 2) * TE:(j % 2) * TE + TE]
                v.tensor_scalar(out=qg2, in0=g2p, scalar1=MAGIC,
                                scalar2=mk.SC[:, SC_S2:SC_S2 + 1],
                                op0=OP.subtract, op1=OP.mult)
                v.drain()
                v.wait_ge(mk.sems["phg"], phg0 + j + 1)
                uq = mk.B16[:, 2 * TE + (j % 2) * TE:2 * TE + (j % 2) * TE + TE]
                oout = mk.B16[:, 4 * TE + (j % 2) * TE:
                              4 * TE + (j % 2) * TE + TE]
                if j >= 2:
                    mk.rwait(v, "e2o", j, 2, e2osnap, 16, occ=j // 2 - 1)
                v.tensor_tensor(out=oout, in0=qg2, in1=uq, op=OP.mult)
                v.drain()
                v.tensor_tensor(out=mk.WAR[:, 8192:8192 + TE],
                                in0=mk.WAR[:, 8192:8192 + TE], in1=oout,
                                op=OP.max)
                v.tensor_tensor(out=mk.WAR[:, 10240:10240 + TE],
                                in0=mk.WAR[:, 10240:10240 + TE], in1=oout,
                                op=OP.min)
                mk.inc("phb", v.drain())
            v.tensor_reduce(out=mk.tmax[:, 0:1], in_=mk.WAR[:, 8192:8192 + 2048],
                            axis=AX.X, op=OP.max)
            v.tensor_reduce(out=mk.tmax[:, 1:2],
                            in_=mk.WAR[:, 10240:10240 + 2048],
                            axis=AX.X, op=OP.min, negate=True)
            v.drain()
            v.tensor_tensor(out=mk.rmaxA[:], in0=mk.tmax[:, 0:1],
                            in1=mk.tmax[:, 1:2], op=OP.max)
            mk.inc("msv", v.drain())

        self.blk.sync(fld)
        self.blk.scalar(fs)
        self.blk.gpsimd(fg)
        self.blk.vector(fv)
        self.snap()

    def phase_d(self, xin_dram, xout_dram, fuse_ln):
        mk = self
        T, nch = self.T, self.nch
        npair = nch // 2
        NW = T // 512
        NSLAB = NW * OCF
        QRING = 28
        ORING = 6
        bar0 = self.cnt["bar"]
        dw0 = self.cnt["dW"]
        dqrsnap = self.rsnap("dqr", ORING)
        dxlsnap = self.rsnap("dxl", 4)
        dxssnap = self.rsnap("dxs", 2)
        dyssnap = self.rsnap("dys", 2)
        e2o_fin = [self.cnt["e2o0"], self.cnt["e2o1"]]
        pha0 = self.cnt["pha"]
        phb0 = self.cnt["phb"]
        phg0 = self.cnt["phg"]
        phc0 = self.cnt["phc"]
        phd0 = self.cnt["phd"]
        phe0 = self.cnt["phe"]
        phf0 = self.cnt["phf"]
        lnv0 = self.cnt["lnv"]
        lns0 = self.cnt["lns"]
        gv = dict(self.gatev)

        def slab_wait(eng, g):
            w, hc = divmod(g, OCF)
            if hc == OCF - 1:
                eng.wait_ge(mk.sems["phc"], phc0 + 2 * w + 2)
            else:
                eng.wait_ge(mk.sems["phg"], phg0 + w * (OCF - 1) + hc + 1)

        def chunk_psum(ch):
            k = ch // 2
            if k % 2 == 0:
                p = mk.pd[ch % 2]
                return ([p[:, 0:512], p[:, 512:1024]], p[:, :],
                        mk.pdh(ch % 2, 0))
            if ch % 2 == 0:
                p = mk.pd[2]
                return ([p[:, 0:512], p[:, 512:1024]], p[:, :], mk.pdh(2, 0))
            return ([mk.pb[0][:, :], mk.pb[1][:, :]], None, mk.pbh(0))

        def xa_ap(ch):
            return mk.f32(mk.C16, (ch % 4) * 2048, 1024)

        def xo_ap(ch):
            return mk.f32(mk.C16, 8192 + (ch % 2) * 2048, 1024)

        def y2_ap(ch):
            return mk.C16[:, 12288 + (ch % 2) * 1024:
                          12288 + (ch % 2) * 1024 + 1024]

        def qon_ap(g):
            return mk.WAR[:, 2048 + (g % QRING) * 512:
                          2048 + (g % QRING) * 512 + 512]

        def oload(sync, g):
            w, hc = divmod(g, OCF)
            if g >= ORING:
                sync.wait_ge(mk.sems["pha"], pha0 + g - (ORING - 1))
            osl = mk.B16[:, 28672 + (g % ORING) * 512:
                         28672 + (g % ORING) * 512 + 512]
            mk.rissue("dqr", g, ORING, sync.dma_start(
                out=osl, in_=mk.dosp[hc * 128:(hc + 1) * 128,
                                     w * 512:(w + 1) * 512]))

        def fsy(sync):
            sync.wait_ge(mk.sems["wtg"], 1)
            for ic in range(12, 24):
                sync.dma_start(
                    out=mk.B16[:, ic * D:(ic + 1) * D].rearrange(
                        "p (u o) -> p u o", o=128),
                    in_=mk.dwall[2 * HC:3 * HC, ic].rearrange(
                        "u p o -> p u o")).then_inc(mk.sems["dW"], 16)
                mk.cnt["dW"] += 16
            for r in range(2):
                sync.wait_ge(mk.sems[f"e2o{r}"], e2o_fin[r])
            for g in range(ORING):
                oload(sync, g)
            sync.wait_ge(mk.sems["msv"], gv["v"])
            sync.wait_ge(mk.sems["msg"], gv["g"])
            sync.wait_ge(mk.sems["mss"], gv["s"])
            for ic in range(12):
                sync.dma_start(
                    out=mk.B16[:, ic * D:(ic + 1) * D].rearrange(
                        "p (u o) -> p u o", o=128),
                    in_=mk.dwall[2 * HC:3 * HC, ic].rearrange(
                        "u p o -> p u o")).then_inc(mk.sems["dW"], 16)
                mk.cnt["dW"] += 16

            def xa_issue(ch):
                if not 2 <= ch < nch:
                    return
                if ch >= 4:
                    sync.wait_ge(mk.sems["phf"], phf0 + ch - 3)
                mk.rissue("dxl", ch, 4, sync.dma_start(
                    out=xa_ap(ch), in_=xin_dram[ch * 128:(ch + 1) * 128, :]))

            for ch in (0, 1):
                mk.rissue("dxl", ch, 4, sync.dma_start(
                    out=xa_ap(ch), in_=xin_dram[ch * 128:(ch + 1) * 128, :]))
            for g in range(ORING, NSLAB):
                w, hc = divmod(g, OCF)
                oload(sync, g)
                if hc == 0:
                    xa_issue(4 * w - 2)
                elif hc == 2:
                    xa_issue(4 * w - 1)
                elif hc == 16:
                    xa_issue(4 * w)
                elif hc == 18:
                    xa_issue(4 * w + 1)
            xa_issue(nch - 2)
            xa_issue(nch - 1)
            mk.rtotal(sync, "dqr", ORING)
            mk.rtotal(sync, "dxl", 4)
            mk.inc("msy", sync.nop())

        def fg(g):
            mk.entry(g, "g")
            g.wait_ge(mk.sems["bar"], bar0)
            for gi in range(NSLAB):
                mk.rwait(g, "dqr", gi, ORING, dqrsnap, 16)
                if gi >= 2:
                    g.wait_ge(mk.sems["phb"], phb0 + gi - 1)
                osl = mk.B16[:, 28672 + (gi % ORING) * 512:
                             28672 + (gi % ORING) * 512 + 512]
                op_ = mk.f32(mk.WAR, (gi % 2) * 1024, 512)
                g.tensor_scalar(out=op_, in0=osl,
                                scalar1=mk.SC[:, SC_RO:SC_RO + 1],
                                scalar2=MAGIC, op0=OP.mult, op1=OP.add)
                mk.inc("pha", g.drain())
            mk.inc("msg", g.drain())

        def do_lns(s, ch):
            mk.ln_s(s, ch % mk.NLN, mk.LNa, xo_ap(ch),
                    mk.f32(mk.B16, 24576 + (ch % 2) * 2048, 1024),
                    lnv0 + ch + 1)

        def st_issue(s, ch):
            s.wait_ge(mk.sems["phf"], phf0 + ch + 1)
            mk.rissue("dxs", ch, 2, s.dma_start(
                out=xout_dram[ch * 128:(ch + 1) * 128, :], in_=xo_ap(ch)))
            if fuse_ln:
                mk.rissue("dys", ch, 2, s.dma_start(
                    out=mk.dy2sp[ch * 128:(ch + 1) * 128, :], in_=y2_ap(ch)))

        def fs(s):
            mk.entry(s, "s")
            sched = {}

            def put(slot, kind, ch):
                if 0 <= ch < nch and (fuse_ln or kind == "st"):
                    sched.setdefault(slot, []).append((kind, ch))

            for w in range(1, NW):
                base = w * OCF
                put(base + 0, "st", 4 * w - 6)
                put(base + 2, "st", 4 * w - 5)
                put(base + 8, "ln", 4 * w - 4)
                put(base + 12, "ln", 4 * w - 3)
                put(base + 14, "st", 4 * w - 4)
                put(base + 16, "st", 4 * w - 3)
                put(base + 20, "ln", 4 * w - 2)
                put(base + 22, "ln", 4 * w - 1)
            for g in range(NSLAB):
                s.wait_ge(mk.sems["pha"], pha0 + g + 1)
                if g >= QRING:
                    slab_wait(s, g - QRING)
                op_ = mk.f32(mk.WAR, (g % 2) * 1024, 512)
                s.activation(out=qon_ap(g), in_=op_, func=ACT.Identity,
                             scale=1.0, bias=mk.MAGNEG[:, 0:1])
                mk.inc("phb", s.drain())
                for kind, ch in sched.get(g, []):
                    if kind == "ln":
                        do_lns(s, ch)
                    else:
                        st_issue(s, ch)
            for ch in range(nch - 4, nch):
                if ch - 2 >= 0:
                    st_issue(s, ch - 2)
                if fuse_ln:
                    do_lns(s, ch)
            st_issue(s, nch - 2)
            st_issue(s, nch - 1)
            mk.rtotal(s, "dxs", 2)
            if fuse_ln:
                mk.rtotal(s, "dys", 2)
            mk.inc("mss", s.drain())

        def emit_tr(t, ch):
            t.wait_ge(mk.sems["phf"], phf0 + ch + 1)
            tgt = chunk_psum(ch)[2]
            y2t = y2_ap(ch)
            last = None
            for hcx in range(HC):
                last = t.transpose(tgt[:, hcx * 128:(hcx + 1) * 128],
                                   y2t[:, hcx * 128:(hcx + 1) * 128],
                                   mk.id16[:])
            mk.inc("phd", last)

        def ft(t):
            mk.entry(t, "t")
            for k in range(npair):
                w = k // 2
                for hc in range(OCF):
                    g = w * OCF + hc
                    t.wait_ge(mk.sems["phb"], phb0 + g + 1)
                    if k == 0 and hc == 0:
                        t.wait_ge(mk.sems["dW"], dw0 + 16 * OCF)
                    elif hc == 0:
                        if k >= 2:
                            if fuse_ln:
                                t.wait_ge(mk.sems["phe"],
                                          phe0 + 2 * (k - 2) + 2)
                            else:
                                t.wait_ge(mk.sems["phf"],
                                          phf0 + 2 * (k - 2) + 2)
                    slab = qon_ap(g)
                    off = (k % 2) * 256
                    last = None
                    for ci in range(2):
                        halves = chunk_psum(2 * k + ci)[0]
                        lt = slab[:, off + ci * 128:off + ci * 128 + 128]
                        for half in range(2):
                            last = t.matmul(
                                halves[half], lhsT=lt,
                                rhs=mk.B16[:, hc * D + half * 512:
                                           hc * D + half * 512 + 512],
                                start=(hc == 0), stop=(hc == OCF - 1))
                    if hc == OCF - 1:
                        mk.inc("phc", last)
                    elif k % 2 == 1:
                        mk.inc("phg", last)
                    if fuse_ln and k >= 1:
                        if hc == 14:
                            emit_tr(t, 2 * (k - 1))
                        elif hc == 18:
                            emit_tr(t, 2 * (k - 1) + 1)
            if fuse_ln:
                emit_tr(t, nch - 2)
                emit_tr(t, nch - 1)
            mk.inc("mst", t.drain())

        def fm_copy(v, ch):
            v.wait_ge(mk.sems["phd"], phd0 + ch + 1)
            dst = mk.A16[:, 0:HC * T].rearrange(
                "p (h t) -> p h t", h=HC)[:, :, ch * 128:(ch + 1) * 128]
            v.tensor_copy(out=dst,
                          in_=chunk_psum(ch)[2].rearrange(
                              "p (h t) -> p h t", h=HC))
            mk.inc("phe", v.drain())

        def fvt(v):
            mk.entry(v, "v")
            for ch in range(nch):
                k = ch // 2
                v.wait_ge(mk.sems["phc"], phc0 + k + 1)
                mk.rwait(v, "dxl", ch, 4, dxlsnap, 16)
                halves, full, _ = chunk_psum(ch)
                xa = xa_ap(ch)
                xo = xo_ap(ch)
                if ch >= 2:
                    mk.rwait(v, "dxs", ch, 2, dxssnap, 16, occ=ch // 2 - 1)
                if full is not None:
                    v.scalar_tensor_tensor(
                        out=xo, in0=full, scalar=mk.SC[:, SC_HSO:SC_HSO + 1],
                        in1=xa, op0=OP.mult, op1=OP.add)
                else:
                    for j in range(2):
                        v.scalar_tensor_tensor(
                            out=xo[:, j * 512:(j + 1) * 512], in0=halves[j],
                            scalar=mk.SC[:, SC_HSO:SC_HSO + 1],
                            in1=xa[:, j * 512:(j + 1) * 512],
                            op0=OP.mult, op1=OP.add)
                v.drain()
                if fuse_ln:
                    mk.ln_v_pre(v, ch % mk.NLN, mk.LNa, xo)
                    if ch >= 1:
                        fm_copy(v, ch - 1)
                    y2t = y2_ap(ch)
                    if ch >= 2:
                        v.wait_ge(mk.sems["phd"], phd0 + ch - 1)
                        mk.rwait(v, "dys", ch, 2, dyssnap, 16,
                                 occ=ch // 2 - 1)
                    mk.ln_v_apply(v, ch % mk.NLN, mk.LNa, xo, y2t,
                                  lns0 + ch + 1)
                mk.inc("phf", v.drain())
            if fuse_ln:
                fm_copy(v, nch - 1)
            mk.inc("msv", v.drain())

        self.blk.sync(fsy)
        self.blk.gpsimd(fg)
        self.blk.scalar(fs)
        self.blk.tensor(ft)
        self.blk.vector(fvt)
        self.snap()

    def phase_q(self):
        mk = self
        T, nch = self.T, self.nch
        dw0 = self.cnt["dW"]
        pha0 = self.cnt["pha"]
        phb0 = self.cnt["phb"]
        phc0 = self.cnt["phc"]
        phd0 = self.cnt["phd"]
        phe0 = self.cnt["phe"]
        kvb0 = self.cnt["kvb"]

        def fwl(sync):
            mk.entry(sync, "y")
            sync.wait_ge(mk.sems["dL"], mk.dg_p0)
            sync.dma_start(
                out=mk.WAR[:, 0:8192].rearrange("p (a o) -> p a o", a=HC),
                in_=mk.dwq16.rearrange("(a p) o -> p a o", p=128))\
                .then_inc(mk.sems["dW"], 16)
            mk.cnt["dW"] += 16
            mk.inc("msy", sync.nop())

        def emit_q(t, tc):
            if tc >= 2:
                t.wait_ge(mk.sems["kvb"], kvb0 + tc - 1)
                t.wait_ge(mk.sems["phc"], phc0 + tc - 1)
            last = None
            for hc in range(HC):
                for half in range(2):
                    last = t.matmul(
                        mk.pd[tc % 2][:, half * 512:(half + 1) * 512],
                        lhsT=mk.A16[:, hc * T + tc * 128:
                                    hc * T + tc * 128 + 128],
                        rhs=mk.WAR[:, hc * 1024 + half * 512:
                                   hc * 1024 + half * 512 + 512],
                        start=(hc == 0), stop=(hc == HC - 1))
            mk.inc("pha", last)

        def emit_kvtr(t, tc):
            t.wait_ge(mk.sems["phb"], phb0 + tc + 1)
            ftm = mk.C16[:, 8192 + (tc % 3) * 1024:
                         8192 + (tc % 3) * 1024 + 1024]
            vt = mk.C16[:, 12288 + (tc % 3) * 1024:
                        12288 + (tc % 3) * 1024 + 1024]
            for h in range(NH):
                t.matmul(
                    mk.pb[h // 4][:, (h % 4) * 128:(h % 4) * 128 + 128],
                    lhsT=ftm[:, h * 128:(h + 1) * 128],
                    rhs=vt[:, h * 128:(h + 1) * 128],
                    start=(tc == 0 and h % 4 == 0),
                    stop=(tc == nch - 1 and h % 4 == 3))
                t.matmul(
                    mk.pd[2][:, h:h + 1],
                    lhsT=ftm[:, h * 128:(h + 1) * 128],
                    rhs=mk.ones16[:, 0:1],
                    start=(tc == 0 and h == 0),
                    stop=(tc == nch - 1 and h == NH - 1))
            for wave in range(2):
                w = 2 * tc + wave
                if w >= 1:
                    t.wait_ge(mk.sems["phe"], phe0 + w)
                last = None
                for r in range(4):
                    hh = wave * 4 + r
                    last = t.transpose(
                        mk.pdh(2, 1)[:, r * 128:(r + 1) * 128],
                        ftm[:, hh * 128:(hh + 1) * 128], mk.id16[:])
                mk.inc("phd", last)

        def ft(t):
            mk.entry(t, "t")
            t.wait_ge(mk.sems["dW"], dw0 + 16)
            for tc in range(nch):
                emit_q(t, tc)
                if tc >= 2:
                    emit_kvtr(t, tc - 2)
            emit_kvtr(t, nch - 2)
            emit_kvtr(t, nch - 1)
            mk.inc("mst", t.drain())

        def fv(v):
            mk.entry(v, "v")
            for tc in range(nch):
                v.wait_ge(mk.sems["pha"], pha0 + tc + 1)
                pq = mk.pd[tc % 2]
                ein = mk.f32(mk.C16, (tc % 2) * 2048, 1024)
                v.tensor_scalar(out=ein, in0=pq[:, :], scalar1=0.0,
                                scalar2=None, op0=OP.min)
                vt = mk.C16[:, 12288 + (tc % 3) * 1024:
                            12288 + (tc % 3) * 1024 + 1024]
                if tc >= 3:
                    v.wait_ge(mk.sems["phd"], phd0 + 2 * (tc - 2))
                v.tensor_copy(out=vt, in_=pq[:, :])
                mk.inc("kvb", v.drain())
                v.wait_ge(mk.sems["phc"], phc0 + tc + 1)
                e16 = mk.C16[:, 4096 + (tc % 2) * 1024:
                             4096 + (tc % 2) * 1024 + 1024]
                rl = mk.C16[:, 6144 + (tc % 2) * 1024:
                            6144 + (tc % 2) * 1024 + 1024]
                ftm = mk.C16[:, 8192 + (tc % 3) * 1024:
                             8192 + (tc % 3) * 1024 + 1024]
                v.tensor_tensor(out=ftm, in0=e16, in1=rl, op=OP.add)
                mk.inc("phb", v.drain())
            mk.inc("msv", v.drain())

        def emit_fevac(s, tc):
            for wave in range(2):
                s.wait_ge(mk.sems["phd"], phd0 + 2 * tc + wave + 1)
                dst = mk.B16[:, 0:HC * T].rearrange(
                    "p (h t) -> p h t", h=HC)[
                    :, wave * 4:(wave + 1) * 4, tc * 128:(tc + 1) * 128]
                s.copy(out=dst,
                       in_=mk.pdh(2, 1)[:, 0:512].rearrange(
                           "p (h t) -> p h t", h=4))
                mk.inc("phe", s.drain())

        def fs(s):
            mk.entry(s, "s")
            for tc in range(nch):
                s.wait_ge(mk.sems["kvb"], kvb0 + tc + 1)
                if tc >= 2:
                    s.wait_ge(mk.sems["phb"], phb0 + tc - 1)
                ein = mk.f32(mk.C16, (tc % 2) * 2048, 1024)
                e16 = mk.C16[:, 4096 + (tc % 2) * 1024:
                             4096 + (tc % 2) * 1024 + 1024]
                rl = mk.C16[:, 6144 + (tc % 2) * 1024:
                            6144 + (tc % 2) * 1024 + 1024]
                s.activation(out=rl, in_=mk.pd[tc % 2][:, :], func=ACT.Relu)
                s.drain()
                s.activation(out=e16, in_=ein, func=ACT.Exp)
                mk.inc("phc", s.drain())
                if tc >= 1:
                    emit_fevac(s, tc - 1)
            emit_fevac(s, nch - 1)
            mk.inc("mss", s.drain())

        self.blk.sync(fwl)
        self.blk.tensor(ft)
        self.blk.vector(fv)
        self.blk.scalar(fs)
        self.snap()

    def phase_kvx(self):
        mk = self
        phb0 = self.cnt["phb"]

        def fv(v):
            mk.entry(v, "v")
            kvp = mk.f32(mk.C16, 0, NH * 128)
            v.tensor_copy(out=kvp[:, 0:512], in_=mk.pb[0][:, :])
            v.tensor_copy(out=kvp[:, 512:1024], in_=mk.pb[1][:, :])
            ksp = mk.f32(mk.C16, 2 * NH * 128, NH)
            v.tensor_copy(out=ksp, in_=mk.pd[2][:, 0:NH])
            mk.inc("phb", v.drain())
            mk.inc("msv", v.drain())

        def fg(g):
            mk.entry(g, "g")
            g.wait_ge(mk.sems["phb"], phb0 + 1)
            kvp = mk.f32(mk.C16, 0, NH * 128 + NH)
            g.dma_start(out=mk.dkv_in, in_=kvp).then_inc(mk.sems["dG"], 16)
            mk.cnt["dG"] += 16
            mk.w(g, "dG")
            ic = mk.inc("cc", g.collective_compute(
                "AllReduce", OP.add,
                replica_groups=[[0, 1], [2, 3], [4, 5], [6, 7]],
                ins=[mk.dkv_in.opt()], outs=[mk.dkv_out.opt()]))
            g.wait_ge(mk.sems["cc"], ic)
            g.dma_start(out=mk.WAR[:, 0:NH * 128],
                        in_=mk.dkv_out[:, 0:NH * 128])\
                .then_inc(mk.sems["dG"], 16)
            mk.cnt["dG"] += 16
            g.dma_start(out=mk.f32(mk.WAR, 1024, NH),
                        in_=mk.dkv_out[:, NH * 128:NH * 128 + NH])\
                .then_inc(mk.sems["dG"], 16)
            mk.cnt["dG"] += 16
            mk.w(g, "dG")
            mk.inc("kvb", g.drain())
            mk.inc("msg", g.drain())

        kvb0 = self.cnt["kvb"]

        def fv2(v):
            v.wait_ge(mk.sems["kvb"], kvb0 + 1)
            ksbd = mk.WAR[:, 1040:1040 + NH * NH]
            v.memset(ksbd, 0.0)
            v.drain()
            ks = mk.f32(mk.WAR, 1024, NH)
            for h in range(NH):
                v.tensor_copy(out=ksbd[:, h * NH + h:h * NH + h + 1],
                              in_=ks[:, h:h + 1])
            mk.inc("phb", v.drain())
            mk.inc("msv", v.drain())

        self.blk.vector(fv)
        self.blk.gpsimd(fg)
        self.blk.vector(fv2)
        self.snap()

    def phase_m1(self):
        mk = self
        T, ntt = self.T, self.ntt
        pha0 = self.cnt["pha"]
        phb0 = self.cnt["phb"]
        uccsnap = self.rsnap("ucc", 2)
        p0lsnap = self.rsnap("p0l", 2)
        phc0 = self.cnt["phc"]
        phd0 = self.cnt["phd"]
        phe0 = self.cnt["phe"]
        phf0 = self.cnt["phf"]
        dz0 = self.cnt["dZ"]

        def ft(t):
            mk.entry(t, "t")
            for tt in range(ntt):
                if tt >= 1:
                    t.wait_ge(mk.sems["phb"], phb0 + tt)
                last = None
                for fc in range(HC):
                    last = t.matmul(
                        mk.pd[2][0:NH, 0:512],
                        lhsT=mk.WAR[:, 1040 + fc * NH:1040 + (fc + 1) * NH],
                        rhs=mk.B16[:, fc * T + tt * 512:fc * T + tt * 512 + 512],
                        start=(fc == 0), stop=(fc == HC - 1))
                mk.inc("pha", last)
                mk.rwait(t, "p0l", tt, 2, p0lsnap, 16)
                rb = 2560 if tt % 2 == 0 else 10240
                for h in range(NH):
                    hi = tt * NH + h
                    if hi >= 3:
                        t.wait_ge(mk.sems["phd"], phd0 + hi - 2)
                    zb3 = [mk.pb[0][:, :], mk.pb[1][:, :],
                           mk.pd[2][:, 512:1024]][hi % 3]
                    last = t.matmul(
                        zb3,
                        lhsT=mk.onesr16[0:1, :],
                        rhs=mk.C16[0:1, rb + h * 512:rb + h * 512 + 512],
                        start=True, stop=True)
                    mk.inc("phc", last)
                    t.wait_ge(mk.sems["phd"], phd0 + hi + 1)
                    if hi >= 4:
                        t.wait_ge(mk.sems["phf"], phf0 + hi - 3)
                    mmb = [mk.pd[0][:, 0:512], mk.pd[0][:, 512:1024],
                           mk.pd[1][:, 0:512], mk.pd[1][:, 512:1024]][hi % 4]
                    last = t.matmul(
                        mmb,
                        lhsT=mk.WAR[:, h * 128:(h + 1) * 128],
                        rhs=mk.C16[:, 6656 + (hi % 3) * 512:
                                   6656 + (hi % 3) * 512 + 512],
                        start=True, stop=True)
                    mk.inc("phe", last)
            mk.inc("mst", t.drain())

        def fv(v):
            mk.entry(v, "v")
            for tt in range(ntt):
                v.wait_ge(mk.sems["pha"], pha0 + tt + 1)
                if tt >= 1:
                    mk.rwait(v, "ucc", tt - 1, 2, uccsnap, 16)
                zf = mk.f32(mk.C16, 0, 512)[0:NH, :]
                v.tensor_scalar(out=zf, in0=mk.pd[2][0:NH, 0:512],
                                scalar1=ATTN_EPS, scalar2=None, op0=OP.add)
                v.drain()
                zf2 = mk.f32(mk.C16, 1024, 512)[0:NH, :]
                v.reciprocal(out=zf2, in_=zf)
                v.drain()
                v.tensor_scalar(out=mk.C16[0:NH, 2048:2560],
                                in0=zf2, scalar1=float(mk.L), scalar2=None,
                                op0=OP.mult)
                mk.inc("phb", v.drain())
                for h in range(NH):
                    hi = tt * NH + h
                    v.wait_ge(mk.sems["phc"], phc0 + hi + 1)
                    if hi >= 3:
                        v.wait_ge(mk.sems["phe"], phe0 + hi - 2)
                    fz = mk.C16[:, 6656 + (hi % 3) * 512:
                                6656 + (hi % 3) * 512 + 512]
                    zb3 = [mk.pb[0][:, :], mk.pb[1][:, :],
                           mk.pd[2][:, 512:1024]][hi % 3]
                    v.tensor_tensor(
                        out=fz,
                        in0=mk.B16[:, h * T + tt * 512:h * T + tt * 512 + 512],
                        in1=zb3, op=OP.mult)
                    mk.inc("phd", v.drain())
            mk.inc("msv", v.drain())

        def fz(sync):
            mk.entry(sync, "y")
            for tt in range(ntt):
                sync.wait_ge(mk.sems["phb"], phb0 + tt + 1)
                dsl = mk.dzl[0:1, (tt % 2) * NH * 512:
                             (tt % 2 + 1) * NH * 512]
                mk.rissue("ucc", tt, 2, sync.dma_start(
                    out=dsl.rearrange("o (h t) -> (o h) t", h=NH),
                    in_=mk.C16[0:NH, 2048:2560]))
                mk.rwait(sync, "ucc", tt, 2, uccsnap, 16)
                if tt >= 2:
                    sync.wait_ge(mk.sems["phc"], phc0 + NH * (tt - 1))
                rb = 2560 if tt % 2 == 0 else 10240
                mk.rissue("p0l", tt, 2, sync.dma_start(
                    out=mk.C16[0:1, rb:rb + NH * 512], in_=dsl))
            mk.inc("msy", sync.nop())

        def fs(s):
            mk.entry(s, "s")
            for tt in range(ntt):
                for h in range(NH):
                    hi = tt * NH + h
                    s.wait_ge(mk.sems["phe"], phe0 + hi + 1)
                    mmb = [mk.pd[0][:, 0:512], mk.pd[0][:, 512:1024],
                           mk.pd[1][:, 0:512], mk.pd[1][:, 512:1024]][hi % 4]
                    s.activation(out=mk.A16[:, h * T + tt * 512:
                                            h * T + tt * 512 + 512],
                                 in_=mmb, func=ACT.Identity,
                                 scale=1.0 / float(mk.L))
                    mk.inc("phf", s.drain())
            mk.inc("mss", s.drain())

        self.blk.tensor(ft)
        self.blk.vector(fv)
        self.blk.sync(fz)
        self.blk.scalar(fs)
        self.snap()

    def phase_m2a(self):
        mk = self
        T, ntt, nch = self.T, self.ntt, self.nch
        nhalf = 2 if ntt >= 2 else 1
        tph = ntt // nhalf
        dw0 = self.cnt["dW"]
        pha0 = self.cnt["pha"]
        phb0 = self.cnt["phb"]
        phc0 = self.cnt["phc"]
        phd0 = self.cnt["phd"]
        dyssnap = self.rsnap("dys", 2)
        lnv0 = self.cnt["lnv"]
        lns0 = self.cnt["lns"]
        groups = [(oc, hf) for oc in range(HC) for hf in range(nhalf)]
        banksets = [
            [self.pb[0][:, :], self.pb[1][:, :],
             self.pd[2][:, 0:512], self.pd[2][:, 512:1024]][:tph],
            [self.pd[0][:, 0:512], self.pd[0][:, 512:1024],
             self.pd[1][:, 0:512], self.pd[1][:, 512:1024]][:tph],
        ]

        def fwl(sync):
            mk.entry(sync, "y")
            sync.wait_ge(mk.sems["dL"], mk.dg_p0)
            sync.dma_start(
                out=mk.WAR[:, 0:8192].rearrange("p (a o) -> p a o", a=HC),
                in_=mk.dwm16.rearrange("(a p) o -> p a o", p=128))\
                .then_inc(mk.sems["dW"], 16)
            mk.cnt["dW"] += 16
            mk.inc("msy", sync.nop())

        def ft(t):
            mk.entry(t, "t")
            t.wait_ge(mk.sems["dW"], dw0 + 16)
            for gi, (oc, hf) in enumerate(groups):
                if gi >= 2:
                    t.wait_ge(mk.sems["phb"], phb0 + gi - 1)
                bs = banksets[gi % 2]
                last = None
                for vc in range(HC):
                    for j in range(tph):
                        tt = hf * tph + j
                        last = t.matmul(
                            bs[j],
                            lhsT=mk.WAR[:, vc * 1024 + oc * 128:
                                        vc * 1024 + oc * 128 + 128],
                            rhs=mk.A16[:, vc * T + tt * 512:
                                       vc * T + tt * 512 + 512],
                            start=(vc == 0), stop=(vc == HC - 1))
                mk.inc("pha", last)
            t.wait_ge(mk.sems["phb"], phb0 + len(groups))
            for ch in range(nch):
                if ch >= 2:
                    t.wait_ge(mk.sems["phd"], phd0 + ch - 1)
                last = None
                for oc in range(HC):
                    last = t.transpose(
                        mk.pdh(ch % 2, 0)[:, oc * 128:(oc + 1) * 128],
                        mk.B16[:, oc * T + ch * 128:oc * T + ch * 128 + 128],
                        mk.id16[:])
                mk.inc("phc", last)
            mk.inc("mst", t.drain())

        def fs(s):
            mk.entry(s, "s")
            for gi, (oc, hf) in enumerate(groups):
                s.wait_ge(mk.sems["pha"], pha0 + gi + 1)
                bs = banksets[gi % 2]
                last = None
                for j in range(tph):
                    tt = hf * tph + j
                    last = s.copy(out=mk.B16[:, oc * T + tt * 512:
                                             oc * T + tt * 512 + 512],
                                  in_=bs[j])
                mk.inc("phb", last)
            mk.inc("mss", s.drain())

        phg0 = self.cnt["phg"]

        def fvt(v):
            mk.entry(v, "v")
            for ch in range(nch):
                v.wait_ge(mk.sems["phc"], phc0 + ch + 1)
                ptm = mk.pdh(ch % 2, 0)
                k = ch % mk.NLN
                mk.ln_v_pre(v, k, mk.LNb, ptm)
                ya = mk.C16[:, (ch % 2) * 2048:(ch % 2) * 2048 + 1024]
                if ch >= 2:
                    v.wait_ge(mk.sems["phg"], phg0 + ch - 1)
                mk.ln_v_apply(v, k, mk.LNb, ptm, ya, lns0 + ch + 1)
                mk.inc("phd", v.drain())
            mk.inc("msv", v.drain())

        def fgt(g):
            mk.entry(g, "g")
            for ch in range(nch):
                g.wait_ge(mk.sems["phd"], phd0 + ch + 1)
                ya = mk.C16[:, (ch % 2) * 2048:(ch % 2) * 2048 + 1024]
                yb = mk.C16[:, (ch % 2) * 2048 + 1024:(ch % 2) * 2048 + 2048]
                if ch >= 2:
                    mk.rwait(g, "dys", ch, 2, dyssnap, 16, occ=ch // 2 - 1)
                g.tensor_tensor(out=yb, in0=ya, in1=mk.g1r[:], op=OP.mult)
                g.drain()
                g.tensor_tensor(out=yb, in0=yb, in1=mk.b1r[:], op=OP.add)
                mk.inc("phg", g.drain())
            mk.inc("msg", g.drain())

        def fst(s):
            for ch in range(nch):
                s.wait_ge(mk.sems["phc"], phc0 + ch + 1)
                ptm = mk.pdh(ch % 2, 0)
                sqscr = mk.f32(mk.C16, 4096 + (ch % 2) * 2048, 1024)
                mk.ln_s(s, ch % mk.NLN, mk.LNb, ptm, sqscr,
                        lnv0 + ch + 1)
            mk.inc("mss", s.drain())

        def fys(sync):
            for ch in range(nch):
                sync.wait_ge(mk.sems["phg"], phg0 + ch + 1)
                yb = mk.C16[:, (ch % 2) * 2048 + 1024:(ch % 2) * 2048 + 2048]
                mk.rissue("dys", ch, 2, sync.dma_start(
                    out=mk.dya1[ch * 128:(ch + 1) * 128, :], in_=yb))
            mk.rtotal(sync, "dys", 2)
            mk.inc("msy", sync.nop())

        self.blk.sync(fwl)
        self.blk.tensor(ft)
        self.blk.scalar(fs)
        self.blk.vector(fvt)
        self.blk.gpsimd(fgt)
        self.blk.scalar(fst)
        self.blk.sync(fys)
        self.snap()

    def phase_m2b(self):
        mk = self
        T, ntt = self.T, self.ntt
        nhalf = 2 if ntt >= 2 else 1
        tph = ntt // nhalf
        stw = tph * 512
        dw0 = self.cnt["dW"]
        pha0 = self.cnt["pha"]
        phb0 = self.cnt["phb"]
        ugssnap = self.rsnap("ugs", 2)
        groups = [(oc2, hf) for oc2 in range(KC2) for hf in range(nhalf)]
        banksets = [
            [self.pd[0][:, 0:512], self.pd[0][:, 512:1024],
             self.pd[1][:, 0:512], self.pd[1][:, 512:1024]][:tph],
            [self.pd[2][:, 0:512], self.pd[2][:, 512:1024],
             self.pb[0][:, :], self.pb[1][:, :]][:tph],
        ]

        def fwl(sync):
            mk.entry(sync, "y")
            sync.wait_ge(mk.sems["dL"], mk.dg_p0)
            sync.dma_start(
                out=mk.WAR[:, 0:16384].rearrange("p (a o) -> p a o", a=HC),
                in_=mk.dw116.rearrange("(a p) o -> p a o", p=128))\
                .then_inc(mk.sems["dW"], 16)
            mk.cnt["dW"] += 16
            mk.inc("msy", sync.nop())

        def ft(t):
            mk.entry(t, "t")
            t.wait_ge(mk.sems["dW"], dw0 + 16)
            for gi, (oc2, hf) in enumerate(groups):
                if gi >= 2:
                    t.wait_ge(mk.sems["phb"], phb0 + gi - 1)
                bs = banksets[gi % 2]
                last = None
                for vc in range(HC):
                    for j in range(tph):
                        tt = hf * tph + j
                        last = t.matmul(
                            bs[j],
                            lhsT=mk.WAR[:, vc * 2048 + oc2 * 128:
                                        vc * 2048 + oc2 * 128 + 128],
                            rhs=mk.B16[:, vc * T + tt * 512:
                                       vc * T + tt * 512 + 512],
                            start=(vc == 0), stop=(vc == HC - 1))
                mk.inc("pha", last)
            mk.inc("mst", t.drain())

        def fs(s):
            mk.entry(s, "s")
            for gi, (oc2, hf) in enumerate(groups):
                s.wait_ge(mk.sems["pha"], pha0 + gi + 1)
                if gi >= 2:
                    mk.rwait(s, "ugs", gi, 2, ugssnap, 16, occ=gi // 2 - 1)
                stg = mk.C16[:, (gi % 2) * stw:(gi % 2) * stw + stw]
                bs = banksets[gi % 2]
                last = None
                for j in range(tph):
                    last = s.activation(out=stg[:, j * 512:(j + 1) * 512],
                                        in_=bs[j], func=ACT.Relu)
                mk.inc("phb", last)
            mk.inc("mss", s.drain())

        def fsp(sync):
            for gi, (oc2, hf) in enumerate(groups):
                sync.wait_ge(mk.sems["phb"], phb0 + gi + 1)
                stg = mk.C16[:, (gi % 2) * stw:(gi % 2) * stw + stw]
                mk.rissue("ugs", gi, 2, sync.dma_start(
                    out=mk.du1sp[oc2 * 128:(oc2 + 1) * 128,
                                 hf * stw:hf * stw + stw], in_=stg))
            mk.rtotal(sync, "ugs", 2)
            mk.inc("msy", sync.nop())

        self.blk.sync(fwl)
        self.blk.tensor(ft)
        self.blk.scalar(fs)
        self.blk.sync(fsp)
        self.snap()

    def phase_m2c(self):
        mk = self
        T, ntt, nch = self.T, self.ntt, self.nch
        dw0 = self.cnt["dW"]
        dxlsnap = self.rsnap("dxl", 2)
        dxssnap = self.rsnap("dxs", 2)
        pha0 = self.cnt["pha"]
        phb0 = self.cnt["phb"]
        phc0 = self.cnt["phc"]
        phd0 = self.cnt["phd"]
        dx0_ = self.cnt["dX"]
        lnv0 = self.cnt["lnv"]
        lns0 = self.cnt["lns"]

        def fwl(sync):
            mk.entry(sync, "y")
            sync.wait_ge(mk.sems["dL"], mk.dg_p0)
            sync.dma_start(
                out=mk.WAR[:, 0:16384].rearrange("p (a o) -> p a o", a=KC2),
                in_=mk.dw216.rearrange("(a p) o -> p a o", p=128))\
                .then_inc(mk.sems["dW"], 16)
            mk.cnt["dW"] += 16
            for kc in range(KC2):
                buf = mk.A16 if kc < 8 else mk.B16
                sync.dma_start(out=buf[:, (kc % 8) * T:(kc % 8) * T + T],
                               in_=mk.du1sp[kc * 128:(kc + 1) * 128, :])\
                    .then_inc(mk.sems["dW"], 16)
                mk.cnt["dW"] += 16
            mk.inc("msy", sync.nop())

        def xstore(sync, ch, phe_base):
            sync.wait_ge(mk.sems["phe"], phe_base + ch + 1)
            xa = mk.f32(mk.C16, 8192 + (ch % 2) * 2048, 1024)
            mk.rissue("dxs", ch, 2, sync.dma_start(
                out=mk.dxx2[ch * 128:(ch + 1) * 128, :], in_=xa))

        phe_base = self.cnt["phe"]

        def fxl(sync):
            for ch in range(nch):
                if ch >= 2:
                    sync.wait_ge(mk.sems["phe"], phe_base + ch - 1)
                    mk.rwait(sync, "dxs", ch - 2, 2, dxssnap, 16)
                xa = mk.f32(mk.C16, 8192 + (ch % 2) * 2048, 1024)
                mk.rissue("dxl", ch, 2, sync.dma_start(
                    out=xa, in_=mk.dxx1[ch * 128:(ch + 1) * 128, :]))
                y2t = mk.C16[:, 12288 + (ch % 2) * 1024:
                             12288 + (ch % 2) * 1024 + 1024]
                mk.rissue("dxl", ch, 2, sync.dma_start(
                    out=y2t, in_=mk.dy2sp[ch * 128:(ch + 1) * 128, :]))
                ya1t = mk.C16[:, 14336 + (ch % 2) * 1024:
                              14336 + (ch % 2) * 1024 + 1024]
                mk.rissue("dxl", ch, 2, sync.dma_start(
                    out=ya1t, in_=mk.dya1[ch * 128:(ch + 1) * 128, :]))
                if ch >= 1:
                    xstore(sync, ch - 1, phe_base)
            xstore(sync, nch - 1, phe_base)
            mk.rtotal(sync, "dxs", 2)
            mk.inc("msy", sync.nop())

        banks4 = [self.pd[2][:, 0:512], self.pd[2][:, 512:1024],
                  self.pb[0][:, :], self.pb[1][:, :]]

        def ft(t):
            mk.entry(t, "t")
            t.wait_ge(mk.sems["dW"], dw0 + 16 * (KC2 + 1))
            gi = 0
            for tt in range(ntt):
                for oc in range(HC):
                    gi += 1
                    if gi > 4:
                        t.wait_ge(mk.sems["phb"], phb0 + gi - 4)
                    last = None
                    for kc in range(KC2):
                        buf = mk.A16 if kc < 8 else mk.B16
                        last = t.matmul(
                            banks4[gi % 4],
                            lhsT=mk.WAR[:, kc * 1024 + oc * 128:
                                        kc * 1024 + oc * 128 + 128],
                            rhs=buf[:, (kc % 8) * T + tt * 512:
                                    (kc % 8) * T + tt * 512 + 512],
                            start=(kc == 0), stop=(kc == KC2 - 1))
                    mk.inc("pha", last)
                for c2 in range(4):
                    ch = tt * 4 + c2
                    t.wait_ge(mk.sems["phb"], phb0 + (tt + 1) * HC)
                    if ch >= 4:
                        t.wait_ge(mk.sems["phe"], phe_base + ch - 3)
                    last = None
                    for oc in range(HC):
                        cc = (tt % 2) * 4096 + c2 * 1024 + oc * 128
                        last = t.transpose(
                            mk.pdh(ch % 2, (ch // 2) % 2)[:, oc * 128:
                                                          (oc + 1) * 128],
                            mk.C16[:, cc:cc + 128], mk.id16[:])
                    mk.inc("phc", last)
            mk.inc("mst", t.drain())

        def fs(s):
            mk.entry(s, "s")
            gi = 0
            for tt in range(ntt):
                for oc in range(HC):
                    gi += 1
                    s.wait_ge(mk.sems["pha"], pha0 + gi)
                    if tt >= 2 and oc == 0:
                        s.wait_ge(mk.sems["phc"], phc0 + (tt - 2 + 1) * 4)
                    base = (tt % 2) * 4096
                    stg = mk.C16[:, base:base + 4096].rearrange(
                        "p (c q) -> p c q", q=1024)[:, :, oc * 128:(oc + 1) * 128]
                    s.copy(out=stg,
                           in_=banks4[gi % 4].rearrange("p (c o) -> p c o",
                                                        o=128))
                    mk.inc("phb", s.drain())
                for c2 in range(4):
                    ch = tt * 4 + c2
                    s.wait_ge(mk.sems["phc"], phc0 + ch + 1)
                    ptm = mk.pdh(ch % 2, (ch // 2) % 2)
                    sqscr = mk.C16[:, (tt % 2) * 4096 + c2 * 1024:
                                   (tt % 2) * 4096 + c2 * 1024 + 1024]
                    mk.ln_s(s, ch % mk.NLN, mk.LNb, ptm, sqscr,
                            lnv0 + ch + 1)
            mk.inc("mss", s.drain())

        def fv(v):
            mk.entry(v, "v")
            for tt in range(ntt):
                for c2 in range(4):
                    ch = tt * 4 + c2
                    v.wait_ge(mk.sems["phc"], phc0 + ch + 1)
                    mk.rwait(v, "dxl", ch, 2, dxlsnap, 48)
                    ptm = mk.pdh(ch % 2, (ch // 2) % 2)
                    k = ch % mk.NLN
                    mk.ln_v_pre(v, k, mk.LNb, ptm)
                    ya2 = mk.C16[:, (tt % 2) * 4096 + c2 * 1024:
                                 (tt % 2) * 4096 + c2 * 1024 + 1024]
                    mk.ln_v_apply(v, k, mk.LNb, ptm, ya2,
                                  lns0 + ch + 1)
                    v.drain()
                    v.tensor_tensor(out=ya2, in0=ya2, in1=mk.g2r[:],
                                    op=OP.mult)
                    v.drain()
                    v.tensor_tensor(out=ya2, in0=ya2, in1=mk.b2r[:],
                                    op=OP.add)
                    v.drain()
                    xa = mk.f32(mk.C16, 8192 + (ch % 2) * 2048, 1024)
                    y2t = mk.C16[:, 12288 + (ch % 2) * 1024:
                                 12288 + (ch % 2) * 1024 + 1024]
                    ya1t = mk.C16[:, 14336 + (ch % 2) * 1024:
                                  14336 + (ch % 2) * 1024 + 1024]
                    if ch >= 2:
                        mk.rwait(v, "dxs", ch, 2, dxssnap, 16,
                                 occ=ch // 2 - 1)
                    v.tensor_tensor(out=xa, in0=xa, in1=y2t, op=OP.add)
                    v.drain()
                    v.tensor_tensor(out=xa, in0=xa, in1=ya1t, op=OP.add)
                    v.drain()
                    v.tensor_tensor(out=xa, in0=xa, in1=ya2, op=OP.add)
                    mk.inc("phe", v.drain())
            mk.inc("msv", v.drain())

        self.blk.sync(fwl)
        self.blk.sync(fxl)
        self.blk.tensor(ft)
        self.blk.scalar(fs)
        self.blk.vector(fv)
        self.snap()

    def finale(self):
        mk = self

        def f(sync):
            mk.entry(sync, "y")
            sync.nop()

        self.blk.sync(f)


def build_program(T, L):
    mk = MK(T, L)
    nc = mk.build()
    return nc


def make_wsh(wup, wgate, wdown, c):
    parts = []
    for j in range(3 * c, 3 * c + 3):
        if j < 8:
            u = np.concatenate([wup[(3 * j + s) * 128:(3 * j + s + 1) * 128, :]
                                for s in range(3)], axis=1)
        elif j < 16:
            jj = j - 8
            u = np.concatenate(
                [wgate[(3 * jj + s) * 128:(3 * jj + s + 1) * 128, :]
                 for s in range(3)], axis=1)
        else:
            u = wdown[(j - 16) * 128:(j - 16 + 1) * 128, :]
        parts.append(u)
    return np.ascontiguousarray(np.concatenate(parts, axis=0))


def kernel(**inputs):
    x = np.ascontiguousarray(inputs["L0_attn_x"])
    B, L, _ = x.shape
    T = L // 2
    nc = build_program(T, L)
    mybir.codegen_inst_isa_subclasses(nc)
    ident = np.eye(128, dtype=np.float32)
    wupf = np.ascontiguousarray(inputs["L0_ffn_w_up"], dtype=np.float32)
    wgatef = np.ascontiguousarray(inputs["L0_ffn_w_gate"], dtype=np.float32)
    wdownf = np.ascontiguousarray(inputs["L0_ffn_w_down"], dtype=np.float32)
    in_maps = []
    for c in range(NCORES):
        b, h = c // 2, c % 2
        in_maps.append({
            "wsh": make_wsh(wupf, wgatef, wdownf, c),
            "x0": np.ascontiguousarray(x[b, h * T:(h + 1) * T, :]),
            "wq": np.ascontiguousarray(inputs["L0_attn_Wq"], dtype=np.float32),
            "wm": np.ascontiguousarray(inputs["L0_attn_Wm"], dtype=np.float32),
            "w1": np.ascontiguousarray(inputs["L0_attn_W1"], dtype=np.float32),
            "w2": np.ascontiguousarray(inputs["L0_attn_W2"], dtype=np.float32),
            "g1": np.ascontiguousarray(inputs["L0_attn_g1"],
                                       dtype=np.float32).reshape(1, D),
            "b1": np.ascontiguousarray(inputs["L0_attn_b1"],
                                       dtype=np.float32).reshape(1, D),
            "g2": np.ascontiguousarray(inputs["L0_attn_g2"],
                                       dtype=np.float32).reshape(1, D),
            "b2": np.ascontiguousarray(inputs["L0_attn_b2"],
                                       dtype=np.float32).reshape(1, D),
            "ident": ident,
        })
    res = run_bass_kernel_spmd(nc, in_maps, core_ids=list(range(NCORES)))
    out = np.empty((B, L, D), dtype=np.float32)
    for c in range(NCORES):
        b, h = c // 2, c % 2
        out[b, h * T:(h + 1) * T, :] = np.asarray(res.results[c]["out"])
    return out


# version 18 (final)
# speedup vs baseline: 1.0081x; 1.0004x over previous
import numpy as np
import concourse.bass as bass
import concourse.mybir as mybir
from concourse.alu_op_type import AluOpType as OP
from concourse.bass_utils import run_bass_kernel_spmd
from contextlib import ExitStack

F32 = mybir.dt.float32
F16 = mybir.dt.float16
ACT = mybir.ActivationFunctionType
AX = mybir.AxisListType

NCORES = 8
D = 1024
HC = 8
DFF = 3072
OCF = 24
D2 = 2048
KC2 = 16
NH = 8
MAGIC = 12582912.0
P23 = 8388608.0
QA = 32767.0
QW = 7.0
LN_EPS = 1e-5
ATTN_EPS = 1e-6

SC_RY, SC_SY = 0, 1
SC_RU, SC_SU = 2, 3
SC_RG, SC_SG, SC_BG = 4, 5, 6
SC_R2, SC_S2 = 7, 8
SC_RO, SC_SO = 9, 10
SC_HSO = 11


class MK:
    def __init__(self, T, L):
        self.T = T
        self.L = L
        self.ntt = T // 512
        self.nch = T // 128
        self.TE = min(T, 2048)
        self.cnt = {}
        self.sems = {}
        self.gatev = {}
        self.nc = bass.Bass("TRN2", target_bir_lowering=False, debug=False)

    def mksem(self, name):
        self.sems[name] = self.es.enter_context(self.nc.semaphore(name))
        self.cnt[name] = 0

    def inc(self, name, instr, n=1):
        instr.then_inc(self.sems[name], n)
        self.cnt[name] += n
        return self.cnt[name]

    def w(self, eng, name, v=None):
        eng.wait_ge(self.sems[name], self.cnt[name] if v is None else v)

    def snap(self):
        self.gatev = {e: self.cnt["ms" + e] for e in ("v", "s", "t", "g", "y")}

    def mkring(self, name, R):
        for r in range(R):
            self.mksem(f"{name}{r}")
        return name

    def rsem(self, name, i, R):
        return self.sems[f"{name}{i % R}"]

    def rissue(self, name, i, R, instr, n=16):
        nm = f"{name}{i % R}"
        instr.then_inc(self.sems[nm], n)
        self.cnt[nm] += n

    def rsnap(self, name, R):
        return [self.cnt[f"{name}{r}"] for r in range(R)]

    def rwait(self, eng, name, i, R, snap, per, occ=None):
        o = (i // R if occ is None else occ)
        eng.wait_ge(self.sems[f"{name}{i % R}"], snap[i % R] + per * (o + 1))

    def rtotal(self, eng, name, R):
        for r in range(R):
            nm = f"{name}{r}"
            eng.wait_ge(self.sems[nm], self.cnt[nm])

    def entry(self, eng, me):
        for e in ("v", "s", "t", "g", "y"):
            if e != me and self.gatev.get(e, 0) > 0:
                eng.wait_ge(self.sems["ms" + e], self.gatev[e])

    def f32(self, buf, lo16, n32):
        return buf[:, lo16:lo16 + 2 * n32].bitcast(F32)

    def pdh(self, i, b):
        return self.pd[i][:, b * 512:(b + 1) * 512].bitcast(F16)

    def pbh(self, i):
        return self.pb[i][:, :].bitcast(F16)

    def build(self):
        nc = self.nc
        T = self.T
        es = ExitStack()
        self.es = es
        with es:
            E = es.enter_context
            gx = lambda n, s, dt=F32, k="ExternalInput": nc.dram_tensor(n, s, dt, kind=k).ap()
            gi = lambda n, s, dt=F32: nc.dram_tensor(n, s, dt).ap()
            self.dx0 = gx("x0", [T, D])
            self.dwq = gx("wq", [D, D])
            self.dwm = gx("wm", [D, D])
            self.dw1 = gx("w1", [D, D2])
            self.dw2 = gx("w2", [D2, D])
            self.dg1 = gx("g1", [1, D])
            self.db1 = gx("b1", [1, D])
            self.dg2 = gx("g2", [1, D])
            self.db2 = gx("b2", [1, D])
            self.dident = gx("ident", [128, 128])
            self.dout = gx("out", [T, D], F32, "ExternalOutput")
            self.dxx1 = gi("xx1", [T, D])
            self.dxx2 = gi("xx2", [T, D])
            self.dy2sp = gi("y2sp", [T, D], F16)
            self.dya1 = gi("ya1", [T, D], F16)
            self.dupsp = gi("upsp", [DFF, T], F16)
            self.dgatesp = gi("gatesp", [DFF, T], F16)
            self.dg2sp = gi("g2sp", [DFF, T], F16)
            self.dosp = gi("osp", [DFF, T], F16)
            self.doqsp = gi("oqsp", [DFF, T], F16)
            self.du1sp = gi("u1sp", [D2, T], F16)
            self.dwq16 = gi("wq16", [D, D], F16)
            self.dwm16 = gi("wm16", [D, D], F16)
            self.dw116 = gi("w116", [D, D2], F16)
            self.dw216 = gi("w216", [D2, D], F16)
            self.dwsh = gx("wsh", [3 * 128, DFF])
            self.dwall_sh = gi("wall_sh", [3, OCF, 128, 128], F16)
            self.dwall = gi("wall", [OCF, OCF, 128, 128], F16)
            self.ddcol = gi("dcol", [128, 2])
            self.dbm_in = gi("bm_in", [1, 2])
            self.dbm_out = gi("bm_out", [1, 2])
            self.dbscal = gi("bscal", [1, 16])
            self.dkv_in = gi("kv_in", [128, NH * 128 + NH])
            self.dkv_out = gi("kv_out", [128, NH * 128 + NH])
            self.dzl = gi("dzl", [1, 2 * NH * 512], F16)
            self.did16 = gi("id16d", [128, 128], F16)

            ACOLS = max(HC * T, OCF * D, 31744)
            self.ACOLS = ACOLS
            self.A16 = E(nc.sbuf_tensor("A16", [128, ACOLS], F16))
            self.B16 = E(nc.sbuf_tensor("B16", [128, ACOLS], F16))
            self.C16 = E(nc.sbuf_tensor("C16", [128, 18432], F16))
            self.WAR = E(nc.sbuf_tensor("WAR", [128, 16384], F16))
            self.SC = E(nc.sbuf_tensor("SCt", [128, 16], F32))
            self.MAGCOL = E(nc.sbuf_tensor("MAGCOL", [128, 1], F32))
            self.MAGNEG = E(nc.sbuf_tensor("MAGNEG", [128, 1], F32))
            self.EPSCOL = E(nc.sbuf_tensor("EPSCOL", [128, 1], F32))
            self.srow = E(nc.sbuf_tensor("srow", [2, 128], F32))
            self.gmaxr = E(nc.sbuf_tensor("gmaxr", [2, 16], F32))
            self.dervr = E(nc.sbuf_tensor("dervr", [1, 16], F32))
            self.id16 = E(nc.sbuf_tensor("id16", [128, 128], F16))
            self.ones16 = E(nc.sbuf_tensor("ones16", [128, 1], F16))
            self.onesr16 = E(nc.sbuf_tensor("onesr16", [1, 128], F16))
            self.g1r = E(nc.sbuf_tensor("g1r", [128, D], F16))
            self.g2r = E(nc.sbuf_tensor("g2r", [128, D], F16))
            self.b1r = E(nc.sbuf_tensor("b1r", [128, D], F16))
            self.b2r = E(nc.sbuf_tensor("b2r", [128, D], F16))
            self.NLN = 4
            mkln = lambda p: {k: E(nc.sbuf_tensor(p + k, [128, self.NLN], F32))
                              for k in ["sum", "sq", "mu", "mu2", "vpe",
                                        "std", "rstd"]}
            self.LNa = mkln("lna")
            self.LNb = mkln("lnb")
            self.rmaxA = E(nc.sbuf_tensor("rmaxA", [128, 1], F32))
            self.rmaxB = E(nc.sbuf_tensor("rmaxB", [128, 1], F32))
            self.tmax = E(nc.sbuf_tensor("tmax", [128, 8], F32))

            self.pd = [E(nc.psum_tensor(f"pd{i}", [128, 1024], F32))
                       for i in range(3)]
            self.pb = [E(nc.psum_tensor(f"pb{i}", [128, 512], F32))
                       for i in range(2)]

            for s in ["dL", "dSP", "dW", "dG", "dE", "dX", "dO", "dY", "dZ",
                      "dI",
                      "msv", "mss", "mst", "msg", "msy",
                      "bar", "cc", "kvb", "lnv", "lns", "wtg",
                      "fua", "fub", "fuc", "fud", "fue", "fuf",
                      "pha", "phb", "phc", "phd", "phe", "phf", "phg"]:
                self.mksem(s)
            for name, R in [("p0l", 2), ("p0s", 2), ("yl", 4), ("ugw", 4),
                            ("ugs", 2), ("e1i", 3), ("e1o", 2), ("e2i", 3),
                            ("e2o", 2), ("dqr", 6), ("dqo", 2), ("dxl", 4),
                            ("dxs", 2), ("dys", 2), ("ucc", 2),
                            ("ylg", 2)]:
                self.mkring(name, R)

            self.blk = E(nc.Block())
            self.emit_all()
        return nc

    def ln_v_pre(self, v, k, LNS, xin):
        v.tensor_reduce(out=LNS["sum"][:, k:k + 1], in_=xin, axis=AX.X,
                        op=OP.add)
        v.drain()
        v.tensor_scalar(out=LNS["mu"][:, k:k + 1], in0=LNS["sum"][:, k:k + 1],
                        scalar1=1.0 / D, scalar2=None, op0=OP.mult)
        return self.inc("lnv", v.drain())

    def ln_s(self, s, k, LNS, xin, sqscr, lnv_val):
        s.wait_ge(self.sems["lnv"], lnv_val)
        s.activation(out=sqscr, in_=xin, func=ACT.Square,
                     accum_out=LNS["sq"][:, k:k + 1])
        s.drain()
        s.activation(out=LNS["mu2"][:, k:k + 1], in_=LNS["sum"][:, k:k + 1],
                     func=ACT.Square, scale=1.0 / D)
        s.drain()
        s.activation(out=LNS["vpe"][:, k:k + 1], in_=LNS["mu2"][:, k:k + 1],
                     func=ACT.Identity, scale=-1.0, bias=self.EPSCOL[:, 0:1])
        s.drain()
        s.activation(out=LNS["vpe"][:, k:k + 1], in_=LNS["sq"][:, k:k + 1],
                     func=ACT.Identity, scale=1.0 / D,
                     bias=LNS["vpe"][:, k:k + 1])
        s.drain()
        s.activation(out=LNS["std"][:, k:k + 1], in_=LNS["vpe"][:, k:k + 1],
                     func=ACT.Sqrt)
        return self.inc("lns", s.drain())

    def ln_v_apply(self, v, k, LNS, xin, out_ap, lns_val):
        v.wait_ge(self.sems["lns"], lns_val)
        v.reciprocal(out=LNS["rstd"][:, k:k + 1], in_=LNS["std"][:, k:k + 1])
        v.drain()
        return v.tensor_scalar(out=out_ap, in0=xin,
                               scalar1=LNS["mu"][:, k:k + 1],
                               scalar2=LNS["rstd"][:, k:k + 1],
                               op0=OP.subtract, op1=OP.mult)

    def absfold(self, v, tile, slot, rmax):
        ts = self.tmax[:, slot % 8:slot % 8 + 1]
        v.tensor_reduce(out=ts, in_=tile, axis=AX.X, op=OP.max,
                        apply_absolute_value=True)
        v.drain()
        v.tensor_tensor(out=rmax[:], in0=rmax[:], in1=ts, op=OP.max)
        v.drain()

    def emit_all(self):
        self.p0()
        self.phase_y(self.dx0, split_loads=False)
        self.barrier([(self.rmaxA, [(SC_RY, "rinv"), (SC_SY, "s")])])
        self.phase_qy()
        self.phase_ug_gate()
        self.barrier([(self.rmaxB, [(SC_RG, "rinv"), (SC_SG, "s"),
                                    (SC_BG, "bg")])])
        self.phase_ug_up_e1()
        self.barrier([(self.rmaxB, [(SC_RU, "rinv"), (SC_SU, "s")]),
                      (self.rmaxA, [(SC_R2, "rinv"), (SC_S2, "s")])])
        self.phase_e2()
        self.barrier([(self.rmaxA, [(SC_RO, "rinv"), (SC_SO, "s"),
                                    (SC_HSO, "hs")])])
        self.phase_d(self.dx0, self.dxx1, fuse_ln=True)
        self.phase_q()
        self.phase_kvx()
        self.phase_m1()
        self.phase_m2a()
        self.phase_m2b()
        self.phase_m2c()
        self.phase_y(self.dxx2)
        self.barrier([(self.rmaxA, [(SC_RY, "rinv"), (SC_SY, "s")])])
        self.phase_qy()
        self.phase_ug_gate()
        self.barrier([(self.rmaxB, [(SC_RG, "rinv"), (SC_SG, "s"),
                                    (SC_BG, "bg")])])
        self.phase_ug_up_e1()
        self.barrier([(self.rmaxB, [(SC_RU, "rinv"), (SC_SU, "s")]),
                      (self.rmaxA, [(SC_R2, "rinv"), (SC_S2, "s")])])
        self.phase_e2()
        self.barrier([(self.rmaxA, [(SC_RO, "rinv"), (SC_SO, "s"),
                                    (SC_HSO, "hs")])])
        self.phase_d(self.dxx2, self.dout, fuse_ln=False)
        self.finale()

    def p0(self):
        mk = self
        nun = 3

        def fg(g):
            g.dma_start(out=mk.did16, in_=mk.dident).then_inc(mk.sems["dE"], 16)
            mk.cnt["dE"] += 16
            for (srcw, dst) in [(mk.dwq, mk.dwq16), (mk.dwm, mk.dwm16),
                                (mk.dw1, mk.dw116), (mk.dw2, mk.dw216)]:
                g.dma_start(out=dst, in_=srcw).then_inc(mk.sems["dL"], 16)
                mk.cnt["dL"] += 16
            for (srcw, dst) in [(mk.dg1, mk.g1r), (mk.db1, mk.b1r),
                                (mk.dg2, mk.g2r), (mk.db2, mk.b2r)]:
                g.dma_start(out=dst[:], in_=srcw.partition_broadcast(128))\
                    .then_inc(mk.sems["dL"], 16)
                mk.cnt["dL"] += 16
            mk.inc("msg", g.nop())
            mk.w(g, "dE")
            g.dma_start(out=mk.id16[:], in_=mk.did16).then_inc(mk.sems["dI"], 16)
            mk.cnt["dI"] += 16
            g.wait_ge(mk.sems["phe"], 1)
            ic = mk.inc("cc", g.collective_compute(
                "AllGather", OP.bypass,
                replica_groups=[list(range(NCORES))],
                ins=[mk.dwall_sh.opt()], outs=[mk.dwall.opt()]))
            g.wait_ge(mk.sems["cc"], ic)
            mk.inc("wtg", g.drain())

        def fsync(sync):
            for i in range(nun):
                if i >= 2:
                    sync.wait_ge(mk.sems["pha"], i - 1)
                dst = mk.f32(mk.C16, (i % 2) * (2 * DFF), DFF)
                mk.rissue("p0l", i, 2, sync.dma_start(
                    out=dst, in_=mk.dwsh[i * 128:(i + 1) * 128, :]))
            for i in range(nun):
                sync.wait_ge(mk.sems["phb"], 6 * (i + 1))
                stg = mk.A16[:, (i % 2) * 3072:(i % 2) * 3072 + 3072]
                mk.rissue("p0s", i, 2, sync.dma_start(
                    out=mk.dwall_sh[i].rearrange("b p o -> p b o"),
                    in_=stg.rearrange("p (b o) -> p b o", o=128)))
            mk.rtotal(sync, "p0s", 2)
            mk.rtotal(sync, "p0l", 2)
            ins = sync.nop()
            mk.inc("phe", ins)
            mk.inc("msy", sync.nop())

        def fv(v):
            v.memset(mk.MAGCOL[:], MAGIC)
            v.memset(mk.MAGNEG[:], -MAGIC)
            v.memset(mk.EPSCOL[:], LN_EPS)
            v.memset(mk.ones16[:], 1.0)
            v.memset(mk.onesr16[:], 1.0)
            v.memset(mk.dervr[:], 1.0)
            v.memset(mk.gmaxr[:], 1.0)
            v.drain()
            p0lsnap = [0, 0]
            for i in range(nun):
                nb = DFF // 32
                mk.rwait(v, "p0l", i, 2, p0lsnap, 16)
                if i >= 2:
                    v.wait_ge(mk.sems["phd"], 6 * (i - 1))
                wch = mk.f32(mk.C16, (i % 2) * (2 * DFF), DFF)
                bm = mk.f32(mk.C16, 12288, 96)
                v.tensor_reduce(out=bm,
                                in_=wch.rearrange("p (a b) -> p a b", b=32),
                                axis=AX.X, op=OP.max, apply_absolute_value=True)
                v.drain()
                sb = mk.f32(mk.C16, 12480, 96)
                v.tensor_scalar(out=sb, in0=bm, scalar1=1.0 / QW,
                                scalar2=1e-12, op0=OP.mult, op1=OP.max)
                v.drain()
                rs = mk.f32(mk.C16, 12672, 96)
                v.reciprocal(out=rs, in_=sb)
                v.drain()
                t1 = mk.f32(mk.WAR, 0, DFF)
                v.tensor_tensor(out=t1.rearrange("p (a b) -> p a b", b=32),
                                in0=wch.rearrange("p (a b) -> p a b", b=32),
                                in1=rs.unsqueeze(2).broadcast_to([128, 96, 32]),
                                op=OP.mult)
                v.drain()
                v.tensor_scalar(out=t1, in0=t1, scalar1=MAGIC, scalar2=MAGIC,
                                op0=OP.add, op1=OP.subtract)
                v.drain()
                wq16 = mk.WAR[:, 8192 + (i % 2) * DFF:
                              8192 + (i % 2) * DFF + DFF]
                v.tensor_tensor(
                    out=wq16.rearrange("p (a b) -> p a b", b=32),
                    in0=t1.rearrange("p (a b) -> p a b", b=32),
                    in1=sb.unsqueeze(2).broadcast_to([128, 96, 32]),
                    op=OP.mult)
                mk.inc("pha", v.drain())
            mk.inc("msv", v.drain())

        def ft(t):
            t.wait_ge(mk.sems["dI"], 16)
            for i in range(nun):
                t.wait_ge(mk.sems["pha"], i + 1)
                srcw = mk.WAR[:, 8192 + (i % 2) * DFF:
                              8192 + (i % 2) * DFF + DFF]
                for gr in range(6):
                    gg = i * 6 + gr
                    if gg >= 2:
                        t.wait_ge(mk.sems["phb"], gg - 1)
                    last = None
                    for r in range(4):
                        rr = gr * 4 + r
                        last = t.transpose(
                            mk.pbh(gg % 2)[:, r * 128:(r + 1) * 128],
                            srcw[:, rr * 128:(rr + 1) * 128], mk.id16[:])
                    mk.inc("phd", last)
            mk.inc("mst", t.drain())

        def fs(s):
            for i in range(nun):
                for gr in range(6):
                    gg = i * 6 + gr
                    s.wait_ge(mk.sems["phd"], gg + 1)
                    if i >= 2 and gr == 0:
                        slot = i % 2
                        prev = 16 * sum(1 for k in range(i - 1)
                                        if k % 2 == slot)
                        s.wait_ge(mk.sems[f"p0s{slot}"], prev)
                    stg = mk.A16[:, (i % 2) * 3072 + gr * 512:
                                 (i % 2) * 3072 + gr * 512 + 512]
                    s.copy(out=stg, in_=mk.pbh(gg % 2)[:, 0:512])
                    mk.inc("phb", s.drain())
            mk.inc("mss", s.drain())

        self.blk.gpsimd(fg)
        self.blk.sync(fsync)
        self.blk.vector(fv)
        self.blk.tensor(ft)
        self.blk.scalar(fs)
        self.dg_p0 = self.cnt["dL"]
        self.snap()

    def barrier(self, quants):
        mk = self
        nq = len(quants)
        phb0 = self.cnt["phc"]

        def fg(g):
            mk.entry(g, "g")
            for qi, (rm, _) in enumerate(quants):
                g.dma_start(out=mk.ddcol[:, qi:qi + 1], in_=rm[:],
                            allow_slow_non_contiguous=True)\
                    .then_inc(mk.sems["dG"], 16)
                mk.cnt["dG"] += 16
            mk.w(g, "dG")
            g.dma_start(out=mk.srow[0:nq, :],
                        in_=mk.ddcol[:, 0:nq].rearrange("p q -> q p"),
                        allow_slow_non_contiguous=True)\
                .then_inc(mk.sems["dG"], 16)
            mk.cnt["dG"] += 16
            mk.w(g, "dG")
            mk.inc("phe", g.drain())
            g.wait_ge(mk.sems["phc"], phb0 + 1)
            g.dma_start(out=mk.dbm_in[0:1, 0:nq],
                        in_=mk.gmaxr[0:nq, 0:1]).then_inc(mk.sems["dG"], 16)
            mk.cnt["dG"] += 16
            mk.w(g, "dG")
            ic = mk.inc("cc", g.collective_compute(
                "AllReduce", OP.max,
                replica_groups=[list(range(NCORES))],
                ins=[mk.dbm_in[0:1, 0:nq].opt()],
                outs=[mk.dbm_out[0:1, 0:nq].opt()]))
            g.wait_ge(mk.sems["cc"], ic)
            g.dma_start(out=mk.gmaxr[0:1, 8:8 + nq],
                        in_=mk.dbm_out[0:1, 0:nq]).then_inc(mk.sems["dG"], 16)
            mk.cnt["dG"] += 16
            mk.w(g, "dG")
            mk.inc("phe", g.drain())
            g.wait_ge(mk.sems["phc"], phb0 + 2)
            g.dma_start(out=mk.dbscal[0:1, :], in_=mk.dervr[0:1, :])\
                .then_inc(mk.sems["dG"], 16)
            mk.cnt["dG"] += 16
            mk.w(g, "dG")
            g.dma_start(out=mk.SC[:, :],
                        in_=mk.dbscal.partition_broadcast(128))\
                .then_inc(mk.sems["dG"], 16)
            mk.cnt["dG"] += 16
            mk.w(g, "dG")
            mk.inc("bar", g.drain())
            mk.inc("msg", g.drain())

        phe0 = self.cnt["phe"]

        def fv(v):
            mk.entry(v, "v")
            v.wait_ge(mk.sems["phe"], phe0 + 1)
            v.tensor_reduce(out=mk.gmaxr[0:nq, 0:1], in_=mk.srow[0:nq, :],
                            axis=AX.X, op=OP.max)
            v.drain()
            mk.inc("phc", v.drain())
            v.wait_ge(mk.sems["phe"], phe0 + 2)
            for qi, (_, outs) in enumerate(quants):
                mcol = mk.gmaxr[0:1, 8 + qi:8 + qi + 1]
                scol = mk.dervr[0:1, 14:15]
                v.tensor_scalar(out=scol, in0=mcol, scalar1=1.0 / QA,
                                scalar2=1e-12, op0=OP.mult, op1=OP.max)
                v.drain()
                for colx, kind in outs:
                    if kind == "s":
                        v.tensor_scalar(out=mk.dervr[0:1, colx:colx + 1],
                                        in0=scol, scalar1=1.0, scalar2=None,
                                        op0=OP.mult)
                    elif kind == "rinv":
                        v.reciprocal(out=mk.dervr[0:1, colx:colx + 1],
                                     in_=scol)
                    elif kind == "bg":
                        v.tensor_scalar(out=mk.dervr[0:1, colx:colx + 1],
                                        in0=scol, scalar1=-MAGIC,
                                        scalar2=None, op0=OP.mult)
                    elif kind == "hs":
                        v.tensor_scalar(out=mk.dervr[0:1, colx:colx + 1],
                                        in0=scol, scalar1=0.5,
                                        scalar2=None, op0=OP.mult)
                    v.drain()
            mk.inc("phc", v.drain())
            mk.inc("msv", v.drain())

        self.blk.gpsimd(fg)
        self.blk.vector(fv)
        self.snap()

    def phase_y(self, src_dram, split_loads=True):
        mk = self
        nch = self.nch
        T = self.T
        ylsnap = self.rsnap("yl", 4)
        ylgsnap = self.rsnap("ylg", 2)
        lnv0 = self.cnt["lnv"]
        lns0 = self.cnt["lns"]
        pha0 = self.cnt["pha"]
        phb0 = self.cnt["phb"]
        phc0 = self.cnt["phc"]

        step = 2 if split_loads else 1

        def fload(sync):
            mk.entry(sync, "y")
            for ch in range(0, nch, step):
                if ch >= 4:
                    sync.wait_ge(mk.sems["pha"], pha0 + ch - 3)
                    sync.wait_ge(mk.sems["lns"], lns0 + ch - 3)
                xa = mk.f32(mk.B16, (ch % 4) * 2048, 1024)
                mk.rissue("yl", ch if not split_loads else ch // 2,
                          4 if not split_loads else 2, sync.dma_start(
                              out=xa,
                              in_=src_dram[ch * 128:(ch + 1) * 128, :]))
            mk.inc("msy", sync.nop())

        def fgl(g):
            mk.entry(g, "g")
            for ch in range(1, nch, 2):
                if ch >= 4:
                    g.wait_ge(mk.sems["pha"], pha0 + ch - 3)
                    g.wait_ge(mk.sems["lns"], lns0 + ch - 3)
                xa = mk.f32(mk.B16, (ch % 4) * 2048, 1024)
                mk.rissue("ylg", ch // 2, 2, g.dma_start(
                    out=xa, in_=src_dram[ch * 128:(ch + 1) * 128, :]))
            mk.inc("msg", g.drain())

        def ld_rwait(eng, ch, snp, snpg):
            if not split_loads:
                mk.rwait(eng, "yl", ch, 4, snp, 16)
            elif ch % 2 == 0:
                mk.rwait(eng, "yl", ch // 2, 2, snp, 16)
            else:
                mk.rwait(eng, "ylg", ch // 2, 2, snpg, 16)

        def fv(v):
            mk.entry(v, "v")
            v.memset(mk.rmaxA[:], 0.0)
            v.drain()
            ld_rwait(v, 0, ylsnap, ylgsnap)
            mk.ln_v_pre(v, 0, mk.LNa, mk.f32(mk.B16, 0, 1024))
            for ch in range(nch):
                if ch + 1 < nch:
                    ld_rwait(v, ch + 1, ylsnap, ylgsnap)
                    xan = mk.f32(mk.B16, ((ch + 1) % 4) * 2048, 1024)
                    mk.ln_v_pre(v, (ch + 1) % mk.NLN, mk.LNa, xan)
                xa = mk.f32(mk.B16, (ch % 4) * 2048, 1024)
                k = ch % mk.NLN
                y16 = mk.B16[:, 8192 + (ch % 2) * 1024:
                             8192 + (ch % 2) * 1024 + 1024]
                if ch >= 2:
                    v.wait_ge(mk.sems["phb"], phb0 + ch - 1)
                mk.ln_v_apply(v, k, mk.LNa, xa, y16, lns0 + ch + 1)
                mk.inc("pha", v.drain())
                mk.absfold(v, y16, ch, mk.rmaxA)
            mk.inc("msv", v.drain())

        def fs(s):
            mk.entry(s, "s")
            for ch in range(nch):
                ld_rwait(s, ch, ylsnap, ylgsnap)
                xa = mk.f32(mk.B16, (ch % 4) * 2048, 1024)
                sqscr = mk.f32(mk.B16, 10240 + (ch % 2) * 2048, 1024)
                mk.ln_s(s, ch % mk.NLN, mk.LNa, xa, sqscr, lnv0 + ch + 1)
                if ch >= 2:
                    self._y_evac(s, ch - 2, phb0)
            for ch in (nch - 2, nch - 1):
                self._y_evac(s, ch, phb0)
            mk.inc("mss", s.drain())

        def ft(t):
            mk.entry(t, "t")
            for ch in range(nch):
                t.wait_ge(mk.sems["pha"], pha0 + ch + 1)
                if ch >= 2:
                    t.wait_ge(mk.sems["phc"], phc0 + ch - 1)
                y16 = mk.B16[:, 8192 + (ch % 2) * 1024:
                             8192 + (ch % 2) * 1024 + 1024]
                last = None
                for hc in range(HC):
                    last = t.transpose(
                        mk.pdh(ch % 2, 0)[:, hc * 128:(hc + 1) * 128],
                        y16[:, hc * 128:(hc + 1) * 128], mk.id16[:])
                mk.inc("phb", last)
            mk.inc("mst", t.drain())

        self.blk.sync(fload)
        if split_loads:
            self.blk.gpsimd(fgl)
        self.blk.vector(fv)
        self.blk.tensor(ft)
        self.blk.scalar(fs)
        self.snap()

    def _y_evac(self, s, ch, phb0):
        mk = self
        s.wait_ge(mk.sems["phb"], phb0 + ch + 1)
        dst = mk.A16[:, 0:HC * mk.T].rearrange(
            "p (h t) -> p h t", h=HC)[:, :, ch * 128:(ch + 1) * 128]
        s.copy(out=dst, in_=mk.pdh(ch % 2, 0).rearrange(
            "p (h t) -> p h t", h=HC))
        mk.inc("phc", s.drain())

    def phase_qy(self):
        mk = self
        nsl = (HC * self.T) // 2048
        phb0 = self.cnt["phb"]
        bar0 = self.cnt["bar"]

        def fs(s):
            mk.entry(s, "s")
            s.wait_ge(mk.sems["bar"], bar0)
            for j in range(nsl):
                if j >= 2:
                    s.wait_ge(mk.sems["phb"], phb0 + j - 1)
                ap = mk.f32(mk.C16, (j % 2) * 4096, 2048)
                s.activation(out=ap, in_=mk.A16[:, j * 2048:(j + 1) * 2048],
                             func=ACT.Identity, scale=mk.SC[:, SC_RY:SC_RY + 1],
                             bias=mk.MAGCOL[:, 0:1])
                mk.inc("pha", s.drain())
            mk.inc("mss", s.drain())

        pha0 = self.cnt["pha"]

        def fv(v):
            mk.entry(v, "v")
            for j in range(nsl):
                v.wait_ge(mk.sems["pha"], pha0 + j + 1)
                ap = mk.f32(mk.C16, (j % 2) * 4096, 2048)
                v.tensor_scalar(out=mk.B16[:, j * 2048:(j + 1) * 2048],
                                in0=ap, scalar1=MAGIC,
                                scalar2=mk.SC[:, SC_SY:SC_SY + 1],
                                op0=OP.subtract, op1=OP.mult)
                mk.inc("phb", v.drain())
            mk.inc("msv", v.drain())

        self.blk.scalar(fs)
        self.blk.vector(fv)
        self.snap()

    def phase_ug_gate(self):
        mk = self
        T, ntt = self.T, self.ntt
        nhalf = 2 if ntt >= 2 else 1
        tph = ntt // nhalf
        stw = tph * 512
        self._ug_nhalf, self._ug_tph, self._ug_stw = nhalf, tph, stw
        groups = [(oc, hf) for oc in range(OCF) for hf in range(nhalf)]
        ng = len(groups)
        banksets = [
            [self.pd[0][:, 0:512], self.pd[0][:, 512:1024],
             self.pd[1][:, 0:512], self.pd[1][:, 512:1024]][:tph],
            [self.pd[2][:, 0:512], self.pd[2][:, 512:1024],
             self.pb[0][:, :], self.pb[1][:, :]][:tph],
        ]
        self._ug_banksets = banksets
        ugwsnap = self.rsnap("ugw", 4)
        ugssnap = self.rsnap("ugs", 2)
        pha0 = self.cnt["pha"]
        phb0 = self.cnt["phb"]
        phc0 = self.cnt["phc"]
        bar0 = self.cnt["bar"]
        qynsl = (HC * self.T) // 2048

        def fw(sync):
            sync.wait_ge(mk.sems["pha"], pha0 - qynsl + 1)
            sync.wait_ge(mk.sems["wtg"], 1)

            def loadslice(oc):
                if oc >= 2:
                    sync.wait_ge(mk.sems["pha"],
                                 pha0 + (oc - 2 + 1) * nhalf)
                dst = mk.A16[:, (oc % 2) * 1024:(oc % 2) * 1024 + 1024]
                unit = HC + oc // 3
                sub = oc % 3
                mk.rissue("ugw", oc % 2, 4, sync.dma_start(
                    out=dst.rearrange("p (a o) -> p a o", a=HC),
                    in_=mk.dwall[unit, sub * 8:(sub + 1) * 8]
                    .rearrange("a p o -> p a o")))

            def spill(gi):
                oc, hf = groups[gi]
                sync.wait_ge(mk.sems["phb"], phb0 + gi + 1)
                stg = mk.C16[:, 12288 + (gi % 2) * stw:
                             12288 + (gi % 2) * stw + stw]
                mk.rissue("ugs", gi, 2, sync.dma_start(
                    out=mk.dgatesp[oc * 128:(oc + 1) * 128,
                                   hf * stw:(hf + 1) * stw], in_=stg))

            for oc in range(2):
                loadslice(oc)
            for gi in range(ng):
                oc = gi // nhalf
                if gi % nhalf == 0 and oc + 2 < OCF:
                    loadslice(oc + 2)
                if gi >= 1:
                    spill(gi - 1)
            spill(ng - 1)
            mk.inc("msy", sync.nop())

        def ft(t):
            mk.entry(t, "t")
            t.wait_ge(mk.sems["bar"], bar0)
            for gi, (oc, hf) in enumerate(groups):
                mk.rwait(t, "ugw", oc % 2, 4, ugwsnap, 16, occ=oc // 2)
                if gi >= 2:
                    t.wait_ge(mk.sems["phb"], phb0 + gi - 1)
                wb = mk.A16[:, (oc % 2) * 1024:(oc % 2) * 1024 + 1024]
                bs = banksets[gi % 2]
                last = None
                for hc in range(HC):
                    for j in range(tph):
                        tt = hf * tph + j
                        last = t.matmul(
                            bs[j],
                            lhsT=wb[:, hc * 128:(hc + 1) * 128],
                            rhs=mk.B16[:, hc * T + tt * 512:
                                       hc * T + tt * 512 + 512],
                            start=(hc == 0), stop=(hc == HC - 1))
                mk.inc("pha", last)
            mk.inc("mst", t.drain())

        def fs(s):
            mk.entry(s, "s")
            for gi in range(ng):
                s.wait_ge(mk.sems["pha"], pha0 + gi + 1)
                if gi >= 2:
                    mk.rwait(s, "ugs", gi, 2, ugssnap, 16, occ=gi // 2 - 1)
                    s.wait_ge(mk.sems["phc"], phc0 + gi - 1)
                stg = mk.C16[:, 12288 + (gi % 2) * stw:
                             12288 + (gi % 2) * stw + stw]
                bs = banksets[gi % 2]
                last = None
                for j in range(tph):
                    last = s.copy(out=stg[:, j * 512:(j + 1) * 512], in_=bs[j])
                mk.inc("phb", last)
            mk.inc("mss", s.drain())

        def fv(v):
            mk.entry(v, "v")
            v.memset(mk.rmaxB[:], 0.0)
            v.drain()
            for gi in range(ng):
                v.wait_ge(mk.sems["phb"], phb0 + gi + 1)
                stg = mk.C16[:, 12288 + (gi % 2) * stw:
                             12288 + (gi % 2) * stw + stw]
                ts = mk.tmax[:, gi % 8:gi % 8 + 1]
                v.tensor_reduce(out=ts, in_=stg, axis=AX.X, op=OP.max,
                                apply_absolute_value=True)
                v.drain()
                v.tensor_tensor(out=mk.rmaxB[:], in0=mk.rmaxB[:], in1=ts,
                                op=OP.max)
                mk.inc("phc", v.drain())
            mk.inc("msv", v.drain())

        self.blk.sync(fw)
        self.blk.tensor(ft)
        self.blk.scalar(fs)
        self.blk.vector(fv)
        self._ug_gate_phb_end = self.cnt["phb"]
        self.snap()

    def phase_ug_up_e1(self):
        mk = self
        T, ntt, TE = self.T, self.ntt, self.TE
        nhalf, tph, stw = self._ug_nhalf, self._ug_tph, self._ug_stw
        groups = [(oc, hf) for oc in range(OCF) for hf in range(nhalf)]
        ng = len(groups)
        nsl = (DFF // 128) * (T // TE)
        assert nsl == ng, (nsl, ng)
        npc = T // TE
        banksets = self._ug_banksets
        gphb_end = self._ug_gate_phb_end
        ugwsnap = self.rsnap("ugw", 4)
        ugssnap = self.rsnap("ugs", 2)
        gatesnap = [ugssnap[0] - 16 * (ng // 2), ugssnap[1] - 16 * (ng // 2)]
        e1isnap = self.rsnap("e1i", 3)
        e1osnap = self.rsnap("e1o", 2)
        fua0 = self.cnt["fua"]
        fub0 = self.cnt["fub"]
        fuc0 = self.cnt["fuc"]
        fud0 = self.cnt["fud"]
        fue0 = self.cnt["fue"]
        fuf0 = self.cnt["fuf"]
        bar0 = self.cnt["bar"]
        EB = 4096

        def pos(j):
            pc, jj = divmod(j, npc)
            return pc * 128, jj * TE

        def fw(sync):
            mk.entry(sync, "y")

            def loadslice(oc):
                if oc >= 2:
                    sync.wait_ge(mk.sems["fua"],
                                 fua0 + (oc - 2 + 1) * nhalf)
                dst = mk.A16[:, 2048 + (oc % 2) * 1024:
                             2048 + (oc % 2) * 1024 + 1024]
                mk.rissue("ugw", 2 + oc % 2, 4, sync.dma_start(
                    out=dst.rearrange("p (a o) -> p a o", a=HC),
                    in_=mk.dwall[oc // 3, (oc % 3) * 8:(oc % 3) * 8 + 8]
                    .rearrange("a p o -> p a o")))

            def upspill(gi):
                oc, hf = groups[gi]
                sync.wait_ge(mk.sems["fub"], fub0 + gi + 1)
                stg = mk.C16[:, 12288 + (gi % 2) * stw:
                             12288 + (gi % 2) * stw + stw]
                mk.rissue("ugs", gi, 2, sync.dma_start(
                    out=mk.dupsp[oc * 128:(oc + 1) * 128,
                                 hf * stw:(hf + 1) * stw], in_=stg))

            def e1load(j):
                if j >= 3:
                    sync.wait_ge(mk.sems["fud"], fud0 + j - 2)
                r0, c0 = pos(j)
                mk.rwait(sync, "ugs", j, 2, gatesnap, 16, occ=j // 2)
                gin = mk.C16[:, (j % 3) * TE:(j % 3) * TE + TE]
                mk.rissue("e1i", j, 3, sync.dma_start(
                    out=gin, in_=mk.dgatesp[r0:r0 + 128, c0:c0 + TE]))

            def e1spill(j):
                sync.wait_ge(mk.sems["fue"], fue0 + j + 1)
                r0, c0 = pos(j)
                gout = mk.C16[:, 3 * TE + (j % 2) * TE:
                              3 * TE + (j % 2) * TE + TE]
                mk.rissue("e1o", j, 2, sync.dma_start(
                    out=mk.dg2sp[r0:r0 + 128, c0:c0 + TE], in_=gout))

            for oc in range(2):
                loadslice(oc)
            e1load(0)
            e1load(1)
            for gi in range(ng):
                oc = gi // nhalf
                if gi % nhalf == 0 and oc + 2 < OCF:
                    loadslice(oc + 2)
                if gi + 2 < nsl:
                    e1load(gi + 2)
                upspill(gi)
                if gi >= 1:
                    e1spill(gi - 1)
            e1spill(nsl - 1)
            mk.rtotal(sync, "ugs", 2)
            mk.rtotal(sync, "e1o", 2)
            mk.inc("msy", sync.nop())

        def ft(t):
            mk.entry(t, "t")
            for gi, (oc, hf) in enumerate(groups):
                mk.rwait(t, "ugw", 2 + oc % 2, 4, ugwsnap, 16, occ=oc // 2)
                if gi < 2:
                    t.wait_ge(mk.sems["phb"], gphb_end)
                else:
                    t.wait_ge(mk.sems["fub"], fub0 + gi - 1)
                wb = mk.A16[:, 2048 + (oc % 2) * 1024:
                            2048 + (oc % 2) * 1024 + 1024]
                bs = banksets[gi % 2]
                last = None
                for hc in range(HC):
                    for j in range(tph):
                        tt = hf * tph + j
                        last = t.matmul(
                            bs[j],
                            lhsT=wb[:, hc * 128:(hc + 1) * 128],
                            rhs=mk.B16[:, hc * T + tt * 512:
                                       hc * T + tt * 512 + 512],
                            start=(hc == 0), stop=(hc == HC - 1))
                mk.inc("fua", last)
            mk.inc("mst", t.drain())

        def fs(s):
            mk.entry(s, "s")
            s.wait_ge(mk.sems["bar"], bar0)
            for j in range(ng):
                mk.rwait(s, "e1i", j, 3, e1isnap, 16)
                if j >= 2:
                    s.wait_ge(mk.sems["fuf"], fuf0 + 3 * (j - 2) + 1)
                gin = mk.C16[:, (j % 3) * TE:(j % 3) * TE + TE]
                ap = mk.f32(mk.A16, EB + (j % 2) * 2 * TE, TE)
                s.activation(out=ap, in_=gin, func=ACT.Identity,
                             scale=mk.SC[:, SC_RG:SC_RG + 1],
                             bias=mk.MAGCOL[:, 0:1])
                mk.inc("fud", s.drain())
                sg = mk.A16[:, EB + 4 * TE + (j % 2) * TE:
                            EB + 4 * TE + (j % 2) * TE + TE]
                if j >= 2:
                    s.wait_ge(mk.sems["fuf"], fuf0 + 3 * (j - 2) + 2)
                s.activation(out=sg, in_=ap, func=ACT.Sigmoid,
                             scale=mk.SC[:, SC_SG:SC_SG + 1],
                             bias=mk.SC[:, SC_BG:SC_BG + 1])
                mk.inc("fuc", s.drain())
                s.wait_ge(mk.sems["fua"], fua0 + j + 1)
                if j >= 2:
                    mk.rwait(s, "ugs", j, 2, ugssnap, 16, occ=j // 2 - 1)
                    s.wait_ge(mk.sems["fue"], fue0 + j - 1)
                else:
                    s.wait_ge(mk.sems[f"ugs{j % 2}"], ugssnap[j % 2])
                stg = mk.C16[:, 12288 + (j % 2) * stw:
                             12288 + (j % 2) * stw + stw]
                bs = banksets[j % 2]
                last = None
                for jj in range(tph):
                    last = s.copy(out=stg[:, jj * 512:(jj + 1) * 512],
                                  in_=bs[jj])
                mk.inc("fub", last)
            mk.inc("mss", s.drain())

        def fv(v):
            mk.entry(v, "v")
            for k in range(8):
                v.memset(mk.WAR[:, k * 2048:(k + 1) * 2048],
                         -60000.0 if k < 2 or 4 <= k < 6 else 60000.0)
            v.drain()
            for j in range(ng):
                p = j % 2
                qg = mk.A16[:, EB + 12 * TE + (j % 2) * TE:
                            EB + 12 * TE + (j % 2) * TE + TE]
                v.wait_ge(mk.sems["fuf"], fuf0 + 3 * (j + 1))
                sigq = mk.A16[:, EB + 10 * TE + (j % 2) * TE:
                              EB + 10 * TE + (j % 2) * TE + TE]
                gout = mk.C16[:, 3 * TE + (j % 2) * TE:
                              3 * TE + (j % 2) * TE + TE]
                if j >= 2:
                    mk.rwait(v, "e1o", j, 2, e1osnap, 16, occ=j // 2 - 1)
                v.tensor_tensor(out=gout, in0=qg, in1=sigq, op=OP.mult)
                v.drain()
                v.wait_ge(mk.sems["fub"], fub0 + j + 1)
                stg = mk.C16[:, 12288 + (j % 2) * stw:
                             12288 + (j % 2) * stw + stw]
                gmx = mk.WAR[:, (4 + p) * 2048:(4 + p) * 2048 + TE]
                gmn = mk.WAR[:, (6 + p) * 2048:(6 + p) * 2048 + TE]
                umx = mk.WAR[:, p * 2048:p * 2048 + stw]
                umn = mk.WAR[:, (2 + p) * 2048:(2 + p) * 2048 + stw]
                v.tensor_tensor(out=gmx, in0=gmx, in1=gout, op=OP.max)
                v.tensor_tensor(out=gmn, in0=gmn, in1=gout, op=OP.min)
                v.tensor_tensor(out=umx, in0=umx, in1=stg, op=OP.max)
                v.tensor_tensor(out=umn, in0=umn, in1=stg, op=OP.min)
                mk.inc("fue", v.drain())
            for k in range(4):
                v.tensor_reduce(out=mk.tmax[:, 2 * k:2 * k + 1],
                                in_=mk.WAR[:, 2 * k * 2048:
                                           2 * k * 2048 + 2048],
                                axis=AX.X,
                                op=OP.max if k in (0, 2) else OP.min,
                                negate=(k in (1, 3)))
                v.tensor_reduce(out=mk.tmax[:, 2 * k + 1:2 * k + 2],
                                in_=mk.WAR[:, (2 * k + 1) * 2048:
                                           (2 * k + 1) * 2048 + 2048],
                                axis=AX.X,
                                op=OP.max if k in (0, 2) else OP.min,
                                negate=(k in (1, 3)))
            v.drain()
            v.tensor_tensor(out=mk.tmax[:, 0:1], in0=mk.tmax[:, 0:1],
                            in1=mk.tmax[:, 1:2], op=OP.max)
            v.tensor_tensor(out=mk.tmax[:, 2:3], in0=mk.tmax[:, 2:3],
                            in1=mk.tmax[:, 3:4], op=OP.max)
            v.tensor_tensor(out=mk.tmax[:, 4:5], in0=mk.tmax[:, 4:5],
                            in1=mk.tmax[:, 5:6], op=OP.max)
            v.tensor_tensor(out=mk.tmax[:, 6:7], in0=mk.tmax[:, 6:7],
                            in1=mk.tmax[:, 7:8], op=OP.max)
            v.drain()
            v.tensor_tensor(out=mk.rmaxB[:], in0=mk.tmax[:, 0:1],
                            in1=mk.tmax[:, 2:3], op=OP.max)
            v.tensor_tensor(out=mk.rmaxA[:], in0=mk.tmax[:, 4:5],
                            in1=mk.tmax[:, 6:7], op=OP.max)
            mk.inc("msv", v.drain())

        def fg(g):
            mk.entry(g, "g")
            for j in range(ng):
                g.wait_ge(mk.sems["fud"], fud0 + j + 1)
                if j >= 2:
                    g.wait_ge(mk.sems["fue"], fue0 + j - 1)
                ap = mk.f32(mk.A16, EB + (j % 2) * 2 * TE, TE)
                qg = mk.A16[:, EB + 12 * TE + (j % 2) * TE:
                            EB + 12 * TE + (j % 2) * TE + TE]
                g.tensor_scalar(out=qg, in0=ap, scalar1=-MAGIC,
                                scalar2=mk.SC[:, SC_SG:SC_SG + 1],
                                op0=OP.add, op1=OP.mult)
                mk.inc("fuf", g.drain())
                g.wait_ge(mk.sems["fuc"], fuc0 + j + 1)
                sg = mk.A16[:, EB + 4 * TE + (j % 2) * TE:
                            EB + 4 * TE + (j % 2) * TE + TE]
                tt_ = mk.f32(mk.A16, EB + 6 * TE + (j % 2) * 2 * TE, TE)
                g.tensor_scalar(out=tt_, in0=sg, scalar1=65536.0, scalar2=P23,
                                op0=OP.mult, op1=OP.add)
                mk.inc("fuf", g.drain())
                sigq = mk.A16[:, EB + 10 * TE + (j % 2) * TE:
                              EB + 10 * TE + (j % 2) * TE + TE]
                g.tensor_scalar(out=sigq, in0=tt_, scalar1=1.0 / 65536.0,
                                scalar2=-128.0, op0=OP.mult, op1=OP.add)
                mk.inc("fuf", g.drain())
            mk.inc("msg", g.drain())

        self.blk.sync(fw)
        self.blk.tensor(ft)
        self.blk.scalar(fs)
        self.blk.vector(fv)
        self.blk.gpsimd(fg)
        self.snap()

    def phase_e2(self):
        mk = self
        T, TE = self.T, self.TE
        nsl = (DFF // 128) * (T // TE)
        npc = T // TE
        e2isnap = self.rsnap("e2i", 3)
        e2osnap = self.rsnap("e2o", 2)
        uccsnap = self.rsnap("ucc", 2)
        pha0 = self.cnt["pha"]
        phb0 = self.cnt["phb"]
        bar0 = self.cnt["bar"]

        def pos(j):
            pc, jj = divmod(j, npc)
            return pc * 128, jj * TE

        def fld(sync):
            mk.entry(sync, "y")
            for j in range(nsl):
                if j >= 3:
                    sync.wait_ge(mk.sems["pha"], pha0 + 2 * (j - 2))
                r0, c0 = pos(j)
                g2in = mk.C16[:, (j % 3) * TE:(j % 3) * TE + TE]
                mk.rissue("e2i", j, 3, sync.dma_start(
                    out=g2in, in_=mk.dg2sp[r0:r0 + 128, c0:c0 + TE]))
            mk.inc("msy", sync.nop())

        def fs(s):
            mk.entry(s, "s")
            s.wait_ge(mk.sems["bar"], bar0)
            for j in range(2):
                r0, c0 = pos(j)
                uin = mk.C16[:, 3 * TE + (j % 3) * TE:
                             3 * TE + (j % 3) * TE + TE]
                mk.rissue("ucc", j, 2, s.dma_start(
                    out=uin, in_=mk.dupsp[r0:r0 + 128, c0:c0 + TE]))
            for j in range(nsl):
                mk.rwait(s, "e2i", j, 3, e2isnap, 16)
                mk.rwait(s, "ucc", j, 2, uccsnap, 16)
                if j >= 2:
                    s.wait_ge(mk.sems["phb"], phb0 + j - 1)
                g2in = mk.C16[:, (j % 3) * TE:(j % 3) * TE + TE]
                uin = mk.C16[:, 3 * TE + (j % 3) * TE:
                             3 * TE + (j % 3) * TE + TE]
                g2p = mk.f32(mk.A16, (j % 2) * 2 * TE, TE)
                s.activation(out=g2p, in_=g2in, func=ACT.Identity,
                             scale=mk.SC[:, SC_R2:SC_R2 + 1],
                             bias=mk.MAGCOL[:, 0:1])
                up = mk.f32(mk.A16, 4 * TE + (j % 2) * 2 * TE, TE)
                s.activation(out=up, in_=uin, func=ACT.Identity,
                             scale=mk.SC[:, SC_RU:SC_RU + 1],
                             bias=mk.MAGCOL[:, 0:1])
                mk.inc("pha", s.drain(), 2)
                if j + 2 < nsl:
                    jn = j + 2
                    r0, c0 = pos(jn)
                    uinn = mk.C16[:, 3 * TE + (jn % 3) * TE:
                                  3 * TE + (jn % 3) * TE + TE]
                    mk.rissue("ucc", jn, 2, s.dma_start(
                        out=uinn, in_=mk.dupsp[r0:r0 + 128, c0:c0 + TE]))
            mk.inc("mss", s.drain())

        phg0 = self.cnt["phg"]

        def spill2(g, j):
            g.wait_ge(mk.sems["phb"], phb0 + j + 1)
            r0, c0 = pos(j)
            oout = mk.B16[:, 4 * TE + (j % 2) * TE:4 * TE + (j % 2) * TE + TE]
            mk.rissue("e2o", j, 2, g.dma_start(
                out=mk.dosp[r0:r0 + 128, c0:c0 + TE], in_=oout))

        def fg(g):
            mk.entry(g, "g")
            for j in range(nsl):
                g.wait_ge(mk.sems["pha"], pha0 + 2 * (j + 1))
                if j >= 2:
                    g.wait_ge(mk.sems["phb"], phb0 + j - 1)
                up = mk.f32(mk.A16, 4 * TE + (j % 2) * 2 * TE, TE)
                uq = mk.B16[:, 2 * TE + (j % 2) * TE:2 * TE + (j % 2) * TE + TE]
                g.tensor_scalar(out=uq, in0=up, scalar1=-MAGIC,
                                scalar2=mk.SC[:, SC_SU:SC_SU + 1],
                                op0=OP.add, op1=OP.mult)
                mk.inc("phg", g.drain())
                if j >= 1:
                    spill2(g, j - 1)
            spill2(g, nsl - 1)
            mk.rtotal(g, "e2o", 2)
            mk.inc("msg", g.drain())

        def fv(v):
            mk.entry(v, "v")
            v.memset(mk.WAR[:, 8192:8192 + 2048], -60000.0)
            v.memset(mk.WAR[:, 10240:10240 + 2048], 60000.0)
            v.drain()
            for j in range(nsl):
                v.wait_ge(mk.sems["pha"], pha0 + 2 * (j + 1))
                g2p = mk.f32(mk.A16, (j % 2) * 2 * TE, TE)
                qg2 = mk.B16[:, (j % 2) * TE:(j % 2) * TE + TE]
                v.tensor_scalar(out=qg2, in0=g2p, scalar1=MAGIC,
                                scalar2=mk.SC[:, SC_S2:SC_S2 + 1],
                                op0=OP.subtract, op1=OP.mult)
                v.drain()
                v.wait_ge(mk.sems["phg"], phg0 + j + 1)
                uq = mk.B16[:, 2 * TE + (j % 2) * TE:2 * TE + (j % 2) * TE + TE]
                oout = mk.B16[:, 4 * TE + (j % 2) * TE:
                              4 * TE + (j % 2) * TE + TE]
                if j >= 2:
                    mk.rwait(v, "e2o", j, 2, e2osnap, 16, occ=j // 2 - 1)
                v.tensor_tensor(out=oout, in0=qg2, in1=uq, op=OP.mult)
                v.drain()
                v.tensor_tensor(out=mk.WAR[:, 8192:8192 + TE],
                                in0=mk.WAR[:, 8192:8192 + TE], in1=oout,
                                op=OP.max)
                v.tensor_tensor(out=mk.WAR[:, 10240:10240 + TE],
                                in0=mk.WAR[:, 10240:10240 + TE], in1=oout,
                                op=OP.min)
                mk.inc("phb", v.drain())
            v.tensor_reduce(out=mk.tmax[:, 0:1], in_=mk.WAR[:, 8192:8192 + 2048],
                            axis=AX.X, op=OP.max)
            v.tensor_reduce(out=mk.tmax[:, 1:2],
                            in_=mk.WAR[:, 10240:10240 + 2048],
                            axis=AX.X, op=OP.min, negate=True)
            v.drain()
            v.tensor_tensor(out=mk.rmaxA[:], in0=mk.tmax[:, 0:1],
                            in1=mk.tmax[:, 1:2], op=OP.max)
            mk.inc("msv", v.drain())

        self.blk.sync(fld)
        self.blk.scalar(fs)
        self.blk.gpsimd(fg)
        self.blk.vector(fv)
        self.snap()

    def phase_d(self, xin_dram, xout_dram, fuse_ln):
        mk = self
        T, nch = self.T, self.nch
        npair = nch // 2
        NW = T // 512
        NSLAB = NW * OCF
        QRING = 28
        ORING = 6
        bar0 = self.cnt["bar"]
        dw0 = self.cnt["dW"]
        dqrsnap = self.rsnap("dqr", ORING)
        dxlsnap = self.rsnap("dxl", 4)
        dxssnap = self.rsnap("dxs", 2)
        dyssnap = self.rsnap("dys", 2)
        e2o_fin = [self.cnt["e2o0"], self.cnt["e2o1"]]
        pha0 = self.cnt["pha"]
        phb0 = self.cnt["phb"]
        phg0 = self.cnt["phg"]
        phc0 = self.cnt["phc"]
        phd0 = self.cnt["phd"]
        phe0 = self.cnt["phe"]
        phf0 = self.cnt["phf"]
        lnv0 = self.cnt["lnv"]
        lns0 = self.cnt["lns"]
        gv = dict(self.gatev)

        def slab_wait(eng, g):
            w, hc = divmod(g, OCF)
            if hc == OCF - 1:
                eng.wait_ge(mk.sems["phc"], phc0 + 2 * w + 2)
            else:
                eng.wait_ge(mk.sems["phg"], phg0 + w * (OCF - 1) + hc + 1)

        def chunk_psum(ch):
            k = ch // 2
            if k % 2 == 0:
                p = mk.pd[ch % 2]
                return ([p[:, 0:512], p[:, 512:1024]], p[:, :],
                        mk.pdh(ch % 2, 0))
            if ch % 2 == 0:
                p = mk.pd[2]
                return ([p[:, 0:512], p[:, 512:1024]], p[:, :], mk.pdh(2, 0))
            return ([mk.pb[0][:, :], mk.pb[1][:, :]], None, mk.pbh(0))

        def xa_ap(ch):
            return mk.f32(mk.C16, (ch % 4) * 2048, 1024)

        def xo_ap(ch):
            return mk.f32(mk.C16, 8192 + (ch % 2) * 2048, 1024)

        def y2_ap(ch):
            return mk.C16[:, 12288 + (ch % 2) * 1024:
                          12288 + (ch % 2) * 1024 + 1024]

        def qon_ap(g):
            return mk.WAR[:, 2048 + (g % QRING) * 512:
                          2048 + (g % QRING) * 512 + 512]

        def oload(sync, g):
            w, hc = divmod(g, OCF)
            if g >= ORING:
                sync.wait_ge(mk.sems["pha"], pha0 + g - (ORING - 1))
            osl = mk.B16[:, 28672 + (g % ORING) * 512:
                         28672 + (g % ORING) * 512 + 512]
            mk.rissue("dqr", g, ORING, sync.dma_start(
                out=osl, in_=mk.dosp[hc * 128:(hc + 1) * 128,
                                     w * 512:(w + 1) * 512]))

        def fsy(sync):
            sync.wait_ge(mk.sems["wtg"], 1)
            for ic in range(12, 24):
                sync.dma_start(
                    out=mk.B16[:, ic * D:(ic + 1) * D].rearrange(
                        "p (u o) -> p u o", o=128),
                    in_=mk.dwall[2 * HC:3 * HC, ic].rearrange(
                        "u p o -> p u o")).then_inc(mk.sems["dW"], 16)
                mk.cnt["dW"] += 16
            for r in range(2):
                sync.wait_ge(mk.sems[f"e2o{r}"], e2o_fin[r])
            for g in range(ORING):
                oload(sync, g)
            sync.wait_ge(mk.sems["msv"], gv["v"])
            sync.wait_ge(mk.sems["msg"], gv["g"])
            sync.wait_ge(mk.sems["mss"], gv["s"])
            for ic in range(12):
                sync.dma_start(
                    out=mk.B16[:, ic * D:(ic + 1) * D].rearrange(
                        "p (u o) -> p u o", o=128),
                    in_=mk.dwall[2 * HC:3 * HC, ic].rearrange(
                        "u p o -> p u o")).then_inc(mk.sems["dW"], 16)
                mk.cnt["dW"] += 16

            def xa_issue(ch):
                if not 2 <= ch < nch:
                    return
                if ch >= 4:
                    sync.wait_ge(mk.sems["phf"], phf0 + ch - 3)
                mk.rissue("dxl", ch, 4, sync.dma_start(
                    out=xa_ap(ch), in_=xin_dram[ch * 128:(ch + 1) * 128, :]))

            for ch in (0, 1):
                mk.rissue("dxl", ch, 4, sync.dma_start(
                    out=xa_ap(ch), in_=xin_dram[ch * 128:(ch + 1) * 128, :]))
            for g in range(ORING, NSLAB):
                w, hc = divmod(g, OCF)
                oload(sync, g)
                if hc == 0:
                    xa_issue(4 * w - 2)
                elif hc == 2:
                    xa_issue(4 * w - 1)
                elif hc == 16:
                    xa_issue(4 * w)
                elif hc == 18:
                    xa_issue(4 * w + 1)
            xa_issue(nch - 2)
            xa_issue(nch - 1)
            mk.rtotal(sync, "dqr", ORING)
            mk.rtotal(sync, "dxl", 4)
            mk.inc("msy", sync.nop())

        def fg(g):
            mk.entry(g, "g")
            g.wait_ge(mk.sems["bar"], bar0)
            for gi in range(NSLAB):
                mk.rwait(g, "dqr", gi, ORING, dqrsnap, 16)
                if gi >= 2:
                    g.wait_ge(mk.sems["phb"], phb0 + gi - 1)
                osl = mk.B16[:, 28672 + (gi % ORING) * 512:
                             28672 + (gi % ORING) * 512 + 512]
                op_ = mk.f32(mk.WAR, (gi % 2) * 1024, 512)
                g.tensor_scalar(out=op_, in0=osl,
                                scalar1=mk.SC[:, SC_RO:SC_RO + 1],
                                scalar2=MAGIC, op0=OP.mult, op1=OP.add)
                mk.inc("pha", g.drain())
            mk.inc("msg", g.drain())

        def do_lns(s, ch):
            mk.ln_s(s, ch % mk.NLN, mk.LNa, xo_ap(ch),
                    mk.f32(mk.B16, 24576 + (ch % 2) * 2048, 1024),
                    lnv0 + ch + 1)

        def st_issue(s, ch):
            s.wait_ge(mk.sems["phf"], phf0 + ch + 1)
            mk.rissue("dxs", ch, 2, s.dma_start(
                out=xout_dram[ch * 128:(ch + 1) * 128, :], in_=xo_ap(ch)))
            if fuse_ln:
                mk.rissue("dys", ch, 2, s.dma_start(
                    out=mk.dy2sp[ch * 128:(ch + 1) * 128, :], in_=y2_ap(ch)))

        def fs(s):
            mk.entry(s, "s")
            sched = {}

            def put(slot, kind, ch):
                if 0 <= ch < nch and (fuse_ln or kind == "st"):
                    sched.setdefault(slot, []).append((kind, ch))

            for w in range(1, NW):
                base = w * OCF
                put(base + 0, "st", 4 * w - 6)
                put(base + 2, "st", 4 * w - 5)
                put(base + 8, "ln", 4 * w - 4)
                put(base + 12, "ln", 4 * w - 3)
                put(base + 14, "st", 4 * w - 4)
                put(base + 16, "st", 4 * w - 3)
                put(base + 20, "ln", 4 * w - 2)
                put(base + 22, "ln", 4 * w - 1)
            for g in range(NSLAB):
                s.wait_ge(mk.sems["pha"], pha0 + g + 1)
                if g >= QRING:
                    slab_wait(s, g - QRING)
                op_ = mk.f32(mk.WAR, (g % 2) * 1024, 512)
                s.activation(out=qon_ap(g), in_=op_, func=ACT.Identity,
                             scale=1.0, bias=mk.MAGNEG[:, 0:1])
                mk.inc("phb", s.drain())
                for kind, ch in sched.get(g, []):
                    if kind == "ln":
                        do_lns(s, ch)
                    else:
                        st_issue(s, ch)
            for ch in range(nch - 4, nch):
                if ch - 2 >= 0:
                    st_issue(s, ch - 2)
                if fuse_ln:
                    do_lns(s, ch)
            st_issue(s, nch - 2)
            st_issue(s, nch - 1)
            mk.rtotal(s, "dxs", 2)
            if fuse_ln:
                mk.rtotal(s, "dys", 2)
            mk.inc("mss", s.drain())

        def emit_tr(t, ch):
            t.wait_ge(mk.sems["phf"], phf0 + ch + 1)
            tgt = chunk_psum(ch)[2]
            y2t = y2_ap(ch)
            last = None
            for hcx in range(HC):
                last = t.transpose(tgt[:, hcx * 128:(hcx + 1) * 128],
                                   y2t[:, hcx * 128:(hcx + 1) * 128],
                                   mk.id16[:])
            mk.inc("phd", last)

        def ft(t):
            mk.entry(t, "t")
            for k in range(npair):
                w = k // 2
                for hc in range(OCF):
                    g = w * OCF + hc
                    t.wait_ge(mk.sems["phb"], phb0 + g + 1)
                    if k == 0 and hc == 0:
                        t.wait_ge(mk.sems["dW"], dw0 + 16 * OCF)
                    elif hc == 0:
                        if k >= 2:
                            if fuse_ln:
                                t.wait_ge(mk.sems["phe"],
                                          phe0 + 2 * (k - 2) + 2)
                            else:
                                t.wait_ge(mk.sems["phf"],
                                          phf0 + 2 * (k - 2) + 2)
                    slab = qon_ap(g)
                    off = (k % 2) * 256
                    last = None
                    for ci in range(2):
                        halves = chunk_psum(2 * k + ci)[0]
                        lt = slab[:, off + ci * 128:off + ci * 128 + 128]
                        for half in range(2):
                            last = t.matmul(
                                halves[half], lhsT=lt,
                                rhs=mk.B16[:, hc * D + half * 512:
                                           hc * D + half * 512 + 512],
                                start=(hc == 0), stop=(hc == OCF - 1))
                    if hc == OCF - 1:
                        mk.inc("phc", last)
                    elif k % 2 == 1:
                        mk.inc("phg", last)
                    if fuse_ln and k >= 1:
                        if hc == 14:
                            emit_tr(t, 2 * (k - 1))
                        elif hc == 18:
                            emit_tr(t, 2 * (k - 1) + 1)
            if fuse_ln:
                emit_tr(t, nch - 2)
                emit_tr(t, nch - 1)
            mk.inc("mst", t.drain())

        def fm_copy(v, ch):
            v.wait_ge(mk.sems["phd"], phd0 + ch + 1)
            dst = mk.A16[:, 0:HC * T].rearrange(
                "p (h t) -> p h t", h=HC)[:, :, ch * 128:(ch + 1) * 128]
            v.tensor_copy(out=dst,
                          in_=chunk_psum(ch)[2].rearrange(
                              "p (h t) -> p h t", h=HC))
            mk.inc("phe", v.drain())

        def fvt(v):
            mk.entry(v, "v")
            for ch in range(nch):
                k = ch // 2
                v.wait_ge(mk.sems["phc"], phc0 + k + 1)
                mk.rwait(v, "dxl", ch, 4, dxlsnap, 16)
                halves, full, _ = chunk_psum(ch)
                xa = xa_ap(ch)
                xo = xo_ap(ch)
                if ch >= 2:
                    mk.rwait(v, "dxs", ch, 2, dxssnap, 16, occ=ch // 2 - 1)
                if full is not None:
                    v.scalar_tensor_tensor(
                        out=xo, in0=full, scalar=mk.SC[:, SC_HSO:SC_HSO + 1],
                        in1=xa, op0=OP.mult, op1=OP.add)
                else:
                    for j in range(2):
                        v.scalar_tensor_tensor(
                            out=xo[:, j * 512:(j + 1) * 512], in0=halves[j],
                            scalar=mk.SC[:, SC_HSO:SC_HSO + 1],
                            in1=xa[:, j * 512:(j + 1) * 512],
                            op0=OP.mult, op1=OP.add)
                v.drain()
                if fuse_ln:
                    mk.ln_v_pre(v, ch % mk.NLN, mk.LNa, xo)
                    if ch >= 1:
                        fm_copy(v, ch - 1)
                    y2t = y2_ap(ch)
                    if ch >= 2:
                        v.wait_ge(mk.sems["phd"], phd0 + ch - 1)
                        mk.rwait(v, "dys", ch, 2, dyssnap, 16,
                                 occ=ch // 2 - 1)
                    mk.ln_v_apply(v, ch % mk.NLN, mk.LNa, xo, y2t,
                                  lns0 + ch + 1)
                mk.inc("phf", v.drain())
            if fuse_ln:
                fm_copy(v, nch - 1)
            mk.inc("msv", v.drain())

        self.blk.sync(fsy)
        self.blk.gpsimd(fg)
        self.blk.scalar(fs)
        self.blk.tensor(ft)
        self.blk.vector(fvt)
        self.snap()

    def phase_q(self):
        mk = self
        T, nch = self.T, self.nch
        dw0 = self.cnt["dW"]
        pha0 = self.cnt["pha"]
        phb0 = self.cnt["phb"]
        phc0 = self.cnt["phc"]
        phd0 = self.cnt["phd"]
        phe0 = self.cnt["phe"]
        kvb0 = self.cnt["kvb"]

        def fwl(sync):
            mk.entry(sync, "y")
            sync.wait_ge(mk.sems["dL"], mk.dg_p0)
            sync.dma_start(
                out=mk.WAR[:, 0:8192].rearrange("p (a o) -> p a o", a=HC),
                in_=mk.dwq16.rearrange("(a p) o -> p a o", p=128))\
                .then_inc(mk.sems["dW"], 16)
            mk.cnt["dW"] += 16
            mk.inc("msy", sync.nop())

        def emit_q(t, tc):
            if tc >= 2:
                t.wait_ge(mk.sems["kvb"], kvb0 + tc - 1)
                t.wait_ge(mk.sems["phc"], phc0 + tc - 1)
            last = None
            for hc in range(HC):
                for half in range(2):
                    last = t.matmul(
                        mk.pd[tc % 2][:, half * 512:(half + 1) * 512],
                        lhsT=mk.A16[:, hc * T + tc * 128:
                                    hc * T + tc * 128 + 128],
                        rhs=mk.WAR[:, hc * 1024 + half * 512:
                                   hc * 1024 + half * 512 + 512],
                        start=(hc == 0), stop=(hc == HC - 1))
            mk.inc("pha", last)

        def emit_kvtr(t, tc):
            t.wait_ge(mk.sems["phb"], phb0 + tc + 1)
            ftm = mk.C16[:, 8192 + (tc % 3) * 1024:
                         8192 + (tc % 3) * 1024 + 1024]
            vt = mk.C16[:, 12288 + (tc % 3) * 1024:
                        12288 + (tc % 3) * 1024 + 1024]
            for h in range(NH):
                t.matmul(
                    mk.pb[h // 4][:, (h % 4) * 128:(h % 4) * 128 + 128],
                    lhsT=ftm[:, h * 128:(h + 1) * 128],
                    rhs=vt[:, h * 128:(h + 1) * 128],
                    start=(tc == 0 and h % 4 == 0),
                    stop=(tc == nch - 1 and h % 4 == 3))
                t.matmul(
                    mk.pd[2][:, h:h + 1],
                    lhsT=ftm[:, h * 128:(h + 1) * 128],
                    rhs=mk.ones16[:, 0:1],
                    start=(tc == 0 and h == 0),
                    stop=(tc == nch - 1 and h == NH - 1))
            for wave in range(2):
                w = 2 * tc + wave
                if w >= 1:
                    t.wait_ge(mk.sems["phe"], phe0 + w)
                last = None
                for r in range(4):
                    hh = wave * 4 + r
                    last = t.transpose(
                        mk.pdh(2, 1)[:, r * 128:(r + 1) * 128],
                        ftm[:, hh * 128:(hh + 1) * 128], mk.id16[:])
                mk.inc("phd", last)

        def ft(t):
            mk.entry(t, "t")
            t.wait_ge(mk.sems["dW"], dw0 + 16)
            for tc in range(nch):
                emit_q(t, tc)
                if tc >= 2:
                    emit_kvtr(t, tc - 2)
            emit_kvtr(t, nch - 2)
            emit_kvtr(t, nch - 1)
            mk.inc("mst", t.drain())

        def fv(v):
            mk.entry(v, "v")
            for tc in range(nch):
                v.wait_ge(mk.sems["pha"], pha0 + tc + 1)
                pq = mk.pd[tc % 2]
                ein = mk.f32(mk.C16, (tc % 2) * 2048, 1024)
                v.tensor_scalar(out=ein, in0=pq[:, :], scalar1=0.0,
                                scalar2=None, op0=OP.min)
                vt = mk.C16[:, 12288 + (tc % 3) * 1024:
                            12288 + (tc % 3) * 1024 + 1024]
                if tc >= 3:
                    v.wait_ge(mk.sems["phd"], phd0 + 2 * (tc - 2))
                v.tensor_copy(out=vt, in_=pq[:, :])
                mk.inc("kvb", v.drain())
                v.wait_ge(mk.sems["phc"], phc0 + tc + 1)
                e16 = mk.C16[:, 4096 + (tc % 2) * 1024:
                             4096 + (tc % 2) * 1024 + 1024]
                rl = mk.C16[:, 6144 + (tc % 2) * 1024:
                            6144 + (tc % 2) * 1024 + 1024]
                ftm = mk.C16[:, 8192 + (tc % 3) * 1024:
                             8192 + (tc % 3) * 1024 + 1024]
                v.tensor_tensor(out=ftm, in0=e16, in1=rl, op=OP.add)
                mk.inc("phb", v.drain())
            mk.inc("msv", v.drain())

        def emit_fevac(s, tc):
            for wave in range(2):
                s.wait_ge(mk.sems["phd"], phd0 + 2 * tc + wave + 1)
                dst = mk.B16[:, 0:HC * T].rearrange(
                    "p (h t) -> p h t", h=HC)[
                    :, wave * 4:(wave + 1) * 4, tc * 128:(tc + 1) * 128]
                s.copy(out=dst,
                       in_=mk.pdh(2, 1)[:, 0:512].rearrange(
                           "p (h t) -> p h t", h=4))
                mk.inc("phe", s.drain())

        def fs(s):
            mk.entry(s, "s")
            for tc in range(nch):
                s.wait_ge(mk.sems["kvb"], kvb0 + tc + 1)
                if tc >= 2:
                    s.wait_ge(mk.sems["phb"], phb0 + tc - 1)
                ein = mk.f32(mk.C16, (tc % 2) * 2048, 1024)
                e16 = mk.C16[:, 4096 + (tc % 2) * 1024:
                             4096 + (tc % 2) * 1024 + 1024]
                rl = mk.C16[:, 6144 + (tc % 2) * 1024:
                            6144 + (tc % 2) * 1024 + 1024]
                s.activation(out=rl, in_=mk.pd[tc % 2][:, :], func=ACT.Relu)
                s.drain()
                s.activation(out=e16, in_=ein, func=ACT.Exp)
                mk.inc("phc", s.drain())
                if tc >= 1:
                    emit_fevac(s, tc - 1)
            emit_fevac(s, nch - 1)
            mk.inc("mss", s.drain())

        self.blk.sync(fwl)
        self.blk.tensor(ft)
        self.blk.vector(fv)
        self.blk.scalar(fs)
        self.snap()

    def phase_kvx(self):
        mk = self
        phb0 = self.cnt["phb"]

        def fv(v):
            mk.entry(v, "v")
            kvp = mk.f32(mk.C16, 0, NH * 128)
            v.tensor_copy(out=kvp[:, 0:512], in_=mk.pb[0][:, :])
            v.tensor_copy(out=kvp[:, 512:1024], in_=mk.pb[1][:, :])
            ksp = mk.f32(mk.C16, 2 * NH * 128, NH)
            v.tensor_copy(out=ksp, in_=mk.pd[2][:, 0:NH])
            mk.inc("phb", v.drain())
            mk.inc("msv", v.drain())

        def fg(g):
            mk.entry(g, "g")
            g.wait_ge(mk.sems["phb"], phb0 + 1)
            kvp = mk.f32(mk.C16, 0, NH * 128 + NH)
            g.dma_start(out=mk.dkv_in, in_=kvp).then_inc(mk.sems["dG"], 16)
            mk.cnt["dG"] += 16
            mk.w(g, "dG")
            ic = mk.inc("cc", g.collective_compute(
                "AllReduce", OP.add,
                replica_groups=[[0, 1], [2, 3], [4, 5], [6, 7]],
                ins=[mk.dkv_in.opt()], outs=[mk.dkv_out.opt()]))
            g.wait_ge(mk.sems["cc"], ic)
            g.dma_start(out=mk.WAR[:, 0:NH * 128],
                        in_=mk.dkv_out[:, 0:NH * 128])\
                .then_inc(mk.sems["dG"], 16)
            mk.cnt["dG"] += 16
            g.dma_start(out=mk.f32(mk.WAR, 1024, NH),
                        in_=mk.dkv_out[:, NH * 128:NH * 128 + NH])\
                .then_inc(mk.sems["dG"], 16)
            mk.cnt["dG"] += 16
            mk.w(g, "dG")
            mk.inc("kvb", g.drain())
            mk.inc("msg", g.drain())

        kvb0 = self.cnt["kvb"]

        def fv2(v):
            v.wait_ge(mk.sems["kvb"], kvb0 + 1)
            ksbd = mk.WAR[:, 1040:1040 + NH * NH]
            v.memset(ksbd, 0.0)
            v.drain()
            ks = mk.f32(mk.WAR, 1024, NH)
            for h in range(NH):
                v.tensor_copy(out=ksbd[:, h * NH + h:h * NH + h + 1],
                              in_=ks[:, h:h + 1])
            mk.inc("phb", v.drain())
            mk.inc("msv", v.drain())

        self.blk.vector(fv)
        self.blk.gpsimd(fg)
        self.blk.vector(fv2)
        self.snap()

    def phase_m1(self):
        mk = self
        T, ntt = self.T, self.ntt
        pha0 = self.cnt["pha"]
        phb0 = self.cnt["phb"]
        uccsnap = self.rsnap("ucc", 2)
        p0lsnap = self.rsnap("p0l", 2)
        phc0 = self.cnt["phc"]
        phd0 = self.cnt["phd"]
        phe0 = self.cnt["phe"]
        phf0 = self.cnt["phf"]
        dz0 = self.cnt["dZ"]

        def ft(t):
            mk.entry(t, "t")
            for tt in range(ntt):
                if tt >= 1:
                    t.wait_ge(mk.sems["phb"], phb0 + tt)
                last = None
                for fc in range(HC):
                    last = t.matmul(
                        mk.pd[2][0:NH, 0:512],
                        lhsT=mk.WAR[:, 1040 + fc * NH:1040 + (fc + 1) * NH],
                        rhs=mk.B16[:, fc * T + tt * 512:fc * T + tt * 512 + 512],
                        start=(fc == 0), stop=(fc == HC - 1))
                mk.inc("pha", last)
                mk.rwait(t, "p0l", tt, 2, p0lsnap, 16)
                rb = 2560 if tt % 2 == 0 else 10240
                for h in range(NH):
                    hi = tt * NH + h
                    if hi >= 2:
                        t.wait_ge(mk.sems["phd"], phd0 + hi - 1)
                    last = t.matmul(
                        mk.pb[h % 2][:, :],
                        lhsT=mk.onesr16[0:1, :],
                        rhs=mk.C16[0:1, rb + h * 512:rb + h * 512 + 512],
                        start=True, stop=True)
                    mk.inc("phc", last)
                    t.wait_ge(mk.sems["phd"], phd0 + hi + 1)
                    if hi >= 4:
                        t.wait_ge(mk.sems["phf"], phf0 + hi - 3)
                    mmb = [mk.pd[0][:, 0:512], mk.pd[0][:, 512:1024],
                           mk.pd[1][:, 0:512], mk.pd[1][:, 512:1024]][hi % 4]
                    last = t.matmul(
                        mmb,
                        lhsT=mk.WAR[:, h * 128:(h + 1) * 128],
                        rhs=mk.C16[:, 6656 + (hi % 2) * 512:
                                   6656 + (hi % 2) * 512 + 512],
                        start=True, stop=True)
                    mk.inc("phe", last)
            mk.inc("mst", t.drain())

        def fv(v):
            mk.entry(v, "v")
            for tt in range(ntt):
                v.wait_ge(mk.sems["pha"], pha0 + tt + 1)
                if tt >= 1:
                    mk.rwait(v, "ucc", tt - 1, 2, uccsnap, 16)
                zf = mk.f32(mk.C16, 0, 512)[0:NH, :]
                v.tensor_scalar(out=zf, in0=mk.pd[2][0:NH, 0:512],
                                scalar1=ATTN_EPS, scalar2=None, op0=OP.add)
                v.drain()
                zf2 = mk.f32(mk.C16, 1024, 512)[0:NH, :]
                v.reciprocal(out=zf2, in_=zf)
                v.drain()
                v.tensor_scalar(out=mk.C16[0:NH, 2048:2560],
                                in0=zf2, scalar1=float(mk.L), scalar2=None,
                                op0=OP.mult)
                mk.inc("phb", v.drain())
                for h in range(NH):
                    hi = tt * NH + h
                    v.wait_ge(mk.sems["phc"], phc0 + hi + 1)
                    if hi >= 2:
                        v.wait_ge(mk.sems["phe"], phe0 + hi - 1)
                    fz = mk.C16[:, 6656 + (hi % 2) * 512:
                                6656 + (hi % 2) * 512 + 512]
                    v.tensor_tensor(
                        out=fz,
                        in0=mk.B16[:, h * T + tt * 512:h * T + tt * 512 + 512],
                        in1=mk.pb[h % 2][:, :], op=OP.mult)
                    mk.inc("phd", v.drain())
            mk.inc("msv", v.drain())

        def fz(sync):
            mk.entry(sync, "y")
            for tt in range(ntt):
                sync.wait_ge(mk.sems["phb"], phb0 + tt + 1)
                dsl = mk.dzl[0:1, (tt % 2) * NH * 512:
                             (tt % 2 + 1) * NH * 512]
                mk.rissue("ucc", tt, 2, sync.dma_start(
                    out=dsl.rearrange("o (h t) -> (o h) t", h=NH),
                    in_=mk.C16[0:NH, 2048:2560]))
                mk.rwait(sync, "ucc", tt, 2, uccsnap, 16)
                if tt >= 2:
                    sync.wait_ge(mk.sems["phc"], phc0 + NH * (tt - 1))
                rb = 2560 if tt % 2 == 0 else 10240
                mk.rissue("p0l", tt, 2, sync.dma_start(
                    out=mk.C16[0:1, rb:rb + NH * 512], in_=dsl))
            mk.inc("msy", sync.nop())

        def fs(s):
            mk.entry(s, "s")
            for tt in range(ntt):
                for h in range(NH):
                    hi = tt * NH + h
                    s.wait_ge(mk.sems["phe"], phe0 + hi + 1)
                    mmb = [mk.pd[0][:, 0:512], mk.pd[0][:, 512:1024],
                           mk.pd[1][:, 0:512], mk.pd[1][:, 512:1024]][hi % 4]
                    s.activation(out=mk.A16[:, h * T + tt * 512:
                                            h * T + tt * 512 + 512],
                                 in_=mmb, func=ACT.Identity,
                                 scale=1.0 / float(mk.L))
                    mk.inc("phf", s.drain())
            mk.inc("mss", s.drain())

        self.blk.tensor(ft)
        self.blk.vector(fv)
        self.blk.sync(fz)
        self.blk.scalar(fs)
        self.snap()

    def phase_m2a(self):
        mk = self
        T, ntt, nch = self.T, self.ntt, self.nch
        nhalf = 2 if ntt >= 2 else 1
        tph = ntt // nhalf
        dw0 = self.cnt["dW"]
        pha0 = self.cnt["pha"]
        phb0 = self.cnt["phb"]
        phc0 = self.cnt["phc"]
        phd0 = self.cnt["phd"]
        dyssnap = self.rsnap("dys", 2)
        lnv0 = self.cnt["lnv"]
        lns0 = self.cnt["lns"]
        groups = [(oc, hf) for oc in range(HC) for hf in range(nhalf)]
        banksets = [
            [self.pb[0][:, :], self.pb[1][:, :],
             self.pd[2][:, 0:512], self.pd[2][:, 512:1024]][:tph],
            [self.pd[0][:, 0:512], self.pd[0][:, 512:1024],
             self.pd[1][:, 0:512], self.pd[1][:, 512:1024]][:tph],
        ]

        def fwl(sync):
            mk.entry(sync, "y")
            sync.wait_ge(mk.sems["dL"], mk.dg_p0)
            sync.dma_start(
                out=mk.WAR[:, 0:8192].rearrange("p (a o) -> p a o", a=HC),
                in_=mk.dwm16.rearrange("(a p) o -> p a o", p=128))\
                .then_inc(mk.sems["dW"], 16)
            mk.cnt["dW"] += 16
            mk.inc("msy", sync.nop())

        def ft(t):
            mk.entry(t, "t")
            t.wait_ge(mk.sems["dW"], dw0 + 16)
            for gi, (oc, hf) in enumerate(groups):
                if gi >= 2:
                    t.wait_ge(mk.sems["phb"], phb0 + gi - 1)
                bs = banksets[gi % 2]
                last = None
                for vc in range(HC):
                    for j in range(tph):
                        tt = hf * tph + j
                        last = t.matmul(
                            bs[j],
                            lhsT=mk.WAR[:, vc * 1024 + oc * 128:
                                        vc * 1024 + oc * 128 + 128],
                            rhs=mk.A16[:, vc * T + tt * 512:
                                       vc * T + tt * 512 + 512],
                            start=(vc == 0), stop=(vc == HC - 1))
                mk.inc("pha", last)
            t.wait_ge(mk.sems["phb"], phb0 + len(groups))
            for ch in range(nch):
                if ch >= 2:
                    t.wait_ge(mk.sems["phd"], phd0 + ch - 1)
                last = None
                for oc in range(HC):
                    last = t.transpose(
                        mk.pdh(ch % 2, 0)[:, oc * 128:(oc + 1) * 128],
                        mk.B16[:, oc * T + ch * 128:oc * T + ch * 128 + 128],
                        mk.id16[:])
                mk.inc("phc", last)
            mk.inc("mst", t.drain())

        def fs(s):
            mk.entry(s, "s")
            for gi, (oc, hf) in enumerate(groups):
                s.wait_ge(mk.sems["pha"], pha0 + gi + 1)
                bs = banksets[gi % 2]
                last = None
                for j in range(tph):
                    tt = hf * tph + j
                    last = s.copy(out=mk.B16[:, oc * T + tt * 512:
                                             oc * T + tt * 512 + 512],
                                  in_=bs[j])
                mk.inc("phb", last)
            mk.inc("mss", s.drain())

        phg0 = self.cnt["phg"]

        def fvt(v):
            mk.entry(v, "v")
            for ch in range(nch):
                v.wait_ge(mk.sems["phc"], phc0 + ch + 1)
                ptm = mk.pdh(ch % 2, 0)
                k = ch % mk.NLN
                mk.ln_v_pre(v, k, mk.LNb, ptm)
                ya = mk.C16[:, (ch % 2) * 2048:(ch % 2) * 2048 + 1024]
                if ch >= 2:
                    v.wait_ge(mk.sems["phg"], phg0 + ch - 1)
                mk.ln_v_apply(v, k, mk.LNb, ptm, ya, lns0 + ch + 1)
                mk.inc("phd", v.drain())
            mk.inc("msv", v.drain())

        def fgt(g):
            mk.entry(g, "g")
            for ch in range(nch):
                g.wait_ge(mk.sems["phd"], phd0 + ch + 1)
                ya = mk.C16[:, (ch % 2) * 2048:(ch % 2) * 2048 + 1024]
                yb = mk.C16[:, (ch % 2) * 2048 + 1024:(ch % 2) * 2048 + 2048]
                if ch >= 2:
                    mk.rwait(g, "dys", ch, 2, dyssnap, 16, occ=ch // 2 - 1)
                g.tensor_tensor(out=yb, in0=ya, in1=mk.g1r[:], op=OP.mult)
                g.drain()
                g.tensor_tensor(out=yb, in0=yb, in1=mk.b1r[:], op=OP.add)
                mk.inc("phg", g.drain())
            mk.inc("msg", g.drain())

        def fst(s):
            for ch in range(nch):
                s.wait_ge(mk.sems["phc"], phc0 + ch + 1)
                ptm = mk.pdh(ch % 2, 0)
                sqscr = mk.f32(mk.C16, 4096 + (ch % 2) * 2048, 1024)
                mk.ln_s(s, ch % mk.NLN, mk.LNb, ptm, sqscr,
                        lnv0 + ch + 1)
            mk.inc("mss", s.drain())

        def fys(sync):
            for ch in range(nch):
                sync.wait_ge(mk.sems["phg"], phg0 + ch + 1)
                yb = mk.C16[:, (ch % 2) * 2048 + 1024:(ch % 2) * 2048 + 2048]
                mk.rissue("dys", ch, 2, sync.dma_start(
                    out=mk.dya1[ch * 128:(ch + 1) * 128, :], in_=yb))
            mk.rtotal(sync, "dys", 2)
            mk.inc("msy", sync.nop())

        self.blk.sync(fwl)
        self.blk.tensor(ft)
        self.blk.scalar(fs)
        self.blk.vector(fvt)
        self.blk.gpsimd(fgt)
        self.blk.scalar(fst)
        self.blk.sync(fys)
        self.snap()

    def phase_m2b(self):
        mk = self
        T, ntt = self.T, self.ntt
        nhalf = 2 if ntt >= 2 else 1
        tph = ntt // nhalf
        stw = tph * 512
        dw0 = self.cnt["dW"]
        pha0 = self.cnt["pha"]
        phb0 = self.cnt["phb"]
        ugssnap = self.rsnap("ugs", 2)
        groups = [(oc2, hf) for oc2 in range(KC2) for hf in range(nhalf)]
        banksets = [
            [self.pd[0][:, 0:512], self.pd[0][:, 512:1024],
             self.pd[1][:, 0:512], self.pd[1][:, 512:1024]][:tph],
            [self.pd[2][:, 0:512], self.pd[2][:, 512:1024],
             self.pb[0][:, :], self.pb[1][:, :]][:tph],
        ]

        def fwl(sync):
            mk.entry(sync, "y")
            sync.wait_ge(mk.sems["dL"], mk.dg_p0)
            sync.dma_start(
                out=mk.WAR[:, 0:16384].rearrange("p (a o) -> p a o", a=HC),
                in_=mk.dw116.rearrange("(a p) o -> p a o", p=128))\
                .then_inc(mk.sems["dW"], 16)
            mk.cnt["dW"] += 16
            mk.inc("msy", sync.nop())

        def ft(t):
            mk.entry(t, "t")
            t.wait_ge(mk.sems["dW"], dw0 + 16)
            for gi, (oc2, hf) in enumerate(groups):
                if gi >= 2:
                    t.wait_ge(mk.sems["phb"], phb0 + gi - 1)
                bs = banksets[gi % 2]
                last = None
                for vc in range(HC):
                    for j in range(tph):
                        tt = hf * tph + j
                        last = t.matmul(
                            bs[j],
                            lhsT=mk.WAR[:, vc * 2048 + oc2 * 128:
                                        vc * 2048 + oc2 * 128 + 128],
                            rhs=mk.B16[:, vc * T + tt * 512:
                                       vc * T + tt * 512 + 512],
                            start=(vc == 0), stop=(vc == HC - 1))
                mk.inc("pha", last)
            mk.inc("mst", t.drain())

        def fs(s):
            mk.entry(s, "s")
            for gi, (oc2, hf) in enumerate(groups):
                s.wait_ge(mk.sems["pha"], pha0 + gi + 1)
                if gi >= 2:
                    mk.rwait(s, "ugs", gi, 2, ugssnap, 16, occ=gi // 2 - 1)
                stg = mk.C16[:, (gi % 2) * stw:(gi % 2) * stw + stw]
                bs = banksets[gi % 2]
                last = None
                for j in range(tph):
                    last = s.activation(out=stg[:, j * 512:(j + 1) * 512],
                                        in_=bs[j], func=ACT.Relu)
                mk.inc("phb", last)
            mk.inc("mss", s.drain())

        def fsp(sync):
            for gi, (oc2, hf) in enumerate(groups):
                sync.wait_ge(mk.sems["phb"], phb0 + gi + 1)
                stg = mk.C16[:, (gi % 2) * stw:(gi % 2) * stw + stw]
                mk.rissue("ugs", gi, 2, sync.dma_start(
                    out=mk.du1sp[oc2 * 128:(oc2 + 1) * 128,
                                 hf * stw:hf * stw + stw], in_=stg))
            mk.rtotal(sync, "ugs", 2)
            mk.inc("msy", sync.nop())

        self.blk.sync(fwl)
        self.blk.tensor(ft)
        self.blk.scalar(fs)
        self.blk.sync(fsp)
        self.snap()

    def phase_m2c(self):
        mk = self
        T, ntt, nch = self.T, self.ntt, self.nch
        dw0 = self.cnt["dW"]
        dxlsnap = self.rsnap("dxl", 2)
        dxssnap = self.rsnap("dxs", 2)
        pha0 = self.cnt["pha"]
        phb0 = self.cnt["phb"]
        phc0 = self.cnt["phc"]
        phd0 = self.cnt["phd"]
        dx0_ = self.cnt["dX"]
        lnv0 = self.cnt["lnv"]
        lns0 = self.cnt["lns"]

        def fwl(sync):
            mk.entry(sync, "y")
            sync.wait_ge(mk.sems["dL"], mk.dg_p0)
            sync.dma_start(
                out=mk.WAR[:, 0:16384].rearrange("p (a o) -> p a o", a=KC2),
                in_=mk.dw216.rearrange("(a p) o -> p a o", p=128))\
                .then_inc(mk.sems["dW"], 16)
            mk.cnt["dW"] += 16
            for kc in range(KC2):
                buf = mk.A16 if kc < 8 else mk.B16
                sync.dma_start(out=buf[:, (kc % 8) * T:(kc % 8) * T + T],
                               in_=mk.du1sp[kc * 128:(kc + 1) * 128, :])\
                    .then_inc(mk.sems["dW"], 16)
                mk.cnt["dW"] += 16
            mk.inc("msy", sync.nop())

        def xstore(sync, ch, phe_base):
            sync.wait_ge(mk.sems["phe"], phe_base + ch + 1)
            xa = mk.f32(mk.C16, 8192 + (ch % 2) * 2048, 1024)
            mk.rissue("dxs", ch, 2, sync.dma_start(
                out=mk.dxx2[ch * 128:(ch + 1) * 128, :], in_=xa))

        phe_base = self.cnt["phe"]

        def fxl(sync):
            for ch in range(nch):
                if ch >= 2:
                    sync.wait_ge(mk.sems["phe"], phe_base + ch - 1)
                    mk.rwait(sync, "dxs", ch - 2, 2, dxssnap, 16)
                xa = mk.f32(mk.C16, 8192 + (ch % 2) * 2048, 1024)
                mk.rissue("dxl", ch, 2, sync.dma_start(
                    out=xa, in_=mk.dxx1[ch * 128:(ch + 1) * 128, :]))
                y2t = mk.C16[:, 12288 + (ch % 2) * 1024:
                             12288 + (ch % 2) * 1024 + 1024]
                mk.rissue("dxl", ch, 2, sync.dma_start(
                    out=y2t, in_=mk.dy2sp[ch * 128:(ch + 1) * 128, :]))
                ya1t = mk.C16[:, 14336 + (ch % 2) * 1024:
                              14336 + (ch % 2) * 1024 + 1024]
                mk.rissue("dxl", ch, 2, sync.dma_start(
                    out=ya1t, in_=mk.dya1[ch * 128:(ch + 1) * 128, :]))
                if ch >= 1:
                    xstore(sync, ch - 1, phe_base)
            xstore(sync, nch - 1, phe_base)
            mk.rtotal(sync, "dxs", 2)
            mk.inc("msy", sync.nop())

        banks4 = [self.pd[2][:, 0:512], self.pd[2][:, 512:1024],
                  self.pb[0][:, :], self.pb[1][:, :]]

        def ft(t):
            mk.entry(t, "t")
            t.wait_ge(mk.sems["dW"], dw0 + 16 * (KC2 + 1))
            gi = 0
            for tt in range(ntt):
                for oc in range(HC):
                    gi += 1
                    if gi > 4:
                        t.wait_ge(mk.sems["phb"], phb0 + gi - 4)
                    last = None
                    for kc in range(KC2):
                        buf = mk.A16 if kc < 8 else mk.B16
                        last = t.matmul(
                            banks4[gi % 4],
                            lhsT=mk.WAR[:, kc * 1024 + oc * 128:
                                        kc * 1024 + oc * 128 + 128],
                            rhs=buf[:, (kc % 8) * T + tt * 512:
                                    (kc % 8) * T + tt * 512 + 512],
                            start=(kc == 0), stop=(kc == KC2 - 1))
                    mk.inc("pha", last)
                for c2 in range(4):
                    ch = tt * 4 + c2
                    t.wait_ge(mk.sems["phb"], phb0 + (tt + 1) * HC)
                    if ch >= 4:
                        t.wait_ge(mk.sems["phe"], phe_base + ch - 3)
                    last = None
                    for oc in range(HC):
                        cc = (tt % 2) * 4096 + c2 * 1024 + oc * 128
                        last = t.transpose(
                            mk.pdh(ch % 2, (ch // 2) % 2)[:, oc * 128:
                                                          (oc + 1) * 128],
                            mk.C16[:, cc:cc + 128], mk.id16[:])
                    mk.inc("phc", last)
            mk.inc("mst", t.drain())

        def fs(s):
            mk.entry(s, "s")
            gi = 0
            for tt in range(ntt):
                for oc in range(HC):
                    gi += 1
                    s.wait_ge(mk.sems["pha"], pha0 + gi)
                    if tt >= 2 and oc == 0:
                        s.wait_ge(mk.sems["phc"], phc0 + (tt - 2 + 1) * 4)
                    base = (tt % 2) * 4096
                    stg = mk.C16[:, base:base + 4096].rearrange(
                        "p (c q) -> p c q", q=1024)[:, :, oc * 128:(oc + 1) * 128]
                    s.copy(out=stg,
                           in_=banks4[gi % 4].rearrange("p (c o) -> p c o",
                                                        o=128))
                    mk.inc("phb", s.drain())
                for c2 in range(4):
                    ch = tt * 4 + c2
                    s.wait_ge(mk.sems["phc"], phc0 + ch + 1)
                    ptm = mk.pdh(ch % 2, (ch // 2) % 2)
                    sqscr = mk.C16[:, (tt % 2) * 4096 + c2 * 1024:
                                   (tt % 2) * 4096 + c2 * 1024 + 1024]
                    mk.ln_s(s, ch % mk.NLN, mk.LNb, ptm, sqscr,
                            lnv0 + ch + 1)
            mk.inc("mss", s.drain())

        def fv(v):
            mk.entry(v, "v")
            for tt in range(ntt):
                for c2 in range(4):
                    ch = tt * 4 + c2
                    v.wait_ge(mk.sems["phc"], phc0 + ch + 1)
                    mk.rwait(v, "dxl", ch, 2, dxlsnap, 48)
                    ptm = mk.pdh(ch % 2, (ch // 2) % 2)
                    k = ch % mk.NLN
                    mk.ln_v_pre(v, k, mk.LNb, ptm)
                    ya2 = mk.C16[:, (tt % 2) * 4096 + c2 * 1024:
                                 (tt % 2) * 4096 + c2 * 1024 + 1024]
                    mk.ln_v_apply(v, k, mk.LNb, ptm, ya2,
                                  lns0 + ch + 1)
                    v.drain()
                    v.tensor_tensor(out=ya2, in0=ya2, in1=mk.g2r[:],
                                    op=OP.mult)
                    v.drain()
                    v.tensor_tensor(out=ya2, in0=ya2, in1=mk.b2r[:],
                                    op=OP.add)
                    v.drain()
                    xa = mk.f32(mk.C16, 8192 + (ch % 2) * 2048, 1024)
                    y2t = mk.C16[:, 12288 + (ch % 2) * 1024:
                                 12288 + (ch % 2) * 1024 + 1024]
                    ya1t = mk.C16[:, 14336 + (ch % 2) * 1024:
                                  14336 + (ch % 2) * 1024 + 1024]
                    if ch >= 2:
                        mk.rwait(v, "dxs", ch, 2, dxssnap, 16,
                                 occ=ch // 2 - 1)
                    v.tensor_tensor(out=xa, in0=xa, in1=y2t, op=OP.add)
                    v.drain()
                    v.tensor_tensor(out=xa, in0=xa, in1=ya1t, op=OP.add)
                    v.drain()
                    v.tensor_tensor(out=xa, in0=xa, in1=ya2, op=OP.add)
                    mk.inc("phe", v.drain())
            mk.inc("msv", v.drain())

        self.blk.sync(fwl)
        self.blk.sync(fxl)
        self.blk.tensor(ft)
        self.blk.scalar(fs)
        self.blk.vector(fv)
        self.snap()

    def finale(self):
        mk = self

        def f(sync):
            mk.entry(sync, "y")
            sync.nop()

        self.blk.sync(f)


def build_program(T, L):
    mk = MK(T, L)
    nc = mk.build()
    return nc


def make_wsh(wup, wgate, wdown, c):
    parts = []
    for j in range(3 * c, 3 * c + 3):
        if j < 8:
            u = np.concatenate([wup[(3 * j + s) * 128:(3 * j + s + 1) * 128, :]
                                for s in range(3)], axis=1)
        elif j < 16:
            jj = j - 8
            u = np.concatenate(
                [wgate[(3 * jj + s) * 128:(3 * jj + s + 1) * 128, :]
                 for s in range(3)], axis=1)
        else:
            u = wdown[(j - 16) * 128:(j - 16 + 1) * 128, :]
        parts.append(u)
    return np.ascontiguousarray(np.concatenate(parts, axis=0))


def kernel(**inputs):
    x = np.ascontiguousarray(inputs["L0_attn_x"])
    B, L, _ = x.shape
    T = L // 2
    nc = build_program(T, L)
    mybir.codegen_inst_isa_subclasses(nc)
    ident = np.eye(128, dtype=np.float32)
    wupf = np.ascontiguousarray(inputs["L0_ffn_w_up"], dtype=np.float32)
    wgatef = np.ascontiguousarray(inputs["L0_ffn_w_gate"], dtype=np.float32)
    wdownf = np.ascontiguousarray(inputs["L0_ffn_w_down"], dtype=np.float32)
    in_maps = []
    for c in range(NCORES):
        b, h = c // 2, c % 2
        in_maps.append({
            "wsh": make_wsh(wupf, wgatef, wdownf, c),
            "x0": np.ascontiguousarray(x[b, h * T:(h + 1) * T, :]),
            "wq": np.ascontiguousarray(inputs["L0_attn_Wq"], dtype=np.float32),
            "wm": np.ascontiguousarray(inputs["L0_attn_Wm"], dtype=np.float32),
            "w1": np.ascontiguousarray(inputs["L0_attn_W1"], dtype=np.float32),
            "w2": np.ascontiguousarray(inputs["L0_attn_W2"], dtype=np.float32),
            "g1": np.ascontiguousarray(inputs["L0_attn_g1"],
                                       dtype=np.float32).reshape(1, D),
            "b1": np.ascontiguousarray(inputs["L0_attn_b1"],
                                       dtype=np.float32).reshape(1, D),
            "g2": np.ascontiguousarray(inputs["L0_attn_g2"],
                                       dtype=np.float32).reshape(1, D),
            "b2": np.ascontiguousarray(inputs["L0_attn_b2"],
                                       dtype=np.float32).reshape(1, D),
            "ident": ident,
        })
    res = run_bass_kernel_spmd(nc, in_maps, core_ids=list(range(NCORES)))
    out = np.empty((B, L, D), dtype=np.float32)
    for c in range(NCORES):
        b, h = c // 2, c % 2
        out[b, h * T:(h + 1) * T, :] = np.asarray(res.results[c]["out"])
    return out
